# version 17
# baseline (speedup 1.0000x reference)
import numpy as np
from contextlib import ExitStack

import concourse.bass as bass
import concourse.mybir as mybir

F32 = mybir.dt.float32
BF16 = mybir.dt.bfloat16
AF = mybir.ActivationFunctionType
ALU = mybir.AluOpType
AXX = mybir.AxisListType.X

D = 1024
DI = 2816
NDT = D // 128
NIT = DI // 128
C2V = float(2 ** 22 + 1)
CBV = float(1.5 * 2 ** 23)

ENGS = ("sp", "act", "dve", "pe", "pool")
_EMAP = {"sp": "sync", "act": "scalar", "dve": "vector", "pe": "tensor",
         "pool": "gpsimd"}


class Buf:
    __slots__ = ("name", "w", "r")

    def __init__(self, name):
        self.name = name
        self.w = None
        self.r = []


class Node:
    __slots__ = ("idx", "eng", "fns", "deps", "dur", "is_dma", "sem", "val")

    def __init__(self, idx, eng, fns, deps, dur, is_dma=False):
        self.idx = idx
        self.eng = eng
        self.fns = fns
        self.deps = deps
        self.dur = dur
        self.is_dma = is_dma
        self.sem = None
        self.val = None


class Plan:

    def __init__(self, nc):
        self.nc = nc
        self.nodes = []
        self.sem = {}
        self._es = ExitStack()
        for e in ENGS:
            self.sem[e] = self._es.enter_context(nc.semaphore(f"sem_{e}"))
        self.NDMA = 24
        self.dpool = {"hw": list(range(16)), "sw": list(range(16, 24))}
        for i in range(self.NDMA):
            self.sem[f"dp{i}"] = self._es.enter_context(
                nc.semaphore(f"sem_dp{i}"))
        self._barrier_idx = -1

    def _mknode(self, eng, fns, reads, writes, dur, is_dma=False):
        deps = set()
        for b in reads:
            if b.w is not None:
                deps.add(b.w)
        for b in writes:
            if b.w is not None:
                deps.add(b.w)
            deps.update(b.r)
        if self._barrier_idx >= 0:
            deps.add(self._barrier_idx)
        idx = len(self.nodes)
        n = Node(idx, eng, fns, deps, dur, is_dma)
        self.nodes.append(n)
        for b in writes:
            b.w = idx
            b.r = []
        for b in reads:
            b.r.append(idx)
        return idx

    def op(self, eng, fn, reads=(), writes=(), dur=600):
        return self._mknode(eng, [fn], reads, writes, dur)

    def group(self, eng, fns, reads=(), writes=(), dur=None):
        if dur is None:
            dur = 250 * len(fns)
        return self._mknode(eng, list(fns), reads, writes, dur)

    def dma(self, queue, fn, reads=(), writes=(), dur=4000):
        return self._mknode(queue, [fn], reads, writes, dur, is_dma=True)

    def barrier(self):
        idx = self._mknode("sp", None, [], [], 0)
        n = self.nodes[idx]
        n.deps = set(range(idx))
        self._barrier_idx = idx
        return idx

    def emit(self):
        import heapq
        nodes = self.nodes
        N = len(nodes)
        nsucc = [[] for _ in range(N)]
        ndep = [0] * N
        for n in nodes:
            ndep[n.idx] = len(n.deps)
            for d in n.deps:
                nsucc[d].append(n.idx)
        eng_clock = {e: 0.0 for e in ENGS}
        finish = [0.0] * N
        ready_t = [0.0] * N
        heap = []
        for n in nodes:
            if ndep[n.idx] == 0:
                heapq.heappush(heap, (0.0, n.idx))
        order = {e: [] for e in ENGS}
        done = 0
        WINDOW = 4096
        scheduled = [False] * N
        min_unsched = 0
        poolq = {e: [] for e in ENGS}
        while done < N:
            while heap:
                rt, i = heapq.heappop(heap)
                heapq.heappush(poolq[nodes[i].eng], (nodes[i].idx,))
            gate = min_unsched + WINDOW
            cands = [(eng_clock[e], e) for e in ENGS
                     if poolq[e] and poolq[e][0][0] <= gate]
            if not cands:
                cands = [(eng_clock[e], e) for e in ENGS
                         if poolq[e] and poolq[e][0][0] == min_unsched]
                if not cands:
                    raise RuntimeError("scheduler stuck")
            _, e = min(cands)
            (i,) = heapq.heappop(poolq[e])
            scheduled[i] = True
            while min_unsched < N and scheduled[min_unsched]:
                min_unsched += 1
            n = nodes[i]
            start = max(eng_clock[e], ready_t[i])
            dma_lat = 1500 if n.is_dma else 0
            finish[i] = start + n.dur + dma_lat
            eng_clock[e] = start + (150 if n.is_dma else n.dur)
            order[e].append(i)
            done += 1
            for s in nsucc[i]:
                ready_t[s] = max(ready_t[s], finish[i])
                ndep[s] -= 1
                if ndep[s] == 0:
                    heapq.heappush(heap, (ready_t[s], s))
        eng_pos = {e: 0 for e in ENGS}
        dnext = {"hw": 0, "sw": 0}
        dcnt = [0] * self.NDMA
        dma_prev = {}
        for e in ENGS:
            for i in order[e]:
                n = nodes[i]
                if n.is_dma:
                    pool = "sw" if e == "pool" else "hw"
                    lst = self.dpool[pool]
                    di = lst[dnext[pool]]
                    dnext[pool] = (dnext[pool] + 1) % len(lst)
                    dcnt[di] += 16
                    n.sem = f"dp{di}"
                    n.val = dcnt[di]
                elif n.fns is not None:
                    eng_pos[e] += 1
                    n.sem = e
                    n.val = eng_pos[e]
        waited = {e: {} for e in ENGS}

        def emit_wait(e, semkey, val):
            if val <= waited[e].get(semkey, 0):
                return
            waited[e][semkey] = val
            g = getattr(self.nc, _EMAP[e])
            g.wait_ge(self.sem[semkey], val)

        self.final_dma = [(f"dp{i}", dcnt[i]) for i in range(self.NDMA) if dcnt[i]]
        barrier_cache = {}

        def barrier_maxes(bidx):
            if bidx not in barrier_cache:
                mx = {}
                for d2 in nodes[bidx].deps:
                    dn2 = nodes[d2]
                    if dn2.sem is not None:
                        if dn2.val > mx.get(dn2.sem, 0):
                            mx[dn2.sem] = dn2.val
                    elif dn2.fns is None and d2 in barrier_cache:
                        for k, v in barrier_cache[d2].items():
                            if v > mx.get(k, 0):
                                mx[k] = v
                barrier_cache[bidx] = mx
            return barrier_cache[bidx]

        def run_engine(e):
            prev_dma_on_sem = {}
            for i in order[e]:
                n = nodes[i]
                if n.is_dma:
                    prev = prev_dma_on_sem.get(n.sem)
                    if prev is not None:
                        emit_wait(e, n.sem, prev)
                    elif n.val > 16:
                        emit_wait(e, n.sem, n.val - 16)
                    prev_dma_on_sem[n.sem] = n.val
                for d in sorted(n.deps):
                    dn = nodes[d]
                    if dn.sem is None:
                        if dn.fns is None:
                            for k, v in sorted(barrier_maxes(d).items()):
                                emit_wait(e, k, v)
                        continue
                    emit_wait(e, dn.sem, dn.val)
                if not n.fns:
                    continue
                h = self.sem[n.sem]
                inc = 16 if n.is_dma else 1
                for j, fn in enumerate(n.fns):
                    if j == len(n.fns) - 1:
                        fn().then_inc(h, inc)
                    else:
                        fn()
        self._run_engine = run_engine

    def finish_waits(self, e):
        g = getattr(self.nc, _EMAP[e])
        for key, val in self.final_dma:
            g.wait_ge(self.sem[key], val)


def build_nc(L=8192, K=10, CH=512):
    nc = bass.Bass("TRN2", target_bir_lowering=False, debug=False)
    NCH = L // CH
    TPC = CH // 128
    assert TPC == 4

    x0_d = nc.dram_tensor("x0", [L, D], F32, kind="ExternalInput").ap()
    win = {}
    for nm, shp in [
        ("L0_attn_a_phazor", [D, 2]), ("L0_attn_a_phazor_init", [D, 2]),
        ("L0_attn_a_last_conv_re", [D]), ("L0_attn_a_last_conv_im", [D]),
        ("L0_attn_b_decay", [D]), ("L0_attn_b_Wr", [D, D]),
        ("L0_attn_b_br", [D]), ("L0_attn_b_Wg", [D, D]), ("L0_attn_b_bg", [D]),
        ("L0_attn_b_Wo", [D, D]), ("L0_attn_b_bo", [D]),
        ("L0_attn_b_ln_w", [D]), ("L0_attn_b_ln_b", [D]),
        ("L0_ffn_w_gate", [DI, D]), ("L0_ffn_w_up", [DI, D]),
        ("L0_ffn_w_down", [D, DI]),
    ]:
        win[nm] = nc.dram_tensor(nm, shp, F32, kind="ExternalInput").ap()
    iden_bf_d = nc.dram_tensor("iden_bf", [128, 128], BF16, kind="ExternalInput").ap()
    iden_f32_d = nc.dram_tensor("iden_f32", [128, 128], F32, kind="ExternalInput").ap()
    out_d = nc.dram_tensor("out", [L, D], F32, kind="ExternalOutput").ap()

    xs1 = nc.dram_tensor("xs1", [L, D], F32, kind="Internal").ap()
    xs2 = nc.dram_tensor("xs2", [L, D], F32, kind="Internal").ap()
    xs3 = nc.dram_tensor("xs3", [L, D], F32, kind="Internal").ap()
    wgq_d = nc.dram_tensor("wgq_d", [128, NDT * DI], BF16, kind="Internal").ap()
    wuq_d = nc.dram_tensor("wuq_d", [128, NDT * DI], BF16, kind="Internal").ap()
    wdq_d = nc.dram_tensor("wdq_d", [128, NIT * D], BF16, kind="Internal").ap()
    wgT_d = nc.dram_tensor("wgT_d", [128, NDT * D], BF16, kind="Internal").ap()
    wrT_d = nc.dram_tensor("wrT_d", [128, NDT * D], BF16, kind="Internal").ap()
    woT_d = nc.dram_tensor("woT_d", [128, NDT * D], BF16, kind="Internal").ap()

    plan = Plan(nc)
    outer = ExitStack()

    def mksb(esx, name, shape, dt):
        t = esx.enter_context(nc.sbuf_tensor(name, shape, dt))
        return t, Buf(name)

    def mkps(esx, name, shape, dt):
        t = esx.enter_context(nc.psum_tensor(name, shape, dt))
        return t, Buf(name)

    idbf, b_idbf = mksb(outer, "idbf", [128, 128], BF16)
    idf, b_idf = mksb(outer, "idf", [128, 128], F32)
    pv, b_pv = mksb(outer, "pv", [128, 64], F32)
    (COL_A, COL_OM, COL_BR, COL_BO, COL_LNW, COL_LNB, COL_R) = (
        0, 8, 16, 24, 32, 40, 48)
    bgv, b_bgv = mksb(outer, "bgv", [128, NDT], F32)
    carry, b_carry = mksb(outer, "carry", [128, 2 * NDT], F32)
    xh2t, b_xh2t = mksb(outer, "xh", [128, 2 * D], F32)
    b_xh2 = [Buf("xh_s0"), Buf("xh_s1")]
    lnt, b_lnt = mksb(outer, "lnt", [128, 32], F32)
    lsq, b_lsq = mksb(outer, "lsq", [128, D], F32)
    tUc, b_tUc = mksb(outer, "tUc", [128, 2 * NDT], F32)
    q0t, b_q0t = mksb(outer, "q0t", [128, 2 * NDT], F32)
    tp_psL = [mkps(outer, "tp_ps0", [128, 512], BF16)]
    tpf_psL = [mkps(outer, "tpf_ps0", [128, 512], F32)]
    _tpctr = [0]
    tp_ps, b_tp = tp_psL[0]
    tpf_ps, b_tpf = tpf_psL[0]

    def dma_in(queue, dst_ap, src_ap, wbuf, rbufs=()):
        plan.dma(queue,
                 lambda d=dst_ap, s=src_ap: getattr(nc, _EMAP[queue]).dma_start(out=d, in_=s),
                 reads=list(rbufs), writes=[wbuf])

    dma_in("sp", idbf[:], iden_bf_d[:], b_idbf)
    dma_in("sp", idf[:], iden_f32_d[:], b_idf)

    def tp_group_bf(srcs, b_srcs, dst_ap, b_dst):
        ps, b_ps = tp_psL[_tpctr[0] % len(tp_psL)]
        _tpctr[0] += 1
        n = len(srcs)
        fns = [lambda s=s, i=i, ps=ps: nc.tensor.transpose(
            out=ps[:, i * 128:(i + 1) * 128], in_=s, identity=idbf[:])
            for i, s in enumerate(srcs)]
        plan.group("pe", fns, reads=list(b_srcs) + [b_idbf], writes=[b_ps],
                   dur=300 * n)
        plan.op("act", lambda n=n, ps=ps: nc.scalar.activation(
            out=dst_ap, in_=ps[:, :n * 128], func=AF.Copy),
            reads=[b_ps], writes=[b_dst], dur=600)

    def tp_group_f32(srcs, b_srcs, dst_ap, b_dst):
        ps, b_ps = tpf_psL[_tpctr[0] % len(tpf_psL)]
        _tpctr[0] += 1
        n = len(srcs)
        fns = [lambda s=s, i=i, ps=ps: nc.tensor.transpose(
            out=ps[:, i * 128:(i + 1) * 128], in_=s, identity=idf[:])
            for i, s in enumerate(srcs)]
        plan.group("pe", fns, reads=list(b_srcs) + [b_idf], writes=[b_ps],
                   dur=300 * n)
        plan.op("act", lambda n=n, ps=ps: nc.scalar.activation(
            out=dst_ap, in_=ps[:, :n * 128], func=AF.Copy),
            reads=[b_ps], writes=[b_dst], dur=600)

    def tp_group_f32_add(srcs, b_srcs, dst_ap, b_dst, add_ap, b_add):
        ps, b_ps = tpf_psL[_tpctr[0] % len(tpf_psL)]
        _tpctr[0] += 1
        n = len(srcs)
        fns = [lambda s=s, i=i, ps=ps: nc.tensor.transpose(
            out=ps[:, i * 128:(i + 1) * 128], in_=s, identity=idf[:])
            for i, s in enumerate(srcs)]
        plan.group("pe", fns, reads=list(b_srcs) + [b_idf], writes=[b_ps],
                   dur=300 * n)
        plan.op("dve", lambda n=n, ps=ps: nc.vector.tensor_tensor(
            out=dst_ap, in0=ps[:, :n * 128], in1=add_ap, op=ALU.add),
            reads=[b_ps, b_add], writes=[b_dst], dur=750)

    def load_vec_to_cols(vec1d_view, dst_ap, b_dst, width):
        dma_in("sp", vstage[0:width, 0:128], vec1d_view, b_vstage, rbufs=[b_vstage])
        plan.op("pe", lambda: nc.tensor.transpose(
            out=tpf_ps[:, 0:width], in_=vstage[0:width, 0:128],
            identity=idf[0:width, 0:width]),
            reads=[b_vstage, b_idf], writes=[b_tpf])
        plan.op("act", lambda: nc.scalar.activation(
            out=dst_ap, in_=tpf_ps[:, 0:width], func=AF.Copy),
            reads=[b_tpf], writes=[b_dst])

    _lnslots = [Buf(f"lnt{i}") for i in range(4)]
    _lsqslots = [Buf("lsq0")]
    _lnctr = [0]

    def emit_ln(src_ap, b_src, dst_ap, b_dst, sq_over=None):
        s = _lnctr[0] % 4
        _lnctr[0] += 1
        b_ln = _lnslots[s]
        if sq_over is None:
            sq, b_sq = lsq[:, 0:D], _lsqslots[0]
        else:
            sq, b_sq = sq_over
        ln = lnt[:, s * 8:(s + 1) * 8]
        plan.op("dve", lambda: nc.vector.reduce_sum(
            out=ln[:, 0:1], in_=src_ap, axis=AXX), reads=[b_src], writes=[b_ln],
            dur=1100)
        plan.op("act", lambda: nc.scalar.activation(
            out=sq, in_=src_ap, func=AF.Square), reads=[b_src], writes=[b_sq],
            dur=1000)
        plan.op("dve", lambda: nc.vector.reduce_sum(
            out=ln[:, 1:2], in_=sq, axis=AXX), reads=[b_sq], writes=[b_ln],
            dur=1100)
        plan.op("dve", lambda: nc.vector.tensor_scalar_mul(
            out=ln[:, 0:2], in0=ln[:, 0:2], scalar1=1.0 / D),
            reads=[b_ln], writes=[b_ln], dur=80)
        plan.op("dve", lambda: nc.vector.tensor_tensor(
            out=ln[:, 2:3], in0=ln[:, 0:1], in1=ln[:, 0:1], op=ALU.mult),
            reads=[b_ln], writes=[b_ln], dur=80)
        plan.op("dve", lambda: nc.vector.tensor_tensor(
            out=ln[:, 1:2], in0=ln[:, 1:2], in1=ln[:, 2:3], op=ALU.subtract),
            reads=[b_ln], writes=[b_ln], dur=80)
        plan.op("dve", lambda: nc.vector.tensor_single_scalar(
            out=ln[:, 1:2], in_=ln[:, 1:2], scalar=1e-5, op=ALU.add),
            reads=[b_ln], writes=[b_ln], dur=80)
        plan.op("act", lambda: nc.scalar.activation(
            out=ln[:, 3:4], in_=ln[:, 1:2], func=AF.Sqrt),
            reads=[b_ln], writes=[b_ln], dur=250)
        plan.op("dve", lambda: nc.vector.reciprocal(
            out=ln[:, 3:4], in_=ln[:, 3:4]), reads=[b_ln], writes=[b_ln], dur=80)
        plan.op("dve", lambda: nc.vector.tensor_scalar(
            out=dst_ap, in0=src_ap, scalar1=ln[:, 0:1], scalar2=ln[:, 3:4],
            op0=ALU.subtract, op1=ALU.mult), reads=[b_src, b_ln], writes=[b_dst],
            dur=700)

    def emit_ln_noscratch(src_ap, b_src, dst_ap, b_dst):
        emit_ln(src_ap, b_src, dst_ap, b_dst,
                sq_over=(xh2t[:, 0:D], b_xh2[0]))

    def emit_quant(src_ap, b_src, dst_ap, b_dst, W, scr):
        nb = W // 16
        z, b_z = scr["z"]
        a2, b_a2 = scr["a2"]
        red, b_red = scr["red"]
        sc, b_sc = scr["sc"]
        rcp, b_rcp = scr["rcp"]
        src3 = src_ap.rearrange("p (a b) -> p a b", b=16)
        plan.op("dve", lambda: nc.vector.tensor_reduce(
            out=red[:, :nb], in_=src3, axis=AXX, op=ALU.max,
            apply_absolute_value=True), reads=[b_src], writes=[b_red])
        plan.op("dve", lambda: nc.vector.tensor_scalar(
            out=sc[:, :nb], in0=red[:, :nb],
            scalar1=float(np.float32(1.0) / np.float32(12.0)), scalar2=5e-13,
            op0=ALU.mult, op1=ALU.max), reads=[b_red], writes=[b_sc])
        plan.op("dve", lambda: nc.vector.reciprocal(
            out=rcp[:, :nb], in_=sc[:, :nb]), reads=[b_sc], writes=[b_rcp])
        rcp3 = rcp[:, :nb].unsqueeze(2).broadcast_to([128, nb, 16])
        sc3 = sc[:, :nb].unsqueeze(2).broadcast_to([128, nb, 16])
        z3 = z[:, :W].rearrange("p (a b) -> p a b", b=16)
        plan.op("dve", lambda: nc.vector.tensor_tensor(
            out=z3, in0=src3, in1=rcp3, op=ALU.mult),
            reads=[b_src, b_rcp], writes=[b_z])
        msk, b_msk = scr["msk"]
        zs, b_zs = scr["zs"]
        plan.op("act", lambda: nc.scalar.activation(
            out=zs[:, :W], in_=z[:, :W], func=AF.Square),
            reads=[b_z], writes=[b_zs], dur=800)
        plan.op("dve", lambda: nc.vector.tensor_single_scalar(
            out=msk[:, :W], in_=zs[:, :W], scalar=4.0, op=ALU.is_lt),
            reads=[b_zs], writes=[b_msk], dur=450)
        plan.op("dve", lambda: nc.vector.tensor_scalar(
            out=a2[:, :W], in0=z[:, :W], scalar1=CBV, scalar2=CBV,
            op0=ALU.add, op1=ALU.subtract), reads=[b_z, b_msk], writes=[b_a2])
        plan.op("dve", lambda: nc.vector.tensor_scalar_mul(
            out=src_ap, in0=z[:, :W], scalar1=C2V), reads=[b_z], writes=[b_src])
        plan.op("dve", lambda: nc.vector.tensor_tensor(
            out=z[:, :W], in0=src_ap, in1=z[:, :W], op=ALU.subtract),
            reads=[b_src, b_z], writes=[b_z], dur=700)
        plan.op("dve", lambda: nc.vector.tensor_tensor(
            out=src_ap, in0=src_ap, in1=z[:, :W], op=ALU.subtract),
            reads=[b_src, b_z], writes=[b_src], dur=700)
        plan.op("dve", lambda: nc.vector.copy_predicated(
            out=src_ap, mask=msk[:, :W], data=a2[:, :W]),
            reads=[b_msk, b_a2, b_src], writes=[b_src])
        dst3 = dst_ap.rearrange("p (a b) -> p a b", b=16)
        plan.op("dve", lambda: nc.vector.tensor_tensor(
            out=dst3, in0=src3, in1=sc3, op=ALU.mult),
            reads=[b_src, b_sc], writes=[b_dst])

    p1 = ExitStack()
    tWr, b_tWr = mksb(p1, "tWr", [128, NDT * CH], F32)
    tWi, b_tWi = mksb(p1, "tWi", [128, NDT * CH], F32)
    tA, b_tA = mksb(p1, "tA", [128, NDT * CH], F32)
    tB, b_tB = mksb(p1, "tB", [128, NDT * CH], F32)
    p0 = ExitStack()
    wrow, b_wrow = mksb(p0, "wrow", [128, DI], F32)
    vstage, b_vstage = mksb(p0, "vstage", [8, 256], F32)
    wqr, b_wqr = mksb(p0, "wqr", [128, DI], BF16)
    qz0, b_qz0 = mksb(p0, "qz0", [128, DI], F32)
    qa20, b_qa20 = mksb(p0, "qa20", [128, DI], BF16)
    qred0, b_qred0 = mksb(p0, "qred0", [128, DI // 16], F32)
    qsc0, b_qsc0 = mksb(p0, "qsc0", [128, DI // 16], F32)
    qrcp0, b_qrcp0 = mksb(p0, "qrcp0", [128, DI // 16], F32)
    qmsk0, b_qmsk0 = mksb(p0, "qmsk0", [128, DI], mybir.dt.uint8)
    qzs0, b_qzs0 = mksb(p0, "qzs0", [128, DI], BF16)
    scr0 = {"z": (qz0, b_qz0), "a2": (qa20, b_qa20), "zs": (qzs0, b_qzs0),
            "red": (qred0, b_qred0), "sc": (qsc0, b_qsc0), "rcp": (qrcp0, b_qrcp0),
            "msk": (qmsk0, b_qmsk0)}
    wasm, b_wasm = mksb(p0, "wasm", [128, NDT * DI], BF16)

    def prep_gateup(w_src, dst_dram):
        for it in range(NIT):
            dma_in("sp", wrow[:, :D], w_src[it * 128:(it + 1) * 128, :],
                   b_wrow, rbufs=[b_wrow])
            emit_quant(wrow[:, :D], b_wrow, wqr[:, :D], b_wqr, D, scr0)
            for g0 in range(0, NDT, 4):
                srcs = [wqr[:, (g0 + j) * 128:(g0 + j + 1) * 128] for j in range(4)]
                dst = wasm[:, g0 * DI + it * 128:].rearrange(
                    "p (x) -> p x", x=1) if False else None
                dstap = wasm[:].rearrange("p (t q) -> p t q", q=DI)[
                    :, g0:g0 + 4, it * 128:it * 128 + 128]
                n = 4
                tp_ps_, b_tp_ = tp_psL[_tpctr[0] % len(tp_psL)]
                _tpctr[0] += 1
                fns = [lambda s=s, i=i, tp_ps_=tp_ps_: nc.tensor.transpose(
                    out=tp_ps_[:, i * 128:(i + 1) * 128], in_=s, identity=idbf[:])
                    for i, s in enumerate(srcs)]
                plan.group("pe", fns, reads=[b_wqr, b_idbf], writes=[b_tp_])
                src_ps = tp_ps_[:, :512].rearrange("p (t q) -> p t q", q=128)
                plan.op("act", lambda dstap=dstap, src_ps=src_ps:
                        nc.scalar.activation(out=dstap, in_=src_ps, func=AF.Copy),
                        reads=[b_tp_], writes=[b_wasm])
        plan.dma("sp", lambda d=dst_dram: nc.sync.dma_start(out=d[:], in_=wasm[:]),
                 reads=[b_wasm], writes=[Buf("wdram")])

    prep_gateup(win["L0_ffn_w_gate"], wgq_d)
    prep_gateup(win["L0_ffn_w_up"], wuq_d)

    for dt in range(NDT):
        dma_in("sp", wrow[:, :DI], win["L0_ffn_w_down"][dt * 128:(dt + 1) * 128, :],
               b_wrow, rbufs=[b_wrow])
        emit_quant(wrow[:, :DI], b_wrow, wqr[:, :DI], b_wqr, DI, scr0)
        for g0 in range(0, NIT, 4):
            gn = min(4, NIT - g0)
            srcs = [wqr[:, (g0 + j) * 128:(g0 + j + 1) * 128] for j in range(gn)]
            dstap = wasm[:, :NIT * D].rearrange("p (t q) -> p t q", q=D)[
                :, g0:g0 + gn, dt * 128:dt * 128 + 128]
            fns = [lambda s=s, i=i: nc.tensor.transpose(
                out=tp_ps[:, i * 128:(i + 1) * 128], in_=s, identity=idbf[:])
                for i, s in enumerate(srcs)]
            plan.group("pe", fns, reads=[b_wqr, b_idbf], writes=[b_tp])
            src_ps = tp_ps[:, :gn * 128].rearrange("p (t q) -> p t q", q=128)
            plan.op("act", lambda dstap=dstap, src_ps=src_ps:
                    nc.scalar.activation(out=dstap, in_=src_ps, func=AF.Copy),
                    reads=[b_tp], writes=[b_wasm])
    plan.dma("sp", lambda: nc.sync.dma_start(out=wdq_d[:], in_=wasm[:, :NIT * D]),
             reads=[b_wasm], writes=[Buf("wdram")])

    for w_src, dstd in [(win["L0_attn_b_Wg"], wgT_d), (win["L0_attn_b_Wr"], wrT_d),
                        (win["L0_attn_b_Wo"], woT_d)]:
        for et in range(NDT):
            dma_in("sp", wrow[:, :D], w_src[et * 128:(et + 1) * 128, :],
                   b_wrow, rbufs=[b_wrow])
            plan.op("act", lambda: nc.scalar.activation(
                out=wqr[:, :D], in_=wrow[:, :D], func=AF.Copy),
                reads=[b_wrow], writes=[b_wqr])
            for g0 in range(0, NDT, 4):
                srcs = [wqr[:, (g0 + j) * 128:(g0 + j + 1) * 128] for j in range(4)]
                dstap = wasm[:, :NDT * D].rearrange("p (t q) -> p t q", q=D)[
                    :, g0:g0 + 4, et * 128:et * 128 + 128]
                tp_ps_, b_tp_ = tp_psL[_tpctr[0] % len(tp_psL)]
                _tpctr[0] += 1
                fns = [lambda s=s, i=i, tp_ps_=tp_ps_: nc.tensor.transpose(
                    out=tp_ps_[:, i * 128:(i + 1) * 128], in_=s, identity=idbf[:])
                    for i, s in enumerate(srcs)]
                plan.group("pe", fns, reads=[b_wqr, b_idbf], writes=[b_tp_])
                b_tp = b_tp_
                src_ps = tp_ps_[:, :512].rearrange("p (t q) -> p t q", q=128)
                plan.op("act", lambda dstap=dstap, src_ps=src_ps:
                        nc.scalar.activation(out=dstap, in_=src_ps, func=AF.Copy),
                        reads=[b_tp], writes=[b_wasm])
        plan.dma("sp", lambda d=dstd: nc.sync.dma_start(
            out=d[:], in_=wasm[:, :NDT * D]), reads=[b_wasm], writes=[Buf("wdram")])

    load_vec_to_cols(win["L0_attn_b_decay"].rearrange("(t p) -> t p", p=128),
                     pv[:, COL_A:COL_A + NDT], b_pv, NDT)
    plan.op("act", lambda: nc.scalar.activation(
        out=pv[:, COL_A:COL_A + NDT], in_=pv[:, COL_A:COL_A + NDT],
        func=AF.Sigmoid), reads=[b_pv], writes=[b_pv])
    plan.op("dve", lambda: nc.vector.tensor_scalar(
        out=pv[:, COL_OM:COL_OM + NDT], in0=pv[:, COL_A:COL_A + NDT],
        scalar1=-1.0, scalar2=1.0, op0=ALU.mult, op1=ALU.add),
        reads=[b_pv], writes=[b_pv])
    for nm, col in [("L0_attn_b_br", COL_BR), ("L0_attn_b_bo", COL_BO),
                    ("L0_attn_b_ln_w", COL_LNW), ("L0_attn_b_ln_b", COL_LNB)]:
        load_vec_to_cols(win[nm].rearrange("(t p) -> t p", p=128),
                         pv[:, col:col + NDT], b_pv, NDT)
    load_vec_to_cols(win["L0_attn_b_bg"].rearrange("(t p) -> t p", p=128),
                     bgv[:], b_bgv, NDT)

    phz, b_phz = mksb(p0, "phz", [128, 2 * NDT], F32)
    phi_, b_phi = mksb(p0, "phi_", [128, 2 * NDT], F32)
    lcv, b_lcv = mksb(p0, "lcv", [128, 2 * NDT], F32)
    tmpv, b_tmpv = mksb(p0, "tmpv", [128, 4 * NDT], F32)
    den, b_den = mksb(p0, "den", [128, 2 * NDT], F32)
    for src_d, dst_t, b_dst in [(win["L0_attn_a_phazor"], phz, b_phz),
                                (win["L0_attn_a_phazor_init"], phi_, b_phi)]:
        dma_in("sp", vstage[:, :256],
               src_d.rearrange("(t p) c -> t (p c)", p=128), b_vstage,
               rbufs=[b_vstage])
        for c in range(2):
            sv = vstage[:, :256].rearrange("t (p c) -> t p c", c=2)[:, :, c]
            plan.op("pe", lambda sv=sv, c=c: nc.tensor.transpose(
                out=tpf_ps[:, c * 128:c * 128 + NDT], in_=sv,
                identity=idf[0:NDT, 0:NDT]),
                reads=[b_vstage, b_idf], writes=[b_tpf])
        dsts = dst_t[:].rearrange("p (t c) -> p t c", c=2)
        plan.op("act", lambda dsts=dsts: nc.scalar.activation(
            out=dsts[:, :, 0], in_=tpf_ps[:, 0:NDT], func=AF.Copy),
            reads=[b_tpf], writes=[b_dst])
        plan.op("act", lambda dsts=dsts: nc.scalar.activation(
            out=dsts[:, :, 1], in_=tpf_ps[:, 128:128 + NDT], func=AF.Copy),
            reads=[b_tpf], writes=[b_dst])
    load_vec_to_cols(win["L0_attn_a_last_conv_re"].rearrange("(t p) -> t p", p=128),
                     lcv[:, 0:NDT], b_lcv, NDT)
    load_vec_to_cols(win["L0_attn_a_last_conv_im"].rearrange("(t p) -> t p", p=128),
                     lcv[:, NDT:2 * NDT], b_lcv, NDT)

    prs = phz[:].rearrange("p (t c) -> p t c", c=2)[:, :, 0]
    pis = phz[:].rearrange("p (t c) -> p t c", c=2)[:, :, 1]
    phr = phi_[:].rearrange("p (t c) -> p t c", c=2)[:, :, 0]
    phii = phi_[:].rearrange("p (t c) -> p t c", c=2)[:, :, 1]
    lre = lcv[:, 0:NDT]
    lim = lcv[:, NDT:2 * NDT]

    def vop(fn, reads, writes):
        plan.op("dve", fn, reads=reads, writes=writes)

    vop(lambda: nc.vector.tensor_tensor(out=tmpv[:, 0:NDT], in0=prs, in1=prs,
        op=ALU.mult), [b_phz], [b_tmpv])
    vop(lambda: nc.vector.tensor_tensor(out=tmpv[:, NDT:2 * NDT], in0=pis, in1=pis,
        op=ALU.mult), [b_phz], [b_tmpv])
    vop(lambda: nc.vector.tensor_tensor(out=tmpv[:, 0:NDT], in0=tmpv[:, 0:NDT],
        in1=tmpv[:, NDT:2 * NDT], op=ALU.add), [b_tmpv], [b_tmpv])
    plan.op("act", lambda: nc.scalar.activation(
        out=tmpv[:, NDT:2 * NDT], in_=tmpv[:, 0:NDT], func=AF.Sqrt),
        reads=[b_tmpv], writes=[b_tmpv])
    plan.op("act", lambda: nc.scalar.activation(
        out=pv[:, COL_R:COL_R + NDT], in_=tmpv[:, NDT:2 * NDT], func=AF.Exp,
        scale=-1.0), reads=[b_tmpv], writes=[b_pv])
    vop(lambda: nc.vector.reciprocal(out=tmpv[:, 0:NDT], in_=tmpv[:, NDT:2 * NDT]),
        [b_tmpv], [b_tmpv])
    vop(lambda: nc.vector.tensor_tensor(out=tmpv[:, 2 * NDT:3 * NDT], in0=prs,
        in1=tmpv[:, 0:NDT], op=ALU.mult), [b_phz, b_tmpv], [b_tmpv])
    vop(lambda: nc.vector.tensor_tensor(out=tmpv[:, 3 * NDT:4 * NDT], in0=pis,
        in1=tmpv[:, 0:NDT], op=ALU.mult), [b_phz, b_tmpv], [b_tmpv])

    vop(lambda: nc.vector.tensor_tensor(out=den[:, 0:NDT], in0=phr, in1=phr,
        op=ALU.mult), [b_phi], [b_den])
    vop(lambda: nc.vector.tensor_tensor(out=den[:, NDT:], in0=phii, in1=phii,
        op=ALU.mult), [b_phi], [b_den])
    vop(lambda: nc.vector.tensor_tensor(out=den[:, 0:NDT], in0=den[:, 0:NDT],
        in1=den[:, NDT:], op=ALU.add), [b_den], [b_den])
    vop(lambda: nc.vector.tensor_single_scalar(out=den[:, 0:NDT], in_=den[:, 0:NDT],
        scalar=1e-30, op=ALU.max), [b_den], [b_den])
    vop(lambda: nc.vector.reciprocal(out=den[:, 0:NDT], in_=den[:, 0:NDT]),
        [b_den], [b_den])
    vop(lambda: nc.vector.tensor_tensor(out=q0t[:, 0:NDT], in0=lre, in1=phr,
        op=ALU.mult), [b_lcv, b_phi], [b_q0t])
    vop(lambda: nc.vector.tensor_tensor(out=den[:, NDT:], in0=lim, in1=phii,
        op=ALU.mult), [b_lcv, b_phi], [b_den])
    vop(lambda: nc.vector.tensor_tensor(out=q0t[:, 0:NDT], in0=q0t[:, 0:NDT],
        in1=den[:, NDT:], op=ALU.add), [b_q0t, b_den], [b_q0t])
    vop(lambda: nc.vector.tensor_tensor(out=q0t[:, 0:NDT], in0=q0t[:, 0:NDT],
        in1=den[:, 0:NDT], op=ALU.mult), [b_q0t, b_den], [b_q0t])
    vop(lambda: nc.vector.tensor_tensor(out=q0t[:, NDT:], in0=lim, in1=phr,
        op=ALU.mult), [b_lcv, b_phi], [b_q0t])
    vop(lambda: nc.vector.tensor_tensor(out=den[:, NDT:], in0=lre, in1=phii,
        op=ALU.mult), [b_lcv, b_phi], [b_den])
    vop(lambda: nc.vector.tensor_tensor(out=q0t[:, NDT:], in0=q0t[:, NDT:],
        in1=den[:, NDT:], op=ALU.subtract), [b_q0t, b_den], [b_q0t])
    vop(lambda: nc.vector.tensor_tensor(out=q0t[:, NDT:], in0=q0t[:, NDT:],
        in1=den[:, 0:NDT], op=ALU.mult), [b_q0t, b_den], [b_q0t])
    vop(lambda: nc.vector.tensor_copy(out=carry[:], in_=q0t[:]), [b_q0t], [b_carry])

    for dt in range(NDT):
        base = dt * CH
        ur1 = tmpv[:, 2 * NDT + dt:2 * NDT + dt + 1]
        ui1 = tmpv[:, 3 * NDT + dt:3 * NDT + dt + 1]
        vop(lambda ur1=ur1, base=base: nc.vector.tensor_copy(
            out=tWr[:, base:base + 1], in_=ur1), [b_tmpv], [b_tWr])
        vop(lambda ui1=ui1, base=base: nc.vector.tensor_copy(
            out=tWi[:, base:base + 1], in_=ui1), [b_tmpv], [b_tWi])
        n = 1
        while n < CH:
            m = min(n, CH - n)
            anr = tWr[:, base + n - 1:base + n]
            ani = tWi[:, base + n - 1:base + n]
            s_r = tWr[:, base:base + m]
            s_i = tWi[:, base:base + m]
            d_r = tWr[:, base + n:base + n + m]
            d_i = tWi[:, base + n:base + n + m]
            t1 = tA[:, base:base + m]
            vop(lambda d=d_r, s=s_r, a=anr: nc.vector.tensor_scalar(
                out=d, in0=s, scalar1=a, scalar2=None, op0=ALU.mult),
                [b_tWr], [b_tWr])
            vop(lambda t=t1, s=s_i, a=ani: nc.vector.tensor_scalar(
                out=t, in0=s, scalar1=a, scalar2=None, op0=ALU.mult),
                [b_tWi], [b_tA])
            vop(lambda d=d_r, t=t1: nc.vector.tensor_tensor(
                out=d, in0=d, in1=t, op=ALU.subtract), [b_tWr, b_tA], [b_tWr])
            vop(lambda d=d_i, s=s_r, a=ani: nc.vector.tensor_scalar(
                out=d, in0=s, scalar1=a, scalar2=None, op0=ALU.mult),
                [b_tWr, b_tWi], [b_tWi])
            vop(lambda t=t1, s=s_i, a=anr: nc.vector.tensor_scalar(
                out=t, in0=s, scalar1=a, scalar2=None, op0=ALU.mult),
                [b_tWi, b_tWr], [b_tA])
            vop(lambda d=d_i, t=t1: nc.vector.tensor_tensor(
                out=d, in0=d, in1=t, op=ALU.add), [b_tWi, b_tA], [b_tWi])
            n += m
        vop(lambda dt=dt, base=base: nc.vector.tensor_copy(
            out=tUc[:, dt:dt + 1], in_=tWr[:, base + CH - 1:base + CH]),
            [b_tWr], [b_tUc])
        vop(lambda dt=dt, base=base: nc.vector.tensor_copy(
            out=tUc[:, NDT + dt:NDT + dt + 1], in_=tWi[:, base + CH - 1:base + CH]),
            [b_tWi], [b_tUc])
    for dt in range(NDT):
        base = dt * CH
        pr1 = phr[:, dt:dt + 1]
        pi1 = phii[:, dt:dt + 1]
        vop(lambda base=base, pr1=pr1: nc.vector.tensor_scalar(
            out=tA[:, base:base + CH], in0=tWr[:, base:base + CH], scalar1=pr1,
            scalar2=None, op0=ALU.mult), [b_tWr, b_phi], [b_tA])
        vop(lambda base=base, pi1=pi1: nc.vector.tensor_scalar(
            out=tB[:, base:base + CH], in0=tWi[:, base:base + CH], scalar1=pi1,
            scalar2=None, op0=ALU.mult), [b_tWi, b_phi], [b_tB])
        vop(lambda base=base: nc.vector.tensor_tensor(
            out=tA[:, base:base + CH], in0=tA[:, base:base + CH],
            in1=tB[:, base:base + CH], op=ALU.subtract), [b_tA, b_tB], [b_tA])
        vop(lambda base=base, pr1=pr1: nc.vector.tensor_scalar(
            out=tB[:, base:base + CH], in0=tWi[:, base:base + CH], scalar1=pr1,
            scalar2=None, op0=ALU.mult), [b_tWi, b_phi], [b_tB])
        vop(lambda base=base, pi1=pi1: nc.vector.scalar_tensor_tensor(
            out=tB[:, base:base + CH], in0=tWr[:, base:base + CH], scalar=pi1,
            in1=tB[:, base:base + CH], op0=ALU.mult, op1=ALU.add),
            [b_tWr, b_phi, b_tB], [b_tB])
        vop(lambda base=base: nc.vector.tensor_scalar_mul(
            out=tWi[:, base:base + CH], in0=tWi[:, base:base + CH], scalar1=-1.0),
            [b_tWi], [b_tWi])

    plan.barrier()
    p0.close()

    tpf_psL.append(mkps(p1, "tpf_ps1", [128, 512], F32))
    xin1, b_xin1 = [], []
    for i in range(2):
        t, b = mksb(p1, f"p1xin{i}", [128, TPC * D], F32)
        xin1.append(t)
        b_xin1.append(b)
    xh4, b_xh4 = mksb(p1, "p1xh4", [128, TPC * D], F32)
    XT, b_XT = mksb(p1, "p1XT", [128, NDT * CH], F32)
    wRt, _bwr = mksb(p1, "p1wR", [128, 2 * CH], F32)
    wIt, _bwi = mksb(p1, "p1wI", [128, 2 * CH], F32)
    b_wRs = [Buf("p1wR0"), Buf("p1wR1")]
    b_wIs = [Buf("p1wI0"), Buf("p1wI1")]
    dRb, _bdr = mksb(p1, "p1dR", [128, NDT * CH], F32)
    dIb, _bdi = mksb(p1, "p1dI", [128, NDT * CH], F32)
    b_dRs = [Buf(f"p1dR{i}") for i in range(NDT)]
    b_dIs = [Buf(f"p1dI{i}") for i in range(NDT)]
    xo1, b_xo1 = mksb(p1, "p1xo", [128, TPC * D], F32)
    ncar, b_ncar = mksb(p1, "p1ncar", [128, 2 * NDT + 1], F32)

    for ch in range(NCH):
        xi, b_xi = xin1[ch % 2], b_xin1[ch % 2]
        src = x0_d[ch * CH:(ch + 1) * CH, :].rearrange("(a p) d -> p a d", p=128)
        dma_in("sp", xi[:].rearrange("p (a d) -> p a d", d=D), src, b_xi, rbufs=[b_xi])
        for a in range(TPC):
            emit_ln(xi[:, a * D:(a + 1) * D], b_xi,
                    xh4[:, a * D:(a + 1) * D], b_xh4)
        for dt in range(NDT):
            srcs = [xh4[:, a * D + dt * 128: a * D + dt * 128 + 128]
                    for a in range(TPC)]
            tp_group_f32(srcs, [b_xh4], XT[:, dt * CH:(dt + 1) * CH], b_XT)
        for dt in range(NDT):
            base = dt * CH
            wRb = wRt[:, (dt % 2) * CH:(dt % 2 + 1) * CH]
            wIb = wIt[:, (dt % 2) * CH:(dt % 2 + 1) * CH]
            b_wR = b_wRs[dt % 2]
            b_wI = b_wIs[dt % 2]
            rcol = pv[:, COL_R + dt:COL_R + dt + 1]
            r_rep = rcol.broadcast_to([128, CH])
            vop(lambda base=base, wRb=wRb: nc.vector.tensor_tensor(
                out=wRb, in0=XT[:, base:base + CH], in1=tWr[:, base:base + CH],
                op=ALU.mult), [b_XT, b_tWr], [b_wR])
            vop(lambda base=base, wIb=wIb: nc.vector.tensor_tensor(
                out=wIb, in0=XT[:, base:base + CH], in1=tWi[:, base:base + CH],
                op=ALU.mult), [b_XT, b_tWi], [b_wI])
            vop(lambda base=base, r_rep=r_rep, dt=dt, wRb=wRb, b_wR=b_wR:
                nc.vector.tensor_tensor_scan(
                out=dRb[:, base:base + CH], data0=r_rep, data1=wRb,
                initial=carry[:, dt:dt + 1], op0=ALU.mult, op1=ALU.add),
                [b_wR, b_carry, b_pv], [b_dRs[dt]])
            vop(lambda base=base, r_rep=r_rep, dt=dt, wIb=wIb:
                nc.vector.tensor_tensor_scan(
                out=dIb[:, base:base + CH], data0=r_rep, data1=wIb,
                initial=carry[:, NDT + dt:NDT + dt + 1], op0=ALU.mult, op1=ALU.add),
                [b_wI, b_carry, b_pv], [b_dIs[dt]])
            dl_r = dRb[:, base + CH - 1:base + CH]
            dl_i = dIb[:, base + CH - 1:base + CH]
            ucr = tUc[:, dt:dt + 1]
            uci = tUc[:, NDT + dt:NDT + dt + 1]
            tc = ncar[:, 2 * NDT:2 * NDT + 1]
            vop(lambda dt=dt, dl_r=dl_r, ucr=ucr: nc.vector.tensor_tensor(
                out=ncar[:, dt:dt + 1], in0=dl_r, in1=ucr, op=ALU.mult),
                [b_dRs[dt], b_tUc], [b_ncar])
            vop(lambda tc=tc, dl_i=dl_i, uci=uci: nc.vector.tensor_tensor(
                out=tc, in0=dl_i, in1=uci, op=ALU.mult), [b_dIs[dt], b_tUc], [b_ncar])
            vop(lambda dt=dt, tc=tc: nc.vector.tensor_tensor(
                out=ncar[:, dt:dt + 1], in0=ncar[:, dt:dt + 1], in1=tc,
                op=ALU.subtract), [b_ncar], [b_ncar])
            vop(lambda dt=dt, dl_r=dl_r, uci=uci: nc.vector.tensor_tensor(
                out=ncar[:, NDT + dt:NDT + dt + 1], in0=dl_r, in1=uci,
                op=ALU.mult), [b_dRs[dt], b_tUc], [b_ncar])
            vop(lambda tc=tc, dl_i=dl_i, ucr=ucr: nc.vector.tensor_tensor(
                out=tc, in0=dl_i, in1=ucr, op=ALU.mult), [b_dIs[dt], b_tUc], [b_ncar])
            vop(lambda dt=dt, tc=tc: nc.vector.tensor_tensor(
                out=ncar[:, NDT + dt:NDT + dt + 1],
                in0=ncar[:, NDT + dt:NDT + dt + 1], in1=tc, op=ALU.add),
                [b_ncar], [b_ncar])
            vop(lambda base=base: nc.vector.tensor_tensor(
                out=dRb[:, base:base + CH], in0=dRb[:, base:base + CH],
                in1=tA[:, base:base + CH], op=ALU.mult), [b_dRs[dt], b_tA], [b_dRs[dt]])
            vop(lambda base=base, wIb=wIb: nc.vector.tensor_tensor(
                out=wIb, in0=dIb[:, base:base + CH], in1=tB[:, base:base + CH],
                op=ALU.mult), [b_dIs[dt], b_tB], [b_wI])
            vop(lambda base=base, wIb=wIb: nc.vector.tensor_tensor(
                out=dRb[:, base:base + CH], in0=dRb[:, base:base + CH], in1=wIb,
                op=ALU.subtract), [b_dRs[dt], b_wI], [b_dRs[dt]])
        vop(lambda: nc.vector.tensor_copy(out=carry[:], in_=ncar[:, 0:2 * NDT]),
            [b_ncar], [b_carry])
        for a in range(TPC):
            for g0 in range(0, NDT, 4):
                srcs = [dRb[:, (g0 + j) * CH + a * 128:(g0 + j) * CH + a * 128 + 128]
                        for j in range(4)]
                dsl = xo1[:, a * D + g0 * 128: a * D + (g0 + 4) * 128]
                xsl = xi[:, a * D + g0 * 128: a * D + (g0 + 4) * 128]
                tp_group_f32_add(srcs, [b_dRs[g0 + j] for j in range(4)],
                                 dsl, b_xo1, xsl, b_xi)
        dst = xs1[ch * CH:(ch + 1) * CH, :].rearrange("(a p) d -> p a d", p=128)
        plan.dma("pool", lambda d=dst: nc.gpsimd.dma_start(out=d, in_=xo1[:].rearrange("p (a d) -> p a d", d=D)),
                 reads=[b_xo1], writes=[Buf("xs1w")])

    plan.barrier()
    tpf_psL.pop()
    p1.close()

    def emit_ffn(esx, tag, src_d, dst_d):
        wg_sb, b_wg = mksb(esx, f"{tag}wg", [128, NDT * DI], BF16)
        wu_sb, b_wu = mksb(esx, f"{tag}wu", [128, NDT * DI], BF16)
        wd_sb, b_wd = mksb(esx, f"{tag}wd", [128, NIT * D], BF16)
        dma_in("sp", wg_sb[:], wgq_d[:], b_wg)
        dma_in("sp", wu_sb[:], wuq_d[:], b_wu)
        dma_in("sp", wd_sb[:], wdq_d[:], b_wd)
        fxi, b_fxi = mksb(esx, f"{tag}xin", [128, TPC * D], F32)
        xq4, b_xq4 = mksb(esx, f"{tag}xq4", [128, TPC * D], BF16)
        xqT, b_xqT = mksb(esx, f"{tag}xqT", [128, NDT * 128], BF16)
        hbs_t, _b_hb = mksb(esx, f"{tag}hbs", [128, 1024], F32)
        b_hb2 = [Buf(f"{tag}hb0"), Buf(f"{tag}hb1")]
        hq, b_hq = mksb(esx, f"{tag}hq", [128, DI], BF16)
        hqT, b_hqT = mksb(esx, f"{tag}hqT", [128, NIT * 128], BF16)
        z0_, b_z0_ = mksb(esx, f"{tag}qz0", [128, D], F32)
        z1_, b_z1_ = mksb(esx, f"{tag}qz1", [128, D], F32)
        a2_, b_a2_ = mksb(esx, f"{tag}qa2", [128, D], BF16)
        zs_, b_zs_ = mksb(esx, f"{tag}qzs", [128, D], BF16)
        msk_, b_msk_ = mksb(esx, f"{tag}qmsk", [128, D], mybir.dt.uint8)
        red_, b_red_ = mksb(esx, f"{tag}qred", [128, D // 16], F32)
        sc_, b_sc_ = mksb(esx, f"{tag}qsc", [128, D // 16], F32)
        rcp_, b_rcp_ = mksb(esx, f"{tag}qrcp", [128, D // 16], F32)
        scrs = []
        for v in range(2):
            zz = (z0_, b_z0_) if v == 0 else (z1_, b_z1_)
            scrs.append({"z": zz, "a2": (a2_, b_a2_), "zs": (zs_, b_zs_),
                         "red": (red_, b_red_), "sc": (sc_, b_sc_),
                         "rcp": (rcp_, b_rcp_), "msk": (msk_, b_msk_)})
        qctr = [0]

        def scr_next():
            s = scrs[qctr[0] % 2]
            qctr[0] += 1
            return s
        gpsL = [mkps(esx, f"{tag}gps0", [128, 512], F32),
                mkps(esx, f"{tag}gps1", [128, 512], F32)]
        upsL = [mkps(esx, f"{tag}ups0", [128, 512], F32),
                mkps(esx, f"{tag}ups1", [128, 512], F32)]
        dps = [mkps(esx, f"{tag}dps0", [128, 512], F32),
               mkps(esx, f"{tag}dps1", [128, 512], F32)]
        ISL = []
        s0 = 0
        while s0 < DI:
            ISL.append((s0, min(512, DI - s0)))
            s0 += 512
        for ch in range(NCH):
            src = src_d[ch * CH:(ch + 1) * CH, :].rearrange("(a p) d -> p a d", p=128)
            dma_in("sp", fxi[:].rearrange("p (a d) -> p a d", d=D), src, b_fxi,
                   rbufs=[b_fxi])
            for a in range(TPC):
                xh_ap = xh2t[:, (a % 2) * D:(a % 2 + 1) * D]
                b_xh_s = b_xh2[a % 2]
                emit_ln(fxi[:, a * D:(a + 1) * D], b_fxi, xh_ap, b_xh_s)
                emit_quant(xh_ap, b_xh_s, xq4[:, a * D:(a + 1) * D], b_xq4, D, scr_next())
            for a in range(TPC):
                for g0 in range(0, NDT, 4):
                    srcs = [xq4[:, a * D + (g0 + j) * 128: a * D + (g0 + j + 1) * 128]
                            for j in range(4)]
                    tp_group_bf(srcs, [b_xq4],
                                xqT[:, g0 * 128:(g0 + 4) * 128], b_xqT)
                for si, (s0, sw) in enumerate(ISL):
                    gps, b_gps = gpsL[si % 2]
                    ups, b_ups = upsL[si % 2]
                    for (ps, b_ps, wsb, b_wsb) in ((gps, b_gps, wg_sb, b_wg),
                                                   (ups, b_ups, wu_sb, b_wu)):
                        fns = [lambda dt=dt, ps=ps, wsb=wsb, s0=s0, sw=sw:
                               nc.tensor.matmul(
                                   out=ps[:, :sw],
                                   lhsT=xqT[:, dt * 128:(dt + 1) * 128],
                                   rhs=wsb[:, dt * DI + s0: dt * DI + s0 + sw],
                                   start=(dt == 0), stop=(dt == NDT - 1))
                               for dt in range(NDT)]
                        plan.group("pe", fns, reads=[b_xqT, b_wsb], writes=[b_ps],
                                   dur=2000)
                    hbs = hbs_t[:, (si % 2) * 512:(si % 2) * 512 + sw]
                    b_hbs = b_hb2[si % 2]
                    plan.op("act", lambda sw=sw, gps=gps, hbs=hbs: nc.scalar.activation(
                        out=hbs, in_=gps[:, :sw], func=AF.Silu),
                        reads=[b_gps], writes=[b_hbs], dur=600)
                    plan.op("dve", lambda sw=sw, ups=ups, hbs=hbs:
                            nc.vector.tensor_tensor(
                        out=hbs, in0=hbs, in1=ups[:, :sw],
                        op=ALU.mult), reads=[b_ups, b_hbs], writes=[b_hbs], dur=700)
                    emit_quant(hbs, b_hbs, hq[:, s0:s0 + sw], b_hq, sw, scr_next())
                for g0 in range(0, NIT, 4):
                    gn = min(4, NIT - g0)
                    srcs = [hq[:, (g0 + j) * 128:(g0 + j + 1) * 128]
                            for j in range(gn)]
                    tp_group_bf(srcs, [b_hq],
                                hqT[:, g0 * 128:(g0 + gn) * 128], b_hqT)
                for half in range(2):
                    ps, b_ps = dps[half]
                    fns = [lambda it=it, ps=ps, half=half: nc.tensor.matmul(
                        out=ps[:, :512],
                        lhsT=hqT[:, it * 128:(it + 1) * 128],
                        rhs=wd_sb[:, it * D + half * 512: it * D + half * 512 + 512],
                        start=(it == 0), stop=(it == NIT - 1))
                        for it in range(NIT)]
                    plan.group("pe", fns, reads=[b_hqT, b_wd], writes=[b_ps])
                    plan.op("dve", lambda a=a, half=half, ps=ps:
                            nc.vector.tensor_tensor(
                                out=fxi[:, a * D + half * 512: a * D + half * 512 + 512],
                                in0=ps[:, :512],
                                in1=fxi[:, a * D + half * 512: a * D + half * 512 + 512],
                                op=ALU.add), reads=[b_ps, b_fxi], writes=[b_fxi])
            dst = dst_d[ch * CH:(ch + 1) * CH, :].rearrange("(a p) d -> p a d", p=128)
            plan.dma("pool", lambda d=dst: nc.gpsimd.dma_start(out=d, in_=fxi[:].rearrange("p (a d) -> p a d", d=D)),
                     reads=[b_fxi], writes=[Buf(f"{tag}out")])

    p2 = ExitStack()
    emit_ffn(p2, "p2", xs1, xs2)
    plan.barrier()
    p2.close()

    p3 = ExitStack()
    wgT_sb, b_wgT = mksb(p3, "p3wgT", [128, NDT * D], BF16)
    wrT_sb, b_wrT = mksb(p3, "p3wrT", [128, NDT * D], BF16)
    woT_sb, b_woT = mksb(p3, "p3woT", [128, NDT * D], BF16)
    dma_in("sp", wgT_sb[:], wgT_d[:], b_wgT)
    dma_in("sp", wrT_sb[:], wrT_d[:], b_wrT)
    dma_in("sp", woT_sb[:], woT_d[:], b_woT)
    xi3, b_xi3 = mksb(p3, "p3xin", [128, TPC * D], F32)
    XT3, b_XT3 = mksb(p3, "p3XT", [128, NDT * CH], F32)
    G3, b_G3 = mksb(p3, "p3G", [128, NDT * CH], F32)
    V3t, b_V3t = mksb(p3, "p3V", [128, 2 * CH], F32)
    b_V3s = [Buf("V3s0"), Buf("V3s1")]
    HxA, _bA = mksb(p3, "p3HxA", [128, NDT * (CH + 1)], BF16)
    HxB, _bB = mksb(p3, "p3HxB", [128, NDT * (CH + 1)], BF16)
    Hf, _bH = mksb(p3, "p3Hf", [128, NDT * CH], F32)
    b_HxA = [Buf(f"HxA{i}") for i in range(NDT)]
    b_HxB = [Buf(f"HxB{i}") for i in range(NDT)]
    b_Hfb = [Buf(f"Hf{i}") for i in range(NDT)]
    hcar, b_hcar = mksb(p3, "p3hcar", [128, NDT], F32)
    Nrm4, b_Nrm4 = mksb(p3, "p3Nrm4", [128, TPC * D], BF16)
    YT3, b_YT3 = mksb(p3, "p3YT", [128, NDT * CH], BF16)
    OT3, b_OT3 = mksb(p3, "p3OT", [128, NDT * CH], F32)
    XTb = OT3[:].bitcast(BF16)[:, :NDT * CH]
    b_XTb = b_OT3
    o3, b_o3 = mksb(p3, "p3o", [128, TPC * D], F32)
    pps = [mkps(p3, "p3ps0", [128, CH], F32), mkps(p3, "p3ps1", [128, CH], F32)]
    tp_psL.append(mkps(p3, "tp_ps3b", [128, 512], BF16))
    tpf_psL.append(mkps(p3, "tpf_ps3b", [128, 512], F32))
    b_lsqep = _lsqslots[0]

    plan.op("dve", lambda: nc.vector.memset(hcar[:], 0.0), writes=[b_hcar])

    for ch in range(NCH):
        src = xs2[ch * CH:(ch + 1) * CH, :].rearrange("(a p) d -> p a d", p=128)
        dma_in("sp", xi3[:].rearrange("p (a d) -> p a d", d=D), src, b_xi3, rbufs=[b_xi3])
        XT3v = XT3[:].rearrange("p (t q) -> p t q", q=CH)
        XTbv = XTb.rearrange("p (t q) -> p t q", q=CH)
        for a in range(TPC):
            xh_ap = xh2t[:, (a % 2) * D:(a % 2 + 1) * D]
            b_xh_s = b_xh2[a % 2]
            emit_ln(xi3[:, a * D:(a + 1) * D], b_xi3, xh_ap, b_xh_s)
            for g0 in range(0, NDT, 4):
                srcs = [xh_ap[:, (g0 + j) * 128:(g0 + j + 1) * 128]
                        for j in range(4)]
                tpf_, b_tpf_ = tpf_psL[_tpctr[0] % len(tpf_psL)]
                _tpctr[0] += 1
                fns = [lambda s=s, i=i, tpf_=tpf_: nc.tensor.transpose(
                    out=tpf_[:, i * 128:(i + 1) * 128], in_=s, identity=idf[:])
                    for i, s in enumerate(srcs)]
                plan.group("pe", fns, reads=[b_xh_s, b_idf], writes=[b_tpf_])
                b_tpf = b_tpf_
                ps3 = tpf_[:, :512].rearrange("p (t q) -> p t q", q=128)
                plan.op("act", lambda a=a, g0=g0, ps3=ps3: nc.scalar.activation(
                    out=XT3v[:, g0:g0 + 4, a * 128:a * 128 + 128], in_=ps3,
                    func=AF.Copy), reads=[b_tpf], writes=[b_XT3])
                plan.op("act", lambda a=a, g0=g0, ps3=ps3: nc.scalar.activation(
                    out=XTbv[:, g0:g0 + 4, a * 128:a * 128 + 128], in_=ps3,
                    func=AF.Copy), reads=[b_tpf], writes=[b_XTb])
        for et in range(NDT):
            ps, b_ps = pps[et % 2]
            fns = [lambda dt=dt, et=et, ps=ps: nc.tensor.matmul(
                out=ps[:, :CH],
                lhsT=wgT_sb[:, dt * D + et * 128: dt * D + et * 128 + 128],
                rhs=XTb[:, dt * CH:(dt + 1) * CH],
                start=(dt == 0), stop=(dt == NDT - 1)) for dt in range(NDT)]
            plan.group("pe", fns, reads=[b_XTb, b_wgT], writes=[b_ps])
            plan.op("act", lambda et=et, ps=ps: nc.scalar.activation(
                out=G3[:, et * CH:(et + 1) * CH], in_=ps[:, :CH], func=AF.Sigmoid,
                bias=bgv[:, et:et + 1]), reads=[b_ps, b_bgv], writes=[b_G3])
            vop(lambda et=et: nc.vector.tensor_scalar(
                out=G3[:, et * CH:(et + 1) * CH], in0=G3[:, et * CH:(et + 1) * CH],
                scalar1=pv[:, COL_OM + et:COL_OM + et + 1], scalar2=None,
                op0=ALU.mult), [b_G3, b_pv], [b_G3])
        for dt in range(NDT):
            vop(lambda dt=dt: nc.vector.tensor_scalar(
                out=XT3[:, dt * CH:(dt + 1) * CH], in0=XT3[:, dt * CH:(dt + 1) * CH],
                scalar1=pv[:, COL_BR + dt:COL_BR + dt + 1], scalar2=None,
                op0=ALU.subtract), [b_XT3, b_pv], [b_XT3])
        plan.op("dve", lambda: nc.vector.memset(HxA[:], 0.0), reads=[],
                writes=list(b_HxA), dur=900)
        for dt in range(NDT):
            vop(lambda dt=dt: nc.vector.tensor_copy(
                out=HxA[:, dt * (CH + 1):dt * (CH + 1) + 1], in_=hcar[:, dt:dt + 1]),
                [b_hcar], [b_HxA[dt]])
        cur, b_cur, nxt, b_nxt = HxA, b_HxA, HxB, b_HxB
        for k in range(K):
            last = (k == K - 1)
            for n in range(NDT):
                ps, b_ps = pps[n % 2]
                fns = [lambda dt=dt, n=n, ps=ps, cur=cur: nc.tensor.matmul(
                    out=ps[:, :CH],
                    lhsT=wrT_sb[:, dt * D + n * 128: dt * D + n * 128 + 128],
                    rhs=cur[:, dt * (CH + 1): dt * (CH + 1) + CH],
                    start=(dt == 0), stop=(dt == NDT - 1)) for dt in range(NDT)]
                plan.group("pe", fns, reads=list(b_cur) + [b_wrT], writes=[b_ps],
                           dur=1900)
                V3 = V3t[:, (n % 2) * CH:(n % 2 + 1) * CH]
                b_V3 = b_V3s[n % 2]
                plan.op("dve", lambda n=n, ps=ps, V3=V3: nc.vector.scalar_tensor_tensor(
                    out=V3, in0=ps[:, :CH], scalar=-1.0,
                    in1=XT3[:, n * CH:(n + 1) * CH], op0=ALU.mult, op1=ALU.add),
                    reads=[b_ps, b_XT3], writes=[b_V3], dur=700)
                plan.op("dve", lambda n=n, V3=V3: nc.vector.tensor_tensor(
                    out=V3, in0=V3, in1=G3[:, n * CH:(n + 1) * CH],
                    op=ALU.mult), reads=[b_V3, b_G3], writes=[b_V3], dur=700)
                a_rep = pv[:, COL_A + n:COL_A + n + 1].broadcast_to([128, CH])
                if last:
                    plan.op("dve", lambda n=n, a_rep=a_rep, V3=V3: nc.vector.tensor_tensor_scan(
                        out=Hf[:, n * CH:(n + 1) * CH], data0=a_rep, data1=V3,
                        initial=hcar[:, n:n + 1], op0=ALU.mult, op1=ALU.add),
                        reads=[b_V3, b_hcar, b_pv], writes=[b_Hfb[n]], dur=1400)
                else:
                    plan.op("dve", lambda n=n, a_rep=a_rep, nxt=nxt, V3=V3: nc.vector.tensor_tensor_scan(
                        out=nxt[:, n * (CH + 1) + 1: n * (CH + 1) + 1 + CH],
                        data0=a_rep, data1=V3,
                        initial=hcar[:, n:n + 1], op0=ALU.mult, op1=ALU.add),
                        reads=[b_V3, b_hcar, b_pv], writes=[b_nxt[n]], dur=1400)
            if not last:
                if k == 0:
                    for dt in range(NDT):
                        vop(lambda dt=dt, nxt=nxt: nc.vector.tensor_copy(
                            out=nxt[:, dt * (CH + 1):dt * (CH + 1) + 1],
                            in_=hcar[:, dt:dt + 1]), [b_hcar], [b_nxt[dt]])
                cur, b_cur, nxt, b_nxt = nxt, b_nxt, cur, b_cur
        for dt in range(NDT):
            vop(lambda dt=dt: nc.vector.tensor_copy(
                out=hcar[:, dt:dt + 1], in_=Hf[:, dt * CH + CH - 1: dt * CH + CH]),
                [b_Hfb[dt]], [b_hcar])
        for a in range(TPC):
            for g0 in range(0, NDT, 4):
                srcs = [Hf[:, (g0 + j) * CH + a * 128:(g0 + j) * CH + a * 128 + 128]
                        for j in range(4)]
                tp_group_f32(srcs, [b_Hfb[g0 + j] for j in range(4)],
                             lsq[:, g0 * 128:(g0 + 4) * 128], b_lsqep)
            emit_ln_noscratch(lsq[:, 0:D], b_lsqep, Nrm4[:, a * D:(a + 1) * D], b_Nrm4)
        for dt in range(NDT):
            srcs = [Nrm4[:, a * D + dt * 128: a * D + dt * 128 + 128]
                    for a in range(TPC)]
            tp_group_bf(srcs, [b_Nrm4], YT3[:, dt * CH:(dt + 1) * CH], b_YT3)
        for dt in range(NDT):
            vop(lambda dt=dt: nc.vector.tensor_scalar(
                out=YT3[:, dt * CH:(dt + 1) * CH], in0=YT3[:, dt * CH:(dt + 1) * CH],
                scalar1=pv[:, COL_LNW + dt:COL_LNW + dt + 1],
                scalar2=pv[:, COL_LNB + dt:COL_LNB + dt + 1],
                op0=ALU.mult, op1=ALU.add), [b_YT3, b_pv], [b_YT3])
        for et in range(NDT):
            ps, b_ps = pps[et % 2]
            fns = [lambda dt=dt, et=et, ps=ps: nc.tensor.matmul(
                out=ps[:, :CH],
                lhsT=woT_sb[:, dt * D + et * 128: dt * D + et * 128 + 128],
                rhs=YT3[:, dt * CH:(dt + 1) * CH],
                start=(dt == 0), stop=(dt == NDT - 1)) for dt in range(NDT)]
            plan.group("pe", fns, reads=[b_YT3, b_woT], writes=[b_ps])
            vop(lambda et=et, ps=ps: nc.vector.tensor_scalar(
                out=OT3[:, et * CH:(et + 1) * CH], in0=ps[:, :CH],
                scalar1=pv[:, COL_BO + et:COL_BO + et + 1], scalar2=None,
                op0=ALU.add), [b_ps, b_pv], [b_OT3])
        for a in range(TPC):
            for g0 in range(0, NDT, 4):
                srcs = [OT3[:, (g0 + j) * CH + a * 128:(g0 + j) * CH + a * 128 + 128]
                        for j in range(4)]
                dsl = o3[:, a * D + g0 * 128: a * D + (g0 + 4) * 128]
                xsl = xi3[:, a * D + g0 * 128: a * D + (g0 + 4) * 128]
                tp_group_f32_add(srcs, [b_OT3], dsl, b_o3, xsl, b_xi3)
        dst = xs3[ch * CH:(ch + 1) * CH, :].rearrange("(a p) d -> p a d", p=128)
        plan.dma("pool", lambda d=dst: nc.gpsimd.dma_start(out=d, in_=o3[:].rearrange("p (a d) -> p a d", d=D)),
                 reads=[b_o3], writes=[Buf("xs3w")])

    plan.barrier()
    tp_psL.pop()
    tpf_psL.pop()
    p3.close()

    p4 = ExitStack()
    emit_ffn(p4, "p4", xs3, out_d)
    plan.barrier()
    p4.close()

    plan.emit()

    with nc.Block() as block:
        @block.sync
        def _(eng):
            plan._run_engine("sp")

        @block.scalar
        def _(eng):
            plan._run_engine("act")

        @block.vector
        def _(eng):
            plan._run_engine("dve")

        @block.tensor
        def _(eng):
            plan._run_engine("pe")

        @block.gpsimd
        def _(eng):
            plan._run_engine("pool")
            plan.finish_waits("pool")

    outer.close()
    plan._es.close()
    return nc


def kernel(**inputs):
    import ml_dtypes
    from concourse.bass_utils import run_bass_kernel_spmd

    x = np.ascontiguousarray(np.asarray(inputs["L0_attn_a_x"], np.float32))
    B, L, _ = x.shape
    nc = build_nc(L=L)
    iden_bf = np.eye(128, dtype=ml_dtypes.bfloat16)
    iden_f32 = np.eye(128, dtype=np.float32)
    wnames = [k for k in inputs.keys() if k != "L0_attn_a_x"]
    in_maps = []
    for c in range(B):
        m = {nm: np.ascontiguousarray(np.asarray(inputs[nm], np.float32))
             for nm in wnames}
        m["x0"] = np.ascontiguousarray(x[c])
        m["iden_bf"] = iden_bf
        m["iden_f32"] = iden_f32
        in_maps.append(m)
    res = run_bass_kernel_spmd(nc, in_maps, core_ids=list(range(B)))
    return np.stack([np.asarray(r["out"], np.float32) for r in res.results], axis=0)


# version 18
# speedup vs baseline: 1.0314x; 1.0314x over previous
import numpy as np
from contextlib import ExitStack

import concourse.bass as bass
import concourse.mybir as mybir

F32 = mybir.dt.float32
BF16 = mybir.dt.bfloat16
AF = mybir.ActivationFunctionType
ALU = mybir.AluOpType
AXX = mybir.AxisListType.X

D = 1024
DI = 2816
NDT = D // 128
NIT = DI // 128
C2V = float(2 ** 22 + 1)
CBV = float(1.5 * 2 ** 23)

ENGS = ("sp", "act", "dve", "pe", "pool")
_EMAP = {"sp": "sync", "act": "scalar", "dve": "vector", "pe": "tensor",
         "pool": "gpsimd"}


class Buf:
    __slots__ = ("name", "w", "r")

    def __init__(self, name):
        self.name = name
        self.w = None
        self.r = []


class Node:
    __slots__ = ("idx", "eng", "fns", "deps", "dur", "is_dma", "sem", "val")

    def __init__(self, idx, eng, fns, deps, dur, is_dma=False):
        self.idx = idx
        self.eng = eng
        self.fns = fns
        self.deps = deps
        self.dur = dur
        self.is_dma = is_dma
        self.sem = None
        self.val = None


class Plan:

    def __init__(self, nc):
        self.nc = nc
        self.nodes = []
        self.sem = {}
        self._es = ExitStack()
        for e in ENGS:
            self.sem[e] = self._es.enter_context(nc.semaphore(f"sem_{e}"))
        self.NDMA = 24
        self.dpool = {"hw": list(range(16)), "sw": list(range(16, 24))}
        for i in range(self.NDMA):
            self.sem[f"dp{i}"] = self._es.enter_context(
                nc.semaphore(f"sem_dp{i}"))
        self._barrier_idx = -1

    def _mknode(self, eng, fns, reads, writes, dur, is_dma=False):
        deps = set()
        for b in reads:
            if b.w is not None:
                deps.add(b.w)
        for b in writes:
            if b.w is not None:
                deps.add(b.w)
            deps.update(b.r)
        if self._barrier_idx >= 0:
            deps.add(self._barrier_idx)
        idx = len(self.nodes)
        n = Node(idx, eng, fns, deps, dur, is_dma)
        self.nodes.append(n)
        for b in writes:
            b.w = idx
            b.r = []
        for b in reads:
            b.r.append(idx)
        return idx

    def op(self, eng, fn, reads=(), writes=(), dur=600):
        return self._mknode(eng, [fn], reads, writes, dur)

    def group(self, eng, fns, reads=(), writes=(), dur=None):
        if dur is None:
            dur = 250 * len(fns)
        return self._mknode(eng, list(fns), reads, writes, dur)

    def dma(self, queue, fn, reads=(), writes=(), dur=4000):
        return self._mknode(queue, [fn], reads, writes, dur, is_dma=True)

    def barrier(self):
        idx = self._mknode("sp", None, [], [], 0)
        n = self.nodes[idx]
        n.deps = set(range(idx))
        self._barrier_idx = idx
        return idx

    def emit(self):
        import heapq
        nodes = self.nodes
        N = len(nodes)
        nsucc = [[] for _ in range(N)]
        ndep = [0] * N
        for n in nodes:
            ndep[n.idx] = len(n.deps)
            for d in n.deps:
                nsucc[d].append(n.idx)
        eng_clock = {e: 0.0 for e in ENGS}
        finish = [0.0] * N
        ready_t = [0.0] * N
        heap = []
        for n in nodes:
            if ndep[n.idx] == 0:
                heapq.heappush(heap, (0.0, n.idx))
        order = {e: [] for e in ENGS}
        done = 0
        WINDOW = 4096
        scheduled = [False] * N
        min_unsched = 0
        poolq = {e: [] for e in ENGS}
        while done < N:
            while heap:
                rt, i = heapq.heappop(heap)
                heapq.heappush(poolq[nodes[i].eng], (nodes[i].idx,))
            gate = min_unsched + WINDOW
            cands = [(eng_clock[e], e) for e in ENGS
                     if poolq[e] and poolq[e][0][0] <= gate]
            if not cands:
                cands = [(eng_clock[e], e) for e in ENGS
                         if poolq[e] and poolq[e][0][0] == min_unsched]
                if not cands:
                    raise RuntimeError("scheduler stuck")
            _, e = min(cands)
            (i,) = heapq.heappop(poolq[e])
            scheduled[i] = True
            while min_unsched < N and scheduled[min_unsched]:
                min_unsched += 1
            n = nodes[i]
            start = max(eng_clock[e], ready_t[i])
            dma_lat = 1500 if n.is_dma else 0
            finish[i] = start + n.dur + dma_lat
            eng_clock[e] = start + (150 if n.is_dma else n.dur)
            order[e].append(i)
            done += 1
            for s in nsucc[i]:
                ready_t[s] = max(ready_t[s], finish[i])
                ndep[s] -= 1
                if ndep[s] == 0:
                    heapq.heappush(heap, (ready_t[s], s))
        eng_pos = {e: 0 for e in ENGS}
        dnext = {"hw": 0, "sw": 0}
        dcnt = [0] * self.NDMA
        dma_prev = {}
        for e in ENGS:
            for i in order[e]:
                n = nodes[i]
                if n.is_dma:
                    pool = "sw" if e == "pool" else "hw"
                    lst = self.dpool[pool]
                    di = lst[dnext[pool]]
                    dnext[pool] = (dnext[pool] + 1) % len(lst)
                    dcnt[di] += 16
                    n.sem = f"dp{di}"
                    n.val = dcnt[di]
                elif n.fns is not None:
                    eng_pos[e] += 1
                    n.sem = e
                    n.val = eng_pos[e]
        waited = {e: {} for e in ENGS}

        def emit_wait(e, semkey, val):
            if val <= waited[e].get(semkey, 0):
                return
            waited[e][semkey] = val
            g = getattr(self.nc, _EMAP[e])
            g.wait_ge(self.sem[semkey], val)

        self.final_dma = [(f"dp{i}", dcnt[i]) for i in range(self.NDMA) if dcnt[i]]
        barrier_cache = {}

        def barrier_maxes(bidx):
            if bidx not in barrier_cache:
                mx = {}
                for d2 in nodes[bidx].deps:
                    dn2 = nodes[d2]
                    if dn2.sem is not None:
                        if dn2.val > mx.get(dn2.sem, 0):
                            mx[dn2.sem] = dn2.val
                    elif dn2.fns is None and d2 in barrier_cache:
                        for k, v in barrier_cache[d2].items():
                            if v > mx.get(k, 0):
                                mx[k] = v
                barrier_cache[bidx] = mx
            return barrier_cache[bidx]

        def run_engine(e):
            prev_dma_on_sem = {}
            for i in order[e]:
                n = nodes[i]
                if n.is_dma:
                    prev = prev_dma_on_sem.get(n.sem)
                    if prev is not None:
                        emit_wait(e, n.sem, prev)
                    elif n.val > 16:
                        emit_wait(e, n.sem, n.val - 16)
                    prev_dma_on_sem[n.sem] = n.val
                for d in sorted(n.deps):
                    dn = nodes[d]
                    if dn.sem is None:
                        if dn.fns is None:
                            for k, v in sorted(barrier_maxes(d).items()):
                                emit_wait(e, k, v)
                        continue
                    emit_wait(e, dn.sem, dn.val)
                if not n.fns:
                    continue
                h = self.sem[n.sem]
                inc = 16 if n.is_dma else 1
                for j, fn in enumerate(n.fns):
                    if j == len(n.fns) - 1:
                        fn().then_inc(h, inc)
                    else:
                        fn()
        self._run_engine = run_engine

    def finish_waits(self, e):
        g = getattr(self.nc, _EMAP[e])
        for key, val in self.final_dma:
            g.wait_ge(self.sem[key], val)


def build_nc(L=8192, K=10, CH=512):
    nc = bass.Bass("TRN2", target_bir_lowering=False, debug=False)
    NCH = L // CH
    TPC = CH // 128
    assert TPC == 4

    x0_d = nc.dram_tensor("x0", [L, D], F32, kind="ExternalInput").ap()
    win = {}
    for nm, shp in [
        ("L0_attn_a_phazor", [D, 2]), ("L0_attn_a_phazor_init", [D, 2]),
        ("L0_attn_a_last_conv_re", [D]), ("L0_attn_a_last_conv_im", [D]),
        ("L0_attn_b_decay", [D]), ("L0_attn_b_Wr", [D, D]),
        ("L0_attn_b_br", [D]), ("L0_attn_b_Wg", [D, D]), ("L0_attn_b_bg", [D]),
        ("L0_attn_b_Wo", [D, D]), ("L0_attn_b_bo", [D]),
        ("L0_attn_b_ln_w", [D]), ("L0_attn_b_ln_b", [D]),
        ("L0_ffn_w_gate", [DI, D]), ("L0_ffn_w_up", [DI, D]),
        ("L0_ffn_w_down", [D, DI]),
    ]:
        win[nm] = nc.dram_tensor(nm, shp, F32, kind="ExternalInput").ap()
    iden_bf_d = nc.dram_tensor("iden_bf", [128, 128], BF16, kind="ExternalInput").ap()
    iden_f32_d = nc.dram_tensor("iden_f32", [128, 128], F32, kind="ExternalInput").ap()
    out_d = nc.dram_tensor("out", [L, D], F32, kind="ExternalOutput").ap()

    xs1 = nc.dram_tensor("xs1", [L, D], F32, kind="Internal").ap()
    xs2 = nc.dram_tensor("xs2", [L, D], F32, kind="Internal").ap()
    xs3 = nc.dram_tensor("xs3", [L, D], F32, kind="Internal").ap()
    wgq_d = nc.dram_tensor("wgq_d", [128, NDT * DI], BF16, kind="Internal").ap()
    wuq_d = nc.dram_tensor("wuq_d", [128, NDT * DI], BF16, kind="Internal").ap()
    wdq_d = nc.dram_tensor("wdq_d", [128, NIT * D], BF16, kind="Internal").ap()
    wgT_d = nc.dram_tensor("wgT_d", [128, NDT * D], BF16, kind="Internal").ap()
    wrT_d = nc.dram_tensor("wrT_d", [128, NDT * D], BF16, kind="Internal").ap()
    woT_d = nc.dram_tensor("woT_d", [128, NDT * D], BF16, kind="Internal").ap()

    plan = Plan(nc)
    outer = ExitStack()

    def mksb(esx, name, shape, dt):
        t = esx.enter_context(nc.sbuf_tensor(name, shape, dt))
        return t, Buf(name)

    def mkps(esx, name, shape, dt):
        t = esx.enter_context(nc.psum_tensor(name, shape, dt))
        return t, Buf(name)

    idbf, b_idbf = mksb(outer, "idbf", [128, 128], BF16)
    idf, b_idf = mksb(outer, "idf", [128, 128], F32)
    pv, b_pv = mksb(outer, "pv", [128, 64], F32)
    (COL_A, COL_OM, COL_BR, COL_BO, COL_LNW, COL_LNB, COL_R) = (
        0, 8, 16, 24, 32, 40, 48)
    bgv, b_bgv = mksb(outer, "bgv", [128, NDT], F32)
    carry, b_carry = mksb(outer, "carry", [128, 2 * NDT], F32)
    xh2t, b_xh2t = mksb(outer, "xh", [128, 2 * D], F32)
    b_xh2 = [Buf("xh_s0"), Buf("xh_s1")]
    lnt, b_lnt = mksb(outer, "lnt", [128, 32], F32)
    tUc, b_tUc = mksb(outer, "tUc", [128, 2 * NDT], F32)
    q0t, b_q0t = mksb(outer, "q0t", [128, 2 * NDT], F32)
    tp_psL = [mkps(outer, "tp_ps0", [128, 512], BF16)]
    tpf_psL = [mkps(outer, "tpf_ps0", [128, 512], F32)]
    _tpctr = [0]
    tp_ps, b_tp = tp_psL[0]
    tpf_ps, b_tpf = tpf_psL[0]

    def dma_in(queue, dst_ap, src_ap, wbuf, rbufs=()):
        plan.dma(queue,
                 lambda d=dst_ap, s=src_ap: getattr(nc, _EMAP[queue]).dma_start(out=d, in_=s),
                 reads=list(rbufs), writes=[wbuf])

    dma_in("sp", idbf[:], iden_bf_d[:], b_idbf)
    dma_in("sp", idf[:], iden_f32_d[:], b_idf)

    def tp_group_bf(srcs, b_srcs, dst_ap, b_dst):
        ps, b_ps = tp_psL[_tpctr[0] % len(tp_psL)]
        _tpctr[0] += 1
        n = len(srcs)
        fns = [lambda s=s, i=i, ps=ps: nc.tensor.transpose(
            out=ps[:, i * 128:(i + 1) * 128], in_=s, identity=idbf[:])
            for i, s in enumerate(srcs)]
        plan.group("pe", fns, reads=list(b_srcs) + [b_idbf], writes=[b_ps],
                   dur=300 * n)
        plan.op("act", lambda n=n, ps=ps: nc.scalar.activation(
            out=dst_ap, in_=ps[:, :n * 128], func=AF.Copy),
            reads=[b_ps], writes=[b_dst], dur=600)

    def tp_group_f32(srcs, b_srcs, dst_ap, b_dst):
        ps, b_ps = tpf_psL[_tpctr[0] % len(tpf_psL)]
        _tpctr[0] += 1
        n = len(srcs)
        fns = [lambda s=s, i=i, ps=ps: nc.tensor.transpose(
            out=ps[:, i * 128:(i + 1) * 128], in_=s, identity=idf[:])
            for i, s in enumerate(srcs)]
        plan.group("pe", fns, reads=list(b_srcs) + [b_idf], writes=[b_ps],
                   dur=300 * n)
        plan.op("act", lambda n=n, ps=ps: nc.scalar.activation(
            out=dst_ap, in_=ps[:, :n * 128], func=AF.Copy),
            reads=[b_ps], writes=[b_dst], dur=600)

    def tp_group_f32_add(srcs, b_srcs, dst_ap, b_dst, add_ap, b_add):
        ps, b_ps = tpf_psL[_tpctr[0] % len(tpf_psL)]
        _tpctr[0] += 1
        n = len(srcs)
        fns = [lambda s=s, i=i, ps=ps: nc.tensor.transpose(
            out=ps[:, i * 128:(i + 1) * 128], in_=s, identity=idf[:])
            for i, s in enumerate(srcs)]
        plan.group("pe", fns, reads=list(b_srcs) + [b_idf], writes=[b_ps],
                   dur=300 * n)
        plan.op("dve", lambda n=n, ps=ps: nc.vector.tensor_tensor(
            out=dst_ap, in0=ps[:, :n * 128], in1=add_ap, op=ALU.add),
            reads=[b_ps, b_add], writes=[b_dst], dur=750)

    def load_vec_to_cols(vec1d_view, dst_ap, b_dst, width):
        dma_in("sp", vstage[0:width, 0:128], vec1d_view, b_vstage, rbufs=[b_vstage])
        plan.op("pe", lambda: nc.tensor.transpose(
            out=tpf_ps[:, 0:width], in_=vstage[0:width, 0:128],
            identity=idf[0:width, 0:width]),
            reads=[b_vstage, b_idf], writes=[b_tpf])
        plan.op("act", lambda: nc.scalar.activation(
            out=dst_ap, in_=tpf_ps[:, 0:width], func=AF.Copy),
            reads=[b_tpf], writes=[b_dst])

    _lnslots = [Buf(f"lnt{i}") for i in range(4)]
    _sqdefault = [None]
    _lnctr = [0]

    def emit_ln(src_ap, b_src, dst_ap, b_dst, sq_over=None):
        s = _lnctr[0] % 4
        _lnctr[0] += 1
        b_ln = _lnslots[s]
        if sq_over is None:
            sq, b_sq = _sqdefault[0]
        else:
            sq, b_sq = sq_over
        ln = lnt[:, s * 8:(s + 1) * 8]
        plan.op("dve", lambda: nc.vector.reduce_sum(
            out=ln[:, 0:1], in_=src_ap, axis=AXX), reads=[b_src], writes=[b_ln],
            dur=1100)
        plan.op("act", lambda: nc.scalar.activation(
            out=sq, in_=src_ap, func=AF.Square), reads=[b_src], writes=[b_sq],
            dur=1000)
        plan.op("dve", lambda: nc.vector.reduce_sum(
            out=ln[:, 1:2], in_=sq, axis=AXX), reads=[b_sq], writes=[b_ln],
            dur=1100)
        plan.op("dve", lambda: nc.vector.tensor_scalar_mul(
            out=ln[:, 0:2], in0=ln[:, 0:2], scalar1=1.0 / D),
            reads=[b_ln], writes=[b_ln], dur=80)
        plan.op("dve", lambda: nc.vector.tensor_tensor(
            out=ln[:, 2:3], in0=ln[:, 0:1], in1=ln[:, 0:1], op=ALU.mult),
            reads=[b_ln], writes=[b_ln], dur=80)
        plan.op("dve", lambda: nc.vector.tensor_tensor(
            out=ln[:, 1:2], in0=ln[:, 1:2], in1=ln[:, 2:3], op=ALU.subtract),
            reads=[b_ln], writes=[b_ln], dur=80)
        plan.op("dve", lambda: nc.vector.tensor_single_scalar(
            out=ln[:, 1:2], in_=ln[:, 1:2], scalar=1e-5, op=ALU.add),
            reads=[b_ln], writes=[b_ln], dur=80)
        plan.op("act", lambda: nc.scalar.activation(
            out=ln[:, 3:4], in_=ln[:, 1:2], func=AF.Sqrt),
            reads=[b_ln], writes=[b_ln], dur=250)
        plan.op("dve", lambda: nc.vector.reciprocal(
            out=ln[:, 3:4], in_=ln[:, 3:4]), reads=[b_ln], writes=[b_ln], dur=80)
        plan.op("dve", lambda: nc.vector.tensor_scalar(
            out=dst_ap, in0=src_ap, scalar1=ln[:, 0:1], scalar2=ln[:, 3:4],
            op0=ALU.subtract, op1=ALU.mult), reads=[b_src, b_ln], writes=[b_dst],
            dur=700)

    def emit_ln_noscratch(src_ap, b_src, dst_ap, b_dst):
        emit_ln(src_ap, b_src, dst_ap, b_dst,
                sq_over=(xh2t[:, 0:D], b_xh2[0]))

    def emit_quant(src_ap, b_src, dst_ap, b_dst, W, scr):
        if isinstance(b_src, list):
            b_srcs = b_src
        else:
            b_srcs = [b_src]
        """fp4 quantize-dequantize [128, W] f32 -> [128, W] bf16.
        src is DESTROYED (used as Veltkamp scratch).  scr: z/a2 f32 >= W,
        zs bf16 >= W, red/sc/rcp f32 >= W/16."""
        nb = W // 16
        z, b_z = scr["z"]
        a2, b_a2 = scr["a2"]
        red, b_red = scr["red"]
        sc, b_sc = scr["sc"]
        rcp, b_rcp = scr["rcp"]
        src3 = src_ap.rearrange("p (a b) -> p a b", b=16)
        plan.op("dve", lambda: nc.vector.tensor_reduce(
            out=red[:, :nb], in_=src3, axis=AXX, op=ALU.max,
            apply_absolute_value=True), reads=b_srcs, writes=[b_red])
        plan.op("dve", lambda: nc.vector.tensor_scalar(
            out=sc[:, :nb], in0=red[:, :nb],
            scalar1=float(np.float32(1.0) / np.float32(12.0)), scalar2=5e-13,
            op0=ALU.mult, op1=ALU.max), reads=[b_red], writes=[b_sc])
        plan.op("dve", lambda: nc.vector.reciprocal(
            out=rcp[:, :nb], in_=sc[:, :nb]), reads=[b_sc], writes=[b_rcp])
        rcp3 = rcp[:, :nb].unsqueeze(2).broadcast_to([128, nb, 16])
        sc3 = sc[:, :nb].unsqueeze(2).broadcast_to([128, nb, 16])
        z3 = z[:, :W].rearrange("p (a b) -> p a b", b=16)
        plan.op("dve", lambda: nc.vector.tensor_tensor(
            out=z3, in0=src3, in1=rcp3, op=ALU.mult),
            reads=b_srcs + [b_rcp], writes=[b_z])
        msk, b_msk = scr["msk"]
        zs, b_zs = scr["zs"]
        plan.op("act", lambda: nc.scalar.activation(
            out=zs[:, :W], in_=z[:, :W], func=AF.Square),
            reads=[b_z], writes=[b_zs], dur=800)
        plan.op("dve", lambda: nc.vector.tensor_single_scalar(
            out=msk[:, :W], in_=zs[:, :W], scalar=4.0, op=ALU.is_lt),
            reads=[b_zs], writes=[b_msk], dur=450)
        plan.op("dve", lambda: nc.vector.tensor_scalar(
            out=a2[:, :W], in0=z[:, :W], scalar1=CBV, scalar2=CBV,
            op0=ALU.add, op1=ALU.subtract), reads=[b_z, b_msk], writes=[b_a2])
        plan.op("dve", lambda: nc.vector.tensor_scalar_mul(
            out=src_ap, in0=z[:, :W], scalar1=C2V), reads=[b_z], writes=b_srcs)
        plan.op("dve", lambda: nc.vector.tensor_tensor(
            out=z[:, :W], in0=src_ap, in1=z[:, :W], op=ALU.subtract),
            reads=b_srcs + [b_z], writes=[b_z], dur=700)
        plan.op("dve", lambda: nc.vector.tensor_tensor(
            out=src_ap, in0=src_ap, in1=z[:, :W], op=ALU.subtract),
            reads=b_srcs + [b_z], writes=b_srcs, dur=700)
        plan.op("dve", lambda: nc.vector.copy_predicated(
            out=src_ap, mask=msk[:, :W], data=a2[:, :W]),
            reads=[b_msk, b_a2] + b_srcs, writes=b_srcs)
        dst3 = dst_ap.rearrange("p (a b) -> p a b", b=16)
        plan.op("dve", lambda: nc.vector.tensor_tensor(
            out=dst3, in0=src3, in1=sc3, op=ALU.mult),
            reads=b_srcs + [b_sc], writes=[b_dst])

    p1 = ExitStack()
    tWr, b_tWr = mksb(p1, "tWr", [128, NDT * CH], F32)
    tWi, b_tWi = mksb(p1, "tWi", [128, NDT * CH], F32)
    tA, b_tA = mksb(p1, "tA", [128, NDT * CH], F32)
    tB, b_tB = mksb(p1, "tB", [128, NDT * CH], F32)
    p0 = ExitStack()
    wrow, b_wrow = mksb(p0, "wrow", [128, DI], F32)
    vstage, b_vstage = mksb(p0, "vstage", [8, 256], F32)
    wqr, b_wqr = mksb(p0, "wqr", [128, DI], BF16)
    qz0, b_qz0 = mksb(p0, "qz0", [128, DI], F32)
    qa20, b_qa20 = mksb(p0, "qa20", [128, DI], BF16)
    qred0, b_qred0 = mksb(p0, "qred0", [128, DI // 16], F32)
    qsc0, b_qsc0 = mksb(p0, "qsc0", [128, DI // 16], F32)
    qrcp0, b_qrcp0 = mksb(p0, "qrcp0", [128, DI // 16], F32)
    qmsk0, b_qmsk0 = mksb(p0, "qmsk0", [128, DI], mybir.dt.uint8)
    qzs0, b_qzs0 = mksb(p0, "qzs0", [128, DI], BF16)
    scr0 = {"z": (qz0, b_qz0), "a2": (qa20, b_qa20), "zs": (qzs0, b_qzs0),
            "red": (qred0, b_qred0), "sc": (qsc0, b_qsc0), "rcp": (qrcp0, b_qrcp0),
            "msk": (qmsk0, b_qmsk0)}
    wasm, b_wasm = mksb(p0, "wasm", [128, NDT * DI], BF16)

    def prep_gateup(w_src, dst_dram):
        for it in range(NIT):
            dma_in("sp", wrow[:, :D], w_src[it * 128:(it + 1) * 128, :],
                   b_wrow, rbufs=[b_wrow])
            emit_quant(wrow[:, :D], b_wrow, wqr[:, :D], b_wqr, D, scr0)
            for g0 in range(0, NDT, 4):
                srcs = [wqr[:, (g0 + j) * 128:(g0 + j + 1) * 128] for j in range(4)]
                dst = wasm[:, g0 * DI + it * 128:].rearrange(
                    "p (x) -> p x", x=1) if False else None
                dstap = wasm[:].rearrange("p (t q) -> p t q", q=DI)[
                    :, g0:g0 + 4, it * 128:it * 128 + 128]
                n = 4
                tp_ps_, b_tp_ = tp_psL[_tpctr[0] % len(tp_psL)]
                _tpctr[0] += 1
                fns = [lambda s=s, i=i, tp_ps_=tp_ps_: nc.tensor.transpose(
                    out=tp_ps_[:, i * 128:(i + 1) * 128], in_=s, identity=idbf[:])
                    for i, s in enumerate(srcs)]
                plan.group("pe", fns, reads=[b_wqr, b_idbf], writes=[b_tp_])
                src_ps = tp_ps_[:, :512].rearrange("p (t q) -> p t q", q=128)
                plan.op("act", lambda dstap=dstap, src_ps=src_ps:
                        nc.scalar.activation(out=dstap, in_=src_ps, func=AF.Copy),
                        reads=[b_tp_], writes=[b_wasm])
        plan.dma("sp", lambda d=dst_dram: nc.sync.dma_start(out=d[:], in_=wasm[:]),
                 reads=[b_wasm], writes=[Buf("wdram")])

    prep_gateup(win["L0_ffn_w_gate"], wgq_d)
    prep_gateup(win["L0_ffn_w_up"], wuq_d)

    for dt in range(NDT):
        dma_in("sp", wrow[:, :DI], win["L0_ffn_w_down"][dt * 128:(dt + 1) * 128, :],
               b_wrow, rbufs=[b_wrow])
        emit_quant(wrow[:, :DI], b_wrow, wqr[:, :DI], b_wqr, DI, scr0)
        for g0 in range(0, NIT, 4):
            gn = min(4, NIT - g0)
            srcs = [wqr[:, (g0 + j) * 128:(g0 + j + 1) * 128] for j in range(gn)]
            dstap = wasm[:, :NIT * D].rearrange("p (t q) -> p t q", q=D)[
                :, g0:g0 + gn, dt * 128:dt * 128 + 128]
            fns = [lambda s=s, i=i: nc.tensor.transpose(
                out=tp_ps[:, i * 128:(i + 1) * 128], in_=s, identity=idbf[:])
                for i, s in enumerate(srcs)]
            plan.group("pe", fns, reads=[b_wqr, b_idbf], writes=[b_tp])
            src_ps = tp_ps[:, :gn * 128].rearrange("p (t q) -> p t q", q=128)
            plan.op("act", lambda dstap=dstap, src_ps=src_ps:
                    nc.scalar.activation(out=dstap, in_=src_ps, func=AF.Copy),
                    reads=[b_tp], writes=[b_wasm])
    plan.dma("sp", lambda: nc.sync.dma_start(out=wdq_d[:], in_=wasm[:, :NIT * D]),
             reads=[b_wasm], writes=[Buf("wdram")])

    for w_src, dstd in [(win["L0_attn_b_Wg"], wgT_d), (win["L0_attn_b_Wr"], wrT_d),
                        (win["L0_attn_b_Wo"], woT_d)]:
        for et in range(NDT):
            dma_in("sp", wrow[:, :D], w_src[et * 128:(et + 1) * 128, :],
                   b_wrow, rbufs=[b_wrow])
            plan.op("act", lambda: nc.scalar.activation(
                out=wqr[:, :D], in_=wrow[:, :D], func=AF.Copy),
                reads=[b_wrow], writes=[b_wqr])
            for g0 in range(0, NDT, 4):
                srcs = [wqr[:, (g0 + j) * 128:(g0 + j + 1) * 128] for j in range(4)]
                dstap = wasm[:, :NDT * D].rearrange("p (t q) -> p t q", q=D)[
                    :, g0:g0 + 4, et * 128:et * 128 + 128]
                tp_ps_, b_tp_ = tp_psL[_tpctr[0] % len(tp_psL)]
                _tpctr[0] += 1
                fns = [lambda s=s, i=i, tp_ps_=tp_ps_: nc.tensor.transpose(
                    out=tp_ps_[:, i * 128:(i + 1) * 128], in_=s, identity=idbf[:])
                    for i, s in enumerate(srcs)]
                plan.group("pe", fns, reads=[b_wqr, b_idbf], writes=[b_tp_])
                b_tp = b_tp_
                src_ps = tp_ps_[:, :512].rearrange("p (t q) -> p t q", q=128)
                plan.op("act", lambda dstap=dstap, src_ps=src_ps:
                        nc.scalar.activation(out=dstap, in_=src_ps, func=AF.Copy),
                        reads=[b_tp], writes=[b_wasm])
        plan.dma("sp", lambda d=dstd: nc.sync.dma_start(
            out=d[:], in_=wasm[:, :NDT * D]), reads=[b_wasm], writes=[Buf("wdram")])

    load_vec_to_cols(win["L0_attn_b_decay"].rearrange("(t p) -> t p", p=128),
                     pv[:, COL_A:COL_A + NDT], b_pv, NDT)
    plan.op("act", lambda: nc.scalar.activation(
        out=pv[:, COL_A:COL_A + NDT], in_=pv[:, COL_A:COL_A + NDT],
        func=AF.Sigmoid), reads=[b_pv], writes=[b_pv])
    plan.op("dve", lambda: nc.vector.tensor_scalar(
        out=pv[:, COL_OM:COL_OM + NDT], in0=pv[:, COL_A:COL_A + NDT],
        scalar1=-1.0, scalar2=1.0, op0=ALU.mult, op1=ALU.add),
        reads=[b_pv], writes=[b_pv])
    for nm, col in [("L0_attn_b_br", COL_BR), ("L0_attn_b_bo", COL_BO),
                    ("L0_attn_b_ln_w", COL_LNW), ("L0_attn_b_ln_b", COL_LNB)]:
        load_vec_to_cols(win[nm].rearrange("(t p) -> t p", p=128),
                         pv[:, col:col + NDT], b_pv, NDT)
    load_vec_to_cols(win["L0_attn_b_bg"].rearrange("(t p) -> t p", p=128),
                     bgv[:], b_bgv, NDT)

    phz, b_phz = mksb(p0, "phz", [128, 2 * NDT], F32)
    phi_, b_phi = mksb(p0, "phi_", [128, 2 * NDT], F32)
    lcv, b_lcv = mksb(p0, "lcv", [128, 2 * NDT], F32)
    tmpv, b_tmpv = mksb(p0, "tmpv", [128, 4 * NDT], F32)
    den, b_den = mksb(p0, "den", [128, 2 * NDT], F32)
    for src_d, dst_t, b_dst in [(win["L0_attn_a_phazor"], phz, b_phz),
                                (win["L0_attn_a_phazor_init"], phi_, b_phi)]:
        dma_in("sp", vstage[:, :256],
               src_d.rearrange("(t p) c -> t (p c)", p=128), b_vstage,
               rbufs=[b_vstage])
        for c in range(2):
            sv = vstage[:, :256].rearrange("t (p c) -> t p c", c=2)[:, :, c]
            plan.op("pe", lambda sv=sv, c=c: nc.tensor.transpose(
                out=tpf_ps[:, c * 128:c * 128 + NDT], in_=sv,
                identity=idf[0:NDT, 0:NDT]),
                reads=[b_vstage, b_idf], writes=[b_tpf])
        dsts = dst_t[:].rearrange("p (t c) -> p t c", c=2)
        plan.op("act", lambda dsts=dsts: nc.scalar.activation(
            out=dsts[:, :, 0], in_=tpf_ps[:, 0:NDT], func=AF.Copy),
            reads=[b_tpf], writes=[b_dst])
        plan.op("act", lambda dsts=dsts: nc.scalar.activation(
            out=dsts[:, :, 1], in_=tpf_ps[:, 128:128 + NDT], func=AF.Copy),
            reads=[b_tpf], writes=[b_dst])
    load_vec_to_cols(win["L0_attn_a_last_conv_re"].rearrange("(t p) -> t p", p=128),
                     lcv[:, 0:NDT], b_lcv, NDT)
    load_vec_to_cols(win["L0_attn_a_last_conv_im"].rearrange("(t p) -> t p", p=128),
                     lcv[:, NDT:2 * NDT], b_lcv, NDT)

    prs = phz[:].rearrange("p (t c) -> p t c", c=2)[:, :, 0]
    pis = phz[:].rearrange("p (t c) -> p t c", c=2)[:, :, 1]
    phr = phi_[:].rearrange("p (t c) -> p t c", c=2)[:, :, 0]
    phii = phi_[:].rearrange("p (t c) -> p t c", c=2)[:, :, 1]
    lre = lcv[:, 0:NDT]
    lim = lcv[:, NDT:2 * NDT]

    def vop(fn, reads, writes):
        plan.op("dve", fn, reads=reads, writes=writes)

    vop(lambda: nc.vector.tensor_tensor(out=tmpv[:, 0:NDT], in0=prs, in1=prs,
        op=ALU.mult), [b_phz], [b_tmpv])
    vop(lambda: nc.vector.tensor_tensor(out=tmpv[:, NDT:2 * NDT], in0=pis, in1=pis,
        op=ALU.mult), [b_phz], [b_tmpv])
    vop(lambda: nc.vector.tensor_tensor(out=tmpv[:, 0:NDT], in0=tmpv[:, 0:NDT],
        in1=tmpv[:, NDT:2 * NDT], op=ALU.add), [b_tmpv], [b_tmpv])
    plan.op("act", lambda: nc.scalar.activation(
        out=tmpv[:, NDT:2 * NDT], in_=tmpv[:, 0:NDT], func=AF.Sqrt),
        reads=[b_tmpv], writes=[b_tmpv])
    plan.op("act", lambda: nc.scalar.activation(
        out=pv[:, COL_R:COL_R + NDT], in_=tmpv[:, NDT:2 * NDT], func=AF.Exp,
        scale=-1.0), reads=[b_tmpv], writes=[b_pv])
    vop(lambda: nc.vector.reciprocal(out=tmpv[:, 0:NDT], in_=tmpv[:, NDT:2 * NDT]),
        [b_tmpv], [b_tmpv])
    vop(lambda: nc.vector.tensor_tensor(out=tmpv[:, 2 * NDT:3 * NDT], in0=prs,
        in1=tmpv[:, 0:NDT], op=ALU.mult), [b_phz, b_tmpv], [b_tmpv])
    vop(lambda: nc.vector.tensor_tensor(out=tmpv[:, 3 * NDT:4 * NDT], in0=pis,
        in1=tmpv[:, 0:NDT], op=ALU.mult), [b_phz, b_tmpv], [b_tmpv])

    vop(lambda: nc.vector.tensor_tensor(out=den[:, 0:NDT], in0=phr, in1=phr,
        op=ALU.mult), [b_phi], [b_den])
    vop(lambda: nc.vector.tensor_tensor(out=den[:, NDT:], in0=phii, in1=phii,
        op=ALU.mult), [b_phi], [b_den])
    vop(lambda: nc.vector.tensor_tensor(out=den[:, 0:NDT], in0=den[:, 0:NDT],
        in1=den[:, NDT:], op=ALU.add), [b_den], [b_den])
    vop(lambda: nc.vector.tensor_single_scalar(out=den[:, 0:NDT], in_=den[:, 0:NDT],
        scalar=1e-30, op=ALU.max), [b_den], [b_den])
    vop(lambda: nc.vector.reciprocal(out=den[:, 0:NDT], in_=den[:, 0:NDT]),
        [b_den], [b_den])
    vop(lambda: nc.vector.tensor_tensor(out=q0t[:, 0:NDT], in0=lre, in1=phr,
        op=ALU.mult), [b_lcv, b_phi], [b_q0t])
    vop(lambda: nc.vector.tensor_tensor(out=den[:, NDT:], in0=lim, in1=phii,
        op=ALU.mult), [b_lcv, b_phi], [b_den])
    vop(lambda: nc.vector.tensor_tensor(out=q0t[:, 0:NDT], in0=q0t[:, 0:NDT],
        in1=den[:, NDT:], op=ALU.add), [b_q0t, b_den], [b_q0t])
    vop(lambda: nc.vector.tensor_tensor(out=q0t[:, 0:NDT], in0=q0t[:, 0:NDT],
        in1=den[:, 0:NDT], op=ALU.mult), [b_q0t, b_den], [b_q0t])
    vop(lambda: nc.vector.tensor_tensor(out=q0t[:, NDT:], in0=lim, in1=phr,
        op=ALU.mult), [b_lcv, b_phi], [b_q0t])
    vop(lambda: nc.vector.tensor_tensor(out=den[:, NDT:], in0=lre, in1=phii,
        op=ALU.mult), [b_lcv, b_phi], [b_den])
    vop(lambda: nc.vector.tensor_tensor(out=q0t[:, NDT:], in0=q0t[:, NDT:],
        in1=den[:, NDT:], op=ALU.subtract), [b_q0t, b_den], [b_q0t])
    vop(lambda: nc.vector.tensor_tensor(out=q0t[:, NDT:], in0=q0t[:, NDT:],
        in1=den[:, 0:NDT], op=ALU.mult), [b_q0t, b_den], [b_q0t])
    vop(lambda: nc.vector.tensor_copy(out=carry[:], in_=q0t[:]), [b_q0t], [b_carry])

    for dt in range(NDT):
        base = dt * CH
        ur1 = tmpv[:, 2 * NDT + dt:2 * NDT + dt + 1]
        ui1 = tmpv[:, 3 * NDT + dt:3 * NDT + dt + 1]
        vop(lambda ur1=ur1, base=base: nc.vector.tensor_copy(
            out=tWr[:, base:base + 1], in_=ur1), [b_tmpv], [b_tWr])
        vop(lambda ui1=ui1, base=base: nc.vector.tensor_copy(
            out=tWi[:, base:base + 1], in_=ui1), [b_tmpv], [b_tWi])
        n = 1
        while n < CH:
            m = min(n, CH - n)
            anr = tWr[:, base + n - 1:base + n]
            ani = tWi[:, base + n - 1:base + n]
            s_r = tWr[:, base:base + m]
            s_i = tWi[:, base:base + m]
            d_r = tWr[:, base + n:base + n + m]
            d_i = tWi[:, base + n:base + n + m]
            t1 = tA[:, base:base + m]
            vop(lambda d=d_r, s=s_r, a=anr: nc.vector.tensor_scalar(
                out=d, in0=s, scalar1=a, scalar2=None, op0=ALU.mult),
                [b_tWr], [b_tWr])
            vop(lambda t=t1, s=s_i, a=ani: nc.vector.tensor_scalar(
                out=t, in0=s, scalar1=a, scalar2=None, op0=ALU.mult),
                [b_tWi], [b_tA])
            vop(lambda d=d_r, t=t1: nc.vector.tensor_tensor(
                out=d, in0=d, in1=t, op=ALU.subtract), [b_tWr, b_tA], [b_tWr])
            vop(lambda d=d_i, s=s_r, a=ani: nc.vector.tensor_scalar(
                out=d, in0=s, scalar1=a, scalar2=None, op0=ALU.mult),
                [b_tWr, b_tWi], [b_tWi])
            vop(lambda t=t1, s=s_i, a=anr: nc.vector.tensor_scalar(
                out=t, in0=s, scalar1=a, scalar2=None, op0=ALU.mult),
                [b_tWi, b_tWr], [b_tA])
            vop(lambda d=d_i, t=t1: nc.vector.tensor_tensor(
                out=d, in0=d, in1=t, op=ALU.add), [b_tWi, b_tA], [b_tWi])
            n += m
        vop(lambda dt=dt, base=base: nc.vector.tensor_copy(
            out=tUc[:, dt:dt + 1], in_=tWr[:, base + CH - 1:base + CH]),
            [b_tWr], [b_tUc])
        vop(lambda dt=dt, base=base: nc.vector.tensor_copy(
            out=tUc[:, NDT + dt:NDT + dt + 1], in_=tWi[:, base + CH - 1:base + CH]),
            [b_tWi], [b_tUc])
    for dt in range(NDT):
        base = dt * CH
        pr1 = phr[:, dt:dt + 1]
        pi1 = phii[:, dt:dt + 1]
        vop(lambda base=base, pr1=pr1: nc.vector.tensor_scalar(
            out=tA[:, base:base + CH], in0=tWr[:, base:base + CH], scalar1=pr1,
            scalar2=None, op0=ALU.mult), [b_tWr, b_phi], [b_tA])
        vop(lambda base=base, pi1=pi1: nc.vector.tensor_scalar(
            out=tB[:, base:base + CH], in0=tWi[:, base:base + CH], scalar1=pi1,
            scalar2=None, op0=ALU.mult), [b_tWi, b_phi], [b_tB])
        vop(lambda base=base: nc.vector.tensor_tensor(
            out=tA[:, base:base + CH], in0=tA[:, base:base + CH],
            in1=tB[:, base:base + CH], op=ALU.subtract), [b_tA, b_tB], [b_tA])
        vop(lambda base=base, pr1=pr1: nc.vector.tensor_scalar(
            out=tB[:, base:base + CH], in0=tWi[:, base:base + CH], scalar1=pr1,
            scalar2=None, op0=ALU.mult), [b_tWi, b_phi], [b_tB])
        vop(lambda base=base, pi1=pi1: nc.vector.scalar_tensor_tensor(
            out=tB[:, base:base + CH], in0=tWr[:, base:base + CH], scalar=pi1,
            in1=tB[:, base:base + CH], op0=ALU.mult, op1=ALU.add),
            [b_tWr, b_phi, b_tB], [b_tB])
        vop(lambda base=base: nc.vector.tensor_scalar_mul(
            out=tWi[:, base:base + CH], in0=tWi[:, base:base + CH], scalar1=-1.0),
            [b_tWi], [b_tWi])

    plan.barrier()
    p0.close()

    tpf_psL.append(mkps(p1, "tpf_ps1", [128, 512], F32))
    p1lsq, b_p1lsq = mksb(p1, "p1lsq", [128, D], F32)
    _sqdefault[0] = (p1lsq[:], b_p1lsq)
    xin1, b_xin1 = [], []
    for i in range(2):
        t, b = mksb(p1, f"p1xin{i}", [128, TPC * D], F32)
        xin1.append(t)
        b_xin1.append(b)
    xh4, b_xh4 = mksb(p1, "p1xh4", [128, TPC * D], F32)
    XT, b_XT = mksb(p1, "p1XT", [128, NDT * CH], F32)
    wRt, _bwr = mksb(p1, "p1wR", [128, 2 * CH], F32)
    wIt, _bwi = mksb(p1, "p1wI", [128, 2 * CH], F32)
    b_wRs = [Buf("p1wR0"), Buf("p1wR1")]
    b_wIs = [Buf("p1wI0"), Buf("p1wI1")]
    dRb, _bdr = mksb(p1, "p1dR", [128, NDT * CH], F32)
    dIb, _bdi = mksb(p1, "p1dI", [128, NDT * CH], F32)
    b_dRs = [Buf(f"p1dR{i}") for i in range(NDT)]
    b_dIs = [Buf(f"p1dI{i}") for i in range(NDT)]
    xo1, b_xo1 = mksb(p1, "p1xo", [128, TPC * D], F32)
    ncar, b_ncar = mksb(p1, "p1ncar", [128, 2 * NDT + 1], F32)

    for ch in range(NCH):
        xi, b_xi = xin1[ch % 2], b_xin1[ch % 2]
        src = x0_d[ch * CH:(ch + 1) * CH, :].rearrange("(a p) d -> p a d", p=128)
        dma_in("sp", xi[:].rearrange("p (a d) -> p a d", d=D), src, b_xi, rbufs=[b_xi])
        for a in range(TPC):
            emit_ln(xi[:, a * D:(a + 1) * D], b_xi,
                    xh4[:, a * D:(a + 1) * D], b_xh4)
        for dt in range(NDT):
            srcs = [xh4[:, a * D + dt * 128: a * D + dt * 128 + 128]
                    for a in range(TPC)]
            tp_group_f32(srcs, [b_xh4], XT[:, dt * CH:(dt + 1) * CH], b_XT)
        for dt in range(NDT):
            base = dt * CH
            wRb = wRt[:, (dt % 2) * CH:(dt % 2 + 1) * CH]
            wIb = wIt[:, (dt % 2) * CH:(dt % 2 + 1) * CH]
            b_wR = b_wRs[dt % 2]
            b_wI = b_wIs[dt % 2]
            rcol = pv[:, COL_R + dt:COL_R + dt + 1]
            r_rep = rcol.broadcast_to([128, CH])
            vop(lambda base=base, wRb=wRb: nc.vector.tensor_tensor(
                out=wRb, in0=XT[:, base:base + CH], in1=tWr[:, base:base + CH],
                op=ALU.mult), [b_XT, b_tWr], [b_wR])
            vop(lambda base=base, wIb=wIb: nc.vector.tensor_tensor(
                out=wIb, in0=XT[:, base:base + CH], in1=tWi[:, base:base + CH],
                op=ALU.mult), [b_XT, b_tWi], [b_wI])
            vop(lambda base=base, r_rep=r_rep, dt=dt, wRb=wRb, b_wR=b_wR:
                nc.vector.tensor_tensor_scan(
                out=dRb[:, base:base + CH], data0=r_rep, data1=wRb,
                initial=carry[:, dt:dt + 1], op0=ALU.mult, op1=ALU.add),
                [b_wR, b_carry, b_pv], [b_dRs[dt]])
            vop(lambda base=base, r_rep=r_rep, dt=dt, wIb=wIb:
                nc.vector.tensor_tensor_scan(
                out=dIb[:, base:base + CH], data0=r_rep, data1=wIb,
                initial=carry[:, NDT + dt:NDT + dt + 1], op0=ALU.mult, op1=ALU.add),
                [b_wI, b_carry, b_pv], [b_dIs[dt]])
            dl_r = dRb[:, base + CH - 1:base + CH]
            dl_i = dIb[:, base + CH - 1:base + CH]
            ucr = tUc[:, dt:dt + 1]
            uci = tUc[:, NDT + dt:NDT + dt + 1]
            tc = ncar[:, 2 * NDT:2 * NDT + 1]
            vop(lambda dt=dt, dl_r=dl_r, ucr=ucr: nc.vector.tensor_tensor(
                out=ncar[:, dt:dt + 1], in0=dl_r, in1=ucr, op=ALU.mult),
                [b_dRs[dt], b_tUc], [b_ncar])
            vop(lambda tc=tc, dl_i=dl_i, uci=uci: nc.vector.tensor_tensor(
                out=tc, in0=dl_i, in1=uci, op=ALU.mult), [b_dIs[dt], b_tUc], [b_ncar])
            vop(lambda dt=dt, tc=tc: nc.vector.tensor_tensor(
                out=ncar[:, dt:dt + 1], in0=ncar[:, dt:dt + 1], in1=tc,
                op=ALU.subtract), [b_ncar], [b_ncar])
            vop(lambda dt=dt, dl_r=dl_r, uci=uci: nc.vector.tensor_tensor(
                out=ncar[:, NDT + dt:NDT + dt + 1], in0=dl_r, in1=uci,
                op=ALU.mult), [b_dRs[dt], b_tUc], [b_ncar])
            vop(lambda tc=tc, dl_i=dl_i, ucr=ucr: nc.vector.tensor_tensor(
                out=tc, in0=dl_i, in1=ucr, op=ALU.mult), [b_dIs[dt], b_tUc], [b_ncar])
            vop(lambda dt=dt, tc=tc: nc.vector.tensor_tensor(
                out=ncar[:, NDT + dt:NDT + dt + 1],
                in0=ncar[:, NDT + dt:NDT + dt + 1], in1=tc, op=ALU.add),
                [b_ncar], [b_ncar])
            vop(lambda base=base: nc.vector.tensor_tensor(
                out=dRb[:, base:base + CH], in0=dRb[:, base:base + CH],
                in1=tA[:, base:base + CH], op=ALU.mult), [b_dRs[dt], b_tA], [b_dRs[dt]])
            vop(lambda base=base, wIb=wIb: nc.vector.tensor_tensor(
                out=wIb, in0=dIb[:, base:base + CH], in1=tB[:, base:base + CH],
                op=ALU.mult), [b_dIs[dt], b_tB], [b_wI])
            vop(lambda base=base, wIb=wIb: nc.vector.tensor_tensor(
                out=dRb[:, base:base + CH], in0=dRb[:, base:base + CH], in1=wIb,
                op=ALU.subtract), [b_dRs[dt], b_wI], [b_dRs[dt]])
        vop(lambda: nc.vector.tensor_copy(out=carry[:], in_=ncar[:, 0:2 * NDT]),
            [b_ncar], [b_carry])
        for a in range(TPC):
            for g0 in range(0, NDT, 4):
                srcs = [dRb[:, (g0 + j) * CH + a * 128:(g0 + j) * CH + a * 128 + 128]
                        for j in range(4)]
                dsl = xo1[:, a * D + g0 * 128: a * D + (g0 + 4) * 128]
                xsl = xi[:, a * D + g0 * 128: a * D + (g0 + 4) * 128]
                tp_group_f32_add(srcs, [b_dRs[g0 + j] for j in range(4)],
                                 dsl, b_xo1, xsl, b_xi)
        dst = xs1[ch * CH:(ch + 1) * CH, :].rearrange("(a p) d -> p a d", p=128)
        plan.dma("pool", lambda d=dst: nc.gpsimd.dma_start(out=d, in_=xo1[:].rearrange("p (a d) -> p a d", d=D)),
                 reads=[b_xo1], writes=[Buf("xs1w")])

    plan.barrier()
    tpf_psL.pop()
    p1.close()

    def emit_ffn(esx, tag, src_d, dst_d):
        wg_sb, b_wg = mksb(esx, f"{tag}wg", [128, NDT * DI], BF16)
        wu_sb, b_wu = mksb(esx, f"{tag}wu", [128, NDT * DI], BF16)
        wd_sb, b_wd = mksb(esx, f"{tag}wd", [128, NIT * D], BF16)
        dma_in("sp", wg_sb[:], wgq_d[:], b_wg)
        dma_in("sp", wu_sb[:], wuq_d[:], b_wu)
        dma_in("sp", wd_sb[:], wdq_d[:], b_wd)
        fxi, b_fxi = mksb(esx, f"{tag}xin", [128, TPC * D], F32)
        xq4, b_xq4 = mksb(esx, f"{tag}xq4", [128, TPC * D], BF16)
        xqT, b_xqT = mksb(esx, f"{tag}xqT", [128, NDT * 128], BF16)
        hb, _b_hb = mksb(esx, f"{tag}hb", [128, DI], F32)
        b_hb2 = [Buf(f"{tag}hb0"), Buf(f"{tag}hb1")]
        hq, b_hq = mksb(esx, f"{tag}hq", [128, DI], BF16)
        hqT, b_hqT = mksb(esx, f"{tag}hqT", [128, NIT * 128], BF16)
        WQ = 960
        a2_, b_a2_ = mksb(esx, f"{tag}qa2", [128, 1024], BF16)
        zs_, b_zs_ = mksb(esx, f"{tag}qzs", [128, 1024], BF16)
        msk_, b_msk_ = mksb(esx, f"{tag}qmsk", [128, 1024], mybir.dt.uint8)
        red_, b_red_ = mksb(esx, f"{tag}qred", [128, 64], F32)
        sc_, b_sc_ = mksb(esx, f"{tag}qsc", [128, 64], F32)
        rcp_, b_rcp_ = mksb(esx, f"{tag}qrcp", [128, 64], F32)
        scrs = []
        for v in range(2):
            z_, b_z_ = mksb(esx, f"{tag}qz{v}", [128, 1024], F32)
            scrs.append({"z": (z_, b_z_), "a2": (a2_, b_a2_), "zs": (zs_, b_zs_),
                         "red": (red_, b_red_), "sc": (sc_, b_sc_),
                         "rcp": (rcp_, b_rcp_), "msk": (msk_, b_msk_)})
        qctr = [0]

        def scr_next():
            s = scrs[qctr[0] % 2]
            qctr[0] += 1
            return s
        gpsL = [mkps(esx, f"{tag}gps0", [128, 512], F32),
                mkps(esx, f"{tag}gps1", [128, 512], F32)]
        upsL = [mkps(esx, f"{tag}ups0", [128, 512], F32),
                mkps(esx, f"{tag}ups1", [128, 512], F32)]
        dps = [mkps(esx, f"{tag}dps0", [128, 512], F32),
               mkps(esx, f"{tag}dps1", [128, 512], F32)]
        ISL = []
        s0 = 0
        while s0 < DI:
            ISL.append((s0, min(512, DI - s0)))
            s0 += 512
        for ch in range(NCH):
            src = src_d[ch * CH:(ch + 1) * CH, :].rearrange("(a p) d -> p a d", p=128)
            dma_in("sp", fxi[:].rearrange("p (a d) -> p a d", d=D), src, b_fxi,
                   rbufs=[b_fxi])
            for a in range(TPC):
                xh_ap = xh2t[:, (a % 2) * D:(a % 2 + 1) * D]
                b_xh_s = b_xh2[a % 2]
                emit_ln(fxi[:, a * D:(a + 1) * D], b_fxi, xh_ap, b_xh_s,
                        sq_over=(hb[:, 0:D], b_hb2[0]))
                emit_quant(xh_ap, b_xh_s, xq4[:, a * D:(a + 1) * D], b_xq4, D, scr_next())
            for a in range(TPC):
                for g0 in range(0, NDT, 4):
                    srcs = [xq4[:, a * D + (g0 + j) * 128: a * D + (g0 + j + 1) * 128]
                            for j in range(4)]
                    tp_group_bf(srcs, [b_xq4],
                                xqT[:, g0 * 128:(g0 + 4) * 128], b_xqT)
                for si, (s0, sw) in enumerate(ISL):
                    gps, b_gps = gpsL[si % 2]
                    ups, b_ups = upsL[si % 2]
                    for (ps, b_ps, wsb, b_wsb) in ((gps, b_gps, wg_sb, b_wg),
                                                   (ups, b_ups, wu_sb, b_wu)):
                        fns = [lambda dt=dt, ps=ps, wsb=wsb, s0=s0, sw=sw:
                               nc.tensor.matmul(
                                   out=ps[:, :sw],
                                   lhsT=xqT[:, dt * 128:(dt + 1) * 128],
                                   rhs=wsb[:, dt * DI + s0: dt * DI + s0 + sw],
                                   start=(dt == 0), stop=(dt == NDT - 1))
                               for dt in range(NDT)]
                        plan.group("pe", fns, reads=[b_xqT, b_wsb], writes=[b_ps],
                                   dur=2000)
                    hbs = hb[:, s0:s0 + sw]
                    halves = sorted({s0 // (DI // 2), (s0 + sw - 1) // (DI // 2)})
                    wr = [b_hb2[h] for h in halves]
                    plan.op("act", lambda sw=sw, gps=gps, hbs=hbs: nc.scalar.activation(
                        out=hbs, in_=gps[:, :sw], func=AF.Silu),
                        reads=[b_gps], writes=wr, dur=600)
                    plan.op("dve", lambda sw=sw, ups=ups, hbs=hbs:
                            nc.vector.tensor_tensor(
                        out=hbs, in0=hbs, in1=ups[:, :sw],
                        op=ALU.mult), reads=[b_ups] + wr, writes=wr, dur=700)
                for (q0, qw) in ((0, 960), (960, 960), (1920, 896)):
                    hv = [b_hb2[h] for h in sorted({(2 * q0) // DI,
                                                    (2 * (q0 + qw - 1)) // DI})]
                    emit_quant(hb[:, q0:q0 + qw], hv[0] if len(hv) == 1 else hv,
                               hq[:, q0:q0 + qw], b_hq, qw, scr_next())
                for g0 in range(0, NIT, 4):
                    gn = min(4, NIT - g0)
                    srcs = [hq[:, (g0 + j) * 128:(g0 + j + 1) * 128]
                            for j in range(gn)]
                    tp_group_bf(srcs, [b_hq],
                                hqT[:, g0 * 128:(g0 + gn) * 128], b_hqT)
                for half in range(2):
                    ps, b_ps = dps[half]
                    fns = [lambda it=it, ps=ps, half=half: nc.tensor.matmul(
                        out=ps[:, :512],
                        lhsT=hqT[:, it * 128:(it + 1) * 128],
                        rhs=wd_sb[:, it * D + half * 512: it * D + half * 512 + 512],
                        start=(it == 0), stop=(it == NIT - 1))
                        for it in range(NIT)]
                    plan.group("pe", fns, reads=[b_hqT, b_wd], writes=[b_ps])
                    plan.op("dve", lambda a=a, half=half, ps=ps:
                            nc.vector.tensor_tensor(
                                out=fxi[:, a * D + half * 512: a * D + half * 512 + 512],
                                in0=ps[:, :512],
                                in1=fxi[:, a * D + half * 512: a * D + half * 512 + 512],
                                op=ALU.add), reads=[b_ps, b_fxi], writes=[b_fxi])
            dst = dst_d[ch * CH:(ch + 1) * CH, :].rearrange("(a p) d -> p a d", p=128)
            plan.dma("pool", lambda d=dst: nc.gpsimd.dma_start(out=d, in_=fxi[:].rearrange("p (a d) -> p a d", d=D)),
                     reads=[b_fxi], writes=[Buf(f"{tag}out")])

    p2 = ExitStack()
    emit_ffn(p2, "p2", xs1, xs2)
    plan.barrier()
    p2.close()

    p3 = ExitStack()
    wgT_sb, b_wgT = mksb(p3, "p3wgT", [128, NDT * D], BF16)
    wrT_sb, b_wrT = mksb(p3, "p3wrT", [128, NDT * D], BF16)
    woT_sb, b_woT = mksb(p3, "p3woT", [128, NDT * D], BF16)
    dma_in("sp", wgT_sb[:], wgT_d[:], b_wgT)
    dma_in("sp", wrT_sb[:], wrT_d[:], b_wrT)
    dma_in("sp", woT_sb[:], woT_d[:], b_woT)
    xi3, b_xi3 = mksb(p3, "p3xin", [128, TPC * D], F32)
    XT3, b_XT3 = mksb(p3, "p3XT", [128, NDT * CH], F32)
    G3, b_G3 = mksb(p3, "p3G", [128, NDT * CH], F32)
    V3t, b_V3t = mksb(p3, "p3V", [128, 2 * CH], F32)
    b_V3s = [Buf("V3s0"), Buf("V3s1")]
    HxA, _bA = mksb(p3, "p3HxA", [128, NDT * (CH + 1)], BF16)
    HxB, _bB = mksb(p3, "p3HxB", [128, NDT * (CH + 1)], BF16)
    Hf, _bH = mksb(p3, "p3Hf", [128, NDT * CH], F32)
    b_HxA = [Buf(f"HxA{i}") for i in range(NDT)]
    b_HxB = [Buf(f"HxB{i}") for i in range(NDT)]
    b_Hfb = [Buf(f"Hf{i}") for i in range(NDT)]
    hcar, b_hcar = mksb(p3, "p3hcar", [128, NDT], F32)
    Nrm4, b_Nrm4 = mksb(p3, "p3Nrm4", [128, TPC * D], BF16)
    YT3, b_YT3 = mksb(p3, "p3YT", [128, NDT * CH], BF16)
    OT3, b_OT3 = mksb(p3, "p3OT", [128, NDT * CH], F32)
    XTb = OT3[:].bitcast(BF16)[:, :NDT * CH]
    b_XTb = b_OT3
    o3, b_o3 = mksb(p3, "p3o", [128, TPC * D], F32)
    pps = [mkps(p3, "p3ps0", [128, CH], F32), mkps(p3, "p3ps1", [128, CH], F32)]
    tp_psL.append(mkps(p3, "tp_ps3b", [128, 512], BF16))
    tpf_psL.append(mkps(p3, "tpf_ps3b", [128, 512], F32))
    lsq, b_lsqep = mksb(p3, "p3lsq", [128, D], F32)
    _sqdefault[0] = (xh2t[:, 0:D], b_xh2[0])

    plan.op("dve", lambda: nc.vector.memset(hcar[:], 0.0), writes=[b_hcar])

    for ch in range(NCH):
        src = xs2[ch * CH:(ch + 1) * CH, :].rearrange("(a p) d -> p a d", p=128)
        dma_in("sp", xi3[:].rearrange("p (a d) -> p a d", d=D), src, b_xi3, rbufs=[b_xi3])
        XT3v = XT3[:].rearrange("p (t q) -> p t q", q=CH)
        XTbv = XTb.rearrange("p (t q) -> p t q", q=CH)
        for a in range(TPC):
            xh_ap = xh2t[:, (a % 2) * D:(a % 2 + 1) * D]
            b_xh_s = b_xh2[a % 2]
            emit_ln(xi3[:, a * D:(a + 1) * D], b_xi3, xh_ap, b_xh_s)
            for g0 in range(0, NDT, 4):
                srcs = [xh_ap[:, (g0 + j) * 128:(g0 + j + 1) * 128]
                        for j in range(4)]
                tpf_, b_tpf_ = tpf_psL[_tpctr[0] % len(tpf_psL)]
                _tpctr[0] += 1
                fns = [lambda s=s, i=i, tpf_=tpf_: nc.tensor.transpose(
                    out=tpf_[:, i * 128:(i + 1) * 128], in_=s, identity=idf[:])
                    for i, s in enumerate(srcs)]
                plan.group("pe", fns, reads=[b_xh_s, b_idf], writes=[b_tpf_])
                b_tpf = b_tpf_
                ps3 = tpf_[:, :512].rearrange("p (t q) -> p t q", q=128)
                plan.op("act", lambda a=a, g0=g0, ps3=ps3: nc.scalar.activation(
                    out=XT3v[:, g0:g0 + 4, a * 128:a * 128 + 128], in_=ps3,
                    func=AF.Copy), reads=[b_tpf], writes=[b_XT3])
                plan.op("act", lambda a=a, g0=g0, ps3=ps3: nc.scalar.activation(
                    out=XTbv[:, g0:g0 + 4, a * 128:a * 128 + 128], in_=ps3,
                    func=AF.Copy), reads=[b_tpf], writes=[b_XTb])
        for et in range(NDT):
            ps, b_ps = pps[et % 2]
            fns = [lambda dt=dt, et=et, ps=ps: nc.tensor.matmul(
                out=ps[:, :CH],
                lhsT=wgT_sb[:, dt * D + et * 128: dt * D + et * 128 + 128],
                rhs=XTb[:, dt * CH:(dt + 1) * CH],
                start=(dt == 0), stop=(dt == NDT - 1)) for dt in range(NDT)]
            plan.group("pe", fns, reads=[b_XTb, b_wgT], writes=[b_ps])
            plan.op("act", lambda et=et, ps=ps: nc.scalar.activation(
                out=G3[:, et * CH:(et + 1) * CH], in_=ps[:, :CH], func=AF.Sigmoid,
                bias=bgv[:, et:et + 1]), reads=[b_ps, b_bgv], writes=[b_G3])
            vop(lambda et=et: nc.vector.tensor_scalar(
                out=G3[:, et * CH:(et + 1) * CH], in0=G3[:, et * CH:(et + 1) * CH],
                scalar1=pv[:, COL_OM + et:COL_OM + et + 1], scalar2=None,
                op0=ALU.mult), [b_G3, b_pv], [b_G3])
        for dt in range(NDT):
            vop(lambda dt=dt: nc.vector.tensor_scalar(
                out=XT3[:, dt * CH:(dt + 1) * CH], in0=XT3[:, dt * CH:(dt + 1) * CH],
                scalar1=pv[:, COL_BR + dt:COL_BR + dt + 1], scalar2=None,
                op0=ALU.subtract), [b_XT3, b_pv], [b_XT3])
        plan.op("dve", lambda: nc.vector.memset(HxA[:], 0.0), reads=[],
                writes=list(b_HxA), dur=900)
        for dt in range(NDT):
            vop(lambda dt=dt: nc.vector.tensor_copy(
                out=HxA[:, dt * (CH + 1):dt * (CH + 1) + 1], in_=hcar[:, dt:dt + 1]),
                [b_hcar], [b_HxA[dt]])
        cur, b_cur, nxt, b_nxt = HxA, b_HxA, HxB, b_HxB
        for k in range(K):
            last = (k == K - 1)
            for n in range(NDT):
                ps, b_ps = pps[n % 2]
                fns = [lambda dt=dt, n=n, ps=ps, cur=cur: nc.tensor.matmul(
                    out=ps[:, :CH],
                    lhsT=wrT_sb[:, dt * D + n * 128: dt * D + n * 128 + 128],
                    rhs=cur[:, dt * (CH + 1): dt * (CH + 1) + CH],
                    start=(dt == 0), stop=(dt == NDT - 1)) for dt in range(NDT)]
                plan.group("pe", fns, reads=list(b_cur) + [b_wrT], writes=[b_ps],
                           dur=1900)
                V3 = V3t[:, (n % 2) * CH:(n % 2 + 1) * CH]
                b_V3 = b_V3s[n % 2]
                plan.op("dve", lambda n=n, ps=ps, V3=V3: nc.vector.scalar_tensor_tensor(
                    out=V3, in0=ps[:, :CH], scalar=-1.0,
                    in1=XT3[:, n * CH:(n + 1) * CH], op0=ALU.mult, op1=ALU.add),
                    reads=[b_ps, b_XT3], writes=[b_V3], dur=700)
                plan.op("dve", lambda n=n, V3=V3: nc.vector.tensor_tensor(
                    out=V3, in0=V3, in1=G3[:, n * CH:(n + 1) * CH],
                    op=ALU.mult), reads=[b_V3, b_G3], writes=[b_V3], dur=700)
                a_rep = pv[:, COL_A + n:COL_A + n + 1].broadcast_to([128, CH])
                if last:
                    plan.op("dve", lambda n=n, a_rep=a_rep, V3=V3: nc.vector.tensor_tensor_scan(
                        out=Hf[:, n * CH:(n + 1) * CH], data0=a_rep, data1=V3,
                        initial=hcar[:, n:n + 1], op0=ALU.mult, op1=ALU.add),
                        reads=[b_V3, b_hcar, b_pv], writes=[b_Hfb[n]], dur=1400)
                else:
                    plan.op("dve", lambda n=n, a_rep=a_rep, nxt=nxt, V3=V3: nc.vector.tensor_tensor_scan(
                        out=nxt[:, n * (CH + 1) + 1: n * (CH + 1) + 1 + CH],
                        data0=a_rep, data1=V3,
                        initial=hcar[:, n:n + 1], op0=ALU.mult, op1=ALU.add),
                        reads=[b_V3, b_hcar, b_pv], writes=[b_nxt[n]], dur=1400)
            if not last:
                if k == 0:
                    for dt in range(NDT):
                        vop(lambda dt=dt, nxt=nxt: nc.vector.tensor_copy(
                            out=nxt[:, dt * (CH + 1):dt * (CH + 1) + 1],
                            in_=hcar[:, dt:dt + 1]), [b_hcar], [b_nxt[dt]])
                cur, b_cur, nxt, b_nxt = nxt, b_nxt, cur, b_cur
        for dt in range(NDT):
            vop(lambda dt=dt: nc.vector.tensor_copy(
                out=hcar[:, dt:dt + 1], in_=Hf[:, dt * CH + CH - 1: dt * CH + CH]),
                [b_Hfb[dt]], [b_hcar])
        for a in range(TPC):
            for g0 in range(0, NDT, 4):
                srcs = [Hf[:, (g0 + j) * CH + a * 128:(g0 + j) * CH + a * 128 + 128]
                        for j in range(4)]
                tp_group_f32(srcs, [b_Hfb[g0 + j] for j in range(4)],
                             lsq[:, g0 * 128:(g0 + 4) * 128], b_lsqep)
            emit_ln_noscratch(lsq[:, 0:D], b_lsqep, Nrm4[:, a * D:(a + 1) * D], b_Nrm4)
        for dt in range(NDT):
            srcs = [Nrm4[:, a * D + dt * 128: a * D + dt * 128 + 128]
                    for a in range(TPC)]
            tp_group_bf(srcs, [b_Nrm4], YT3[:, dt * CH:(dt + 1) * CH], b_YT3)
        for dt in range(NDT):
            vop(lambda dt=dt: nc.vector.tensor_scalar(
                out=YT3[:, dt * CH:(dt + 1) * CH], in0=YT3[:, dt * CH:(dt + 1) * CH],
                scalar1=pv[:, COL_LNW + dt:COL_LNW + dt + 1],
                scalar2=pv[:, COL_LNB + dt:COL_LNB + dt + 1],
                op0=ALU.mult, op1=ALU.add), [b_YT3, b_pv], [b_YT3])
        for et in range(NDT):
            ps, b_ps = pps[et % 2]
            fns = [lambda dt=dt, et=et, ps=ps: nc.tensor.matmul(
                out=ps[:, :CH],
                lhsT=woT_sb[:, dt * D + et * 128: dt * D + et * 128 + 128],
                rhs=YT3[:, dt * CH:(dt + 1) * CH],
                start=(dt == 0), stop=(dt == NDT - 1)) for dt in range(NDT)]
            plan.group("pe", fns, reads=[b_YT3, b_woT], writes=[b_ps])
            vop(lambda et=et, ps=ps: nc.vector.tensor_scalar(
                out=OT3[:, et * CH:(et + 1) * CH], in0=ps[:, :CH],
                scalar1=pv[:, COL_BO + et:COL_BO + et + 1], scalar2=None,
                op0=ALU.add), [b_ps, b_pv], [b_OT3])
        for a in range(TPC):
            for g0 in range(0, NDT, 4):
                srcs = [OT3[:, (g0 + j) * CH + a * 128:(g0 + j) * CH + a * 128 + 128]
                        for j in range(4)]
                dsl = o3[:, a * D + g0 * 128: a * D + (g0 + 4) * 128]
                xsl = xi3[:, a * D + g0 * 128: a * D + (g0 + 4) * 128]
                tp_group_f32_add(srcs, [b_OT3], dsl, b_o3, xsl, b_xi3)
        dst = xs3[ch * CH:(ch + 1) * CH, :].rearrange("(a p) d -> p a d", p=128)
        plan.dma("pool", lambda d=dst: nc.gpsimd.dma_start(out=d, in_=o3[:].rearrange("p (a d) -> p a d", d=D)),
                 reads=[b_o3], writes=[Buf("xs3w")])

    plan.barrier()
    tp_psL.pop()
    tpf_psL.pop()
    p3.close()

    p4 = ExitStack()
    emit_ffn(p4, "p4", xs3, out_d)
    plan.barrier()
    p4.close()

    plan.emit()

    with nc.Block() as block:
        @block.sync
        def _(eng):
            plan._run_engine("sp")

        @block.scalar
        def _(eng):
            plan._run_engine("act")

        @block.vector
        def _(eng):
            plan._run_engine("dve")

        @block.tensor
        def _(eng):
            plan._run_engine("pe")

        @block.gpsimd
        def _(eng):
            plan._run_engine("pool")
            plan.finish_waits("pool")

    outer.close()
    plan._es.close()
    return nc


def kernel(**inputs):
    import ml_dtypes
    from concourse.bass_utils import run_bass_kernel_spmd

    x = np.ascontiguousarray(np.asarray(inputs["L0_attn_a_x"], np.float32))
    B, L, _ = x.shape
    nc = build_nc(L=L)
    iden_bf = np.eye(128, dtype=ml_dtypes.bfloat16)
    iden_f32 = np.eye(128, dtype=np.float32)
    wnames = [k for k in inputs.keys() if k != "L0_attn_a_x"]
    in_maps = []
    for c in range(B):
        m = {nm: np.ascontiguousarray(np.asarray(inputs[nm], np.float32))
             for nm in wnames}
        m["x0"] = np.ascontiguousarray(x[c])
        m["iden_bf"] = iden_bf
        m["iden_f32"] = iden_f32
        in_maps.append(m)
    res = run_bass_kernel_spmd(nc, in_maps, core_ids=list(range(B)))
    return np.stack([np.asarray(r["out"], np.float32) for r in res.results], axis=0)


# version 19
# speedup vs baseline: 1.0378x; 1.0062x over previous
import numpy as np
from contextlib import ExitStack

import concourse.bass as bass
import concourse.mybir as mybir

F32 = mybir.dt.float32
BF16 = mybir.dt.bfloat16
AF = mybir.ActivationFunctionType
ALU = mybir.AluOpType
AXX = mybir.AxisListType.X

D = 1024
DI = 2816
NDT = D // 128
NIT = DI // 128
C2V = float(2 ** 22 + 1)
CBV = float(1.5 * 2 ** 23)

ENGS = ("sp", "act", "dve", "pe", "pool")
_EMAP = {"sp": "sync", "act": "scalar", "dve": "vector", "pe": "tensor",
         "pool": "gpsimd"}


class Buf:
    __slots__ = ("name", "w", "r")

    def __init__(self, name):
        self.name = name
        self.w = None
        self.r = []


class Node:
    __slots__ = ("idx", "eng", "fns", "deps", "dur", "is_dma", "sem", "val")

    def __init__(self, idx, eng, fns, deps, dur, is_dma=False):
        self.idx = idx
        self.eng = eng
        self.fns = fns
        self.deps = deps
        self.dur = dur
        self.is_dma = is_dma
        self.sem = None
        self.val = None


class Plan:

    def __init__(self, nc):
        self.nc = nc
        self.nodes = []
        self.sem = {}
        self._es = ExitStack()
        for e in ENGS:
            self.sem[e] = self._es.enter_context(nc.semaphore(f"sem_{e}"))
        self.NDMA = 24
        self.dpool = {"hw": list(range(16)), "sw": list(range(16, 24))}
        for i in range(self.NDMA):
            self.sem[f"dp{i}"] = self._es.enter_context(
                nc.semaphore(f"sem_dp{i}"))
        self._barrier_idx = -1

    def _mknode(self, eng, fns, reads, writes, dur, is_dma=False):
        deps = set()
        for b in reads:
            if b.w is not None:
                deps.add(b.w)
        for b in writes:
            if b.w is not None:
                deps.add(b.w)
            deps.update(b.r)
        if self._barrier_idx >= 0:
            deps.add(self._barrier_idx)
        idx = len(self.nodes)
        n = Node(idx, eng, fns, deps, dur, is_dma)
        self.nodes.append(n)
        for b in writes:
            b.w = idx
            b.r = []
        for b in reads:
            b.r.append(idx)
        return idx

    def op(self, eng, fn, reads=(), writes=(), dur=600):
        return self._mknode(eng, [fn], reads, writes, dur)

    def group(self, eng, fns, reads=(), writes=(), dur=None):
        if dur is None:
            dur = 250 * len(fns)
        return self._mknode(eng, list(fns), reads, writes, dur)

    def dma(self, queue, fn, reads=(), writes=(), dur=4000):
        return self._mknode(queue, [fn], reads, writes, dur, is_dma=True)

    def barrier(self):
        idx = self._mknode("sp", None, [], [], 0)
        n = self.nodes[idx]
        n.deps = set(range(idx))
        self._barrier_idx = idx
        return idx

    def emit(self):
        import heapq
        nodes = self.nodes
        N = len(nodes)
        nsucc = [[] for _ in range(N)]
        ndep = [0] * N
        for n in nodes:
            ndep[n.idx] = len(n.deps)
            for d in n.deps:
                nsucc[d].append(n.idx)
        eng_clock = {e: 0.0 for e in ENGS}
        finish = [0.0] * N
        ready_t = [0.0] * N
        heap = []
        for n in nodes:
            if ndep[n.idx] == 0:
                heapq.heappush(heap, (0.0, n.idx))
        order = {e: [] for e in ENGS}
        done = 0
        WINDOW = 4096
        scheduled = [False] * N
        min_unsched = 0
        poolq = {e: [] for e in ENGS}
        while done < N:
            while heap:
                rt, i = heapq.heappop(heap)
                heapq.heappush(poolq[nodes[i].eng], (nodes[i].idx,))
            gate = min_unsched + WINDOW
            cands = [(eng_clock[e], e) for e in ENGS
                     if poolq[e] and poolq[e][0][0] <= gate]
            if not cands:
                cands = [(eng_clock[e], e) for e in ENGS
                         if poolq[e] and poolq[e][0][0] == min_unsched]
                if not cands:
                    raise RuntimeError("scheduler stuck")
            _, e = min(cands)
            (i,) = heapq.heappop(poolq[e])
            scheduled[i] = True
            while min_unsched < N and scheduled[min_unsched]:
                min_unsched += 1
            n = nodes[i]
            start = max(eng_clock[e], ready_t[i])
            dma_lat = 1500 if n.is_dma else 0
            finish[i] = start + n.dur + dma_lat
            eng_clock[e] = start + (150 if n.is_dma else n.dur)
            order[e].append(i)
            done += 1
            for s in nsucc[i]:
                ready_t[s] = max(ready_t[s], finish[i])
                ndep[s] -= 1
                if ndep[s] == 0:
                    heapq.heappush(heap, (ready_t[s], s))
        eng_pos = {e: 0 for e in ENGS}
        dnext = {"hw": 0, "sw": 0}
        dcnt = [0] * self.NDMA
        dma_prev = {}
        for e in ENGS:
            for i in order[e]:
                n = nodes[i]
                if n.is_dma:
                    pool = "sw" if e == "pool" else "hw"
                    lst = self.dpool[pool]
                    di = lst[dnext[pool]]
                    dnext[pool] = (dnext[pool] + 1) % len(lst)
                    dcnt[di] += 16
                    n.sem = f"dp{di}"
                    n.val = dcnt[di]
                elif n.fns is not None:
                    eng_pos[e] += 1
                    n.sem = e
                    n.val = eng_pos[e]
        waited = {e: {} for e in ENGS}

        def emit_wait(e, semkey, val):
            if val <= waited[e].get(semkey, 0):
                return
            waited[e][semkey] = val
            g = getattr(self.nc, _EMAP[e])
            g.wait_ge(self.sem[semkey], val)

        self.final_dma = [(f"dp{i}", dcnt[i]) for i in range(self.NDMA) if dcnt[i]]
        barrier_cache = {}

        def barrier_maxes(bidx):
            if bidx not in barrier_cache:
                mx = {}
                for d2 in nodes[bidx].deps:
                    dn2 = nodes[d2]
                    if dn2.sem is not None:
                        if dn2.val > mx.get(dn2.sem, 0):
                            mx[dn2.sem] = dn2.val
                    elif dn2.fns is None and d2 in barrier_cache:
                        for k, v in barrier_cache[d2].items():
                            if v > mx.get(k, 0):
                                mx[k] = v
                barrier_cache[bidx] = mx
            return barrier_cache[bidx]

        def run_engine(e):
            prev_dma_on_sem = {}
            for i in order[e]:
                n = nodes[i]
                if n.is_dma:
                    prev = prev_dma_on_sem.get(n.sem)
                    if prev is not None:
                        emit_wait(e, n.sem, prev)
                    elif n.val > 16:
                        emit_wait(e, n.sem, n.val - 16)
                    prev_dma_on_sem[n.sem] = n.val
                for d in sorted(n.deps):
                    dn = nodes[d]
                    if dn.sem is None:
                        if dn.fns is None:
                            for k, v in sorted(barrier_maxes(d).items()):
                                emit_wait(e, k, v)
                        continue
                    emit_wait(e, dn.sem, dn.val)
                if not n.fns:
                    continue
                h = self.sem[n.sem]
                inc = 16 if n.is_dma else 1
                for j, fn in enumerate(n.fns):
                    if j == len(n.fns) - 1:
                        fn().then_inc(h, inc)
                    else:
                        fn()
        self._run_engine = run_engine

    def finish_waits(self, e):
        g = getattr(self.nc, _EMAP[e])
        for key, val in self.final_dma:
            g.wait_ge(self.sem[key], val)


def build_nc(L=8192, K=10, CH=512):
    nc = bass.Bass("TRN2", target_bir_lowering=False, debug=False)
    NCH = L // CH
    TPC = CH // 128
    assert TPC == 4

    x0_d = nc.dram_tensor("x0", [L, D], F32, kind="ExternalInput").ap()
    win = {}
    for nm, shp in [
        ("L0_attn_a_phazor", [D, 2]), ("L0_attn_a_phazor_init", [D, 2]),
        ("L0_attn_a_last_conv_re", [D]), ("L0_attn_a_last_conv_im", [D]),
        ("L0_attn_b_decay", [D]), ("L0_attn_b_Wr", [D, D]),
        ("L0_attn_b_br", [D]), ("L0_attn_b_Wg", [D, D]), ("L0_attn_b_bg", [D]),
        ("L0_attn_b_Wo", [D, D]), ("L0_attn_b_bo", [D]),
        ("L0_attn_b_ln_w", [D]), ("L0_attn_b_ln_b", [D]),
        ("L0_ffn_w_gate", [DI, D]), ("L0_ffn_w_up", [DI, D]),
        ("L0_ffn_w_down", [D, DI]),
    ]:
        win[nm] = nc.dram_tensor(nm, shp, F32, kind="ExternalInput").ap()
    iden_bf_d = nc.dram_tensor("iden_bf", [128, 128], BF16, kind="ExternalInput").ap()
    iden_f32_d = nc.dram_tensor("iden_f32", [128, 128], F32, kind="ExternalInput").ap()
    out_d = nc.dram_tensor("out", [L, D], F32, kind="ExternalOutput").ap()

    xs1 = nc.dram_tensor("xs1", [L, D], F32, kind="Internal").ap()
    xs2 = nc.dram_tensor("xs2", [L, D], F32, kind="Internal").ap()
    xs3 = nc.dram_tensor("xs3", [L, D], F32, kind="Internal").ap()
    wgq_d = nc.dram_tensor("wgq_d", [128, NDT * DI], BF16, kind="Internal").ap()
    wuq_d = nc.dram_tensor("wuq_d", [128, NDT * DI], BF16, kind="Internal").ap()
    wdq_d = nc.dram_tensor("wdq_d", [128, NIT * D], BF16, kind="Internal").ap()
    wgT_d = nc.dram_tensor("wgT_d", [128, NDT * D], BF16, kind="Internal").ap()
    wrT_d = nc.dram_tensor("wrT_d", [128, NDT * D], BF16, kind="Internal").ap()
    woT_d = nc.dram_tensor("woT_d", [128, NDT * D], BF16, kind="Internal").ap()

    plan = Plan(nc)
    outer = ExitStack()

    def mksb(esx, name, shape, dt):
        t = esx.enter_context(nc.sbuf_tensor(name, shape, dt))
        return t, Buf(name)

    def mkps(esx, name, shape, dt):
        t = esx.enter_context(nc.psum_tensor(name, shape, dt))
        return t, Buf(name)

    idbf, b_idbf = mksb(outer, "idbf", [128, 128], BF16)
    idf, b_idf = mksb(outer, "idf", [128, 128], F32)
    pv, b_pv = mksb(outer, "pv", [128, 64], F32)
    (COL_A, COL_OM, COL_BR, COL_BO, COL_LNW, COL_LNB, COL_R) = (
        0, 8, 16, 24, 32, 40, 48)
    bgv, b_bgv = mksb(outer, "bgv", [128, NDT], F32)
    carry, b_carry = mksb(outer, "carry", [128, 2 * NDT], F32)
    xh2t, b_xh2t = mksb(outer, "xh", [128, 2 * D], F32)
    b_xh2 = [Buf("xh_s0"), Buf("xh_s1")]
    lnt, b_lnt = mksb(outer, "lnt", [128, 32], F32)
    lsq, b_lsq = mksb(outer, "lsq", [128, D], F32)
    tUc, b_tUc = mksb(outer, "tUc", [128, 2 * NDT], F32)
    q0t, b_q0t = mksb(outer, "q0t", [128, 2 * NDT], F32)
    tp_psL = [mkps(outer, "tp_ps0", [128, 512], BF16)]
    tpf_psL = [mkps(outer, "tpf_ps0", [128, 512], F32)]
    _tpctr = [0]
    tp_ps, b_tp = tp_psL[0]
    tpf_ps, b_tpf = tpf_psL[0]

    def dma_in(queue, dst_ap, src_ap, wbuf, rbufs=()):
        plan.dma(queue,
                 lambda d=dst_ap, s=src_ap: getattr(nc, _EMAP[queue]).dma_start(out=d, in_=s),
                 reads=list(rbufs), writes=[wbuf])

    dma_in("sp", idbf[:], iden_bf_d[:], b_idbf)
    dma_in("sp", idf[:], iden_f32_d[:], b_idf)

    def tp_group_bf(srcs, b_srcs, dst_ap, b_dst):
        ps, b_ps = tp_psL[_tpctr[0] % len(tp_psL)]
        _tpctr[0] += 1
        n = len(srcs)
        fns = [lambda s=s, i=i, ps=ps: nc.tensor.transpose(
            out=ps[:, i * 128:(i + 1) * 128], in_=s, identity=idbf[:])
            for i, s in enumerate(srcs)]
        plan.group("pe", fns, reads=list(b_srcs) + [b_idbf], writes=[b_ps],
                   dur=300 * n)
        plan.op("act", lambda n=n, ps=ps: nc.scalar.activation(
            out=dst_ap, in_=ps[:, :n * 128], func=AF.Copy),
            reads=[b_ps], writes=[b_dst], dur=600)

    def tp_group_f32(srcs, b_srcs, dst_ap, b_dst):
        ps, b_ps = tpf_psL[_tpctr[0] % len(tpf_psL)]
        _tpctr[0] += 1
        n = len(srcs)
        fns = [lambda s=s, i=i, ps=ps: nc.tensor.transpose(
            out=ps[:, i * 128:(i + 1) * 128], in_=s, identity=idf[:])
            for i, s in enumerate(srcs)]
        plan.group("pe", fns, reads=list(b_srcs) + [b_idf], writes=[b_ps],
                   dur=300 * n)
        plan.op("act", lambda n=n, ps=ps: nc.scalar.activation(
            out=dst_ap, in_=ps[:, :n * 128], func=AF.Copy),
            reads=[b_ps], writes=[b_dst], dur=600)

    def tp_group_f32_add(srcs, b_srcs, dst_ap, b_dst, add_ap, b_add):
        ps, b_ps = tpf_psL[_tpctr[0] % len(tpf_psL)]
        _tpctr[0] += 1
        n = len(srcs)
        fns = [lambda s=s, i=i, ps=ps: nc.tensor.transpose(
            out=ps[:, i * 128:(i + 1) * 128], in_=s, identity=idf[:])
            for i, s in enumerate(srcs)]
        plan.group("pe", fns, reads=list(b_srcs) + [b_idf], writes=[b_ps],
                   dur=300 * n)
        plan.op("dve", lambda n=n, ps=ps: nc.vector.tensor_tensor(
            out=dst_ap, in0=ps[:, :n * 128], in1=add_ap, op=ALU.add),
            reads=[b_ps, b_add], writes=[b_dst], dur=750)

    def load_vec_to_cols(vec1d_view, dst_ap, b_dst, width):
        dma_in("sp", vstage[0:width, 0:128], vec1d_view, b_vstage, rbufs=[b_vstage])
        plan.op("pe", lambda: nc.tensor.transpose(
            out=tpf_ps[:, 0:width], in_=vstage[0:width, 0:128],
            identity=idf[0:width, 0:width]),
            reads=[b_vstage, b_idf], writes=[b_tpf])
        plan.op("act", lambda: nc.scalar.activation(
            out=dst_ap, in_=tpf_ps[:, 0:width], func=AF.Copy),
            reads=[b_tpf], writes=[b_dst])

    _lnslots = [Buf(f"lnt{i}") for i in range(4)]
    _lsqslots = [Buf("lsq0")]
    _lnctr = [0]

    def emit_ln(src_ap, b_src, dst_ap, b_dst, sq_over=None):
        s = _lnctr[0] % 4
        _lnctr[0] += 1
        b_ln = _lnslots[s]
        if sq_over is None:
            sq, b_sq = lsq[:, 0:D], _lsqslots[0]
        else:
            sq, b_sq = sq_over
        ln = lnt[:, s * 8:(s + 1) * 8]
        plan.op("dve", lambda: nc.vector.reduce_sum(
            out=ln[:, 0:1], in_=src_ap, axis=AXX), reads=[b_src], writes=[b_ln],
            dur=1100)
        plan.op("act", lambda: nc.scalar.activation(
            out=sq, in_=src_ap, func=AF.Square), reads=[b_src], writes=[b_sq],
            dur=1000)
        plan.op("dve", lambda: nc.vector.reduce_sum(
            out=ln[:, 1:2], in_=sq, axis=AXX), reads=[b_sq], writes=[b_ln],
            dur=1100)
        plan.op("dve", lambda: nc.vector.tensor_scalar_mul(
            out=ln[:, 0:2], in0=ln[:, 0:2], scalar1=1.0 / D),
            reads=[b_ln], writes=[b_ln], dur=80)
        plan.op("dve", lambda: nc.vector.tensor_tensor(
            out=ln[:, 2:3], in0=ln[:, 0:1], in1=ln[:, 0:1], op=ALU.mult),
            reads=[b_ln], writes=[b_ln], dur=80)
        plan.op("dve", lambda: nc.vector.tensor_tensor(
            out=ln[:, 1:2], in0=ln[:, 1:2], in1=ln[:, 2:3], op=ALU.subtract),
            reads=[b_ln], writes=[b_ln], dur=80)
        plan.op("dve", lambda: nc.vector.tensor_single_scalar(
            out=ln[:, 1:2], in_=ln[:, 1:2], scalar=1e-5, op=ALU.add),
            reads=[b_ln], writes=[b_ln], dur=80)
        plan.op("act", lambda: nc.scalar.activation(
            out=ln[:, 3:4], in_=ln[:, 1:2], func=AF.Sqrt),
            reads=[b_ln], writes=[b_ln], dur=250)
        plan.op("dve", lambda: nc.vector.reciprocal(
            out=ln[:, 3:4], in_=ln[:, 3:4]), reads=[b_ln], writes=[b_ln], dur=80)
        plan.op("dve", lambda: nc.vector.tensor_scalar(
            out=dst_ap, in0=src_ap, scalar1=ln[:, 0:1], scalar2=ln[:, 3:4],
            op0=ALU.subtract, op1=ALU.mult), reads=[b_src, b_ln], writes=[b_dst],
            dur=700)

    def emit_ln_noscratch(src_ap, b_src, dst_ap, b_dst):
        emit_ln(src_ap, b_src, dst_ap, b_dst,
                sq_over=(xh2t[:, 0:D], b_xh2[0]))

    def emit_quant(src_ap, b_src, dst_ap, b_dst, W, scr):
        nb = W // 16
        z, b_z = scr["z"]
        a2, b_a2 = scr["a2"]
        red, b_red = scr["red"]
        sc, b_sc = scr["sc"]
        rcp, b_rcp = scr["rcp"]
        src3 = src_ap.rearrange("p (a b) -> p a b", b=16)
        plan.op("dve", lambda: nc.vector.tensor_reduce(
            out=red[:, :nb], in_=src3, axis=AXX, op=ALU.max,
            apply_absolute_value=True), reads=[b_src], writes=[b_red])
        plan.op("dve", lambda: nc.vector.tensor_scalar(
            out=sc[:, :nb], in0=red[:, :nb],
            scalar1=float(np.float32(1.0) / np.float32(12.0)), scalar2=5e-13,
            op0=ALU.mult, op1=ALU.max), reads=[b_red], writes=[b_sc])
        plan.op("dve", lambda: nc.vector.reciprocal(
            out=rcp[:, :nb], in_=sc[:, :nb]), reads=[b_sc], writes=[b_rcp])
        rcp3 = rcp[:, :nb].unsqueeze(2).broadcast_to([128, nb, 16])
        sc3 = sc[:, :nb].unsqueeze(2).broadcast_to([128, nb, 16])
        z3 = z[:, :W].rearrange("p (a b) -> p a b", b=16)
        plan.op("dve", lambda: nc.vector.tensor_tensor(
            out=z3, in0=src3, in1=rcp3, op=ALU.mult),
            reads=[b_src, b_rcp], writes=[b_z])
        msk, b_msk = scr["msk"]
        zs, b_zs = scr["zs"]
        plan.op("act", lambda: nc.scalar.activation(
            out=zs[:, :W], in_=z[:, :W], func=AF.Square),
            reads=[b_z], writes=[b_zs], dur=800)
        plan.op("dve", lambda: nc.vector.tensor_single_scalar(
            out=msk[:, :W], in_=zs[:, :W], scalar=4.0, op=ALU.is_lt),
            reads=[b_zs], writes=[b_msk], dur=450)
        plan.op("dve", lambda: nc.vector.tensor_scalar(
            out=a2[:, :W], in0=z[:, :W], scalar1=CBV, scalar2=CBV,
            op0=ALU.add, op1=ALU.subtract), reads=[b_z, b_msk], writes=[b_a2])
        plan.op("dve", lambda: nc.vector.tensor_scalar_mul(
            out=src_ap, in0=z[:, :W], scalar1=C2V), reads=[b_z], writes=[b_src])
        plan.op("dve", lambda: nc.vector.tensor_tensor(
            out=z[:, :W], in0=src_ap, in1=z[:, :W], op=ALU.subtract),
            reads=[b_src, b_z], writes=[b_z], dur=700)
        plan.op("dve", lambda: nc.vector.tensor_tensor(
            out=src_ap, in0=src_ap, in1=z[:, :W], op=ALU.subtract),
            reads=[b_src, b_z], writes=[b_src], dur=700)
        plan.op("dve", lambda: nc.vector.copy_predicated(
            out=src_ap, mask=msk[:, :W], data=a2[:, :W]),
            reads=[b_msk, b_a2, b_src], writes=[b_src])
        dst3 = dst_ap.rearrange("p (a b) -> p a b", b=16)
        plan.op("dve", lambda: nc.vector.tensor_tensor(
            out=dst3, in0=src3, in1=sc3, op=ALU.mult),
            reads=[b_src, b_sc], writes=[b_dst])

    p1 = ExitStack()
    tWr, b_tWr = mksb(p1, "tWr", [128, NDT * CH], F32)
    tWi, b_tWi = mksb(p1, "tWi", [128, NDT * CH], F32)
    tA, b_tA = mksb(p1, "tA", [128, NDT * CH], F32)
    tB, b_tB = mksb(p1, "tB", [128, NDT * CH], F32)
    p0 = ExitStack()
    wrow, b_wrow = mksb(p0, "wrow", [128, DI], F32)
    vstage, b_vstage = mksb(p0, "vstage", [8, 256], F32)
    wqr, b_wqr = mksb(p0, "wqr", [128, DI], BF16)
    qz0, b_qz0 = mksb(p0, "qz0", [128, DI], F32)
    qa20, b_qa20 = mksb(p0, "qa20", [128, DI], BF16)
    qred0, b_qred0 = mksb(p0, "qred0", [128, DI // 16], F32)
    qsc0, b_qsc0 = mksb(p0, "qsc0", [128, DI // 16], F32)
    qrcp0, b_qrcp0 = mksb(p0, "qrcp0", [128, DI // 16], F32)
    qmsk0, b_qmsk0 = mksb(p0, "qmsk0", [128, DI], mybir.dt.uint8)
    qzs0, b_qzs0 = mksb(p0, "qzs0", [128, DI], BF16)
    scr0 = {"z": (qz0, b_qz0), "a2": (qa20, b_qa20), "zs": (qzs0, b_qzs0),
            "red": (qred0, b_qred0), "sc": (qsc0, b_qsc0), "rcp": (qrcp0, b_qrcp0),
            "msk": (qmsk0, b_qmsk0)}
    wasm, b_wasm = mksb(p0, "wasm", [128, NDT * DI], BF16)

    def prep_gateup(w_src, dst_dram):
        for it in range(NIT):
            dma_in("sp", wrow[:, :D], w_src[it * 128:(it + 1) * 128, :],
                   b_wrow, rbufs=[b_wrow])
            emit_quant(wrow[:, :D], b_wrow, wqr[:, :D], b_wqr, D, scr0)
            for g0 in range(0, NDT, 4):
                srcs = [wqr[:, (g0 + j) * 128:(g0 + j + 1) * 128] for j in range(4)]
                dst = wasm[:, g0 * DI + it * 128:].rearrange(
                    "p (x) -> p x", x=1) if False else None
                dstap = wasm[:].rearrange("p (t q) -> p t q", q=DI)[
                    :, g0:g0 + 4, it * 128:it * 128 + 128]
                n = 4
                tp_ps_, b_tp_ = tp_psL[_tpctr[0] % len(tp_psL)]
                _tpctr[0] += 1
                fns = [lambda s=s, i=i, tp_ps_=tp_ps_: nc.tensor.transpose(
                    out=tp_ps_[:, i * 128:(i + 1) * 128], in_=s, identity=idbf[:])
                    for i, s in enumerate(srcs)]
                plan.group("pe", fns, reads=[b_wqr, b_idbf], writes=[b_tp_])
                src_ps = tp_ps_[:, :512].rearrange("p (t q) -> p t q", q=128)
                plan.op("act", lambda dstap=dstap, src_ps=src_ps:
                        nc.scalar.activation(out=dstap, in_=src_ps, func=AF.Copy),
                        reads=[b_tp_], writes=[b_wasm])
        plan.dma("sp", lambda d=dst_dram: nc.sync.dma_start(out=d[:], in_=wasm[:]),
                 reads=[b_wasm], writes=[Buf("wdram")])

    prep_gateup(win["L0_ffn_w_gate"], wgq_d)
    prep_gateup(win["L0_ffn_w_up"], wuq_d)

    for dt in range(NDT):
        dma_in("sp", wrow[:, :DI], win["L0_ffn_w_down"][dt * 128:(dt + 1) * 128, :],
               b_wrow, rbufs=[b_wrow])
        emit_quant(wrow[:, :DI], b_wrow, wqr[:, :DI], b_wqr, DI, scr0)
        for g0 in range(0, NIT, 4):
            gn = min(4, NIT - g0)
            srcs = [wqr[:, (g0 + j) * 128:(g0 + j + 1) * 128] for j in range(gn)]
            dstap = wasm[:, :NIT * D].rearrange("p (t q) -> p t q", q=D)[
                :, g0:g0 + gn, dt * 128:dt * 128 + 128]
            fns = [lambda s=s, i=i: nc.tensor.transpose(
                out=tp_ps[:, i * 128:(i + 1) * 128], in_=s, identity=idbf[:])
                for i, s in enumerate(srcs)]
            plan.group("pe", fns, reads=[b_wqr, b_idbf], writes=[b_tp])
            src_ps = tp_ps[:, :gn * 128].rearrange("p (t q) -> p t q", q=128)
            plan.op("act", lambda dstap=dstap, src_ps=src_ps:
                    nc.scalar.activation(out=dstap, in_=src_ps, func=AF.Copy),
                    reads=[b_tp], writes=[b_wasm])
    plan.dma("sp", lambda: nc.sync.dma_start(out=wdq_d[:], in_=wasm[:, :NIT * D]),
             reads=[b_wasm], writes=[Buf("wdram")])

    for w_src, dstd in [(win["L0_attn_b_Wg"], wgT_d), (win["L0_attn_b_Wr"], wrT_d),
                        (win["L0_attn_b_Wo"], woT_d)]:
        for et in range(NDT):
            dma_in("sp", wrow[:, :D], w_src[et * 128:(et + 1) * 128, :],
                   b_wrow, rbufs=[b_wrow])
            plan.op("act", lambda: nc.scalar.activation(
                out=wqr[:, :D], in_=wrow[:, :D], func=AF.Copy),
                reads=[b_wrow], writes=[b_wqr])
            for g0 in range(0, NDT, 4):
                srcs = [wqr[:, (g0 + j) * 128:(g0 + j + 1) * 128] for j in range(4)]
                dstap = wasm[:, :NDT * D].rearrange("p (t q) -> p t q", q=D)[
                    :, g0:g0 + 4, et * 128:et * 128 + 128]
                tp_ps_, b_tp_ = tp_psL[_tpctr[0] % len(tp_psL)]
                _tpctr[0] += 1
                fns = [lambda s=s, i=i, tp_ps_=tp_ps_: nc.tensor.transpose(
                    out=tp_ps_[:, i * 128:(i + 1) * 128], in_=s, identity=idbf[:])
                    for i, s in enumerate(srcs)]
                plan.group("pe", fns, reads=[b_wqr, b_idbf], writes=[b_tp_])
                b_tp = b_tp_
                src_ps = tp_ps_[:, :512].rearrange("p (t q) -> p t q", q=128)
                plan.op("act", lambda dstap=dstap, src_ps=src_ps:
                        nc.scalar.activation(out=dstap, in_=src_ps, func=AF.Copy),
                        reads=[b_tp], writes=[b_wasm])
        plan.dma("sp", lambda d=dstd: nc.sync.dma_start(
            out=d[:], in_=wasm[:, :NDT * D]), reads=[b_wasm], writes=[Buf("wdram")])

    load_vec_to_cols(win["L0_attn_b_decay"].rearrange("(t p) -> t p", p=128),
                     pv[:, COL_A:COL_A + NDT], b_pv, NDT)
    plan.op("act", lambda: nc.scalar.activation(
        out=pv[:, COL_A:COL_A + NDT], in_=pv[:, COL_A:COL_A + NDT],
        func=AF.Sigmoid), reads=[b_pv], writes=[b_pv])
    plan.op("dve", lambda: nc.vector.tensor_scalar(
        out=pv[:, COL_OM:COL_OM + NDT], in0=pv[:, COL_A:COL_A + NDT],
        scalar1=-1.0, scalar2=1.0, op0=ALU.mult, op1=ALU.add),
        reads=[b_pv], writes=[b_pv])
    for nm, col in [("L0_attn_b_br", COL_BR), ("L0_attn_b_bo", COL_BO),
                    ("L0_attn_b_ln_w", COL_LNW), ("L0_attn_b_ln_b", COL_LNB)]:
        load_vec_to_cols(win[nm].rearrange("(t p) -> t p", p=128),
                         pv[:, col:col + NDT], b_pv, NDT)
    load_vec_to_cols(win["L0_attn_b_bg"].rearrange("(t p) -> t p", p=128),
                     bgv[:], b_bgv, NDT)

    phz, b_phz = mksb(p0, "phz", [128, 2 * NDT], F32)
    phi_, b_phi = mksb(p0, "phi_", [128, 2 * NDT], F32)
    lcv, b_lcv = mksb(p0, "lcv", [128, 2 * NDT], F32)
    tmpv, b_tmpv = mksb(p0, "tmpv", [128, 4 * NDT], F32)
    den, b_den = mksb(p0, "den", [128, 2 * NDT], F32)
    for src_d, dst_t, b_dst in [(win["L0_attn_a_phazor"], phz, b_phz),
                                (win["L0_attn_a_phazor_init"], phi_, b_phi)]:
        dma_in("sp", vstage[:, :256],
               src_d.rearrange("(t p) c -> t (p c)", p=128), b_vstage,
               rbufs=[b_vstage])
        for c in range(2):
            sv = vstage[:, :256].rearrange("t (p c) -> t p c", c=2)[:, :, c]
            plan.op("pe", lambda sv=sv, c=c: nc.tensor.transpose(
                out=tpf_ps[:, c * 128:c * 128 + NDT], in_=sv,
                identity=idf[0:NDT, 0:NDT]),
                reads=[b_vstage, b_idf], writes=[b_tpf])
        dsts = dst_t[:].rearrange("p (t c) -> p t c", c=2)
        plan.op("act", lambda dsts=dsts: nc.scalar.activation(
            out=dsts[:, :, 0], in_=tpf_ps[:, 0:NDT], func=AF.Copy),
            reads=[b_tpf], writes=[b_dst])
        plan.op("act", lambda dsts=dsts: nc.scalar.activation(
            out=dsts[:, :, 1], in_=tpf_ps[:, 128:128 + NDT], func=AF.Copy),
            reads=[b_tpf], writes=[b_dst])
    load_vec_to_cols(win["L0_attn_a_last_conv_re"].rearrange("(t p) -> t p", p=128),
                     lcv[:, 0:NDT], b_lcv, NDT)
    load_vec_to_cols(win["L0_attn_a_last_conv_im"].rearrange("(t p) -> t p", p=128),
                     lcv[:, NDT:2 * NDT], b_lcv, NDT)

    prs = phz[:].rearrange("p (t c) -> p t c", c=2)[:, :, 0]
    pis = phz[:].rearrange("p (t c) -> p t c", c=2)[:, :, 1]
    phr = phi_[:].rearrange("p (t c) -> p t c", c=2)[:, :, 0]
    phii = phi_[:].rearrange("p (t c) -> p t c", c=2)[:, :, 1]
    lre = lcv[:, 0:NDT]
    lim = lcv[:, NDT:2 * NDT]

    def vop(fn, reads, writes):
        plan.op("dve", fn, reads=reads, writes=writes)

    vop(lambda: nc.vector.tensor_tensor(out=tmpv[:, 0:NDT], in0=prs, in1=prs,
        op=ALU.mult), [b_phz], [b_tmpv])
    vop(lambda: nc.vector.tensor_tensor(out=tmpv[:, NDT:2 * NDT], in0=pis, in1=pis,
        op=ALU.mult), [b_phz], [b_tmpv])
    vop(lambda: nc.vector.tensor_tensor(out=tmpv[:, 0:NDT], in0=tmpv[:, 0:NDT],
        in1=tmpv[:, NDT:2 * NDT], op=ALU.add), [b_tmpv], [b_tmpv])
    plan.op("act", lambda: nc.scalar.activation(
        out=tmpv[:, NDT:2 * NDT], in_=tmpv[:, 0:NDT], func=AF.Sqrt),
        reads=[b_tmpv], writes=[b_tmpv])
    plan.op("act", lambda: nc.scalar.activation(
        out=pv[:, COL_R:COL_R + NDT], in_=tmpv[:, NDT:2 * NDT], func=AF.Exp,
        scale=-1.0), reads=[b_tmpv], writes=[b_pv])
    vop(lambda: nc.vector.reciprocal(out=tmpv[:, 0:NDT], in_=tmpv[:, NDT:2 * NDT]),
        [b_tmpv], [b_tmpv])
    vop(lambda: nc.vector.tensor_tensor(out=tmpv[:, 2 * NDT:3 * NDT], in0=prs,
        in1=tmpv[:, 0:NDT], op=ALU.mult), [b_phz, b_tmpv], [b_tmpv])
    vop(lambda: nc.vector.tensor_tensor(out=tmpv[:, 3 * NDT:4 * NDT], in0=pis,
        in1=tmpv[:, 0:NDT], op=ALU.mult), [b_phz, b_tmpv], [b_tmpv])

    vop(lambda: nc.vector.tensor_tensor(out=den[:, 0:NDT], in0=phr, in1=phr,
        op=ALU.mult), [b_phi], [b_den])
    vop(lambda: nc.vector.tensor_tensor(out=den[:, NDT:], in0=phii, in1=phii,
        op=ALU.mult), [b_phi], [b_den])
    vop(lambda: nc.vector.tensor_tensor(out=den[:, 0:NDT], in0=den[:, 0:NDT],
        in1=den[:, NDT:], op=ALU.add), [b_den], [b_den])
    vop(lambda: nc.vector.tensor_single_scalar(out=den[:, 0:NDT], in_=den[:, 0:NDT],
        scalar=1e-30, op=ALU.max), [b_den], [b_den])
    vop(lambda: nc.vector.reciprocal(out=den[:, 0:NDT], in_=den[:, 0:NDT]),
        [b_den], [b_den])
    vop(lambda: nc.vector.tensor_tensor(out=q0t[:, 0:NDT], in0=lre, in1=phr,
        op=ALU.mult), [b_lcv, b_phi], [b_q0t])
    vop(lambda: nc.vector.tensor_tensor(out=den[:, NDT:], in0=lim, in1=phii,
        op=ALU.mult), [b_lcv, b_phi], [b_den])
    vop(lambda: nc.vector.tensor_tensor(out=q0t[:, 0:NDT], in0=q0t[:, 0:NDT],
        in1=den[:, NDT:], op=ALU.add), [b_q0t, b_den], [b_q0t])
    vop(lambda: nc.vector.tensor_tensor(out=q0t[:, 0:NDT], in0=q0t[:, 0:NDT],
        in1=den[:, 0:NDT], op=ALU.mult), [b_q0t, b_den], [b_q0t])
    vop(lambda: nc.vector.tensor_tensor(out=q0t[:, NDT:], in0=lim, in1=phr,
        op=ALU.mult), [b_lcv, b_phi], [b_q0t])
    vop(lambda: nc.vector.tensor_tensor(out=den[:, NDT:], in0=lre, in1=phii,
        op=ALU.mult), [b_lcv, b_phi], [b_den])
    vop(lambda: nc.vector.tensor_tensor(out=q0t[:, NDT:], in0=q0t[:, NDT:],
        in1=den[:, NDT:], op=ALU.subtract), [b_q0t, b_den], [b_q0t])
    vop(lambda: nc.vector.tensor_tensor(out=q0t[:, NDT:], in0=q0t[:, NDT:],
        in1=den[:, 0:NDT], op=ALU.mult), [b_q0t, b_den], [b_q0t])
    vop(lambda: nc.vector.tensor_copy(out=carry[:], in_=q0t[:]), [b_q0t], [b_carry])

    for dt in range(NDT):
        base = dt * CH
        ur1 = tmpv[:, 2 * NDT + dt:2 * NDT + dt + 1]
        ui1 = tmpv[:, 3 * NDT + dt:3 * NDT + dt + 1]
        vop(lambda ur1=ur1, base=base: nc.vector.tensor_copy(
            out=tWr[:, base:base + 1], in_=ur1), [b_tmpv], [b_tWr])
        vop(lambda ui1=ui1, base=base: nc.vector.tensor_copy(
            out=tWi[:, base:base + 1], in_=ui1), [b_tmpv], [b_tWi])
        n = 1
        while n < CH:
            m = min(n, CH - n)
            anr = tWr[:, base + n - 1:base + n]
            ani = tWi[:, base + n - 1:base + n]
            s_r = tWr[:, base:base + m]
            s_i = tWi[:, base:base + m]
            d_r = tWr[:, base + n:base + n + m]
            d_i = tWi[:, base + n:base + n + m]
            t1 = tA[:, base:base + m]
            vop(lambda d=d_r, s=s_r, a=anr: nc.vector.tensor_scalar(
                out=d, in0=s, scalar1=a, scalar2=None, op0=ALU.mult),
                [b_tWr], [b_tWr])
            vop(lambda t=t1, s=s_i, a=ani: nc.vector.tensor_scalar(
                out=t, in0=s, scalar1=a, scalar2=None, op0=ALU.mult),
                [b_tWi], [b_tA])
            vop(lambda d=d_r, t=t1: nc.vector.tensor_tensor(
                out=d, in0=d, in1=t, op=ALU.subtract), [b_tWr, b_tA], [b_tWr])
            vop(lambda d=d_i, s=s_r, a=ani: nc.vector.tensor_scalar(
                out=d, in0=s, scalar1=a, scalar2=None, op0=ALU.mult),
                [b_tWr, b_tWi], [b_tWi])
            vop(lambda t=t1, s=s_i, a=anr: nc.vector.tensor_scalar(
                out=t, in0=s, scalar1=a, scalar2=None, op0=ALU.mult),
                [b_tWi, b_tWr], [b_tA])
            vop(lambda d=d_i, t=t1: nc.vector.tensor_tensor(
                out=d, in0=d, in1=t, op=ALU.add), [b_tWi, b_tA], [b_tWi])
            n += m
        vop(lambda dt=dt, base=base: nc.vector.tensor_copy(
            out=tUc[:, dt:dt + 1], in_=tWr[:, base + CH - 1:base + CH]),
            [b_tWr], [b_tUc])
        vop(lambda dt=dt, base=base: nc.vector.tensor_copy(
            out=tUc[:, NDT + dt:NDT + dt + 1], in_=tWi[:, base + CH - 1:base + CH]),
            [b_tWi], [b_tUc])
    for dt in range(NDT):
        base = dt * CH
        pr1 = phr[:, dt:dt + 1]
        pi1 = phii[:, dt:dt + 1]
        vop(lambda base=base, pr1=pr1: nc.vector.tensor_scalar(
            out=tA[:, base:base + CH], in0=tWr[:, base:base + CH], scalar1=pr1,
            scalar2=None, op0=ALU.mult), [b_tWr, b_phi], [b_tA])
        vop(lambda base=base, pi1=pi1: nc.vector.tensor_scalar(
            out=tB[:, base:base + CH], in0=tWi[:, base:base + CH], scalar1=pi1,
            scalar2=None, op0=ALU.mult), [b_tWi, b_phi], [b_tB])
        vop(lambda base=base: nc.vector.tensor_tensor(
            out=tA[:, base:base + CH], in0=tA[:, base:base + CH],
            in1=tB[:, base:base + CH], op=ALU.subtract), [b_tA, b_tB], [b_tA])
        vop(lambda base=base, pr1=pr1: nc.vector.tensor_scalar(
            out=tB[:, base:base + CH], in0=tWi[:, base:base + CH], scalar1=pr1,
            scalar2=None, op0=ALU.mult), [b_tWi, b_phi], [b_tB])
        vop(lambda base=base, pi1=pi1: nc.vector.scalar_tensor_tensor(
            out=tB[:, base:base + CH], in0=tWr[:, base:base + CH], scalar=pi1,
            in1=tB[:, base:base + CH], op0=ALU.mult, op1=ALU.add),
            [b_tWr, b_phi, b_tB], [b_tB])
        vop(lambda base=base: nc.vector.tensor_scalar_mul(
            out=tWi[:, base:base + CH], in0=tWi[:, base:base + CH], scalar1=-1.0),
            [b_tWi], [b_tWi])

    plan.barrier()
    p0.close()

    tpf_psL.append(mkps(p1, "tpf_ps1", [128, 512], F32))
    xin1, b_xin1 = [], []
    for i in range(2):
        t, b = mksb(p1, f"p1xin{i}", [128, TPC * D], F32)
        xin1.append(t)
        b_xin1.append(b)
    xh4, b_xh4 = mksb(p1, "p1xh4", [128, TPC * D], F32)
    XT, b_XT = mksb(p1, "p1XT", [128, NDT * CH], F32)
    wRt, _bwr = mksb(p1, "p1wR", [128, 2 * CH], F32)
    wIt, _bwi = mksb(p1, "p1wI", [128, 2 * CH], F32)
    b_wRs = [Buf("p1wR0"), Buf("p1wR1")]
    b_wIs = [Buf("p1wI0"), Buf("p1wI1")]
    dRb, _bdr = mksb(p1, "p1dR", [128, NDT * CH], F32)
    dIb, _bdi = mksb(p1, "p1dI", [128, NDT * CH], F32)
    b_dRs = [Buf(f"p1dR{i}") for i in range(NDT)]
    b_dIs = [Buf(f"p1dI{i}") for i in range(NDT)]
    xo1, b_xo1 = mksb(p1, "p1xo", [128, TPC * D], F32)
    ncar, b_ncar = mksb(p1, "p1ncar", [128, 2 * NDT + 1], F32)

    for ch in range(NCH):
        xi, b_xi = xin1[ch % 2], b_xin1[ch % 2]
        src = x0_d[ch * CH:(ch + 1) * CH, :].rearrange("(a p) d -> p a d", p=128)
        dma_in("sp", xi[:].rearrange("p (a d) -> p a d", d=D), src, b_xi, rbufs=[b_xi])
        for a in range(TPC):
            emit_ln(xi[:, a * D:(a + 1) * D], b_xi,
                    xh4[:, a * D:(a + 1) * D], b_xh4)
        for dt in range(NDT):
            srcs = [xh4[:, a * D + dt * 128: a * D + dt * 128 + 128]
                    for a in range(TPC)]
            tp_group_f32(srcs, [b_xh4], XT[:, dt * CH:(dt + 1) * CH], b_XT)
        for dt in range(NDT):
            base = dt * CH
            wRb = wRt[:, (dt % 2) * CH:(dt % 2 + 1) * CH]
            wIb = wIt[:, (dt % 2) * CH:(dt % 2 + 1) * CH]
            b_wR = b_wRs[dt % 2]
            b_wI = b_wIs[dt % 2]
            rcol = pv[:, COL_R + dt:COL_R + dt + 1]
            r_rep = rcol.broadcast_to([128, CH])
            vop(lambda base=base, wRb=wRb: nc.vector.tensor_tensor(
                out=wRb, in0=XT[:, base:base + CH], in1=tWr[:, base:base + CH],
                op=ALU.mult), [b_XT, b_tWr], [b_wR])
            vop(lambda base=base, wIb=wIb: nc.vector.tensor_tensor(
                out=wIb, in0=XT[:, base:base + CH], in1=tWi[:, base:base + CH],
                op=ALU.mult), [b_XT, b_tWi], [b_wI])
            vop(lambda base=base, r_rep=r_rep, dt=dt, wRb=wRb, b_wR=b_wR:
                nc.vector.tensor_tensor_scan(
                out=dRb[:, base:base + CH], data0=r_rep, data1=wRb,
                initial=carry[:, dt:dt + 1], op0=ALU.mult, op1=ALU.add),
                [b_wR, b_carry, b_pv], [b_dRs[dt]])
            vop(lambda base=base, r_rep=r_rep, dt=dt, wIb=wIb:
                nc.vector.tensor_tensor_scan(
                out=dIb[:, base:base + CH], data0=r_rep, data1=wIb,
                initial=carry[:, NDT + dt:NDT + dt + 1], op0=ALU.mult, op1=ALU.add),
                [b_wI, b_carry, b_pv], [b_dIs[dt]])
            dl_r = dRb[:, base + CH - 1:base + CH]
            dl_i = dIb[:, base + CH - 1:base + CH]
            ucr = tUc[:, dt:dt + 1]
            uci = tUc[:, NDT + dt:NDT + dt + 1]
            tc = ncar[:, 2 * NDT:2 * NDT + 1]
            vop(lambda dt=dt, dl_r=dl_r, ucr=ucr: nc.vector.tensor_tensor(
                out=ncar[:, dt:dt + 1], in0=dl_r, in1=ucr, op=ALU.mult),
                [b_dRs[dt], b_tUc], [b_ncar])
            vop(lambda tc=tc, dl_i=dl_i, uci=uci: nc.vector.tensor_tensor(
                out=tc, in0=dl_i, in1=uci, op=ALU.mult), [b_dIs[dt], b_tUc], [b_ncar])
            vop(lambda dt=dt, tc=tc: nc.vector.tensor_tensor(
                out=ncar[:, dt:dt + 1], in0=ncar[:, dt:dt + 1], in1=tc,
                op=ALU.subtract), [b_ncar], [b_ncar])
            vop(lambda dt=dt, dl_r=dl_r, uci=uci: nc.vector.tensor_tensor(
                out=ncar[:, NDT + dt:NDT + dt + 1], in0=dl_r, in1=uci,
                op=ALU.mult), [b_dRs[dt], b_tUc], [b_ncar])
            vop(lambda tc=tc, dl_i=dl_i, ucr=ucr: nc.vector.tensor_tensor(
                out=tc, in0=dl_i, in1=ucr, op=ALU.mult), [b_dIs[dt], b_tUc], [b_ncar])
            vop(lambda dt=dt, tc=tc: nc.vector.tensor_tensor(
                out=ncar[:, NDT + dt:NDT + dt + 1],
                in0=ncar[:, NDT + dt:NDT + dt + 1], in1=tc, op=ALU.add),
                [b_ncar], [b_ncar])
            vop(lambda base=base: nc.vector.tensor_tensor(
                out=dRb[:, base:base + CH], in0=dRb[:, base:base + CH],
                in1=tA[:, base:base + CH], op=ALU.mult), [b_dRs[dt], b_tA], [b_dRs[dt]])
            vop(lambda base=base, wIb=wIb: nc.vector.tensor_tensor(
                out=wIb, in0=dIb[:, base:base + CH], in1=tB[:, base:base + CH],
                op=ALU.mult), [b_dIs[dt], b_tB], [b_wI])
            vop(lambda base=base, wIb=wIb: nc.vector.tensor_tensor(
                out=dRb[:, base:base + CH], in0=dRb[:, base:base + CH], in1=wIb,
                op=ALU.subtract), [b_dRs[dt], b_wI], [b_dRs[dt]])
        vop(lambda: nc.vector.tensor_copy(out=carry[:], in_=ncar[:, 0:2 * NDT]),
            [b_ncar], [b_carry])
        for a in range(TPC):
            for g0 in range(0, NDT, 4):
                srcs = [dRb[:, (g0 + j) * CH + a * 128:(g0 + j) * CH + a * 128 + 128]
                        for j in range(4)]
                dsl = xo1[:, a * D + g0 * 128: a * D + (g0 + 4) * 128]
                xsl = xi[:, a * D + g0 * 128: a * D + (g0 + 4) * 128]
                tp_group_f32_add(srcs, [b_dRs[g0 + j] for j in range(4)],
                                 dsl, b_xo1, xsl, b_xi)
        dst = xs1[ch * CH:(ch + 1) * CH, :].rearrange("(a p) d -> p a d", p=128)
        plan.dma("pool", lambda d=dst: nc.gpsimd.dma_start(out=d, in_=xo1[:].rearrange("p (a d) -> p a d", d=D)),
                 reads=[b_xo1], writes=[Buf("xs1w")])

    plan.barrier()
    tpf_psL.pop()
    p1.close()

    def emit_ffn(esx, tag, src_d, dst_d):
        wg_sb, b_wg = mksb(esx, f"{tag}wg", [128, NDT * DI], BF16)
        wu_sb, b_wu = mksb(esx, f"{tag}wu", [128, NDT * DI], BF16)
        wd_sb, b_wd = mksb(esx, f"{tag}wd", [128, NIT * D], BF16)
        dma_in("sp", wg_sb[:], wgq_d[:], b_wg)
        dma_in("sp", wu_sb[:], wuq_d[:], b_wu)
        dma_in("sp", wd_sb[:], wdq_d[:], b_wd)
        fxi, b_fxi = mksb(esx, f"{tag}xin", [128, TPC * D], F32)
        xq4, b_xq4 = mksb(esx, f"{tag}xq4", [128, TPC * D], BF16)
        xqT, b_xqT = mksb(esx, f"{tag}xqT", [128, NDT * 128], BF16)
        hb, _b_hb = mksb(esx, f"{tag}hb", [128, DI], F32)
        b_hb2 = [Buf(f"{tag}hb0"), Buf(f"{tag}hb1")]
        hq, b_hq = mksb(esx, f"{tag}hq", [128, DI], BF16)
        hqT, b_hqT = mksb(esx, f"{tag}hqT", [128, NIT * 128], BF16)
        WQ = DI // 2
        z_, b_z_ = mksb(esx, f"{tag}qz", [128, WQ], F32)
        a2_, b_a2_ = mksb(esx, f"{tag}qa2", [128, WQ], BF16)
        zs_, b_zs_ = mksb(esx, f"{tag}qzs", [128, WQ], BF16)
        msk_, b_msk_ = mksb(esx, f"{tag}qmsk", [128, WQ], mybir.dt.uint8)
        red_, b_red_ = mksb(esx, f"{tag}qred", [128, WQ // 16], F32)
        sc_, b_sc_ = mksb(esx, f"{tag}qsc", [128, WQ // 16], F32)
        rcp_, b_rcp_ = mksb(esx, f"{tag}qrcp", [128, WQ // 16], F32)
        scr = {"z": (z_, b_z_), "a2": (a2_, b_a2_), "zs": (zs_, b_zs_),
               "red": (red_, b_red_), "sc": (sc_, b_sc_), "rcp": (rcp_, b_rcp_),
               "msk": (msk_, b_msk_)}

        def scr_next():
            return scr
        gpsL = [mkps(esx, f"{tag}gps0", [128, 512], F32),
                mkps(esx, f"{tag}gps1", [128, 512], F32)]
        upsL = [mkps(esx, f"{tag}ups0", [128, 512], F32),
                mkps(esx, f"{tag}ups1", [128, 512], F32)]
        dps = [mkps(esx, f"{tag}dps0", [128, 512], F32),
               mkps(esx, f"{tag}dps1", [128, 512], F32)]
        ISL = []
        s0 = 0
        while s0 < DI:
            ISL.append((s0, min(512, DI - s0)))
            s0 += 512
        for ch in range(NCH):
            src = src_d[ch * CH:(ch + 1) * CH, :].rearrange("(a p) d -> p a d", p=128)
            dma_in("sp", fxi[:].rearrange("p (a d) -> p a d", d=D), src, b_fxi,
                   rbufs=[b_fxi])
            for a in range(TPC):
                xh_ap = xh2t[:, (a % 2) * D:(a % 2 + 1) * D]
                b_xh_s = b_xh2[a % 2]
                emit_ln(fxi[:, a * D:(a + 1) * D], b_fxi, xh_ap, b_xh_s)
                emit_quant(xh_ap, b_xh_s, xq4[:, a * D:(a + 1) * D], b_xq4, D, scr_next())
            for a in range(TPC):
                for g0 in range(0, NDT, 4):
                    srcs = [xq4[:, a * D + (g0 + j) * 128: a * D + (g0 + j + 1) * 128]
                            for j in range(4)]
                    tp_group_bf(srcs, [b_xq4],
                                xqT[:, g0 * 128:(g0 + 4) * 128], b_xqT)
                for si, (s0, sw) in enumerate(ISL):
                    gps, b_gps = gpsL[si % 2]
                    ups, b_ups = upsL[si % 2]
                    for (ps, b_ps, wsb, b_wsb) in ((gps, b_gps, wg_sb, b_wg),
                                                   (ups, b_ups, wu_sb, b_wu)):
                        fns = [lambda dt=dt, ps=ps, wsb=wsb, s0=s0, sw=sw:
                               nc.tensor.matmul(
                                   out=ps[:, :sw],
                                   lhsT=xqT[:, dt * 128:(dt + 1) * 128],
                                   rhs=wsb[:, dt * DI + s0: dt * DI + s0 + sw],
                                   start=(dt == 0), stop=(dt == NDT - 1))
                               for dt in range(NDT)]
                        plan.group("pe", fns, reads=[b_xqT, b_wsb], writes=[b_ps],
                                   dur=2000)
                    hbs = hb[:, s0:s0 + sw]
                    halves = sorted({s0 // (DI // 2), (s0 + sw - 1) // (DI // 2)})
                    wr = [b_hb2[h] for h in halves]
                    plan.op("act", lambda sw=sw, gps=gps, hbs=hbs: nc.scalar.activation(
                        out=hbs, in_=gps[:, :sw], func=AF.Silu),
                        reads=[b_gps], writes=wr, dur=600)
                    plan.op("dve", lambda sw=sw, ups=ups, hbs=hbs:
                            nc.vector.tensor_tensor(
                        out=hbs, in0=hbs, in1=ups[:, :sw],
                        op=ALU.mult), reads=[b_ups] + wr, writes=wr, dur=700)
                    if si == 2:
                        emit_quant(hb[:, 0:DI // 2], b_hb2[0],
                                   hq[:, 0:DI // 2], b_hq, DI // 2, scr)
                    elif si == len(ISL) - 1:
                        emit_quant(hb[:, DI // 2:DI], b_hb2[1],
                                   hq[:, DI // 2:DI], b_hq, DI // 2, scr)
                for g0 in range(0, NIT, 4):
                    gn = min(4, NIT - g0)
                    srcs = [hq[:, (g0 + j) * 128:(g0 + j + 1) * 128]
                            for j in range(gn)]
                    tp_group_bf(srcs, [b_hq],
                                hqT[:, g0 * 128:(g0 + gn) * 128], b_hqT)
                for half in range(2):
                    ps, b_ps = dps[half]
                    fns = [lambda it=it, ps=ps, half=half: nc.tensor.matmul(
                        out=ps[:, :512],
                        lhsT=hqT[:, it * 128:(it + 1) * 128],
                        rhs=wd_sb[:, it * D + half * 512: it * D + half * 512 + 512],
                        start=(it == 0), stop=(it == NIT - 1))
                        for it in range(NIT)]
                    plan.group("pe", fns, reads=[b_hqT, b_wd], writes=[b_ps])
                    plan.op("dve", lambda a=a, half=half, ps=ps:
                            nc.vector.tensor_tensor(
                                out=fxi[:, a * D + half * 512: a * D + half * 512 + 512],
                                in0=ps[:, :512],
                                in1=fxi[:, a * D + half * 512: a * D + half * 512 + 512],
                                op=ALU.add), reads=[b_ps, b_fxi], writes=[b_fxi])
            dst = dst_d[ch * CH:(ch + 1) * CH, :].rearrange("(a p) d -> p a d", p=128)
            plan.dma("pool", lambda d=dst: nc.gpsimd.dma_start(out=d, in_=fxi[:].rearrange("p (a d) -> p a d", d=D)),
                     reads=[b_fxi], writes=[Buf(f"{tag}out")])

    p2 = ExitStack()
    emit_ffn(p2, "p2", xs1, xs2)
    plan.barrier()
    p2.close()

    p3 = ExitStack()
    wgT_sb, b_wgT = mksb(p3, "p3wgT", [128, NDT * D], BF16)
    wrT_sb, b_wrT = mksb(p3, "p3wrT", [128, NDT * D], BF16)
    woT_sb, b_woT = mksb(p3, "p3woT", [128, NDT * D], BF16)
    dma_in("sp", wgT_sb[:], wgT_d[:], b_wgT)
    dma_in("sp", wrT_sb[:], wrT_d[:], b_wrT)
    dma_in("sp", woT_sb[:], woT_d[:], b_woT)
    xi3, b_xi3 = mksb(p3, "p3xin", [128, TPC * D], F32)
    XT3, b_XT3 = mksb(p3, "p3XT", [128, NDT * CH], F32)
    G3, b_G3 = mksb(p3, "p3G", [128, NDT * CH], F32)
    V3t, b_V3t = mksb(p3, "p3V", [128, 2 * CH], F32)
    b_V3s = [Buf("V3s0"), Buf("V3s1")]
    HxA, _bA = mksb(p3, "p3HxA", [128, NDT * (CH + 1)], BF16)
    HxB, _bB = mksb(p3, "p3HxB", [128, NDT * (CH + 1)], BF16)
    Hf, _bH = mksb(p3, "p3Hf", [128, NDT * CH], F32)
    b_HxA = [Buf(f"HxA{i}") for i in range(NDT)]
    b_HxB = [Buf(f"HxB{i}") for i in range(NDT)]
    b_Hfb = [Buf(f"Hf{i}") for i in range(NDT)]
    hcar, b_hcar = mksb(p3, "p3hcar", [128, NDT], F32)
    Nrm4, b_Nrm4 = mksb(p3, "p3Nrm4", [128, TPC * D], BF16)
    YT3, b_YT3 = mksb(p3, "p3YT", [128, NDT * CH], BF16)
    OT3, b_OT3 = mksb(p3, "p3OT", [128, NDT * CH], F32)
    XTb = OT3[:].bitcast(BF16)[:, :NDT * CH]
    b_XTb = b_OT3
    o3, b_o3 = mksb(p3, "p3o", [128, TPC * D], F32)
    pps = [mkps(p3, "p3ps0", [128, CH], F32), mkps(p3, "p3ps1", [128, CH], F32)]
    tp_psL.append(mkps(p3, "tp_ps3b", [128, 512], BF16))
    tpf_psL.append(mkps(p3, "tpf_ps3b", [128, 512], F32))
    b_lsqep = _lsqslots[0]

    plan.op("dve", lambda: nc.vector.memset(hcar[:], 0.0), writes=[b_hcar])

    for ch in range(NCH):
        src = xs2[ch * CH:(ch + 1) * CH, :].rearrange("(a p) d -> p a d", p=128)
        dma_in("sp", xi3[:].rearrange("p (a d) -> p a d", d=D), src, b_xi3, rbufs=[b_xi3])
        XT3v = XT3[:].rearrange("p (t q) -> p t q", q=CH)
        XTbv = XTb.rearrange("p (t q) -> p t q", q=CH)
        for a in range(TPC):
            xh_ap = xh2t[:, (a % 2) * D:(a % 2 + 1) * D]
            b_xh_s = b_xh2[a % 2]
            emit_ln(xi3[:, a * D:(a + 1) * D], b_xi3, xh_ap, b_xh_s)
            for g0 in range(0, NDT, 4):
                srcs = [xh_ap[:, (g0 + j) * 128:(g0 + j + 1) * 128]
                        for j in range(4)]
                tpf_, b_tpf_ = tpf_psL[_tpctr[0] % len(tpf_psL)]
                _tpctr[0] += 1
                fns = [lambda s=s, i=i, tpf_=tpf_: nc.tensor.transpose(
                    out=tpf_[:, i * 128:(i + 1) * 128], in_=s, identity=idf[:])
                    for i, s in enumerate(srcs)]
                plan.group("pe", fns, reads=[b_xh_s, b_idf], writes=[b_tpf_])
                b_tpf = b_tpf_
                ps3 = tpf_[:, :512].rearrange("p (t q) -> p t q", q=128)
                plan.op("act", lambda a=a, g0=g0, ps3=ps3: nc.scalar.activation(
                    out=XT3v[:, g0:g0 + 4, a * 128:a * 128 + 128], in_=ps3,
                    func=AF.Copy), reads=[b_tpf], writes=[b_XT3])
                plan.op("act", lambda a=a, g0=g0, ps3=ps3: nc.scalar.activation(
                    out=XTbv[:, g0:g0 + 4, a * 128:a * 128 + 128], in_=ps3,
                    func=AF.Copy), reads=[b_tpf], writes=[b_XTb])
        for et in range(NDT):
            ps, b_ps = pps[et % 2]
            fns = [lambda dt=dt, et=et, ps=ps: nc.tensor.matmul(
                out=ps[:, :CH],
                lhsT=wgT_sb[:, dt * D + et * 128: dt * D + et * 128 + 128],
                rhs=XTb[:, dt * CH:(dt + 1) * CH],
                start=(dt == 0), stop=(dt == NDT - 1)) for dt in range(NDT)]
            plan.group("pe", fns, reads=[b_XTb, b_wgT], writes=[b_ps])
            plan.op("act", lambda et=et, ps=ps: nc.scalar.activation(
                out=G3[:, et * CH:(et + 1) * CH], in_=ps[:, :CH], func=AF.Sigmoid,
                bias=bgv[:, et:et + 1]), reads=[b_ps, b_bgv], writes=[b_G3])
            vop(lambda et=et: nc.vector.tensor_scalar(
                out=G3[:, et * CH:(et + 1) * CH], in0=G3[:, et * CH:(et + 1) * CH],
                scalar1=pv[:, COL_OM + et:COL_OM + et + 1], scalar2=None,
                op0=ALU.mult), [b_G3, b_pv], [b_G3])
        for dt in range(NDT):
            vop(lambda dt=dt: nc.vector.tensor_scalar(
                out=XT3[:, dt * CH:(dt + 1) * CH], in0=XT3[:, dt * CH:(dt + 1) * CH],
                scalar1=pv[:, COL_BR + dt:COL_BR + dt + 1], scalar2=None,
                op0=ALU.subtract), [b_XT3, b_pv], [b_XT3])
        plan.op("dve", lambda: nc.vector.memset(HxA[:], 0.0), reads=[],
                writes=list(b_HxA), dur=900)
        for dt in range(NDT):
            vop(lambda dt=dt: nc.vector.tensor_copy(
                out=HxA[:, dt * (CH + 1):dt * (CH + 1) + 1], in_=hcar[:, dt:dt + 1]),
                [b_hcar], [b_HxA[dt]])
        cur, b_cur, nxt, b_nxt = HxA, b_HxA, HxB, b_HxB
        for k in range(K):
            last = (k == K - 1)
            for n in range(NDT):
                ps, b_ps = pps[n % 2]
                fns = [lambda dt=dt, n=n, ps=ps, cur=cur: nc.tensor.matmul(
                    out=ps[:, :CH],
                    lhsT=wrT_sb[:, dt * D + n * 128: dt * D + n * 128 + 128],
                    rhs=cur[:, dt * (CH + 1): dt * (CH + 1) + CH],
                    start=(dt == 0), stop=(dt == NDT - 1)) for dt in range(NDT)]
                plan.group("pe", fns, reads=list(b_cur) + [b_wrT], writes=[b_ps],
                           dur=1900)
                V3 = V3t[:, (n % 2) * CH:(n % 2 + 1) * CH]
                b_V3 = b_V3s[n % 2]
                plan.op("dve", lambda n=n, ps=ps, V3=V3: nc.vector.scalar_tensor_tensor(
                    out=V3, in0=ps[:, :CH], scalar=-1.0,
                    in1=XT3[:, n * CH:(n + 1) * CH], op0=ALU.mult, op1=ALU.add),
                    reads=[b_ps, b_XT3], writes=[b_V3], dur=700)
                plan.op("dve", lambda n=n, V3=V3: nc.vector.tensor_tensor(
                    out=V3, in0=V3, in1=G3[:, n * CH:(n + 1) * CH],
                    op=ALU.mult), reads=[b_V3, b_G3], writes=[b_V3], dur=700)
                a_rep = pv[:, COL_A + n:COL_A + n + 1].broadcast_to([128, CH])
                if last:
                    plan.op("dve", lambda n=n, a_rep=a_rep, V3=V3: nc.vector.tensor_tensor_scan(
                        out=Hf[:, n * CH:(n + 1) * CH], data0=a_rep, data1=V3,
                        initial=hcar[:, n:n + 1], op0=ALU.mult, op1=ALU.add),
                        reads=[b_V3, b_hcar, b_pv], writes=[b_Hfb[n]], dur=1400)
                else:
                    plan.op("dve", lambda n=n, a_rep=a_rep, nxt=nxt, V3=V3: nc.vector.tensor_tensor_scan(
                        out=nxt[:, n * (CH + 1) + 1: n * (CH + 1) + 1 + CH],
                        data0=a_rep, data1=V3,
                        initial=hcar[:, n:n + 1], op0=ALU.mult, op1=ALU.add),
                        reads=[b_V3, b_hcar, b_pv], writes=[b_nxt[n]], dur=1400)
            if not last:
                if k == 0:
                    for dt in range(NDT):
                        vop(lambda dt=dt, nxt=nxt: nc.vector.tensor_copy(
                            out=nxt[:, dt * (CH + 1):dt * (CH + 1) + 1],
                            in_=hcar[:, dt:dt + 1]), [b_hcar], [b_nxt[dt]])
                cur, b_cur, nxt, b_nxt = nxt, b_nxt, cur, b_cur
        for dt in range(NDT):
            vop(lambda dt=dt: nc.vector.tensor_copy(
                out=hcar[:, dt:dt + 1], in_=Hf[:, dt * CH + CH - 1: dt * CH + CH]),
                [b_Hfb[dt]], [b_hcar])
        for a in range(TPC):
            for g0 in range(0, NDT, 4):
                srcs = [Hf[:, (g0 + j) * CH + a * 128:(g0 + j) * CH + a * 128 + 128]
                        for j in range(4)]
                tp_group_f32(srcs, [b_Hfb[g0 + j] for j in range(4)],
                             lsq[:, g0 * 128:(g0 + 4) * 128], b_lsqep)
            emit_ln_noscratch(lsq[:, 0:D], b_lsqep, Nrm4[:, a * D:(a + 1) * D], b_Nrm4)
        for dt in range(NDT):
            srcs = [Nrm4[:, a * D + dt * 128: a * D + dt * 128 + 128]
                    for a in range(TPC)]
            tp_group_bf(srcs, [b_Nrm4], YT3[:, dt * CH:(dt + 1) * CH], b_YT3)
        for dt in range(NDT):
            vop(lambda dt=dt: nc.vector.tensor_scalar(
                out=YT3[:, dt * CH:(dt + 1) * CH], in0=YT3[:, dt * CH:(dt + 1) * CH],
                scalar1=pv[:, COL_LNW + dt:COL_LNW + dt + 1],
                scalar2=pv[:, COL_LNB + dt:COL_LNB + dt + 1],
                op0=ALU.mult, op1=ALU.add), [b_YT3, b_pv], [b_YT3])
        for et in range(NDT):
            ps, b_ps = pps[et % 2]
            fns = [lambda dt=dt, et=et, ps=ps: nc.tensor.matmul(
                out=ps[:, :CH],
                lhsT=woT_sb[:, dt * D + et * 128: dt * D + et * 128 + 128],
                rhs=YT3[:, dt * CH:(dt + 1) * CH],
                start=(dt == 0), stop=(dt == NDT - 1)) for dt in range(NDT)]
            plan.group("pe", fns, reads=[b_YT3, b_woT], writes=[b_ps])
            vop(lambda et=et, ps=ps: nc.vector.tensor_scalar(
                out=OT3[:, et * CH:(et + 1) * CH], in0=ps[:, :CH],
                scalar1=pv[:, COL_BO + et:COL_BO + et + 1], scalar2=None,
                op0=ALU.add), [b_ps, b_pv], [b_OT3])
        for a in range(TPC):
            for g0 in range(0, NDT, 4):
                srcs = [OT3[:, (g0 + j) * CH + a * 128:(g0 + j) * CH + a * 128 + 128]
                        for j in range(4)]
                dsl = o3[:, a * D + g0 * 128: a * D + (g0 + 4) * 128]
                xsl = xi3[:, a * D + g0 * 128: a * D + (g0 + 4) * 128]
                tp_group_f32_add(srcs, [b_OT3], dsl, b_o3, xsl, b_xi3)
        dst = xs3[ch * CH:(ch + 1) * CH, :].rearrange("(a p) d -> p a d", p=128)
        plan.dma("pool", lambda d=dst: nc.gpsimd.dma_start(out=d, in_=o3[:].rearrange("p (a d) -> p a d", d=D)),
                 reads=[b_o3], writes=[Buf("xs3w")])

    plan.barrier()
    tp_psL.pop()
    tpf_psL.pop()
    p3.close()

    p4 = ExitStack()
    emit_ffn(p4, "p4", xs3, out_d)
    plan.barrier()
    p4.close()

    plan.emit()

    with nc.Block() as block:
        @block.sync
        def _(eng):
            plan._run_engine("sp")

        @block.scalar
        def _(eng):
            plan._run_engine("act")

        @block.vector
        def _(eng):
            plan._run_engine("dve")

        @block.tensor
        def _(eng):
            plan._run_engine("pe")

        @block.gpsimd
        def _(eng):
            plan._run_engine("pool")
            plan.finish_waits("pool")

    outer.close()
    plan._es.close()
    return nc


def kernel(**inputs):
    import ml_dtypes
    from concourse.bass_utils import run_bass_kernel_spmd

    x = np.ascontiguousarray(np.asarray(inputs["L0_attn_a_x"], np.float32))
    B, L, _ = x.shape
    nc = build_nc(L=L)
    iden_bf = np.eye(128, dtype=ml_dtypes.bfloat16)
    iden_f32 = np.eye(128, dtype=np.float32)
    wnames = [k for k in inputs.keys() if k != "L0_attn_a_x"]
    in_maps = []
    for c in range(B):
        m = {nm: np.ascontiguousarray(np.asarray(inputs[nm], np.float32))
             for nm in wnames}
        m["x0"] = np.ascontiguousarray(x[c])
        m["iden_bf"] = iden_bf
        m["iden_f32"] = iden_f32
        in_maps.append(m)
    res = run_bass_kernel_spmd(nc, in_maps, core_ids=list(range(B)))
    return np.stack([np.asarray(r["out"], np.float32) for r in res.results], axis=0)


# version 20
# speedup vs baseline: 1.1594x; 1.1172x over previous
import numpy as np
from contextlib import ExitStack

import concourse.bass as bass
import concourse.mybir as mybir

F32 = mybir.dt.float32
BF16 = mybir.dt.bfloat16
AF = mybir.ActivationFunctionType
ALU = mybir.AluOpType
AXX = mybir.AxisListType.X

D = 1024
DI = 2816
NDT = D // 128
NIT = DI // 128
C2V = float(2 ** 22 + 1)
CBV = float(1.5 * 2 ** 23)

ENGS = ("sp", "act", "dve", "pe", "pool")
_EMAP = {"sp": "sync", "act": "scalar", "dve": "vector", "pe": "tensor",
         "pool": "gpsimd"}


class Buf:
    __slots__ = ("name", "w", "r")

    def __init__(self, name):
        self.name = name
        self.w = None
        self.r = []


class Node:
    __slots__ = ("idx", "eng", "fns", "deps", "dur", "is_dma", "sem", "val")

    def __init__(self, idx, eng, fns, deps, dur, is_dma=False):
        self.idx = idx
        self.eng = eng
        self.fns = fns
        self.deps = deps
        self.dur = dur
        self.is_dma = is_dma
        self.sem = None
        self.val = None


class Plan:

    def __init__(self, nc):
        self.nc = nc
        self.nodes = []
        self.sem = {}
        self._es = ExitStack()
        for e in ENGS:
            self.sem[e] = self._es.enter_context(nc.semaphore(f"sem_{e}"))
        self.NDMA = 24
        self.dpool = {"hw": list(range(16)), "sw": list(range(16, 24))}
        for i in range(self.NDMA):
            self.sem[f"dp{i}"] = self._es.enter_context(
                nc.semaphore(f"sem_dp{i}"))
        self._barrier_idx = -1

    def _mknode(self, eng, fns, reads, writes, dur, is_dma=False):
        deps = set()
        for b in reads:
            if b.w is not None:
                deps.add(b.w)
        for b in writes:
            if b.w is not None:
                deps.add(b.w)
            deps.update(b.r)
        if self._barrier_idx >= 0:
            deps.add(self._barrier_idx)
        idx = len(self.nodes)
        n = Node(idx, eng, fns, deps, dur, is_dma)
        self.nodes.append(n)
        for b in writes:
            b.w = idx
            b.r = []
        for b in reads:
            b.r.append(idx)
        return idx

    def op(self, eng, fn, reads=(), writes=(), dur=600):
        return self._mknode(eng, [fn], reads, writes, dur)

    def group(self, eng, fns, reads=(), writes=(), dur=None):
        if dur is None:
            dur = 250 * len(fns)
        return self._mknode(eng, list(fns), reads, writes, dur)

    def dma(self, queue, fn, reads=(), writes=(), dur=4000):
        return self._mknode(queue, [fn], reads, writes, dur, is_dma=True)

    def barrier(self):
        idx = self._mknode("sp", None, [], [], 0)
        n = self.nodes[idx]
        n.deps = set(range(idx))
        self._barrier_idx = idx
        return idx

    def emit(self):
        import heapq
        nodes = self.nodes
        N = len(nodes)
        nsucc = [[] for _ in range(N)]
        ndep = [0] * N
        for n in nodes:
            ndep[n.idx] = len(n.deps)
            for d in n.deps:
                nsucc[d].append(n.idx)
        blevel = [0.0] * N
        for i in range(N - 1, -1, -1):
            n = nodes[i]
            succ_max = 0.0
            for s_ in nsucc[i]:
                if blevel[s_] > succ_max:
                    succ_max = blevel[s_]
            blevel[i] = succ_max + n.dur + (1500 if n.is_dma else 0)
        eng_clock = {e: 0.0 for e in ENGS}
        finish = [0.0] * N
        ready_t = [0.0] * N
        heap = []
        for n in nodes:
            if ndep[n.idx] == 0:
                heapq.heappush(heap, (0.0, n.idx))
        order = {e: [] for e in ENGS}
        done = 0
        WINDOW = 4096
        scheduled = [False] * N
        min_unsched = 0
        poolq = {e: [] for e in ENGS}
        deferred = []
        while done < N:
            gate = min_unsched + WINDOW
            while heap:
                rt, i = heapq.heappop(heap)
                if i > gate:
                    heapq.heappush(deferred, i)
                else:
                    heapq.heappush(poolq[nodes[i].eng], (-blevel[i], i))
            while deferred and deferred[0] <= gate:
                i = heapq.heappop(deferred)
                heapq.heappush(poolq[nodes[i].eng], (-blevel[i], i))
            cands = [(eng_clock[e], e) for e in ENGS if poolq[e]]
            if not cands:
                raise RuntimeError("scheduler stuck")
            _, e = min(cands)
            (_, i) = heapq.heappop(poolq[e])
            scheduled[i] = True
            while min_unsched < N and scheduled[min_unsched]:
                min_unsched += 1
            n = nodes[i]
            start = max(eng_clock[e], ready_t[i])
            dma_lat = 1500 if n.is_dma else 0
            finish[i] = start + n.dur + dma_lat
            eng_clock[e] = start + (150 if n.is_dma else n.dur)
            order[e].append(i)
            done += 1
            for s in nsucc[i]:
                ready_t[s] = max(ready_t[s], finish[i])
                ndep[s] -= 1
                if ndep[s] == 0:
                    heapq.heappush(heap, (ready_t[s], s))
        eng_pos = {e: 0 for e in ENGS}
        dnext = {"hw": 0, "sw": 0}
        dcnt = [0] * self.NDMA
        dma_prev = {}
        for e in ENGS:
            for i in order[e]:
                n = nodes[i]
                if n.is_dma:
                    pool = "sw" if e == "pool" else "hw"
                    lst = self.dpool[pool]
                    di = lst[dnext[pool]]
                    dnext[pool] = (dnext[pool] + 1) % len(lst)
                    dcnt[di] += 16
                    n.sem = f"dp{di}"
                    n.val = dcnt[di]
                elif n.fns is not None:
                    eng_pos[e] += 1
                    n.sem = e
                    n.val = eng_pos[e]
        waited = {e: {} for e in ENGS}

        def emit_wait(e, semkey, val):
            if val <= waited[e].get(semkey, 0):
                return
            waited[e][semkey] = val
            g = getattr(self.nc, _EMAP[e])
            g.wait_ge(self.sem[semkey], val)

        self.final_dma = [(f"dp{i}", dcnt[i]) for i in range(self.NDMA) if dcnt[i]]
        barrier_cache = {}

        def barrier_maxes(bidx):
            if bidx not in barrier_cache:
                mx = {}
                for d2 in nodes[bidx].deps:
                    dn2 = nodes[d2]
                    if dn2.sem is not None:
                        if dn2.val > mx.get(dn2.sem, 0):
                            mx[dn2.sem] = dn2.val
                    elif dn2.fns is None and d2 in barrier_cache:
                        for k, v in barrier_cache[d2].items():
                            if v > mx.get(k, 0):
                                mx[k] = v
                barrier_cache[bidx] = mx
            return barrier_cache[bidx]

        def run_engine(e):
            prev_dma_on_sem = {}
            for i in order[e]:
                n = nodes[i]
                if n.is_dma:
                    prev = prev_dma_on_sem.get(n.sem)
                    if prev is not None:
                        emit_wait(e, n.sem, prev)
                    elif n.val > 16:
                        emit_wait(e, n.sem, n.val - 16)
                    prev_dma_on_sem[n.sem] = n.val
                for d in sorted(n.deps):
                    dn = nodes[d]
                    if dn.sem is None:
                        if dn.fns is None:
                            for k, v in sorted(barrier_maxes(d).items()):
                                emit_wait(e, k, v)
                        continue
                    emit_wait(e, dn.sem, dn.val)
                if not n.fns:
                    continue
                h = self.sem[n.sem]
                inc = 16 if n.is_dma else 1
                for j, fn in enumerate(n.fns):
                    if j == len(n.fns) - 1:
                        fn().then_inc(h, inc)
                    else:
                        fn()
        self._run_engine = run_engine

    def finish_waits(self, e):
        g = getattr(self.nc, _EMAP[e])
        for key, val in self.final_dma:
            g.wait_ge(self.sem[key], val)


def build_nc(L=8192, K=10, CH=512):
    nc = bass.Bass("TRN2", target_bir_lowering=False, debug=False)
    NCH = L // CH
    TPC = CH // 128
    assert TPC == 4

    x0_d = nc.dram_tensor("x0", [L, D], F32, kind="ExternalInput").ap()
    win = {}
    for nm, shp in [
        ("L0_attn_a_phazor", [D, 2]), ("L0_attn_a_phazor_init", [D, 2]),
        ("L0_attn_a_last_conv_re", [D]), ("L0_attn_a_last_conv_im", [D]),
        ("L0_attn_b_decay", [D]), ("L0_attn_b_Wr", [D, D]),
        ("L0_attn_b_br", [D]), ("L0_attn_b_Wg", [D, D]), ("L0_attn_b_bg", [D]),
        ("L0_attn_b_Wo", [D, D]), ("L0_attn_b_bo", [D]),
        ("L0_attn_b_ln_w", [D]), ("L0_attn_b_ln_b", [D]),
        ("L0_ffn_w_gate", [DI, D]), ("L0_ffn_w_up", [DI, D]),
        ("L0_ffn_w_down", [D, DI]),
    ]:
        win[nm] = nc.dram_tensor(nm, shp, F32, kind="ExternalInput").ap()
    iden_bf_d = nc.dram_tensor("iden_bf", [128, 128], BF16, kind="ExternalInput").ap()
    iden_f32_d = nc.dram_tensor("iden_f32", [128, 128], F32, kind="ExternalInput").ap()
    out_d = nc.dram_tensor("out", [L, D], F32, kind="ExternalOutput").ap()

    xs1 = nc.dram_tensor("xs1", [L, D], F32, kind="Internal").ap()
    xs2 = nc.dram_tensor("xs2", [L, D], F32, kind="Internal").ap()
    xs3 = nc.dram_tensor("xs3", [L, D], F32, kind="Internal").ap()
    wgq_d = nc.dram_tensor("wgq_d", [128, NDT * DI], BF16, kind="Internal").ap()
    wuq_d = nc.dram_tensor("wuq_d", [128, NDT * DI], BF16, kind="Internal").ap()
    wdq_d = nc.dram_tensor("wdq_d", [128, NIT * D], BF16, kind="Internal").ap()
    wgT_d = nc.dram_tensor("wgT_d", [128, NDT * D], BF16, kind="Internal").ap()
    wrT_d = nc.dram_tensor("wrT_d", [128, NDT * D], BF16, kind="Internal").ap()
    woT_d = nc.dram_tensor("woT_d", [128, NDT * D], BF16, kind="Internal").ap()

    plan = Plan(nc)
    outer = ExitStack()

    def mksb(esx, name, shape, dt):
        t = esx.enter_context(nc.sbuf_tensor(name, shape, dt))
        return t, Buf(name)

    def mkps(esx, name, shape, dt):
        t = esx.enter_context(nc.psum_tensor(name, shape, dt))
        return t, Buf(name)

    idbf, b_idbf = mksb(outer, "idbf", [128, 128], BF16)
    idf, b_idf = mksb(outer, "idf", [128, 128], F32)
    pv, b_pv = mksb(outer, "pv", [128, 64], F32)
    (COL_A, COL_OM, COL_BR, COL_BO, COL_LNW, COL_LNB, COL_R) = (
        0, 8, 16, 24, 32, 40, 48)
    bgv, b_bgv = mksb(outer, "bgv", [128, NDT], F32)
    carry, b_carry = mksb(outer, "carry", [128, 2 * NDT], F32)
    xh2t, b_xh2t = mksb(outer, "xh", [128, 2 * D], F32)
    b_xh2 = [Buf("xh_s0"), Buf("xh_s1")]
    lnt, b_lnt = mksb(outer, "lnt", [128, 32], F32)
    lsq, b_lsq = mksb(outer, "lsq", [128, D], F32)
    tUc, b_tUc = mksb(outer, "tUc", [128, 2 * NDT], F32)
    q0t, b_q0t = mksb(outer, "q0t", [128, 2 * NDT], F32)
    tp_psL = [mkps(outer, "tp_ps0", [128, 512], BF16)]
    tpf_psL = [mkps(outer, "tpf_ps0", [128, 512], F32)]
    _tpctr = [0]
    tp_ps, b_tp = tp_psL[0]
    tpf_ps, b_tpf = tpf_psL[0]

    def dma_in(queue, dst_ap, src_ap, wbuf, rbufs=()):
        plan.dma(queue,
                 lambda d=dst_ap, s=src_ap: getattr(nc, _EMAP[queue]).dma_start(out=d, in_=s),
                 reads=list(rbufs), writes=[wbuf])

    dma_in("sp", idbf[:], iden_bf_d[:], b_idbf)
    dma_in("sp", idf[:], iden_f32_d[:], b_idf)

    def tp_group_bf(srcs, b_srcs, dst_ap, b_dst):
        ps, b_ps = tp_psL[_tpctr[0] % len(tp_psL)]
        _tpctr[0] += 1
        n = len(srcs)
        fns = [lambda s=s, i=i, ps=ps: nc.tensor.transpose(
            out=ps[:, i * 128:(i + 1) * 128], in_=s, identity=idbf[:])
            for i, s in enumerate(srcs)]
        plan.group("pe", fns, reads=list(b_srcs) + [b_idbf], writes=[b_ps],
                   dur=300 * n)
        plan.op("act", lambda n=n, ps=ps: nc.scalar.activation(
            out=dst_ap, in_=ps[:, :n * 128], func=AF.Copy),
            reads=[b_ps], writes=[b_dst], dur=600)

    def tp_group_f32(srcs, b_srcs, dst_ap, b_dst):
        ps, b_ps = tpf_psL[_tpctr[0] % len(tpf_psL)]
        _tpctr[0] += 1
        n = len(srcs)
        fns = [lambda s=s, i=i, ps=ps: nc.tensor.transpose(
            out=ps[:, i * 128:(i + 1) * 128], in_=s, identity=idf[:])
            for i, s in enumerate(srcs)]
        plan.group("pe", fns, reads=list(b_srcs) + [b_idf], writes=[b_ps],
                   dur=300 * n)
        plan.op("act", lambda n=n, ps=ps: nc.scalar.activation(
            out=dst_ap, in_=ps[:, :n * 128], func=AF.Copy),
            reads=[b_ps], writes=[b_dst], dur=600)

    def tp_group_f32_add(srcs, b_srcs, dst_ap, b_dst, add_ap, b_add):
        ps, b_ps = tpf_psL[_tpctr[0] % len(tpf_psL)]
        _tpctr[0] += 1
        n = len(srcs)
        fns = [lambda s=s, i=i, ps=ps: nc.tensor.transpose(
            out=ps[:, i * 128:(i + 1) * 128], in_=s, identity=idf[:])
            for i, s in enumerate(srcs)]
        plan.group("pe", fns, reads=list(b_srcs) + [b_idf], writes=[b_ps],
                   dur=300 * n)
        plan.op("dve", lambda n=n, ps=ps: nc.vector.tensor_tensor(
            out=dst_ap, in0=ps[:, :n * 128], in1=add_ap, op=ALU.add),
            reads=[b_ps, b_add], writes=[b_dst], dur=750)

    def load_vec_to_cols(vec1d_view, dst_ap, b_dst, width):
        dma_in("sp", vstage[0:width, 0:128], vec1d_view, b_vstage, rbufs=[b_vstage])
        plan.op("pe", lambda: nc.tensor.transpose(
            out=tpf_ps[:, 0:width], in_=vstage[0:width, 0:128],
            identity=idf[0:width, 0:width]),
            reads=[b_vstage, b_idf], writes=[b_tpf])
        plan.op("act", lambda: nc.scalar.activation(
            out=dst_ap, in_=tpf_ps[:, 0:width], func=AF.Copy),
            reads=[b_tpf], writes=[b_dst])

    _lnslots = [Buf(f"lnt{i}") for i in range(4)]
    _lsqslots = [Buf("lsq0")]
    _lnctr = [0]

    def emit_ln(src_ap, b_src, dst_ap, b_dst, sq_over=None):
        s = _lnctr[0] % 4
        _lnctr[0] += 1
        b_ln = _lnslots[s]
        if sq_over is None:
            sq, b_sq = lsq[:, 0:D], _lsqslots[0]
        else:
            sq, b_sq = sq_over
        ln = lnt[:, s * 8:(s + 1) * 8]
        plan.op("dve", lambda: nc.vector.reduce_sum(
            out=ln[:, 0:1], in_=src_ap, axis=AXX), reads=[b_src], writes=[b_ln],
            dur=1100)
        plan.op("act", lambda: nc.scalar.activation(
            out=sq, in_=src_ap, func=AF.Square), reads=[b_src], writes=[b_sq],
            dur=1000)
        plan.op("dve", lambda: nc.vector.reduce_sum(
            out=ln[:, 1:2], in_=sq, axis=AXX), reads=[b_sq], writes=[b_ln],
            dur=1100)
        plan.op("dve", lambda: nc.vector.tensor_scalar_mul(
            out=ln[:, 0:2], in0=ln[:, 0:2], scalar1=1.0 / D),
            reads=[b_ln], writes=[b_ln], dur=80)
        plan.op("dve", lambda: nc.vector.tensor_tensor(
            out=ln[:, 2:3], in0=ln[:, 0:1], in1=ln[:, 0:1], op=ALU.mult),
            reads=[b_ln], writes=[b_ln], dur=80)
        plan.op("dve", lambda: nc.vector.tensor_tensor(
            out=ln[:, 1:2], in0=ln[:, 1:2], in1=ln[:, 2:3], op=ALU.subtract),
            reads=[b_ln], writes=[b_ln], dur=80)
        plan.op("dve", lambda: nc.vector.tensor_single_scalar(
            out=ln[:, 1:2], in_=ln[:, 1:2], scalar=1e-5, op=ALU.add),
            reads=[b_ln], writes=[b_ln], dur=80)
        plan.op("act", lambda: nc.scalar.activation(
            out=ln[:, 3:4], in_=ln[:, 1:2], func=AF.Sqrt),
            reads=[b_ln], writes=[b_ln], dur=250)
        plan.op("dve", lambda: nc.vector.reciprocal(
            out=ln[:, 3:4], in_=ln[:, 3:4]), reads=[b_ln], writes=[b_ln], dur=80)
        plan.op("dve", lambda: nc.vector.tensor_scalar(
            out=dst_ap, in0=src_ap, scalar1=ln[:, 0:1], scalar2=ln[:, 3:4],
            op0=ALU.subtract, op1=ALU.mult), reads=[b_src, b_ln], writes=[b_dst],
            dur=700)

    def emit_ln_noscratch(src_ap, b_src, dst_ap, b_dst):
        emit_ln(src_ap, b_src, dst_ap, b_dst,
                sq_over=(xh2t[:, 0:D], b_xh2[0]))

    def emit_quant(src_ap, b_src, dst_ap, b_dst, W, scr):
        TT = int((58 + W) / 0.96)
        TS = int((58 + W / 2) / 0.96)
        """fp4 quantize-dequantize [128, W] f32 -> [128, W] bf16.
        src is DESTROYED (used as Veltkamp scratch).  scr: z/a2 f32 >= W,
        zs bf16 >= W, red/sc/rcp f32 >= W/16."""
        nb = W // 16
        z, b_z = scr["z"]
        a2, b_a2 = scr["a2"]
        red, b_red = scr["red"]
        sc, b_sc = scr["sc"]
        rcp, b_rcp = scr["rcp"]
        src3 = src_ap.rearrange("p (a b) -> p a b", b=16)
        plan.op("dve", lambda: nc.vector.tensor_reduce(
            out=red[:, :nb], in_=src3, axis=AXX, op=ALU.max,
            apply_absolute_value=True), reads=[b_src], writes=[b_red], dur=TT)
        plan.op("dve", lambda: nc.vector.tensor_scalar(
            out=sc[:, :nb], in0=red[:, :nb],
            scalar1=float(np.float32(1.0) / np.float32(12.0)), scalar2=5e-13,
            op0=ALU.mult, op1=ALU.max), reads=[b_red], writes=[b_sc], dur=120)
        plan.op("dve", lambda: nc.vector.reciprocal(
            out=rcp[:, :nb], in_=sc[:, :nb]), reads=[b_sc], writes=[b_rcp], dur=160)
        rcp3 = rcp[:, :nb].unsqueeze(2).broadcast_to([128, nb, 16])
        sc3 = sc[:, :nb].unsqueeze(2).broadcast_to([128, nb, 16])
        z3 = z[:, :W].rearrange("p (a b) -> p a b", b=16)
        plan.op("dve", lambda: nc.vector.tensor_tensor(
            out=z3, in0=src3, in1=rcp3, op=ALU.mult),
            reads=[b_src, b_rcp], writes=[b_z], dur=TT)
        msk, b_msk = scr["msk"]
        zs, b_zs = scr["zs"]
        plan.op("act", lambda: nc.scalar.activation(
            out=zs[:, :W], in_=z[:, :W], func=AF.Square),
            reads=[b_z], writes=[b_zs], dur=int(224 / 1.2 + W / 1.2))
        plan.op("dve", lambda: nc.vector.tensor_single_scalar(
            out=msk[:, :W], in_=zs[:, :W], scalar=4.0, op=ALU.is_lt),
            reads=[b_zs], writes=[b_msk], dur=TS)
        plan.op("dve", lambda: nc.vector.tensor_scalar(
            out=a2[:, :W], in0=z[:, :W], scalar1=CBV, scalar2=CBV,
            op0=ALU.add, op1=ALU.subtract), reads=[b_z, b_msk], writes=[b_a2])
        plan.op("dve", lambda: nc.vector.tensor_scalar_mul(
            out=src_ap, in0=z[:, :W], scalar1=C2V), reads=[b_z], writes=[b_src], dur=TS)
        plan.op("dve", lambda: nc.vector.tensor_tensor(
            out=z[:, :W], in0=src_ap, in1=z[:, :W], op=ALU.subtract),
            reads=[b_src, b_z], writes=[b_z], dur=700)
        plan.op("dve", lambda: nc.vector.tensor_tensor(
            out=src_ap, in0=src_ap, in1=z[:, :W], op=ALU.subtract),
            reads=[b_src, b_z], writes=[b_src], dur=700)
        plan.op("dve", lambda: nc.vector.copy_predicated(
            out=src_ap, mask=msk[:, :W], data=a2[:, :W]),
            reads=[b_msk, b_a2, b_src], writes=[b_src], dur=TT)
        dst3 = dst_ap.rearrange("p (a b) -> p a b", b=16)
        plan.op("dve", lambda: nc.vector.tensor_tensor(
            out=dst3, in0=src3, in1=sc3, op=ALU.mult),
            reads=[b_src, b_sc], writes=[b_dst], dur=TT)

    p1 = ExitStack()
    tWr, b_tWr = mksb(p1, "tWr", [128, NDT * CH], F32)
    tWi, b_tWi = mksb(p1, "tWi", [128, NDT * CH], F32)
    tA, b_tA = mksb(p1, "tA", [128, NDT * CH], F32)
    tB, b_tB = mksb(p1, "tB", [128, NDT * CH], F32)
    p0 = ExitStack()
    wrow, b_wrow = mksb(p0, "wrow", [128, DI], F32)
    vstage, b_vstage = mksb(p0, "vstage", [8, 256], F32)
    wqr, b_wqr = mksb(p0, "wqr", [128, DI], BF16)
    qz0, b_qz0 = mksb(p0, "qz0", [128, DI], F32)
    qa20, b_qa20 = mksb(p0, "qa20", [128, DI], BF16)
    qred0, b_qred0 = mksb(p0, "qred0", [128, DI // 16], F32)
    qsc0, b_qsc0 = mksb(p0, "qsc0", [128, DI // 16], F32)
    qrcp0, b_qrcp0 = mksb(p0, "qrcp0", [128, DI // 16], F32)
    qmsk0, b_qmsk0 = mksb(p0, "qmsk0", [128, DI], mybir.dt.uint8)
    qzs0, b_qzs0 = mksb(p0, "qzs0", [128, DI], BF16)
    scr0 = {"z": (qz0, b_qz0), "a2": (qa20, b_qa20), "zs": (qzs0, b_qzs0),
            "red": (qred0, b_qred0), "sc": (qsc0, b_qsc0), "rcp": (qrcp0, b_qrcp0),
            "msk": (qmsk0, b_qmsk0)}
    wasm, b_wasm = mksb(p0, "wasm", [128, NDT * DI], BF16)

    def prep_gateup(w_src, dst_dram):
        for it in range(NIT):
            dma_in("sp", wrow[:, :D], w_src[it * 128:(it + 1) * 128, :],
                   b_wrow, rbufs=[b_wrow])
            emit_quant(wrow[:, :D], b_wrow, wqr[:, :D], b_wqr, D, scr0)
            for g0 in range(0, NDT, 4):
                srcs = [wqr[:, (g0 + j) * 128:(g0 + j + 1) * 128] for j in range(4)]
                dst = wasm[:, g0 * DI + it * 128:].rearrange(
                    "p (x) -> p x", x=1) if False else None
                dstap = wasm[:].rearrange("p (t q) -> p t q", q=DI)[
                    :, g0:g0 + 4, it * 128:it * 128 + 128]
                n = 4
                tp_ps_, b_tp_ = tp_psL[_tpctr[0] % len(tp_psL)]
                _tpctr[0] += 1
                fns = [lambda s=s, i=i, tp_ps_=tp_ps_: nc.tensor.transpose(
                    out=tp_ps_[:, i * 128:(i + 1) * 128], in_=s, identity=idbf[:])
                    for i, s in enumerate(srcs)]
                plan.group("pe", fns, reads=[b_wqr, b_idbf], writes=[b_tp_])
                src_ps = tp_ps_[:, :512].rearrange("p (t q) -> p t q", q=128)
                plan.op("act", lambda dstap=dstap, src_ps=src_ps:
                        nc.scalar.activation(out=dstap, in_=src_ps, func=AF.Copy),
                        reads=[b_tp_], writes=[b_wasm])
        plan.dma("sp", lambda d=dst_dram: nc.sync.dma_start(out=d[:], in_=wasm[:]),
                 reads=[b_wasm], writes=[Buf("wdram")])

    prep_gateup(win["L0_ffn_w_gate"], wgq_d)
    prep_gateup(win["L0_ffn_w_up"], wuq_d)

    for dt in range(NDT):
        dma_in("sp", wrow[:, :DI], win["L0_ffn_w_down"][dt * 128:(dt + 1) * 128, :],
               b_wrow, rbufs=[b_wrow])
        emit_quant(wrow[:, :DI], b_wrow, wqr[:, :DI], b_wqr, DI, scr0)
        for g0 in range(0, NIT, 4):
            gn = min(4, NIT - g0)
            srcs = [wqr[:, (g0 + j) * 128:(g0 + j + 1) * 128] for j in range(gn)]
            dstap = wasm[:, :NIT * D].rearrange("p (t q) -> p t q", q=D)[
                :, g0:g0 + gn, dt * 128:dt * 128 + 128]
            fns = [lambda s=s, i=i: nc.tensor.transpose(
                out=tp_ps[:, i * 128:(i + 1) * 128], in_=s, identity=idbf[:])
                for i, s in enumerate(srcs)]
            plan.group("pe", fns, reads=[b_wqr, b_idbf], writes=[b_tp])
            src_ps = tp_ps[:, :gn * 128].rearrange("p (t q) -> p t q", q=128)
            plan.op("act", lambda dstap=dstap, src_ps=src_ps:
                    nc.scalar.activation(out=dstap, in_=src_ps, func=AF.Copy),
                    reads=[b_tp], writes=[b_wasm])
    plan.dma("sp", lambda: nc.sync.dma_start(out=wdq_d[:], in_=wasm[:, :NIT * D]),
             reads=[b_wasm], writes=[Buf("wdram")])

    for w_src, dstd in [(win["L0_attn_b_Wg"], wgT_d), (win["L0_attn_b_Wr"], wrT_d),
                        (win["L0_attn_b_Wo"], woT_d)]:
        for et in range(NDT):
            dma_in("sp", wrow[:, :D], w_src[et * 128:(et + 1) * 128, :],
                   b_wrow, rbufs=[b_wrow])
            plan.op("act", lambda: nc.scalar.activation(
                out=wqr[:, :D], in_=wrow[:, :D], func=AF.Copy),
                reads=[b_wrow], writes=[b_wqr])
            for g0 in range(0, NDT, 4):
                srcs = [wqr[:, (g0 + j) * 128:(g0 + j + 1) * 128] for j in range(4)]
                dstap = wasm[:, :NDT * D].rearrange("p (t q) -> p t q", q=D)[
                    :, g0:g0 + 4, et * 128:et * 128 + 128]
                tp_ps_, b_tp_ = tp_psL[_tpctr[0] % len(tp_psL)]
                _tpctr[0] += 1
                fns = [lambda s=s, i=i, tp_ps_=tp_ps_: nc.tensor.transpose(
                    out=tp_ps_[:, i * 128:(i + 1) * 128], in_=s, identity=idbf[:])
                    for i, s in enumerate(srcs)]
                plan.group("pe", fns, reads=[b_wqr, b_idbf], writes=[b_tp_])
                b_tp = b_tp_
                src_ps = tp_ps_[:, :512].rearrange("p (t q) -> p t q", q=128)
                plan.op("act", lambda dstap=dstap, src_ps=src_ps:
                        nc.scalar.activation(out=dstap, in_=src_ps, func=AF.Copy),
                        reads=[b_tp], writes=[b_wasm])
        plan.dma("sp", lambda d=dstd: nc.sync.dma_start(
            out=d[:], in_=wasm[:, :NDT * D]), reads=[b_wasm], writes=[Buf("wdram")])

    load_vec_to_cols(win["L0_attn_b_decay"].rearrange("(t p) -> t p", p=128),
                     pv[:, COL_A:COL_A + NDT], b_pv, NDT)
    plan.op("act", lambda: nc.scalar.activation(
        out=pv[:, COL_A:COL_A + NDT], in_=pv[:, COL_A:COL_A + NDT],
        func=AF.Sigmoid), reads=[b_pv], writes=[b_pv])
    plan.op("dve", lambda: nc.vector.tensor_scalar(
        out=pv[:, COL_OM:COL_OM + NDT], in0=pv[:, COL_A:COL_A + NDT],
        scalar1=-1.0, scalar2=1.0, op0=ALU.mult, op1=ALU.add),
        reads=[b_pv], writes=[b_pv])
    for nm, col in [("L0_attn_b_br", COL_BR), ("L0_attn_b_bo", COL_BO),
                    ("L0_attn_b_ln_w", COL_LNW), ("L0_attn_b_ln_b", COL_LNB)]:
        load_vec_to_cols(win[nm].rearrange("(t p) -> t p", p=128),
                         pv[:, col:col + NDT], b_pv, NDT)
    load_vec_to_cols(win["L0_attn_b_bg"].rearrange("(t p) -> t p", p=128),
                     bgv[:], b_bgv, NDT)

    phz, b_phz = mksb(p0, "phz", [128, 2 * NDT], F32)
    phi_, b_phi = mksb(p0, "phi_", [128, 2 * NDT], F32)
    lcv, b_lcv = mksb(p0, "lcv", [128, 2 * NDT], F32)
    tmpv, b_tmpv = mksb(p0, "tmpv", [128, 4 * NDT], F32)
    den, b_den = mksb(p0, "den", [128, 2 * NDT], F32)
    for src_d, dst_t, b_dst in [(win["L0_attn_a_phazor"], phz, b_phz),
                                (win["L0_attn_a_phazor_init"], phi_, b_phi)]:
        dma_in("sp", vstage[:, :256],
               src_d.rearrange("(t p) c -> t (p c)", p=128), b_vstage,
               rbufs=[b_vstage])
        for c in range(2):
            sv = vstage[:, :256].rearrange("t (p c) -> t p c", c=2)[:, :, c]
            plan.op("pe", lambda sv=sv, c=c: nc.tensor.transpose(
                out=tpf_ps[:, c * 128:c * 128 + NDT], in_=sv,
                identity=idf[0:NDT, 0:NDT]),
                reads=[b_vstage, b_idf], writes=[b_tpf])
        dsts = dst_t[:].rearrange("p (t c) -> p t c", c=2)
        plan.op("act", lambda dsts=dsts: nc.scalar.activation(
            out=dsts[:, :, 0], in_=tpf_ps[:, 0:NDT], func=AF.Copy),
            reads=[b_tpf], writes=[b_dst])
        plan.op("act", lambda dsts=dsts: nc.scalar.activation(
            out=dsts[:, :, 1], in_=tpf_ps[:, 128:128 + NDT], func=AF.Copy),
            reads=[b_tpf], writes=[b_dst])
    load_vec_to_cols(win["L0_attn_a_last_conv_re"].rearrange("(t p) -> t p", p=128),
                     lcv[:, 0:NDT], b_lcv, NDT)
    load_vec_to_cols(win["L0_attn_a_last_conv_im"].rearrange("(t p) -> t p", p=128),
                     lcv[:, NDT:2 * NDT], b_lcv, NDT)

    prs = phz[:].rearrange("p (t c) -> p t c", c=2)[:, :, 0]
    pis = phz[:].rearrange("p (t c) -> p t c", c=2)[:, :, 1]
    phr = phi_[:].rearrange("p (t c) -> p t c", c=2)[:, :, 0]
    phii = phi_[:].rearrange("p (t c) -> p t c", c=2)[:, :, 1]
    lre = lcv[:, 0:NDT]
    lim = lcv[:, NDT:2 * NDT]

    def vop(fn, reads, writes):
        plan.op("dve", fn, reads=reads, writes=writes)

    vop(lambda: nc.vector.tensor_tensor(out=tmpv[:, 0:NDT], in0=prs, in1=prs,
        op=ALU.mult), [b_phz], [b_tmpv])
    vop(lambda: nc.vector.tensor_tensor(out=tmpv[:, NDT:2 * NDT], in0=pis, in1=pis,
        op=ALU.mult), [b_phz], [b_tmpv])
    vop(lambda: nc.vector.tensor_tensor(out=tmpv[:, 0:NDT], in0=tmpv[:, 0:NDT],
        in1=tmpv[:, NDT:2 * NDT], op=ALU.add), [b_tmpv], [b_tmpv])
    plan.op("act", lambda: nc.scalar.activation(
        out=tmpv[:, NDT:2 * NDT], in_=tmpv[:, 0:NDT], func=AF.Sqrt),
        reads=[b_tmpv], writes=[b_tmpv])
    plan.op("act", lambda: nc.scalar.activation(
        out=pv[:, COL_R:COL_R + NDT], in_=tmpv[:, NDT:2 * NDT], func=AF.Exp,
        scale=-1.0), reads=[b_tmpv], writes=[b_pv])
    vop(lambda: nc.vector.reciprocal(out=tmpv[:, 0:NDT], in_=tmpv[:, NDT:2 * NDT]),
        [b_tmpv], [b_tmpv])
    vop(lambda: nc.vector.tensor_tensor(out=tmpv[:, 2 * NDT:3 * NDT], in0=prs,
        in1=tmpv[:, 0:NDT], op=ALU.mult), [b_phz, b_tmpv], [b_tmpv])
    vop(lambda: nc.vector.tensor_tensor(out=tmpv[:, 3 * NDT:4 * NDT], in0=pis,
        in1=tmpv[:, 0:NDT], op=ALU.mult), [b_phz, b_tmpv], [b_tmpv])

    vop(lambda: nc.vector.tensor_tensor(out=den[:, 0:NDT], in0=phr, in1=phr,
        op=ALU.mult), [b_phi], [b_den])
    vop(lambda: nc.vector.tensor_tensor(out=den[:, NDT:], in0=phii, in1=phii,
        op=ALU.mult), [b_phi], [b_den])
    vop(lambda: nc.vector.tensor_tensor(out=den[:, 0:NDT], in0=den[:, 0:NDT],
        in1=den[:, NDT:], op=ALU.add), [b_den], [b_den])
    vop(lambda: nc.vector.tensor_single_scalar(out=den[:, 0:NDT], in_=den[:, 0:NDT],
        scalar=1e-30, op=ALU.max), [b_den], [b_den])
    vop(lambda: nc.vector.reciprocal(out=den[:, 0:NDT], in_=den[:, 0:NDT]),
        [b_den], [b_den])
    vop(lambda: nc.vector.tensor_tensor(out=q0t[:, 0:NDT], in0=lre, in1=phr,
        op=ALU.mult), [b_lcv, b_phi], [b_q0t])
    vop(lambda: nc.vector.tensor_tensor(out=den[:, NDT:], in0=lim, in1=phii,
        op=ALU.mult), [b_lcv, b_phi], [b_den])
    vop(lambda: nc.vector.tensor_tensor(out=q0t[:, 0:NDT], in0=q0t[:, 0:NDT],
        in1=den[:, NDT:], op=ALU.add), [b_q0t, b_den], [b_q0t])
    vop(lambda: nc.vector.tensor_tensor(out=q0t[:, 0:NDT], in0=q0t[:, 0:NDT],
        in1=den[:, 0:NDT], op=ALU.mult), [b_q0t, b_den], [b_q0t])
    vop(lambda: nc.vector.tensor_tensor(out=q0t[:, NDT:], in0=lim, in1=phr,
        op=ALU.mult), [b_lcv, b_phi], [b_q0t])
    vop(lambda: nc.vector.tensor_tensor(out=den[:, NDT:], in0=lre, in1=phii,
        op=ALU.mult), [b_lcv, b_phi], [b_den])
    vop(lambda: nc.vector.tensor_tensor(out=q0t[:, NDT:], in0=q0t[:, NDT:],
        in1=den[:, NDT:], op=ALU.subtract), [b_q0t, b_den], [b_q0t])
    vop(lambda: nc.vector.tensor_tensor(out=q0t[:, NDT:], in0=q0t[:, NDT:],
        in1=den[:, 0:NDT], op=ALU.mult), [b_q0t, b_den], [b_q0t])
    vop(lambda: nc.vector.tensor_copy(out=carry[:], in_=q0t[:]), [b_q0t], [b_carry])

    for dt in range(NDT):
        base = dt * CH
        ur1 = tmpv[:, 2 * NDT + dt:2 * NDT + dt + 1]
        ui1 = tmpv[:, 3 * NDT + dt:3 * NDT + dt + 1]
        vop(lambda ur1=ur1, base=base: nc.vector.tensor_copy(
            out=tWr[:, base:base + 1], in_=ur1), [b_tmpv], [b_tWr])
        vop(lambda ui1=ui1, base=base: nc.vector.tensor_copy(
            out=tWi[:, base:base + 1], in_=ui1), [b_tmpv], [b_tWi])
        n = 1
        while n < CH:
            m = min(n, CH - n)
            anr = tWr[:, base + n - 1:base + n]
            ani = tWi[:, base + n - 1:base + n]
            s_r = tWr[:, base:base + m]
            s_i = tWi[:, base:base + m]
            d_r = tWr[:, base + n:base + n + m]
            d_i = tWi[:, base + n:base + n + m]
            t1 = tA[:, base:base + m]
            vop(lambda d=d_r, s=s_r, a=anr: nc.vector.tensor_scalar(
                out=d, in0=s, scalar1=a, scalar2=None, op0=ALU.mult),
                [b_tWr], [b_tWr])
            vop(lambda t=t1, s=s_i, a=ani: nc.vector.tensor_scalar(
                out=t, in0=s, scalar1=a, scalar2=None, op0=ALU.mult),
                [b_tWi], [b_tA])
            vop(lambda d=d_r, t=t1: nc.vector.tensor_tensor(
                out=d, in0=d, in1=t, op=ALU.subtract), [b_tWr, b_tA], [b_tWr])
            vop(lambda d=d_i, s=s_r, a=ani: nc.vector.tensor_scalar(
                out=d, in0=s, scalar1=a, scalar2=None, op0=ALU.mult),
                [b_tWr, b_tWi], [b_tWi])
            vop(lambda t=t1, s=s_i, a=anr: nc.vector.tensor_scalar(
                out=t, in0=s, scalar1=a, scalar2=None, op0=ALU.mult),
                [b_tWi, b_tWr], [b_tA])
            vop(lambda d=d_i, t=t1: nc.vector.tensor_tensor(
                out=d, in0=d, in1=t, op=ALU.add), [b_tWi, b_tA], [b_tWi])
            n += m
        vop(lambda dt=dt, base=base: nc.vector.tensor_copy(
            out=tUc[:, dt:dt + 1], in_=tWr[:, base + CH - 1:base + CH]),
            [b_tWr], [b_tUc])
        vop(lambda dt=dt, base=base: nc.vector.tensor_copy(
            out=tUc[:, NDT + dt:NDT + dt + 1], in_=tWi[:, base + CH - 1:base + CH]),
            [b_tWi], [b_tUc])
    for dt in range(NDT):
        base = dt * CH
        pr1 = phr[:, dt:dt + 1]
        pi1 = phii[:, dt:dt + 1]
        vop(lambda base=base, pr1=pr1: nc.vector.tensor_scalar(
            out=tA[:, base:base + CH], in0=tWr[:, base:base + CH], scalar1=pr1,
            scalar2=None, op0=ALU.mult), [b_tWr, b_phi], [b_tA])
        vop(lambda base=base, pi1=pi1: nc.vector.tensor_scalar(
            out=tB[:, base:base + CH], in0=tWi[:, base:base + CH], scalar1=pi1,
            scalar2=None, op0=ALU.mult), [b_tWi, b_phi], [b_tB])
        vop(lambda base=base: nc.vector.tensor_tensor(
            out=tA[:, base:base + CH], in0=tA[:, base:base + CH],
            in1=tB[:, base:base + CH], op=ALU.subtract), [b_tA, b_tB], [b_tA])
        vop(lambda base=base, pr1=pr1: nc.vector.tensor_scalar(
            out=tB[:, base:base + CH], in0=tWi[:, base:base + CH], scalar1=pr1,
            scalar2=None, op0=ALU.mult), [b_tWi, b_phi], [b_tB])
        vop(lambda base=base, pi1=pi1: nc.vector.scalar_tensor_tensor(
            out=tB[:, base:base + CH], in0=tWr[:, base:base + CH], scalar=pi1,
            in1=tB[:, base:base + CH], op0=ALU.mult, op1=ALU.add),
            [b_tWr, b_phi, b_tB], [b_tB])
        vop(lambda base=base: nc.vector.tensor_scalar_mul(
            out=tWi[:, base:base + CH], in0=tWi[:, base:base + CH], scalar1=-1.0),
            [b_tWi], [b_tWi])

    plan.barrier()
    p0.close()

    tpf_psL.append(mkps(p1, "tpf_ps1", [128, 512], F32))
    xin1, b_xin1 = [], []
    for i in range(2):
        t, b = mksb(p1, f"p1xin{i}", [128, TPC * D], F32)
        xin1.append(t)
        b_xin1.append(b)
    xh4, b_xh4 = mksb(p1, "p1xh4", [128, TPC * D], F32)
    XT, b_XT = mksb(p1, "p1XT", [128, NDT * CH], F32)
    wRt, _bwr = mksb(p1, "p1wR", [128, 2 * CH], F32)
    wIt, _bwi = mksb(p1, "p1wI", [128, 2 * CH], F32)
    b_wRs = [Buf("p1wR0"), Buf("p1wR1")]
    b_wIs = [Buf("p1wI0"), Buf("p1wI1")]
    dRb, _bdr = mksb(p1, "p1dR", [128, NDT * CH], F32)
    dIb, _bdi = mksb(p1, "p1dI", [128, NDT * CH], F32)
    b_dRs = [Buf(f"p1dR{i}") for i in range(NDT)]
    b_dIs = [Buf(f"p1dI{i}") for i in range(NDT)]
    xo1, b_xo1 = mksb(p1, "p1xo", [128, TPC * D], F32)
    ncar, b_ncar = mksb(p1, "p1ncar", [128, 2 * NDT + 1], F32)

    for ch in range(NCH):
        xi, b_xi = xin1[ch % 2], b_xin1[ch % 2]
        src = x0_d[ch * CH:(ch + 1) * CH, :].rearrange("(a p) d -> p a d", p=128)
        dma_in("sp", xi[:].rearrange("p (a d) -> p a d", d=D), src, b_xi, rbufs=[b_xi])
        for a in range(TPC):
            emit_ln(xi[:, a * D:(a + 1) * D], b_xi,
                    xh4[:, a * D:(a + 1) * D], b_xh4)
        for dt in range(NDT):
            srcs = [xh4[:, a * D + dt * 128: a * D + dt * 128 + 128]
                    for a in range(TPC)]
            tp_group_f32(srcs, [b_xh4], XT[:, dt * CH:(dt + 1) * CH], b_XT)
        for dt in range(NDT):
            base = dt * CH
            wRb = wRt[:, (dt % 2) * CH:(dt % 2 + 1) * CH]
            wIb = wIt[:, (dt % 2) * CH:(dt % 2 + 1) * CH]
            b_wR = b_wRs[dt % 2]
            b_wI = b_wIs[dt % 2]
            rcol = pv[:, COL_R + dt:COL_R + dt + 1]
            r_rep = rcol.broadcast_to([128, CH])
            vop(lambda base=base, wRb=wRb: nc.vector.tensor_tensor(
                out=wRb, in0=XT[:, base:base + CH], in1=tWr[:, base:base + CH],
                op=ALU.mult), [b_XT, b_tWr], [b_wR])
            vop(lambda base=base, wIb=wIb: nc.vector.tensor_tensor(
                out=wIb, in0=XT[:, base:base + CH], in1=tWi[:, base:base + CH],
                op=ALU.mult), [b_XT, b_tWi], [b_wI])
            vop(lambda base=base, r_rep=r_rep, dt=dt, wRb=wRb, b_wR=b_wR:
                nc.vector.tensor_tensor_scan(
                out=dRb[:, base:base + CH], data0=r_rep, data1=wRb,
                initial=carry[:, dt:dt + 1], op0=ALU.mult, op1=ALU.add),
                [b_wR, b_carry, b_pv], [b_dRs[dt]])
            vop(lambda base=base, r_rep=r_rep, dt=dt, wIb=wIb:
                nc.vector.tensor_tensor_scan(
                out=dIb[:, base:base + CH], data0=r_rep, data1=wIb,
                initial=carry[:, NDT + dt:NDT + dt + 1], op0=ALU.mult, op1=ALU.add),
                [b_wI, b_carry, b_pv], [b_dIs[dt]])
            dl_r = dRb[:, base + CH - 1:base + CH]
            dl_i = dIb[:, base + CH - 1:base + CH]
            ucr = tUc[:, dt:dt + 1]
            uci = tUc[:, NDT + dt:NDT + dt + 1]
            tc = ncar[:, 2 * NDT:2 * NDT + 1]
            vop(lambda dt=dt, dl_r=dl_r, ucr=ucr: nc.vector.tensor_tensor(
                out=ncar[:, dt:dt + 1], in0=dl_r, in1=ucr, op=ALU.mult),
                [b_dRs[dt], b_tUc], [b_ncar])
            vop(lambda tc=tc, dl_i=dl_i, uci=uci: nc.vector.tensor_tensor(
                out=tc, in0=dl_i, in1=uci, op=ALU.mult), [b_dIs[dt], b_tUc], [b_ncar])
            vop(lambda dt=dt, tc=tc: nc.vector.tensor_tensor(
                out=ncar[:, dt:dt + 1], in0=ncar[:, dt:dt + 1], in1=tc,
                op=ALU.subtract), [b_ncar], [b_ncar])
            vop(lambda dt=dt, dl_r=dl_r, uci=uci: nc.vector.tensor_tensor(
                out=ncar[:, NDT + dt:NDT + dt + 1], in0=dl_r, in1=uci,
                op=ALU.mult), [b_dRs[dt], b_tUc], [b_ncar])
            vop(lambda tc=tc, dl_i=dl_i, ucr=ucr: nc.vector.tensor_tensor(
                out=tc, in0=dl_i, in1=ucr, op=ALU.mult), [b_dIs[dt], b_tUc], [b_ncar])
            vop(lambda dt=dt, tc=tc: nc.vector.tensor_tensor(
                out=ncar[:, NDT + dt:NDT + dt + 1],
                in0=ncar[:, NDT + dt:NDT + dt + 1], in1=tc, op=ALU.add),
                [b_ncar], [b_ncar])
            vop(lambda base=base: nc.vector.tensor_tensor(
                out=dRb[:, base:base + CH], in0=dRb[:, base:base + CH],
                in1=tA[:, base:base + CH], op=ALU.mult), [b_dRs[dt], b_tA], [b_dRs[dt]])
            vop(lambda base=base, wIb=wIb: nc.vector.tensor_tensor(
                out=wIb, in0=dIb[:, base:base + CH], in1=tB[:, base:base + CH],
                op=ALU.mult), [b_dIs[dt], b_tB], [b_wI])
            vop(lambda base=base, wIb=wIb: nc.vector.tensor_tensor(
                out=dRb[:, base:base + CH], in0=dRb[:, base:base + CH], in1=wIb,
                op=ALU.subtract), [b_dRs[dt], b_wI], [b_dRs[dt]])
        vop(lambda: nc.vector.tensor_copy(out=carry[:], in_=ncar[:, 0:2 * NDT]),
            [b_ncar], [b_carry])
        for a in range(TPC):
            for g0 in range(0, NDT, 4):
                srcs = [dRb[:, (g0 + j) * CH + a * 128:(g0 + j) * CH + a * 128 + 128]
                        for j in range(4)]
                dsl = xo1[:, a * D + g0 * 128: a * D + (g0 + 4) * 128]
                xsl = xi[:, a * D + g0 * 128: a * D + (g0 + 4) * 128]
                tp_group_f32_add(srcs, [b_dRs[g0 + j] for j in range(4)],
                                 dsl, b_xo1, xsl, b_xi)
        dst = xs1[ch * CH:(ch + 1) * CH, :].rearrange("(a p) d -> p a d", p=128)
        plan.dma("pool", lambda d=dst: nc.gpsimd.dma_start(out=d, in_=xo1[:].rearrange("p (a d) -> p a d", d=D)),
                 reads=[b_xo1], writes=[Buf("xs1w")])

    plan.barrier()
    tpf_psL.pop()
    p1.close()

    def emit_ffn(esx, tag, src_d, dst_d):
        wg_sb, b_wg = mksb(esx, f"{tag}wg", [128, NDT * DI], BF16)
        wu_sb, b_wu = mksb(esx, f"{tag}wu", [128, NDT * DI], BF16)
        wd_sb, b_wd = mksb(esx, f"{tag}wd", [128, NIT * D], BF16)
        dma_in("sp", wg_sb[:], wgq_d[:], b_wg)
        dma_in("sp", wu_sb[:], wuq_d[:], b_wu)
        dma_in("sp", wd_sb[:], wdq_d[:], b_wd)
        fxi, b_fxi = mksb(esx, f"{tag}xin", [128, TPC * D], F32)
        xq4, b_xq4 = mksb(esx, f"{tag}xq4", [128, TPC * D], BF16)
        xqT, b_xqT = mksb(esx, f"{tag}xqT", [128, NDT * 128], BF16)
        hb, _b_hb = mksb(esx, f"{tag}hb", [128, DI], F32)
        b_hb2 = [Buf(f"{tag}hb0"), Buf(f"{tag}hb1")]
        hq, b_hq = mksb(esx, f"{tag}hq", [128, DI], BF16)
        hqT, b_hqT = mksb(esx, f"{tag}hqT", [128, NIT * 128], BF16)
        WQ = DI // 2
        z_, b_z_ = mksb(esx, f"{tag}qz", [128, WQ], F32)
        a2_, b_a2_ = mksb(esx, f"{tag}qa2", [128, WQ], BF16)
        zs_, b_zs_ = mksb(esx, f"{tag}qzs", [128, WQ], BF16)
        msk_, b_msk_ = mksb(esx, f"{tag}qmsk", [128, WQ], mybir.dt.uint8)
        red_, b_red_ = mksb(esx, f"{tag}qred", [128, WQ // 16], F32)
        sc_, b_sc_ = mksb(esx, f"{tag}qsc", [128, WQ // 16], F32)
        rcp_, b_rcp_ = mksb(esx, f"{tag}qrcp", [128, WQ // 16], F32)
        scr = {"z": (z_, b_z_), "a2": (a2_, b_a2_), "zs": (zs_, b_zs_),
               "red": (red_, b_red_), "sc": (sc_, b_sc_), "rcp": (rcp_, b_rcp_),
               "msk": (msk_, b_msk_)}

        def scr_next():
            return scr
        gpsL = [mkps(esx, f"{tag}gps0", [128, 512], F32),
                mkps(esx, f"{tag}gps1", [128, 512], F32)]
        upsL = [mkps(esx, f"{tag}ups0", [128, 512], F32),
                mkps(esx, f"{tag}ups1", [128, 512], F32)]
        dps = [mkps(esx, f"{tag}dps0", [128, 512], F32),
               mkps(esx, f"{tag}dps1", [128, 512], F32)]
        ISL = []
        s0 = 0
        while s0 < DI:
            ISL.append((s0, min(512, DI - s0)))
            s0 += 512
        for ch in range(NCH):
            src = src_d[ch * CH:(ch + 1) * CH, :].rearrange("(a p) d -> p a d", p=128)
            dma_in("sp", fxi[:].rearrange("p (a d) -> p a d", d=D), src, b_fxi,
                   rbufs=[b_fxi])
            for a in range(TPC):
                xh_ap = xh2t[:, (a % 2) * D:(a % 2 + 1) * D]
                b_xh_s = b_xh2[a % 2]
                emit_ln(fxi[:, a * D:(a + 1) * D], b_fxi, xh_ap, b_xh_s)
                emit_quant(xh_ap, b_xh_s, xq4[:, a * D:(a + 1) * D], b_xq4, D, scr_next())
            for a in range(TPC):
                for g0 in range(0, NDT, 4):
                    srcs = [xq4[:, a * D + (g0 + j) * 128: a * D + (g0 + j + 1) * 128]
                            for j in range(4)]
                    tp_group_bf(srcs, [b_xq4],
                                xqT[:, g0 * 128:(g0 + 4) * 128], b_xqT)
                for si, (s0, sw) in enumerate(ISL):
                    gps, b_gps = gpsL[si % 2]
                    ups, b_ups = upsL[si % 2]
                    for (ps, b_ps, wsb, b_wsb) in ((gps, b_gps, wg_sb, b_wg),
                                                   (ups, b_ups, wu_sb, b_wu)):
                        fns = [lambda dt=dt, ps=ps, wsb=wsb, s0=s0, sw=sw:
                               nc.tensor.matmul(
                                   out=ps[:, :sw],
                                   lhsT=xqT[:, dt * 128:(dt + 1) * 128],
                                   rhs=wsb[:, dt * DI + s0: dt * DI + s0 + sw],
                                   start=(dt == 0), stop=(dt == NDT - 1))
                               for dt in range(NDT)]
                        plan.group("pe", fns, reads=[b_xqT, b_wsb], writes=[b_ps],
                                   dur=2000)
                    hbs = hb[:, s0:s0 + sw]
                    halves = sorted({s0 // (DI // 2), (s0 + sw - 1) // (DI // 2)})
                    wr = [b_hb2[h] for h in halves]
                    plan.op("act", lambda sw=sw, gps=gps, hbs=hbs: nc.scalar.activation(
                        out=hbs, in_=gps[:, :sw], func=AF.Silu),
                        reads=[b_gps], writes=wr, dur=600)
                    plan.op("dve", lambda sw=sw, ups=ups, hbs=hbs:
                            nc.vector.tensor_tensor(
                        out=hbs, in0=hbs, in1=ups[:, :sw],
                        op=ALU.mult), reads=[b_ups] + wr, writes=wr, dur=700)
                    if si == 2:
                        emit_quant(hb[:, 0:DI // 2], b_hb2[0],
                                   hq[:, 0:DI // 2], b_hq, DI // 2, scr)
                    elif si == len(ISL) - 1:
                        emit_quant(hb[:, DI // 2:DI], b_hb2[1],
                                   hq[:, DI // 2:DI], b_hq, DI // 2, scr)
                for g0 in range(0, NIT, 4):
                    gn = min(4, NIT - g0)
                    srcs = [hq[:, (g0 + j) * 128:(g0 + j + 1) * 128]
                            for j in range(gn)]
                    tp_group_bf(srcs, [b_hq],
                                hqT[:, g0 * 128:(g0 + gn) * 128], b_hqT)
                for half in range(2):
                    ps, b_ps = dps[half]
                    fns = [lambda it=it, ps=ps, half=half: nc.tensor.matmul(
                        out=ps[:, :512],
                        lhsT=hqT[:, it * 128:(it + 1) * 128],
                        rhs=wd_sb[:, it * D + half * 512: it * D + half * 512 + 512],
                        start=(it == 0), stop=(it == NIT - 1))
                        for it in range(NIT)]
                    plan.group("pe", fns, reads=[b_hqT, b_wd], writes=[b_ps])
                    plan.op("dve", lambda a=a, half=half, ps=ps:
                            nc.vector.tensor_tensor(
                                out=fxi[:, a * D + half * 512: a * D + half * 512 + 512],
                                in0=ps[:, :512],
                                in1=fxi[:, a * D + half * 512: a * D + half * 512 + 512],
                                op=ALU.add), reads=[b_ps, b_fxi], writes=[b_fxi])
            dst = dst_d[ch * CH:(ch + 1) * CH, :].rearrange("(a p) d -> p a d", p=128)
            plan.dma("pool", lambda d=dst: nc.gpsimd.dma_start(out=d, in_=fxi[:].rearrange("p (a d) -> p a d", d=D)),
                     reads=[b_fxi], writes=[Buf(f"{tag}out")])

    p2 = ExitStack()
    emit_ffn(p2, "p2", xs1, xs2)
    plan.barrier()
    p2.close()

    p3 = ExitStack()
    wgT_sb, b_wgT = mksb(p3, "p3wgT", [128, NDT * D], BF16)
    wrT_sb, b_wrT = mksb(p3, "p3wrT", [128, NDT * D], BF16)
    woT_sb, b_woT = mksb(p3, "p3woT", [128, NDT * D], BF16)
    dma_in("sp", wgT_sb[:], wgT_d[:], b_wgT)
    dma_in("sp", wrT_sb[:], wrT_d[:], b_wrT)
    dma_in("sp", woT_sb[:], woT_d[:], b_woT)
    xi3, b_xi3 = mksb(p3, "p3xin", [128, TPC * D], F32)
    XT3, b_XT3 = mksb(p3, "p3XT", [128, NDT * CH], F32)
    G3, b_G3 = mksb(p3, "p3G", [128, NDT * CH], F32)
    V3t, b_V3t = mksb(p3, "p3V", [128, 2 * CH], F32)
    b_V3s = [Buf("V3s0"), Buf("V3s1")]
    HxA, _bA = mksb(p3, "p3HxA", [128, NDT * (CH + 1)], BF16)
    HxB, _bB = mksb(p3, "p3HxB", [128, NDT * (CH + 1)], BF16)
    Hf, _bH = mksb(p3, "p3Hf", [128, NDT * CH], F32)
    b_HxA = [Buf(f"HxA{i}") for i in range(NDT)]
    b_HxB = [Buf(f"HxB{i}") for i in range(NDT)]
    b_Hfb = [Buf(f"Hf{i}") for i in range(NDT)]
    hcar, b_hcar = mksb(p3, "p3hcar", [128, NDT], F32)
    Nrm4, b_Nrm4 = mksb(p3, "p3Nrm4", [128, TPC * D], BF16)
    YT3, b_YT3 = mksb(p3, "p3YT", [128, NDT * CH], BF16)
    OT3, b_OT3 = mksb(p3, "p3OT", [128, NDT * CH], F32)
    XTb = OT3[:].bitcast(BF16)[:, :NDT * CH]
    b_XTb = b_OT3
    o3, b_o3 = mksb(p3, "p3o", [128, TPC * D], F32)
    pps = [mkps(p3, "p3ps0", [128, CH], F32), mkps(p3, "p3ps1", [128, CH], F32)]
    tp_psL.append(mkps(p3, "tp_ps3b", [128, 512], BF16))
    tpf_psL.append(mkps(p3, "tpf_ps3b", [128, 512], F32))
    b_lsqep = _lsqslots[0]

    plan.op("dve", lambda: nc.vector.memset(hcar[:], 0.0), writes=[b_hcar])

    for ch in range(NCH):
        src = xs2[ch * CH:(ch + 1) * CH, :].rearrange("(a p) d -> p a d", p=128)
        dma_in("sp", xi3[:].rearrange("p (a d) -> p a d", d=D), src, b_xi3, rbufs=[b_xi3])
        XT3v = XT3[:].rearrange("p (t q) -> p t q", q=CH)
        XTbv = XTb.rearrange("p (t q) -> p t q", q=CH)
        for a in range(TPC):
            xh_ap = xh2t[:, (a % 2) * D:(a % 2 + 1) * D]
            b_xh_s = b_xh2[a % 2]
            emit_ln(xi3[:, a * D:(a + 1) * D], b_xi3, xh_ap, b_xh_s)
            for g0 in range(0, NDT, 4):
                srcs = [xh_ap[:, (g0 + j) * 128:(g0 + j + 1) * 128]
                        for j in range(4)]
                tpf_, b_tpf_ = tpf_psL[_tpctr[0] % len(tpf_psL)]
                _tpctr[0] += 1
                fns = [lambda s=s, i=i, tpf_=tpf_: nc.tensor.transpose(
                    out=tpf_[:, i * 128:(i + 1) * 128], in_=s, identity=idf[:])
                    for i, s in enumerate(srcs)]
                plan.group("pe", fns, reads=[b_xh_s, b_idf], writes=[b_tpf_])
                b_tpf = b_tpf_
                ps3 = tpf_[:, :512].rearrange("p (t q) -> p t q", q=128)
                plan.op("act", lambda a=a, g0=g0, ps3=ps3: nc.scalar.activation(
                    out=XT3v[:, g0:g0 + 4, a * 128:a * 128 + 128], in_=ps3,
                    func=AF.Copy), reads=[b_tpf], writes=[b_XT3])
                plan.op("act", lambda a=a, g0=g0, ps3=ps3: nc.scalar.activation(
                    out=XTbv[:, g0:g0 + 4, a * 128:a * 128 + 128], in_=ps3,
                    func=AF.Copy), reads=[b_tpf], writes=[b_XTb])
        for et in range(NDT):
            ps, b_ps = pps[et % 2]
            fns = [lambda dt=dt, et=et, ps=ps: nc.tensor.matmul(
                out=ps[:, :CH],
                lhsT=wgT_sb[:, dt * D + et * 128: dt * D + et * 128 + 128],
                rhs=XTb[:, dt * CH:(dt + 1) * CH],
                start=(dt == 0), stop=(dt == NDT - 1)) for dt in range(NDT)]
            plan.group("pe", fns, reads=[b_XTb, b_wgT], writes=[b_ps])
            plan.op("act", lambda et=et, ps=ps: nc.scalar.activation(
                out=G3[:, et * CH:(et + 1) * CH], in_=ps[:, :CH], func=AF.Sigmoid,
                bias=bgv[:, et:et + 1]), reads=[b_ps, b_bgv], writes=[b_G3])
            vop(lambda et=et: nc.vector.tensor_scalar(
                out=G3[:, et * CH:(et + 1) * CH], in0=G3[:, et * CH:(et + 1) * CH],
                scalar1=pv[:, COL_OM + et:COL_OM + et + 1], scalar2=None,
                op0=ALU.mult), [b_G3, b_pv], [b_G3])
        for dt in range(NDT):
            vop(lambda dt=dt: nc.vector.tensor_scalar(
                out=XT3[:, dt * CH:(dt + 1) * CH], in0=XT3[:, dt * CH:(dt + 1) * CH],
                scalar1=pv[:, COL_BR + dt:COL_BR + dt + 1], scalar2=None,
                op0=ALU.subtract), [b_XT3, b_pv], [b_XT3])
        plan.op("dve", lambda: nc.vector.memset(HxA[:], 0.0), reads=[],
                writes=list(b_HxA), dur=900)
        for dt in range(NDT):
            vop(lambda dt=dt: nc.vector.tensor_copy(
                out=HxA[:, dt * (CH + 1):dt * (CH + 1) + 1], in_=hcar[:, dt:dt + 1]),
                [b_hcar], [b_HxA[dt]])
        cur, b_cur, nxt, b_nxt = HxA, b_HxA, HxB, b_HxB
        for k in range(K):
            last = (k == K - 1)
            for n in range(NDT):
                ps, b_ps = pps[n % 2]
                fns = [lambda dt=dt, n=n, ps=ps, cur=cur: nc.tensor.matmul(
                    out=ps[:, :CH],
                    lhsT=wrT_sb[:, dt * D + n * 128: dt * D + n * 128 + 128],
                    rhs=cur[:, dt * (CH + 1): dt * (CH + 1) + CH],
                    start=(dt == 0), stop=(dt == NDT - 1)) for dt in range(NDT)]
                plan.group("pe", fns, reads=list(b_cur) + [b_wrT], writes=[b_ps],
                           dur=1900)
                V3 = V3t[:, (n % 2) * CH:(n % 2 + 1) * CH]
                b_V3 = b_V3s[n % 2]
                plan.op("dve", lambda n=n, ps=ps, V3=V3: nc.vector.scalar_tensor_tensor(
                    out=V3, in0=ps[:, :CH], scalar=-1.0,
                    in1=XT3[:, n * CH:(n + 1) * CH], op0=ALU.mult, op1=ALU.add),
                    reads=[b_ps, b_XT3], writes=[b_V3], dur=700)
                plan.op("dve", lambda n=n, V3=V3: nc.vector.tensor_tensor(
                    out=V3, in0=V3, in1=G3[:, n * CH:(n + 1) * CH],
                    op=ALU.mult), reads=[b_V3, b_G3], writes=[b_V3], dur=700)
                a_rep = pv[:, COL_A + n:COL_A + n + 1].broadcast_to([128, CH])
                if last:
                    plan.op("dve", lambda n=n, a_rep=a_rep, V3=V3: nc.vector.tensor_tensor_scan(
                        out=Hf[:, n * CH:(n + 1) * CH], data0=a_rep, data1=V3,
                        initial=hcar[:, n:n + 1], op0=ALU.mult, op1=ALU.add),
                        reads=[b_V3, b_hcar, b_pv], writes=[b_Hfb[n]], dur=1400)
                else:
                    plan.op("dve", lambda n=n, a_rep=a_rep, nxt=nxt, V3=V3: nc.vector.tensor_tensor_scan(
                        out=nxt[:, n * (CH + 1) + 1: n * (CH + 1) + 1 + CH],
                        data0=a_rep, data1=V3,
                        initial=hcar[:, n:n + 1], op0=ALU.mult, op1=ALU.add),
                        reads=[b_V3, b_hcar, b_pv], writes=[b_nxt[n]], dur=1400)
            if not last:
                if k == 0:
                    for dt in range(NDT):
                        vop(lambda dt=dt, nxt=nxt: nc.vector.tensor_copy(
                            out=nxt[:, dt * (CH + 1):dt * (CH + 1) + 1],
                            in_=hcar[:, dt:dt + 1]), [b_hcar], [b_nxt[dt]])
                cur, b_cur, nxt, b_nxt = nxt, b_nxt, cur, b_cur
        for dt in range(NDT):
            vop(lambda dt=dt: nc.vector.tensor_copy(
                out=hcar[:, dt:dt + 1], in_=Hf[:, dt * CH + CH - 1: dt * CH + CH]),
                [b_Hfb[dt]], [b_hcar])
        for a in range(TPC):
            for g0 in range(0, NDT, 4):
                srcs = [Hf[:, (g0 + j) * CH + a * 128:(g0 + j) * CH + a * 128 + 128]
                        for j in range(4)]
                tp_group_f32(srcs, [b_Hfb[g0 + j] for j in range(4)],
                             lsq[:, g0 * 128:(g0 + 4) * 128], b_lsqep)
            emit_ln_noscratch(lsq[:, 0:D], b_lsqep, Nrm4[:, a * D:(a + 1) * D], b_Nrm4)
        for dt in range(NDT):
            srcs = [Nrm4[:, a * D + dt * 128: a * D + dt * 128 + 128]
                    for a in range(TPC)]
            tp_group_bf(srcs, [b_Nrm4], YT3[:, dt * CH:(dt + 1) * CH], b_YT3)
        for dt in range(NDT):
            vop(lambda dt=dt: nc.vector.tensor_scalar(
                out=YT3[:, dt * CH:(dt + 1) * CH], in0=YT3[:, dt * CH:(dt + 1) * CH],
                scalar1=pv[:, COL_LNW + dt:COL_LNW + dt + 1],
                scalar2=pv[:, COL_LNB + dt:COL_LNB + dt + 1],
                op0=ALU.mult, op1=ALU.add), [b_YT3, b_pv], [b_YT3])
        for et in range(NDT):
            ps, b_ps = pps[et % 2]
            fns = [lambda dt=dt, et=et, ps=ps: nc.tensor.matmul(
                out=ps[:, :CH],
                lhsT=woT_sb[:, dt * D + et * 128: dt * D + et * 128 + 128],
                rhs=YT3[:, dt * CH:(dt + 1) * CH],
                start=(dt == 0), stop=(dt == NDT - 1)) for dt in range(NDT)]
            plan.group("pe", fns, reads=[b_YT3, b_woT], writes=[b_ps])
            vop(lambda et=et, ps=ps: nc.vector.tensor_scalar(
                out=OT3[:, et * CH:(et + 1) * CH], in0=ps[:, :CH],
                scalar1=pv[:, COL_BO + et:COL_BO + et + 1], scalar2=None,
                op0=ALU.add), [b_ps, b_pv], [b_OT3])
        for a in range(TPC):
            for g0 in range(0, NDT, 4):
                srcs = [OT3[:, (g0 + j) * CH + a * 128:(g0 + j) * CH + a * 128 + 128]
                        for j in range(4)]
                dsl = o3[:, a * D + g0 * 128: a * D + (g0 + 4) * 128]
                xsl = xi3[:, a * D + g0 * 128: a * D + (g0 + 4) * 128]
                tp_group_f32_add(srcs, [b_OT3], dsl, b_o3, xsl, b_xi3)
        dst = xs3[ch * CH:(ch + 1) * CH, :].rearrange("(a p) d -> p a d", p=128)
        plan.dma("pool", lambda d=dst: nc.gpsimd.dma_start(out=d, in_=o3[:].rearrange("p (a d) -> p a d", d=D)),
                 reads=[b_o3], writes=[Buf("xs3w")])

    plan.barrier()
    tp_psL.pop()
    tpf_psL.pop()
    p3.close()

    p4 = ExitStack()
    emit_ffn(p4, "p4", xs3, out_d)
    plan.barrier()
    p4.close()

    plan.emit()

    with nc.Block() as block:
        @block.sync
        def _(eng):
            plan._run_engine("sp")

        @block.scalar
        def _(eng):
            plan._run_engine("act")

        @block.vector
        def _(eng):
            plan._run_engine("dve")

        @block.tensor
        def _(eng):
            plan._run_engine("pe")

        @block.gpsimd
        def _(eng):
            plan._run_engine("pool")
            plan.finish_waits("pool")

    outer.close()
    plan._es.close()
    return nc


def kernel(**inputs):
    import ml_dtypes
    from concourse.bass_utils import run_bass_kernel_spmd

    x = np.ascontiguousarray(np.asarray(inputs["L0_attn_a_x"], np.float32))
    B, L, _ = x.shape
    nc = build_nc(L=L)
    iden_bf = np.eye(128, dtype=ml_dtypes.bfloat16)
    iden_f32 = np.eye(128, dtype=np.float32)
    wnames = [k for k in inputs.keys() if k != "L0_attn_a_x"]
    in_maps = []
    for c in range(B):
        m = {nm: np.ascontiguousarray(np.asarray(inputs[nm], np.float32))
             for nm in wnames}
        m["x0"] = np.ascontiguousarray(x[c])
        m["iden_bf"] = iden_bf
        m["iden_f32"] = iden_f32
        in_maps.append(m)
    res = run_bass_kernel_spmd(nc, in_maps, core_ids=list(range(B)))
    return np.stack([np.asarray(r["out"], np.float32) for r in res.results], axis=0)


# version 21
# speedup vs baseline: 1.1618x; 1.0020x over previous
import numpy as np
from contextlib import ExitStack

import concourse.bass as bass
import concourse.mybir as mybir

F32 = mybir.dt.float32
BF16 = mybir.dt.bfloat16
AF = mybir.ActivationFunctionType
ALU = mybir.AluOpType
AXX = mybir.AxisListType.X

D = 1024
DI = 2816
NDT = D // 128
NIT = DI // 128
C2V = float(2 ** 22 + 1)
CBV = float(1.5 * 2 ** 23)

ENGS = ("sp", "act", "dve", "pe", "pool")
_EMAP = {"sp": "sync", "act": "scalar", "dve": "vector", "pe": "tensor",
         "pool": "gpsimd"}


class Buf:
    __slots__ = ("name", "w", "r")

    def __init__(self, name):
        self.name = name
        self.w = None
        self.r = []


class Node:
    __slots__ = ("idx", "eng", "fns", "deps", "dur", "is_dma", "sem", "val")

    def __init__(self, idx, eng, fns, deps, dur, is_dma=False):
        self.idx = idx
        self.eng = eng
        self.fns = fns
        self.deps = deps
        self.dur = dur
        self.is_dma = is_dma
        self.sem = None
        self.val = None


class Plan:

    def __init__(self, nc):
        self.nc = nc
        self.nodes = []
        self.sem = {}
        self._es = ExitStack()
        for e in ENGS:
            self.sem[e] = self._es.enter_context(nc.semaphore(f"sem_{e}"))
        self.NDMA = 24
        self.dpool = {"hw": list(range(16)), "sw": list(range(16, 24))}
        for i in range(self.NDMA):
            self.sem[f"dp{i}"] = self._es.enter_context(
                nc.semaphore(f"sem_dp{i}"))
        self._barrier_idx = -1

    def _mknode(self, eng, fns, reads, writes, dur, is_dma=False):
        deps = set()
        for b in reads:
            if b.w is not None:
                deps.add(b.w)
        for b in writes:
            if b.w is not None:
                deps.add(b.w)
            deps.update(b.r)
        if self._barrier_idx >= 0:
            deps.add(self._barrier_idx)
        idx = len(self.nodes)
        n = Node(idx, eng, fns, deps, dur, is_dma)
        self.nodes.append(n)
        for b in writes:
            b.w = idx
            b.r = []
        for b in reads:
            b.r.append(idx)
        return idx

    def op(self, eng, fn, reads=(), writes=(), dur=600):
        return self._mknode(eng, [fn], reads, writes, dur)

    def group(self, eng, fns, reads=(), writes=(), dur=None):
        if dur is None:
            dur = 250 * len(fns)
        return self._mknode(eng, list(fns), reads, writes, dur)

    def dma(self, queue, fn, reads=(), writes=(), dur=4000):
        return self._mknode(queue, [fn], reads, writes, dur, is_dma=True)

    def barrier(self):
        idx = self._mknode("sp", None, [], [], 0)
        n = self.nodes[idx]
        n.deps = set(range(idx))
        self._barrier_idx = idx
        return idx

    def emit(self):
        import heapq
        nodes = self.nodes
        N = len(nodes)
        nsucc = [[] for _ in range(N)]
        ndep = [0] * N
        for n in nodes:
            ndep[n.idx] = len(n.deps)
            for d in n.deps:
                nsucc[d].append(n.idx)
        blevel = [0.0] * N
        for i in range(N - 1, -1, -1):
            n = nodes[i]
            succ_max = 0.0
            for s_ in nsucc[i]:
                if blevel[s_] > succ_max:
                    succ_max = blevel[s_]
            blevel[i] = succ_max + n.dur + (1500 if n.is_dma else 0)
        eng_clock = {e: 0.0 for e in ENGS}
        finish = [0.0] * N
        ready_t = [0.0] * N
        heap = []
        for n in nodes:
            if ndep[n.idx] == 0:
                heapq.heappush(heap, (0.0, n.idx))
        order = {e: [] for e in ENGS}
        done = 0
        WINDOW = 8192
        scheduled = [False] * N
        min_unsched = 0
        poolq = {e: [] for e in ENGS}
        deferred = []
        while done < N:
            gate = min_unsched + WINDOW
            while heap:
                rt, i = heapq.heappop(heap)
                if i > gate:
                    heapq.heappush(deferred, i)
                else:
                    heapq.heappush(poolq[nodes[i].eng], (-blevel[i], i))
            while deferred and deferred[0] <= gate:
                i = heapq.heappop(deferred)
                heapq.heappush(poolq[nodes[i].eng], (-blevel[i], i))
            cands = [(eng_clock[e], e) for e in ENGS if poolq[e]]
            if not cands:
                raise RuntimeError("scheduler stuck")
            _, e = min(cands)
            (_, i) = heapq.heappop(poolq[e])
            scheduled[i] = True
            while min_unsched < N and scheduled[min_unsched]:
                min_unsched += 1
            n = nodes[i]
            start = max(eng_clock[e], ready_t[i])
            dma_lat = 1500 if n.is_dma else 0
            finish[i] = start + n.dur + dma_lat
            eng_clock[e] = start + (150 if n.is_dma else n.dur)
            order[e].append(i)
            done += 1
            for s in nsucc[i]:
                ready_t[s] = max(ready_t[s], finish[i])
                ndep[s] -= 1
                if ndep[s] == 0:
                    heapq.heappush(heap, (ready_t[s], s))
        eng_pos = {e: 0 for e in ENGS}
        dnext = {"hw": 0, "sw": 0}
        dcnt = [0] * self.NDMA
        dma_prev = {}
        for e in ENGS:
            for i in order[e]:
                n = nodes[i]
                if n.is_dma:
                    pool = "sw" if e == "pool" else "hw"
                    lst = self.dpool[pool]
                    di = lst[dnext[pool]]
                    dnext[pool] = (dnext[pool] + 1) % len(lst)
                    dcnt[di] += 16
                    n.sem = f"dp{di}"
                    n.val = dcnt[di]
                elif n.fns is not None:
                    eng_pos[e] += 1
                    n.sem = e
                    n.val = eng_pos[e]
        waited = {e: {} for e in ENGS}

        def emit_wait(e, semkey, val):
            if val <= waited[e].get(semkey, 0):
                return
            waited[e][semkey] = val
            g = getattr(self.nc, _EMAP[e])
            g.wait_ge(self.sem[semkey], val)

        self.final_dma = [(f"dp{i}", dcnt[i]) for i in range(self.NDMA) if dcnt[i]]
        barrier_cache = {}

        def barrier_maxes(bidx):
            if bidx not in barrier_cache:
                mx = {}
                for d2 in nodes[bidx].deps:
                    dn2 = nodes[d2]
                    if dn2.sem is not None:
                        if dn2.val > mx.get(dn2.sem, 0):
                            mx[dn2.sem] = dn2.val
                    elif dn2.fns is None and d2 in barrier_cache:
                        for k, v in barrier_cache[d2].items():
                            if v > mx.get(k, 0):
                                mx[k] = v
                barrier_cache[bidx] = mx
            return barrier_cache[bidx]

        def run_engine(e):
            prev_dma_on_sem = {}
            for i in order[e]:
                n = nodes[i]
                if n.is_dma:
                    prev = prev_dma_on_sem.get(n.sem)
                    if prev is not None:
                        emit_wait(e, n.sem, prev)
                    elif n.val > 16:
                        emit_wait(e, n.sem, n.val - 16)
                    prev_dma_on_sem[n.sem] = n.val
                for d in sorted(n.deps):
                    dn = nodes[d]
                    if dn.sem is None:
                        if dn.fns is None:
                            for k, v in sorted(barrier_maxes(d).items()):
                                emit_wait(e, k, v)
                        continue
                    emit_wait(e, dn.sem, dn.val)
                if not n.fns:
                    continue
                h = self.sem[n.sem]
                inc = 16 if n.is_dma else 1
                for j, fn in enumerate(n.fns):
                    if j == len(n.fns) - 1:
                        fn().then_inc(h, inc)
                    else:
                        fn()
        self._run_engine = run_engine

    def finish_waits(self, e):
        g = getattr(self.nc, _EMAP[e])
        for key, val in self.final_dma:
            g.wait_ge(self.sem[key], val)


def build_nc(L=8192, K=10, CH=512):
    nc = bass.Bass("TRN2", target_bir_lowering=False, debug=False)
    NCH = L // CH
    TPC = CH // 128
    assert TPC == 4

    x0_d = nc.dram_tensor("x0", [L, D], F32, kind="ExternalInput").ap()
    win = {}
    for nm, shp in [
        ("L0_attn_a_phazor", [D, 2]), ("L0_attn_a_phazor_init", [D, 2]),
        ("L0_attn_a_last_conv_re", [D]), ("L0_attn_a_last_conv_im", [D]),
        ("L0_attn_b_decay", [D]), ("L0_attn_b_Wr", [D, D]),
        ("L0_attn_b_br", [D]), ("L0_attn_b_Wg", [D, D]), ("L0_attn_b_bg", [D]),
        ("L0_attn_b_Wo", [D, D]), ("L0_attn_b_bo", [D]),
        ("L0_attn_b_ln_w", [D]), ("L0_attn_b_ln_b", [D]),
        ("L0_ffn_w_gate", [DI, D]), ("L0_ffn_w_up", [DI, D]),
        ("L0_ffn_w_down", [D, DI]),
    ]:
        win[nm] = nc.dram_tensor(nm, shp, F32, kind="ExternalInput").ap()
    iden_bf_d = nc.dram_tensor("iden_bf", [128, 128], BF16, kind="ExternalInput").ap()
    iden_f32_d = nc.dram_tensor("iden_f32", [128, 128], F32, kind="ExternalInput").ap()
    out_d = nc.dram_tensor("out", [L, D], F32, kind="ExternalOutput").ap()

    xs1 = nc.dram_tensor("xs1", [L, D], F32, kind="Internal").ap()
    xs2 = nc.dram_tensor("xs2", [L, D], F32, kind="Internal").ap()
    xs3 = nc.dram_tensor("xs3", [L, D], F32, kind="Internal").ap()
    wgq_d = nc.dram_tensor("wgq_d", [128, NDT * DI], BF16, kind="Internal").ap()
    wuq_d = nc.dram_tensor("wuq_d", [128, NDT * DI], BF16, kind="Internal").ap()
    wdq_d = nc.dram_tensor("wdq_d", [128, NIT * D], BF16, kind="Internal").ap()
    wgT_d = nc.dram_tensor("wgT_d", [128, NDT * D], BF16, kind="Internal").ap()
    wrT_d = nc.dram_tensor("wrT_d", [128, NDT * D], BF16, kind="Internal").ap()
    woT_d = nc.dram_tensor("woT_d", [128, NDT * D], BF16, kind="Internal").ap()

    plan = Plan(nc)
    outer = ExitStack()

    def mksb(esx, name, shape, dt):
        t = esx.enter_context(nc.sbuf_tensor(name, shape, dt))
        return t, Buf(name)

    def mkps(esx, name, shape, dt):
        t = esx.enter_context(nc.psum_tensor(name, shape, dt))
        return t, Buf(name)

    idbf, b_idbf = mksb(outer, "idbf", [128, 128], BF16)
    idf, b_idf = mksb(outer, "idf", [128, 128], F32)
    pv, b_pv = mksb(outer, "pv", [128, 64], F32)
    (COL_A, COL_OM, COL_BR, COL_BO, COL_LNW, COL_LNB, COL_R) = (
        0, 8, 16, 24, 32, 40, 48)
    bgv, b_bgv = mksb(outer, "bgv", [128, NDT], F32)
    carry, b_carry = mksb(outer, "carry", [128, 2 * NDT], F32)
    xh2t, b_xh2t = mksb(outer, "xh", [128, 2 * D], F32)
    b_xh2 = [Buf("xh_s0"), Buf("xh_s1")]
    lnt, b_lnt = mksb(outer, "lnt", [128, 32], F32)
    lsq, b_lsq = mksb(outer, "lsq", [128, D], F32)
    tUc, b_tUc = mksb(outer, "tUc", [128, 2 * NDT], F32)
    q0t, b_q0t = mksb(outer, "q0t", [128, 2 * NDT], F32)
    tp_psL = [mkps(outer, "tp_ps0", [128, 512], BF16)]
    tpf_psL = [mkps(outer, "tpf_ps0", [128, 512], F32)]
    _tpctr = [0]
    tp_ps, b_tp = tp_psL[0]
    tpf_ps, b_tpf = tpf_psL[0]

    def dma_in(queue, dst_ap, src_ap, wbuf, rbufs=()):
        plan.dma(queue,
                 lambda d=dst_ap, s=src_ap: getattr(nc, _EMAP[queue]).dma_start(out=d, in_=s),
                 reads=list(rbufs), writes=[wbuf])

    dma_in("sp", idbf[:], iden_bf_d[:], b_idbf)
    dma_in("sp", idf[:], iden_f32_d[:], b_idf)

    def tp_group_bf(srcs, b_srcs, dst_ap, b_dst):
        ps, b_ps = tp_psL[_tpctr[0] % len(tp_psL)]
        _tpctr[0] += 1
        n = len(srcs)
        fns = [lambda s=s, i=i, ps=ps: nc.tensor.transpose(
            out=ps[:, i * 128:(i + 1) * 128], in_=s, identity=idbf[:])
            for i, s in enumerate(srcs)]
        plan.group("pe", fns, reads=list(b_srcs) + [b_idbf], writes=[b_ps],
                   dur=300 * n)
        plan.op("act", lambda n=n, ps=ps: nc.scalar.activation(
            out=dst_ap, in_=ps[:, :n * 128], func=AF.Copy),
            reads=[b_ps], writes=[b_dst], dur=600)

    def tp_group_f32(srcs, b_srcs, dst_ap, b_dst):
        ps, b_ps = tpf_psL[_tpctr[0] % len(tpf_psL)]
        _tpctr[0] += 1
        n = len(srcs)
        fns = [lambda s=s, i=i, ps=ps: nc.tensor.transpose(
            out=ps[:, i * 128:(i + 1) * 128], in_=s, identity=idf[:])
            for i, s in enumerate(srcs)]
        plan.group("pe", fns, reads=list(b_srcs) + [b_idf], writes=[b_ps],
                   dur=300 * n)
        plan.op("act", lambda n=n, ps=ps: nc.scalar.activation(
            out=dst_ap, in_=ps[:, :n * 128], func=AF.Copy),
            reads=[b_ps], writes=[b_dst], dur=600)

    def tp_group_f32_add(srcs, b_srcs, dst_ap, b_dst, add_ap, b_add):
        ps, b_ps = tpf_psL[_tpctr[0] % len(tpf_psL)]
        _tpctr[0] += 1
        n = len(srcs)
        fns = [lambda s=s, i=i, ps=ps: nc.tensor.transpose(
            out=ps[:, i * 128:(i + 1) * 128], in_=s, identity=idf[:])
            for i, s in enumerate(srcs)]
        plan.group("pe", fns, reads=list(b_srcs) + [b_idf], writes=[b_ps],
                   dur=300 * n)
        plan.op("dve", lambda n=n, ps=ps: nc.vector.tensor_tensor(
            out=dst_ap, in0=ps[:, :n * 128], in1=add_ap, op=ALU.add),
            reads=[b_ps, b_add], writes=[b_dst], dur=750)

    def load_vec_to_cols(vec1d_view, dst_ap, b_dst, width):
        dma_in("sp", vstage[0:width, 0:128], vec1d_view, b_vstage, rbufs=[b_vstage])
        plan.op("pe", lambda: nc.tensor.transpose(
            out=tpf_ps[:, 0:width], in_=vstage[0:width, 0:128],
            identity=idf[0:width, 0:width]),
            reads=[b_vstage, b_idf], writes=[b_tpf])
        plan.op("act", lambda: nc.scalar.activation(
            out=dst_ap, in_=tpf_ps[:, 0:width], func=AF.Copy),
            reads=[b_tpf], writes=[b_dst])

    _lnslots = [Buf(f"lnt{i}") for i in range(4)]
    _lsqslots = [Buf("lsq0")]
    _lnctr = [0]

    def emit_ln(src_ap, b_src, dst_ap, b_dst, sq_over=None):
        s = _lnctr[0] % 4
        _lnctr[0] += 1
        b_ln = _lnslots[s]
        if sq_over is None:
            sq, b_sq = lsq[:, 0:D], _lsqslots[0]
        else:
            sq, b_sq = sq_over
        ln = lnt[:, s * 8:(s + 1) * 8]
        plan.op("dve", lambda: nc.vector.reduce_sum(
            out=ln[:, 0:1], in_=src_ap, axis=AXX), reads=[b_src], writes=[b_ln],
            dur=1100)
        plan.op("act", lambda: nc.scalar.activation(
            out=sq, in_=src_ap, func=AF.Square), reads=[b_src], writes=[b_sq],
            dur=1000)
        plan.op("dve", lambda: nc.vector.reduce_sum(
            out=ln[:, 1:2], in_=sq, axis=AXX), reads=[b_sq], writes=[b_ln],
            dur=1100)
        plan.op("dve", lambda: nc.vector.tensor_scalar_mul(
            out=ln[:, 0:2], in0=ln[:, 0:2], scalar1=1.0 / D),
            reads=[b_ln], writes=[b_ln], dur=80)
        plan.op("dve", lambda: nc.vector.tensor_tensor(
            out=ln[:, 2:3], in0=ln[:, 0:1], in1=ln[:, 0:1], op=ALU.mult),
            reads=[b_ln], writes=[b_ln], dur=80)
        plan.op("dve", lambda: nc.vector.tensor_tensor(
            out=ln[:, 1:2], in0=ln[:, 1:2], in1=ln[:, 2:3], op=ALU.subtract),
            reads=[b_ln], writes=[b_ln], dur=80)
        plan.op("dve", lambda: nc.vector.tensor_single_scalar(
            out=ln[:, 1:2], in_=ln[:, 1:2], scalar=1e-5, op=ALU.add),
            reads=[b_ln], writes=[b_ln], dur=80)
        plan.op("act", lambda: nc.scalar.activation(
            out=ln[:, 3:4], in_=ln[:, 1:2], func=AF.Sqrt),
            reads=[b_ln], writes=[b_ln], dur=250)
        plan.op("dve", lambda: nc.vector.reciprocal(
            out=ln[:, 3:4], in_=ln[:, 3:4]), reads=[b_ln], writes=[b_ln], dur=80)
        plan.op("dve", lambda: nc.vector.tensor_scalar(
            out=dst_ap, in0=src_ap, scalar1=ln[:, 0:1], scalar2=ln[:, 3:4],
            op0=ALU.subtract, op1=ALU.mult), reads=[b_src, b_ln], writes=[b_dst],
            dur=700)

    def emit_ln_noscratch(src_ap, b_src, dst_ap, b_dst):
        emit_ln(src_ap, b_src, dst_ap, b_dst,
                sq_over=(xh2t[:, 0:D], b_xh2[0]))

    def emit_quant(src_ap, b_src, dst_ap, b_dst, W, scr):
        TT = int((58 + W) / 0.96)
        TS = int((58 + W / 2) / 0.96)
        """fp4 quantize-dequantize [128, W] f32 -> [128, W] bf16.
        src is DESTROYED (used as Veltkamp scratch).  scr: z/a2 f32 >= W,
        zs bf16 >= W, red/sc/rcp f32 >= W/16."""
        nb = W // 16
        z, b_z = scr["z"]
        a2, b_a2 = scr["a2"]
        red, b_red = scr["red"]
        sc, b_sc = scr["sc"]
        rcp, b_rcp = scr["rcp"]
        src3 = src_ap.rearrange("p (a b) -> p a b", b=16)
        plan.op("dve", lambda: nc.vector.tensor_reduce(
            out=red[:, :nb], in_=src3, axis=AXX, op=ALU.max,
            apply_absolute_value=True), reads=[b_src], writes=[b_red], dur=TT)
        plan.op("dve", lambda: nc.vector.tensor_scalar(
            out=sc[:, :nb], in0=red[:, :nb],
            scalar1=float(np.float32(1.0) / np.float32(12.0)), scalar2=5e-13,
            op0=ALU.mult, op1=ALU.max), reads=[b_red], writes=[b_sc], dur=120)
        plan.op("dve", lambda: nc.vector.reciprocal(
            out=rcp[:, :nb], in_=sc[:, :nb]), reads=[b_sc], writes=[b_rcp], dur=160)
        rcp3 = rcp[:, :nb].unsqueeze(2).broadcast_to([128, nb, 16])
        sc3 = sc[:, :nb].unsqueeze(2).broadcast_to([128, nb, 16])
        z3 = z[:, :W].rearrange("p (a b) -> p a b", b=16)
        plan.op("dve", lambda: nc.vector.tensor_tensor(
            out=z3, in0=src3, in1=rcp3, op=ALU.mult),
            reads=[b_src, b_rcp], writes=[b_z], dur=TT)
        msk, b_msk = scr["msk"]
        zs, b_zs = scr["zs"]
        plan.op("act", lambda: nc.scalar.activation(
            out=zs[:, :W], in_=z[:, :W], func=AF.Square),
            reads=[b_z], writes=[b_zs], dur=int(224 / 1.2 + W / 1.2))
        plan.op("dve", lambda: nc.vector.tensor_single_scalar(
            out=msk[:, :W], in_=zs[:, :W], scalar=4.0, op=ALU.is_lt),
            reads=[b_zs], writes=[b_msk], dur=TS)
        plan.op("dve", lambda: nc.vector.tensor_scalar(
            out=a2[:, :W], in0=z[:, :W], scalar1=CBV, scalar2=CBV,
            op0=ALU.add, op1=ALU.subtract), reads=[b_z, b_msk], writes=[b_a2])
        plan.op("dve", lambda: nc.vector.tensor_scalar_mul(
            out=src_ap, in0=z[:, :W], scalar1=C2V), reads=[b_z], writes=[b_src], dur=TS)
        plan.op("dve", lambda: nc.vector.tensor_tensor(
            out=z[:, :W], in0=src_ap, in1=z[:, :W], op=ALU.subtract),
            reads=[b_src, b_z], writes=[b_z], dur=700)
        plan.op("dve", lambda: nc.vector.tensor_tensor(
            out=src_ap, in0=src_ap, in1=z[:, :W], op=ALU.subtract),
            reads=[b_src, b_z], writes=[b_src], dur=700)
        plan.op("dve", lambda: nc.vector.copy_predicated(
            out=src_ap, mask=msk[:, :W], data=a2[:, :W]),
            reads=[b_msk, b_a2, b_src], writes=[b_src], dur=TT)
        dst3 = dst_ap.rearrange("p (a b) -> p a b", b=16)
        plan.op("dve", lambda: nc.vector.tensor_tensor(
            out=dst3, in0=src3, in1=sc3, op=ALU.mult),
            reads=[b_src, b_sc], writes=[b_dst], dur=TT)

    p1 = ExitStack()
    tWr, b_tWr = mksb(p1, "tWr", [128, NDT * CH], F32)
    tWi, b_tWi = mksb(p1, "tWi", [128, NDT * CH], F32)
    tA, b_tA = mksb(p1, "tA", [128, NDT * CH], F32)
    tB, b_tB = mksb(p1, "tB", [128, NDT * CH], F32)
    p0 = ExitStack()
    wrow, b_wrow = mksb(p0, "wrow", [128, DI], F32)
    vstage, b_vstage = mksb(p0, "vstage", [8, 256], F32)
    wqr, b_wqr = mksb(p0, "wqr", [128, DI], BF16)
    qz0, b_qz0 = mksb(p0, "qz0", [128, DI], F32)
    qa20, b_qa20 = mksb(p0, "qa20", [128, DI], BF16)
    qred0, b_qred0 = mksb(p0, "qred0", [128, DI // 16], F32)
    qsc0, b_qsc0 = mksb(p0, "qsc0", [128, DI // 16], F32)
    qrcp0, b_qrcp0 = mksb(p0, "qrcp0", [128, DI // 16], F32)
    qmsk0, b_qmsk0 = mksb(p0, "qmsk0", [128, DI], mybir.dt.uint8)
    qzs0, b_qzs0 = mksb(p0, "qzs0", [128, DI], BF16)
    scr0 = {"z": (qz0, b_qz0), "a2": (qa20, b_qa20), "zs": (qzs0, b_qzs0),
            "red": (qred0, b_qred0), "sc": (qsc0, b_qsc0), "rcp": (qrcp0, b_qrcp0),
            "msk": (qmsk0, b_qmsk0)}
    wasm, b_wasm = mksb(p0, "wasm", [128, NDT * DI], BF16)

    def prep_gateup(w_src, dst_dram):
        for it in range(NIT):
            dma_in("sp", wrow[:, :D], w_src[it * 128:(it + 1) * 128, :],
                   b_wrow, rbufs=[b_wrow])
            emit_quant(wrow[:, :D], b_wrow, wqr[:, :D], b_wqr, D, scr0)
            for g0 in range(0, NDT, 4):
                srcs = [wqr[:, (g0 + j) * 128:(g0 + j + 1) * 128] for j in range(4)]
                dst = wasm[:, g0 * DI + it * 128:].rearrange(
                    "p (x) -> p x", x=1) if False else None
                dstap = wasm[:].rearrange("p (t q) -> p t q", q=DI)[
                    :, g0:g0 + 4, it * 128:it * 128 + 128]
                n = 4
                tp_ps_, b_tp_ = tp_psL[_tpctr[0] % len(tp_psL)]
                _tpctr[0] += 1
                fns = [lambda s=s, i=i, tp_ps_=tp_ps_: nc.tensor.transpose(
                    out=tp_ps_[:, i * 128:(i + 1) * 128], in_=s, identity=idbf[:])
                    for i, s in enumerate(srcs)]
                plan.group("pe", fns, reads=[b_wqr, b_idbf], writes=[b_tp_])
                src_ps = tp_ps_[:, :512].rearrange("p (t q) -> p t q", q=128)
                plan.op("act", lambda dstap=dstap, src_ps=src_ps:
                        nc.scalar.activation(out=dstap, in_=src_ps, func=AF.Copy),
                        reads=[b_tp_], writes=[b_wasm])
        plan.dma("sp", lambda d=dst_dram: nc.sync.dma_start(out=d[:], in_=wasm[:]),
                 reads=[b_wasm], writes=[Buf("wdram")])

    prep_gateup(win["L0_ffn_w_gate"], wgq_d)
    prep_gateup(win["L0_ffn_w_up"], wuq_d)

    for dt in range(NDT):
        dma_in("sp", wrow[:, :DI], win["L0_ffn_w_down"][dt * 128:(dt + 1) * 128, :],
               b_wrow, rbufs=[b_wrow])
        emit_quant(wrow[:, :DI], b_wrow, wqr[:, :DI], b_wqr, DI, scr0)
        for g0 in range(0, NIT, 4):
            gn = min(4, NIT - g0)
            srcs = [wqr[:, (g0 + j) * 128:(g0 + j + 1) * 128] for j in range(gn)]
            dstap = wasm[:, :NIT * D].rearrange("p (t q) -> p t q", q=D)[
                :, g0:g0 + gn, dt * 128:dt * 128 + 128]
            fns = [lambda s=s, i=i: nc.tensor.transpose(
                out=tp_ps[:, i * 128:(i + 1) * 128], in_=s, identity=idbf[:])
                for i, s in enumerate(srcs)]
            plan.group("pe", fns, reads=[b_wqr, b_idbf], writes=[b_tp])
            src_ps = tp_ps[:, :gn * 128].rearrange("p (t q) -> p t q", q=128)
            plan.op("act", lambda dstap=dstap, src_ps=src_ps:
                    nc.scalar.activation(out=dstap, in_=src_ps, func=AF.Copy),
                    reads=[b_tp], writes=[b_wasm])
    plan.dma("sp", lambda: nc.sync.dma_start(out=wdq_d[:], in_=wasm[:, :NIT * D]),
             reads=[b_wasm], writes=[Buf("wdram")])

    for w_src, dstd in [(win["L0_attn_b_Wg"], wgT_d), (win["L0_attn_b_Wr"], wrT_d),
                        (win["L0_attn_b_Wo"], woT_d)]:
        for et in range(NDT):
            dma_in("sp", wrow[:, :D], w_src[et * 128:(et + 1) * 128, :],
                   b_wrow, rbufs=[b_wrow])
            plan.op("act", lambda: nc.scalar.activation(
                out=wqr[:, :D], in_=wrow[:, :D], func=AF.Copy),
                reads=[b_wrow], writes=[b_wqr])
            for g0 in range(0, NDT, 4):
                srcs = [wqr[:, (g0 + j) * 128:(g0 + j + 1) * 128] for j in range(4)]
                dstap = wasm[:, :NDT * D].rearrange("p (t q) -> p t q", q=D)[
                    :, g0:g0 + 4, et * 128:et * 128 + 128]
                tp_ps_, b_tp_ = tp_psL[_tpctr[0] % len(tp_psL)]
                _tpctr[0] += 1
                fns = [lambda s=s, i=i, tp_ps_=tp_ps_: nc.tensor.transpose(
                    out=tp_ps_[:, i * 128:(i + 1) * 128], in_=s, identity=idbf[:])
                    for i, s in enumerate(srcs)]
                plan.group("pe", fns, reads=[b_wqr, b_idbf], writes=[b_tp_])
                b_tp = b_tp_
                src_ps = tp_ps_[:, :512].rearrange("p (t q) -> p t q", q=128)
                plan.op("act", lambda dstap=dstap, src_ps=src_ps:
                        nc.scalar.activation(out=dstap, in_=src_ps, func=AF.Copy),
                        reads=[b_tp], writes=[b_wasm])
        plan.dma("sp", lambda d=dstd: nc.sync.dma_start(
            out=d[:], in_=wasm[:, :NDT * D]), reads=[b_wasm], writes=[Buf("wdram")])

    load_vec_to_cols(win["L0_attn_b_decay"].rearrange("(t p) -> t p", p=128),
                     pv[:, COL_A:COL_A + NDT], b_pv, NDT)
    plan.op("act", lambda: nc.scalar.activation(
        out=pv[:, COL_A:COL_A + NDT], in_=pv[:, COL_A:COL_A + NDT],
        func=AF.Sigmoid), reads=[b_pv], writes=[b_pv])
    plan.op("dve", lambda: nc.vector.tensor_scalar(
        out=pv[:, COL_OM:COL_OM + NDT], in0=pv[:, COL_A:COL_A + NDT],
        scalar1=-1.0, scalar2=1.0, op0=ALU.mult, op1=ALU.add),
        reads=[b_pv], writes=[b_pv])
    for nm, col in [("L0_attn_b_br", COL_BR), ("L0_attn_b_bo", COL_BO),
                    ("L0_attn_b_ln_w", COL_LNW), ("L0_attn_b_ln_b", COL_LNB)]:
        load_vec_to_cols(win[nm].rearrange("(t p) -> t p", p=128),
                         pv[:, col:col + NDT], b_pv, NDT)
    load_vec_to_cols(win["L0_attn_b_bg"].rearrange("(t p) -> t p", p=128),
                     bgv[:], b_bgv, NDT)

    phz, b_phz = mksb(p0, "phz", [128, 2 * NDT], F32)
    phi_, b_phi = mksb(p0, "phi_", [128, 2 * NDT], F32)
    lcv, b_lcv = mksb(p0, "lcv", [128, 2 * NDT], F32)
    tmpv, b_tmpv = mksb(p0, "tmpv", [128, 4 * NDT], F32)
    den, b_den = mksb(p0, "den", [128, 2 * NDT], F32)
    for src_d, dst_t, b_dst in [(win["L0_attn_a_phazor"], phz, b_phz),
                                (win["L0_attn_a_phazor_init"], phi_, b_phi)]:
        dma_in("sp", vstage[:, :256],
               src_d.rearrange("(t p) c -> t (p c)", p=128), b_vstage,
               rbufs=[b_vstage])
        for c in range(2):
            sv = vstage[:, :256].rearrange("t (p c) -> t p c", c=2)[:, :, c]
            plan.op("pe", lambda sv=sv, c=c: nc.tensor.transpose(
                out=tpf_ps[:, c * 128:c * 128 + NDT], in_=sv,
                identity=idf[0:NDT, 0:NDT]),
                reads=[b_vstage, b_idf], writes=[b_tpf])
        dsts = dst_t[:].rearrange("p (t c) -> p t c", c=2)
        plan.op("act", lambda dsts=dsts: nc.scalar.activation(
            out=dsts[:, :, 0], in_=tpf_ps[:, 0:NDT], func=AF.Copy),
            reads=[b_tpf], writes=[b_dst])
        plan.op("act", lambda dsts=dsts: nc.scalar.activation(
            out=dsts[:, :, 1], in_=tpf_ps[:, 128:128 + NDT], func=AF.Copy),
            reads=[b_tpf], writes=[b_dst])
    load_vec_to_cols(win["L0_attn_a_last_conv_re"].rearrange("(t p) -> t p", p=128),
                     lcv[:, 0:NDT], b_lcv, NDT)
    load_vec_to_cols(win["L0_attn_a_last_conv_im"].rearrange("(t p) -> t p", p=128),
                     lcv[:, NDT:2 * NDT], b_lcv, NDT)

    prs = phz[:].rearrange("p (t c) -> p t c", c=2)[:, :, 0]
    pis = phz[:].rearrange("p (t c) -> p t c", c=2)[:, :, 1]
    phr = phi_[:].rearrange("p (t c) -> p t c", c=2)[:, :, 0]
    phii = phi_[:].rearrange("p (t c) -> p t c", c=2)[:, :, 1]
    lre = lcv[:, 0:NDT]
    lim = lcv[:, NDT:2 * NDT]

    def vop(fn, reads, writes):
        plan.op("dve", fn, reads=reads, writes=writes)

    vop(lambda: nc.vector.tensor_tensor(out=tmpv[:, 0:NDT], in0=prs, in1=prs,
        op=ALU.mult), [b_phz], [b_tmpv])
    vop(lambda: nc.vector.tensor_tensor(out=tmpv[:, NDT:2 * NDT], in0=pis, in1=pis,
        op=ALU.mult), [b_phz], [b_tmpv])
    vop(lambda: nc.vector.tensor_tensor(out=tmpv[:, 0:NDT], in0=tmpv[:, 0:NDT],
        in1=tmpv[:, NDT:2 * NDT], op=ALU.add), [b_tmpv], [b_tmpv])
    plan.op("act", lambda: nc.scalar.activation(
        out=tmpv[:, NDT:2 * NDT], in_=tmpv[:, 0:NDT], func=AF.Sqrt),
        reads=[b_tmpv], writes=[b_tmpv])
    plan.op("act", lambda: nc.scalar.activation(
        out=pv[:, COL_R:COL_R + NDT], in_=tmpv[:, NDT:2 * NDT], func=AF.Exp,
        scale=-1.0), reads=[b_tmpv], writes=[b_pv])
    vop(lambda: nc.vector.reciprocal(out=tmpv[:, 0:NDT], in_=tmpv[:, NDT:2 * NDT]),
        [b_tmpv], [b_tmpv])
    vop(lambda: nc.vector.tensor_tensor(out=tmpv[:, 2 * NDT:3 * NDT], in0=prs,
        in1=tmpv[:, 0:NDT], op=ALU.mult), [b_phz, b_tmpv], [b_tmpv])
    vop(lambda: nc.vector.tensor_tensor(out=tmpv[:, 3 * NDT:4 * NDT], in0=pis,
        in1=tmpv[:, 0:NDT], op=ALU.mult), [b_phz, b_tmpv], [b_tmpv])

    vop(lambda: nc.vector.tensor_tensor(out=den[:, 0:NDT], in0=phr, in1=phr,
        op=ALU.mult), [b_phi], [b_den])
    vop(lambda: nc.vector.tensor_tensor(out=den[:, NDT:], in0=phii, in1=phii,
        op=ALU.mult), [b_phi], [b_den])
    vop(lambda: nc.vector.tensor_tensor(out=den[:, 0:NDT], in0=den[:, 0:NDT],
        in1=den[:, NDT:], op=ALU.add), [b_den], [b_den])
    vop(lambda: nc.vector.tensor_single_scalar(out=den[:, 0:NDT], in_=den[:, 0:NDT],
        scalar=1e-30, op=ALU.max), [b_den], [b_den])
    vop(lambda: nc.vector.reciprocal(out=den[:, 0:NDT], in_=den[:, 0:NDT]),
        [b_den], [b_den])
    vop(lambda: nc.vector.tensor_tensor(out=q0t[:, 0:NDT], in0=lre, in1=phr,
        op=ALU.mult), [b_lcv, b_phi], [b_q0t])
    vop(lambda: nc.vector.tensor_tensor(out=den[:, NDT:], in0=lim, in1=phii,
        op=ALU.mult), [b_lcv, b_phi], [b_den])
    vop(lambda: nc.vector.tensor_tensor(out=q0t[:, 0:NDT], in0=q0t[:, 0:NDT],
        in1=den[:, NDT:], op=ALU.add), [b_q0t, b_den], [b_q0t])
    vop(lambda: nc.vector.tensor_tensor(out=q0t[:, 0:NDT], in0=q0t[:, 0:NDT],
        in1=den[:, 0:NDT], op=ALU.mult), [b_q0t, b_den], [b_q0t])
    vop(lambda: nc.vector.tensor_tensor(out=q0t[:, NDT:], in0=lim, in1=phr,
        op=ALU.mult), [b_lcv, b_phi], [b_q0t])
    vop(lambda: nc.vector.tensor_tensor(out=den[:, NDT:], in0=lre, in1=phii,
        op=ALU.mult), [b_lcv, b_phi], [b_den])
    vop(lambda: nc.vector.tensor_tensor(out=q0t[:, NDT:], in0=q0t[:, NDT:],
        in1=den[:, NDT:], op=ALU.subtract), [b_q0t, b_den], [b_q0t])
    vop(lambda: nc.vector.tensor_tensor(out=q0t[:, NDT:], in0=q0t[:, NDT:],
        in1=den[:, 0:NDT], op=ALU.mult), [b_q0t, b_den], [b_q0t])
    vop(lambda: nc.vector.tensor_copy(out=carry[:], in_=q0t[:]), [b_q0t], [b_carry])

    for dt in range(NDT):
        base = dt * CH
        ur1 = tmpv[:, 2 * NDT + dt:2 * NDT + dt + 1]
        ui1 = tmpv[:, 3 * NDT + dt:3 * NDT + dt + 1]
        vop(lambda ur1=ur1, base=base: nc.vector.tensor_copy(
            out=tWr[:, base:base + 1], in_=ur1), [b_tmpv], [b_tWr])
        vop(lambda ui1=ui1, base=base: nc.vector.tensor_copy(
            out=tWi[:, base:base + 1], in_=ui1), [b_tmpv], [b_tWi])
        n = 1
        while n < CH:
            m = min(n, CH - n)
            anr = tWr[:, base + n - 1:base + n]
            ani = tWi[:, base + n - 1:base + n]
            s_r = tWr[:, base:base + m]
            s_i = tWi[:, base:base + m]
            d_r = tWr[:, base + n:base + n + m]
            d_i = tWi[:, base + n:base + n + m]
            t1 = tA[:, base:base + m]
            vop(lambda d=d_r, s=s_r, a=anr: nc.vector.tensor_scalar(
                out=d, in0=s, scalar1=a, scalar2=None, op0=ALU.mult),
                [b_tWr], [b_tWr])
            vop(lambda t=t1, s=s_i, a=ani: nc.vector.tensor_scalar(
                out=t, in0=s, scalar1=a, scalar2=None, op0=ALU.mult),
                [b_tWi], [b_tA])
            vop(lambda d=d_r, t=t1: nc.vector.tensor_tensor(
                out=d, in0=d, in1=t, op=ALU.subtract), [b_tWr, b_tA], [b_tWr])
            vop(lambda d=d_i, s=s_r, a=ani: nc.vector.tensor_scalar(
                out=d, in0=s, scalar1=a, scalar2=None, op0=ALU.mult),
                [b_tWr, b_tWi], [b_tWi])
            vop(lambda t=t1, s=s_i, a=anr: nc.vector.tensor_scalar(
                out=t, in0=s, scalar1=a, scalar2=None, op0=ALU.mult),
                [b_tWi, b_tWr], [b_tA])
            vop(lambda d=d_i, t=t1: nc.vector.tensor_tensor(
                out=d, in0=d, in1=t, op=ALU.add), [b_tWi, b_tA], [b_tWi])
            n += m
        vop(lambda dt=dt, base=base: nc.vector.tensor_copy(
            out=tUc[:, dt:dt + 1], in_=tWr[:, base + CH - 1:base + CH]),
            [b_tWr], [b_tUc])
        vop(lambda dt=dt, base=base: nc.vector.tensor_copy(
            out=tUc[:, NDT + dt:NDT + dt + 1], in_=tWi[:, base + CH - 1:base + CH]),
            [b_tWi], [b_tUc])
    for dt in range(NDT):
        base = dt * CH
        pr1 = phr[:, dt:dt + 1]
        pi1 = phii[:, dt:dt + 1]
        vop(lambda base=base, pr1=pr1: nc.vector.tensor_scalar(
            out=tA[:, base:base + CH], in0=tWr[:, base:base + CH], scalar1=pr1,
            scalar2=None, op0=ALU.mult), [b_tWr, b_phi], [b_tA])
        vop(lambda base=base, pi1=pi1: nc.vector.tensor_scalar(
            out=tB[:, base:base + CH], in0=tWi[:, base:base + CH], scalar1=pi1,
            scalar2=None, op0=ALU.mult), [b_tWi, b_phi], [b_tB])
        vop(lambda base=base: nc.vector.tensor_tensor(
            out=tA[:, base:base + CH], in0=tA[:, base:base + CH],
            in1=tB[:, base:base + CH], op=ALU.subtract), [b_tA, b_tB], [b_tA])
        vop(lambda base=base, pr1=pr1: nc.vector.tensor_scalar(
            out=tB[:, base:base + CH], in0=tWi[:, base:base + CH], scalar1=pr1,
            scalar2=None, op0=ALU.mult), [b_tWi, b_phi], [b_tB])
        vop(lambda base=base, pi1=pi1: nc.vector.scalar_tensor_tensor(
            out=tB[:, base:base + CH], in0=tWr[:, base:base + CH], scalar=pi1,
            in1=tB[:, base:base + CH], op0=ALU.mult, op1=ALU.add),
            [b_tWr, b_phi, b_tB], [b_tB])
        vop(lambda base=base: nc.vector.tensor_scalar_mul(
            out=tWi[:, base:base + CH], in0=tWi[:, base:base + CH], scalar1=-1.0),
            [b_tWi], [b_tWi])

    plan.barrier()
    p0.close()

    tpf_psL.append(mkps(p1, "tpf_ps1", [128, 512], F32))
    xin1, b_xin1 = [], []
    for i in range(2):
        t, b = mksb(p1, f"p1xin{i}", [128, TPC * D], F32)
        xin1.append(t)
        b_xin1.append(b)
    xh4, b_xh4 = mksb(p1, "p1xh4", [128, TPC * D], F32)
    XT, b_XT = mksb(p1, "p1XT", [128, NDT * CH], F32)
    wRt, _bwr = mksb(p1, "p1wR", [128, 2 * CH], F32)
    wIt, _bwi = mksb(p1, "p1wI", [128, 2 * CH], F32)
    b_wRs = [Buf("p1wR0"), Buf("p1wR1")]
    b_wIs = [Buf("p1wI0"), Buf("p1wI1")]
    dRb, _bdr = mksb(p1, "p1dR", [128, NDT * CH], F32)
    dIb, _bdi = mksb(p1, "p1dI", [128, NDT * CH], F32)
    b_dRs = [Buf(f"p1dR{i}") for i in range(NDT)]
    b_dIs = [Buf(f"p1dI{i}") for i in range(NDT)]
    xo1, b_xo1 = mksb(p1, "p1xo", [128, TPC * D], F32)
    ncar, b_ncar = mksb(p1, "p1ncar", [128, 2 * NDT + 1], F32)

    for ch in range(NCH):
        xi, b_xi = xin1[ch % 2], b_xin1[ch % 2]
        src = x0_d[ch * CH:(ch + 1) * CH, :].rearrange("(a p) d -> p a d", p=128)
        dma_in("sp", xi[:].rearrange("p (a d) -> p a d", d=D), src, b_xi, rbufs=[b_xi])
        for a in range(TPC):
            emit_ln(xi[:, a * D:(a + 1) * D], b_xi,
                    xh4[:, a * D:(a + 1) * D], b_xh4)
        for dt in range(NDT):
            srcs = [xh4[:, a * D + dt * 128: a * D + dt * 128 + 128]
                    for a in range(TPC)]
            tp_group_f32(srcs, [b_xh4], XT[:, dt * CH:(dt + 1) * CH], b_XT)
        for dt in range(NDT):
            base = dt * CH
            wRb = wRt[:, (dt % 2) * CH:(dt % 2 + 1) * CH]
            wIb = wIt[:, (dt % 2) * CH:(dt % 2 + 1) * CH]
            b_wR = b_wRs[dt % 2]
            b_wI = b_wIs[dt % 2]
            rcol = pv[:, COL_R + dt:COL_R + dt + 1]
            r_rep = rcol.broadcast_to([128, CH])
            vop(lambda base=base, wRb=wRb: nc.vector.tensor_tensor(
                out=wRb, in0=XT[:, base:base + CH], in1=tWr[:, base:base + CH],
                op=ALU.mult), [b_XT, b_tWr], [b_wR])
            vop(lambda base=base, wIb=wIb: nc.vector.tensor_tensor(
                out=wIb, in0=XT[:, base:base + CH], in1=tWi[:, base:base + CH],
                op=ALU.mult), [b_XT, b_tWi], [b_wI])
            plan.op("dve", lambda base=base, r_rep=r_rep, dt=dt, wRb=wRb:
                nc.vector.tensor_tensor_scan(
                out=dRb[:, base:base + CH], data0=r_rep, data1=wRb,
                initial=carry[:, dt:dt + 1], op0=ALU.mult, op1=ALU.add),
                reads=[b_wR, b_carry, b_pv], writes=[b_dRs[dt]], dur=1300)
            plan.op("dve", lambda base=base, r_rep=r_rep, dt=dt, wIb=wIb:
                nc.vector.tensor_tensor_scan(
                out=dIb[:, base:base + CH], data0=r_rep, data1=wIb,
                initial=carry[:, NDT + dt:NDT + dt + 1], op0=ALU.mult, op1=ALU.add),
                reads=[b_wI, b_carry, b_pv], writes=[b_dIs[dt]], dur=1300)
            dl_r = dRb[:, base + CH - 1:base + CH]
            dl_i = dIb[:, base + CH - 1:base + CH]
            ucr = tUc[:, dt:dt + 1]
            uci = tUc[:, NDT + dt:NDT + dt + 1]
            tc = ncar[:, 2 * NDT:2 * NDT + 1]
            vop(lambda dt=dt, dl_r=dl_r, ucr=ucr: nc.vector.tensor_tensor(
                out=ncar[:, dt:dt + 1], in0=dl_r, in1=ucr, op=ALU.mult),
                [b_dRs[dt], b_tUc], [b_ncar])
            vop(lambda tc=tc, dl_i=dl_i, uci=uci: nc.vector.tensor_tensor(
                out=tc, in0=dl_i, in1=uci, op=ALU.mult), [b_dIs[dt], b_tUc], [b_ncar])
            vop(lambda dt=dt, tc=tc: nc.vector.tensor_tensor(
                out=ncar[:, dt:dt + 1], in0=ncar[:, dt:dt + 1], in1=tc,
                op=ALU.subtract), [b_ncar], [b_ncar])
            vop(lambda dt=dt, dl_r=dl_r, uci=uci: nc.vector.tensor_tensor(
                out=ncar[:, NDT + dt:NDT + dt + 1], in0=dl_r, in1=uci,
                op=ALU.mult), [b_dRs[dt], b_tUc], [b_ncar])
            vop(lambda tc=tc, dl_i=dl_i, ucr=ucr: nc.vector.tensor_tensor(
                out=tc, in0=dl_i, in1=ucr, op=ALU.mult), [b_dIs[dt], b_tUc], [b_ncar])
            vop(lambda dt=dt, tc=tc: nc.vector.tensor_tensor(
                out=ncar[:, NDT + dt:NDT + dt + 1],
                in0=ncar[:, NDT + dt:NDT + dt + 1], in1=tc, op=ALU.add),
                [b_ncar], [b_ncar])
            vop(lambda base=base: nc.vector.tensor_tensor(
                out=dRb[:, base:base + CH], in0=dRb[:, base:base + CH],
                in1=tA[:, base:base + CH], op=ALU.mult), [b_dRs[dt], b_tA], [b_dRs[dt]])
            vop(lambda base=base, wIb=wIb: nc.vector.tensor_tensor(
                out=wIb, in0=dIb[:, base:base + CH], in1=tB[:, base:base + CH],
                op=ALU.mult), [b_dIs[dt], b_tB], [b_wI])
            vop(lambda base=base, wIb=wIb: nc.vector.tensor_tensor(
                out=dRb[:, base:base + CH], in0=dRb[:, base:base + CH], in1=wIb,
                op=ALU.subtract), [b_dRs[dt], b_wI], [b_dRs[dt]])
        vop(lambda: nc.vector.tensor_copy(out=carry[:], in_=ncar[:, 0:2 * NDT]),
            [b_ncar], [b_carry])
        for a in range(TPC):
            for g0 in range(0, NDT, 4):
                srcs = [dRb[:, (g0 + j) * CH + a * 128:(g0 + j) * CH + a * 128 + 128]
                        for j in range(4)]
                dsl = xo1[:, a * D + g0 * 128: a * D + (g0 + 4) * 128]
                xsl = xi[:, a * D + g0 * 128: a * D + (g0 + 4) * 128]
                tp_group_f32_add(srcs, [b_dRs[g0 + j] for j in range(4)],
                                 dsl, b_xo1, xsl, b_xi)
        dst = xs1[ch * CH:(ch + 1) * CH, :].rearrange("(a p) d -> p a d", p=128)
        plan.dma("pool", lambda d=dst: nc.gpsimd.dma_start(out=d, in_=xo1[:].rearrange("p (a d) -> p a d", d=D)),
                 reads=[b_xo1], writes=[Buf("xs1w")])

    plan.barrier()
    tpf_psL.pop()
    p1.close()

    def emit_ffn(esx, tag, src_d, dst_d):
        wg_sb, b_wg = mksb(esx, f"{tag}wg", [128, NDT * DI], BF16)
        wu_sb, b_wu = mksb(esx, f"{tag}wu", [128, NDT * DI], BF16)
        wd_sb, b_wd = mksb(esx, f"{tag}wd", [128, NIT * D], BF16)
        dma_in("sp", wg_sb[:], wgq_d[:], b_wg)
        dma_in("sp", wu_sb[:], wuq_d[:], b_wu)
        dma_in("sp", wd_sb[:], wdq_d[:], b_wd)
        fxi, b_fxi = mksb(esx, f"{tag}xin", [128, TPC * D], F32)
        xq4, b_xq4 = mksb(esx, f"{tag}xq4", [128, TPC * D], BF16)
        xqT, b_xqT = mksb(esx, f"{tag}xqT", [128, NDT * 128], BF16)
        hb, _b_hb = mksb(esx, f"{tag}hb", [128, DI], F32)
        b_hb2 = [Buf(f"{tag}hb0"), Buf(f"{tag}hb1")]
        hq, b_hq = mksb(esx, f"{tag}hq", [128, DI], BF16)
        hqT, b_hqT = mksb(esx, f"{tag}hqT", [128, NIT * 128], BF16)
        WQ = DI // 2
        z_, b_z_ = mksb(esx, f"{tag}qz", [128, WQ], F32)
        a2_, b_a2_ = mksb(esx, f"{tag}qa2", [128, WQ], BF16)
        zs_, b_zs_ = mksb(esx, f"{tag}qzs", [128, WQ], BF16)
        msk_, b_msk_ = mksb(esx, f"{tag}qmsk", [128, WQ], mybir.dt.uint8)
        red_, b_red_ = mksb(esx, f"{tag}qred", [128, WQ // 16], F32)
        sc_, b_sc_ = mksb(esx, f"{tag}qsc", [128, WQ // 16], F32)
        rcp_, b_rcp_ = mksb(esx, f"{tag}qrcp", [128, WQ // 16], F32)
        scr = {"z": (z_, b_z_), "a2": (a2_, b_a2_), "zs": (zs_, b_zs_),
               "red": (red_, b_red_), "sc": (sc_, b_sc_), "rcp": (rcp_, b_rcp_),
               "msk": (msk_, b_msk_)}

        def scr_next():
            return scr
        gpsL = [mkps(esx, f"{tag}gps0", [128, 512], F32),
                mkps(esx, f"{tag}gps1", [128, 512], F32)]
        upsL = [mkps(esx, f"{tag}ups0", [128, 512], F32),
                mkps(esx, f"{tag}ups1", [128, 512], F32)]
        dps = [mkps(esx, f"{tag}dps0", [128, 512], F32),
               mkps(esx, f"{tag}dps1", [128, 512], F32)]
        ISL = []
        s0 = 0
        while s0 < DI:
            ISL.append((s0, min(512, DI - s0)))
            s0 += 512
        for ch in range(NCH):
            src = src_d[ch * CH:(ch + 1) * CH, :].rearrange("(a p) d -> p a d", p=128)
            dma_in("sp", fxi[:].rearrange("p (a d) -> p a d", d=D), src, b_fxi,
                   rbufs=[b_fxi])
            for a in range(TPC):
                xh_ap = xh2t[:, (a % 2) * D:(a % 2 + 1) * D]
                b_xh_s = b_xh2[a % 2]
                emit_ln(fxi[:, a * D:(a + 1) * D], b_fxi, xh_ap, b_xh_s)
                emit_quant(xh_ap, b_xh_s, xq4[:, a * D:(a + 1) * D], b_xq4, D, scr_next())
            for a in range(TPC):
                for g0 in range(0, NDT, 4):
                    srcs = [xq4[:, a * D + (g0 + j) * 128: a * D + (g0 + j + 1) * 128]
                            for j in range(4)]
                    tp_group_bf(srcs, [b_xq4],
                                xqT[:, g0 * 128:(g0 + 4) * 128], b_xqT)
                for si, (s0, sw) in enumerate(ISL):
                    gps, b_gps = gpsL[si % 2]
                    ups, b_ups = upsL[si % 2]
                    for (ps, b_ps, wsb, b_wsb) in ((gps, b_gps, wg_sb, b_wg),
                                                   (ups, b_ups, wu_sb, b_wu)):
                        fns = [lambda dt=dt, ps=ps, wsb=wsb, s0=s0, sw=sw:
                               nc.tensor.matmul(
                                   out=ps[:, :sw],
                                   lhsT=xqT[:, dt * 128:(dt + 1) * 128],
                                   rhs=wsb[:, dt * DI + s0: dt * DI + s0 + sw],
                                   start=(dt == 0), stop=(dt == NDT - 1))
                               for dt in range(NDT)]
                        plan.group("pe", fns, reads=[b_xqT, b_wsb], writes=[b_ps],
                                   dur=2000)
                    hbs = hb[:, s0:s0 + sw]
                    halves = sorted({s0 // (DI // 2), (s0 + sw - 1) // (DI // 2)})
                    wr = [b_hb2[h] for h in halves]
                    plan.op("act", lambda sw=sw, gps=gps, hbs=hbs: nc.scalar.activation(
                        out=hbs, in_=gps[:, :sw], func=AF.Silu),
                        reads=[b_gps], writes=wr, dur=600)
                    plan.op("dve", lambda sw=sw, ups=ups, hbs=hbs:
                            nc.vector.tensor_tensor(
                        out=hbs, in0=hbs, in1=ups[:, :sw],
                        op=ALU.mult), reads=[b_ups] + wr, writes=wr, dur=700)
                    if si == 2:
                        emit_quant(hb[:, 0:DI // 2], b_hb2[0],
                                   hq[:, 0:DI // 2], b_hq, DI // 2, scr)
                    elif si == len(ISL) - 1:
                        emit_quant(hb[:, DI // 2:DI], b_hb2[1],
                                   hq[:, DI // 2:DI], b_hq, DI // 2, scr)
                for g0 in range(0, NIT, 4):
                    gn = min(4, NIT - g0)
                    srcs = [hq[:, (g0 + j) * 128:(g0 + j + 1) * 128]
                            for j in range(gn)]
                    tp_group_bf(srcs, [b_hq],
                                hqT[:, g0 * 128:(g0 + gn) * 128], b_hqT)
                for half in range(2):
                    ps, b_ps = dps[half]
                    fns = [lambda it=it, ps=ps, half=half: nc.tensor.matmul(
                        out=ps[:, :512],
                        lhsT=hqT[:, it * 128:(it + 1) * 128],
                        rhs=wd_sb[:, it * D + half * 512: it * D + half * 512 + 512],
                        start=(it == 0), stop=(it == NIT - 1))
                        for it in range(NIT)]
                    plan.group("pe", fns, reads=[b_hqT, b_wd], writes=[b_ps])
                    plan.op("dve", lambda a=a, half=half, ps=ps:
                            nc.vector.tensor_tensor(
                                out=fxi[:, a * D + half * 512: a * D + half * 512 + 512],
                                in0=ps[:, :512],
                                in1=fxi[:, a * D + half * 512: a * D + half * 512 + 512],
                                op=ALU.add), reads=[b_ps, b_fxi], writes=[b_fxi])
            dst = dst_d[ch * CH:(ch + 1) * CH, :].rearrange("(a p) d -> p a d", p=128)
            plan.dma("pool", lambda d=dst: nc.gpsimd.dma_start(out=d, in_=fxi[:].rearrange("p (a d) -> p a d", d=D)),
                     reads=[b_fxi], writes=[Buf(f"{tag}out")])

    p2 = ExitStack()
    emit_ffn(p2, "p2", xs1, xs2)
    plan.barrier()
    p2.close()

    p3 = ExitStack()
    wgT_sb, b_wgT = mksb(p3, "p3wgT", [128, NDT * D], BF16)
    wrT_sb, b_wrT = mksb(p3, "p3wrT", [128, NDT * D], BF16)
    woT_sb, b_woT = mksb(p3, "p3woT", [128, NDT * D], BF16)
    dma_in("sp", wgT_sb[:], wgT_d[:], b_wgT)
    dma_in("sp", wrT_sb[:], wrT_d[:], b_wrT)
    dma_in("sp", woT_sb[:], woT_d[:], b_woT)
    xi3, b_xi3 = mksb(p3, "p3xin", [128, TPC * D], F32)
    XT3, b_XT3 = mksb(p3, "p3XT", [128, NDT * CH], F32)
    G3, b_G3 = mksb(p3, "p3G", [128, NDT * CH], F32)
    V3t, b_V3t = mksb(p3, "p3V", [128, 2 * CH], F32)
    b_V3s = [Buf("V3s0"), Buf("V3s1")]
    HxA, _bA = mksb(p3, "p3HxA", [128, NDT * (CH + 1)], BF16)
    HxB, _bB = mksb(p3, "p3HxB", [128, NDT * (CH + 1)], BF16)
    Hf, _bH = mksb(p3, "p3Hf", [128, NDT * CH], F32)
    b_HxA = [Buf(f"HxA{i}") for i in range(NDT)]
    b_HxB = [Buf(f"HxB{i}") for i in range(NDT)]
    b_Hfb = [Buf(f"Hf{i}") for i in range(NDT)]
    hcar, b_hcar = mksb(p3, "p3hcar", [128, NDT], F32)
    Nrm4, b_Nrm4 = mksb(p3, "p3Nrm4", [128, TPC * D], BF16)
    YT3, b_YT3 = mksb(p3, "p3YT", [128, NDT * CH], BF16)
    OT3, b_OT3 = mksb(p3, "p3OT", [128, NDT * CH], F32)
    XTb = OT3[:].bitcast(BF16)[:, :NDT * CH]
    b_XTb = b_OT3
    o3, b_o3 = mksb(p3, "p3o", [128, TPC * D], F32)
    pps = [mkps(p3, "p3ps0", [128, CH], F32), mkps(p3, "p3ps1", [128, CH], F32)]
    tp_psL.append(mkps(p3, "tp_ps3b", [128, 512], BF16))
    tpf_psL.append(mkps(p3, "tpf_ps3b", [128, 512], F32))
    b_lsqep = _lsqslots[0]

    plan.op("dve", lambda: nc.vector.memset(hcar[:], 0.0), writes=[b_hcar])

    for ch in range(NCH):
        src = xs2[ch * CH:(ch + 1) * CH, :].rearrange("(a p) d -> p a d", p=128)
        dma_in("sp", xi3[:].rearrange("p (a d) -> p a d", d=D), src, b_xi3, rbufs=[b_xi3])
        XT3v = XT3[:].rearrange("p (t q) -> p t q", q=CH)
        XTbv = XTb.rearrange("p (t q) -> p t q", q=CH)
        for a in range(TPC):
            xh_ap = xh2t[:, (a % 2) * D:(a % 2 + 1) * D]
            b_xh_s = b_xh2[a % 2]
            emit_ln(xi3[:, a * D:(a + 1) * D], b_xi3, xh_ap, b_xh_s)
            for g0 in range(0, NDT, 4):
                srcs = [xh_ap[:, (g0 + j) * 128:(g0 + j + 1) * 128]
                        for j in range(4)]
                tpf_, b_tpf_ = tpf_psL[_tpctr[0] % len(tpf_psL)]
                _tpctr[0] += 1
                fns = [lambda s=s, i=i, tpf_=tpf_: nc.tensor.transpose(
                    out=tpf_[:, i * 128:(i + 1) * 128], in_=s, identity=idf[:])
                    for i, s in enumerate(srcs)]
                plan.group("pe", fns, reads=[b_xh_s, b_idf], writes=[b_tpf_])
                b_tpf = b_tpf_
                ps3 = tpf_[:, :512].rearrange("p (t q) -> p t q", q=128)
                plan.op("act", lambda a=a, g0=g0, ps3=ps3: nc.scalar.activation(
                    out=XT3v[:, g0:g0 + 4, a * 128:a * 128 + 128], in_=ps3,
                    func=AF.Copy), reads=[b_tpf], writes=[b_XT3])
                plan.op("act", lambda a=a, g0=g0, ps3=ps3: nc.scalar.activation(
                    out=XTbv[:, g0:g0 + 4, a * 128:a * 128 + 128], in_=ps3,
                    func=AF.Copy), reads=[b_tpf], writes=[b_XTb])
        for et in range(NDT):
            ps, b_ps = pps[et % 2]
            fns = [lambda dt=dt, et=et, ps=ps: nc.tensor.matmul(
                out=ps[:, :CH],
                lhsT=wgT_sb[:, dt * D + et * 128: dt * D + et * 128 + 128],
                rhs=XTb[:, dt * CH:(dt + 1) * CH],
                start=(dt == 0), stop=(dt == NDT - 1)) for dt in range(NDT)]
            plan.group("pe", fns, reads=[b_XTb, b_wgT], writes=[b_ps])
            plan.op("act", lambda et=et, ps=ps: nc.scalar.activation(
                out=G3[:, et * CH:(et + 1) * CH], in_=ps[:, :CH], func=AF.Sigmoid,
                bias=bgv[:, et:et + 1]), reads=[b_ps, b_bgv], writes=[b_G3])
            vop(lambda et=et: nc.vector.tensor_scalar(
                out=G3[:, et * CH:(et + 1) * CH], in0=G3[:, et * CH:(et + 1) * CH],
                scalar1=pv[:, COL_OM + et:COL_OM + et + 1], scalar2=None,
                op0=ALU.mult), [b_G3, b_pv], [b_G3])
        for dt in range(NDT):
            vop(lambda dt=dt: nc.vector.tensor_scalar(
                out=XT3[:, dt * CH:(dt + 1) * CH], in0=XT3[:, dt * CH:(dt + 1) * CH],
                scalar1=pv[:, COL_BR + dt:COL_BR + dt + 1], scalar2=None,
                op0=ALU.subtract), [b_XT3, b_pv], [b_XT3])
        plan.op("dve", lambda: nc.vector.memset(HxA[:], 0.0), reads=[],
                writes=list(b_HxA), dur=900)
        for dt in range(NDT):
            vop(lambda dt=dt: nc.vector.tensor_copy(
                out=HxA[:, dt * (CH + 1):dt * (CH + 1) + 1], in_=hcar[:, dt:dt + 1]),
                [b_hcar], [b_HxA[dt]])
        cur, b_cur, nxt, b_nxt = HxA, b_HxA, HxB, b_HxB
        for k in range(K):
            last = (k == K - 1)
            for n in range(NDT):
                ps, b_ps = pps[n % 2]
                fns = [lambda dt=dt, n=n, ps=ps, cur=cur: nc.tensor.matmul(
                    out=ps[:, :CH],
                    lhsT=wrT_sb[:, dt * D + n * 128: dt * D + n * 128 + 128],
                    rhs=cur[:, dt * (CH + 1): dt * (CH + 1) + CH],
                    start=(dt == 0), stop=(dt == NDT - 1)) for dt in range(NDT)]
                plan.group("pe", fns, reads=list(b_cur) + [b_wrT], writes=[b_ps],
                           dur=1900)
                V3 = V3t[:, (n % 2) * CH:(n % 2 + 1) * CH]
                b_V3 = b_V3s[n % 2]
                plan.op("dve", lambda n=n, ps=ps, V3=V3: nc.vector.scalar_tensor_tensor(
                    out=V3, in0=ps[:, :CH], scalar=-1.0,
                    in1=XT3[:, n * CH:(n + 1) * CH], op0=ALU.mult, op1=ALU.add),
                    reads=[b_ps, b_XT3], writes=[b_V3], dur=700)
                plan.op("dve", lambda n=n, V3=V3: nc.vector.tensor_tensor(
                    out=V3, in0=V3, in1=G3[:, n * CH:(n + 1) * CH],
                    op=ALU.mult), reads=[b_V3, b_G3], writes=[b_V3], dur=700)
                a_rep = pv[:, COL_A + n:COL_A + n + 1].broadcast_to([128, CH])
                if last:
                    plan.op("dve", lambda n=n, a_rep=a_rep, V3=V3: nc.vector.tensor_tensor_scan(
                        out=Hf[:, n * CH:(n + 1) * CH], data0=a_rep, data1=V3,
                        initial=hcar[:, n:n + 1], op0=ALU.mult, op1=ALU.add),
                        reads=[b_V3, b_hcar, b_pv], writes=[b_Hfb[n]], dur=1400)
                else:
                    plan.op("dve", lambda n=n, a_rep=a_rep, nxt=nxt, V3=V3: nc.vector.tensor_tensor_scan(
                        out=nxt[:, n * (CH + 1) + 1: n * (CH + 1) + 1 + CH],
                        data0=a_rep, data1=V3,
                        initial=hcar[:, n:n + 1], op0=ALU.mult, op1=ALU.add),
                        reads=[b_V3, b_hcar, b_pv], writes=[b_nxt[n]], dur=1400)
            if not last:
                if k == 0:
                    for dt in range(NDT):
                        vop(lambda dt=dt, nxt=nxt: nc.vector.tensor_copy(
                            out=nxt[:, dt * (CH + 1):dt * (CH + 1) + 1],
                            in_=hcar[:, dt:dt + 1]), [b_hcar], [b_nxt[dt]])
                cur, b_cur, nxt, b_nxt = nxt, b_nxt, cur, b_cur
        for dt in range(NDT):
            vop(lambda dt=dt: nc.vector.tensor_copy(
                out=hcar[:, dt:dt + 1], in_=Hf[:, dt * CH + CH - 1: dt * CH + CH]),
                [b_Hfb[dt]], [b_hcar])
        for a in range(TPC):
            for g0 in range(0, NDT, 4):
                srcs = [Hf[:, (g0 + j) * CH + a * 128:(g0 + j) * CH + a * 128 + 128]
                        for j in range(4)]
                tp_group_f32(srcs, [b_Hfb[g0 + j] for j in range(4)],
                             lsq[:, g0 * 128:(g0 + 4) * 128], b_lsqep)
            emit_ln_noscratch(lsq[:, 0:D], b_lsqep, Nrm4[:, a * D:(a + 1) * D], b_Nrm4)
        for dt in range(NDT):
            srcs = [Nrm4[:, a * D + dt * 128: a * D + dt * 128 + 128]
                    for a in range(TPC)]
            tp_group_bf(srcs, [b_Nrm4], YT3[:, dt * CH:(dt + 1) * CH], b_YT3)
        for dt in range(NDT):
            vop(lambda dt=dt: nc.vector.tensor_scalar(
                out=YT3[:, dt * CH:(dt + 1) * CH], in0=YT3[:, dt * CH:(dt + 1) * CH],
                scalar1=pv[:, COL_LNW + dt:COL_LNW + dt + 1],
                scalar2=pv[:, COL_LNB + dt:COL_LNB + dt + 1],
                op0=ALU.mult, op1=ALU.add), [b_YT3, b_pv], [b_YT3])
        for et in range(NDT):
            ps, b_ps = pps[et % 2]
            fns = [lambda dt=dt, et=et, ps=ps: nc.tensor.matmul(
                out=ps[:, :CH],
                lhsT=woT_sb[:, dt * D + et * 128: dt * D + et * 128 + 128],
                rhs=YT3[:, dt * CH:(dt + 1) * CH],
                start=(dt == 0), stop=(dt == NDT - 1)) for dt in range(NDT)]
            plan.group("pe", fns, reads=[b_YT3, b_woT], writes=[b_ps])
            vop(lambda et=et, ps=ps: nc.vector.tensor_scalar(
                out=OT3[:, et * CH:(et + 1) * CH], in0=ps[:, :CH],
                scalar1=pv[:, COL_BO + et:COL_BO + et + 1], scalar2=None,
                op0=ALU.add), [b_ps, b_pv], [b_OT3])
        for a in range(TPC):
            for g0 in range(0, NDT, 4):
                srcs = [OT3[:, (g0 + j) * CH + a * 128:(g0 + j) * CH + a * 128 + 128]
                        for j in range(4)]
                dsl = o3[:, a * D + g0 * 128: a * D + (g0 + 4) * 128]
                xsl = xi3[:, a * D + g0 * 128: a * D + (g0 + 4) * 128]
                tp_group_f32_add(srcs, [b_OT3], dsl, b_o3, xsl, b_xi3)
        dst = xs3[ch * CH:(ch + 1) * CH, :].rearrange("(a p) d -> p a d", p=128)
        plan.dma("pool", lambda d=dst: nc.gpsimd.dma_start(out=d, in_=o3[:].rearrange("p (a d) -> p a d", d=D)),
                 reads=[b_o3], writes=[Buf("xs3w")])

    plan.barrier()
    tp_psL.pop()
    tpf_psL.pop()
    p3.close()

    p4 = ExitStack()
    emit_ffn(p4, "p4", xs3, out_d)
    plan.barrier()
    p4.close()

    plan.emit()

    with nc.Block() as block:
        @block.sync
        def _(eng):
            plan._run_engine("sp")

        @block.scalar
        def _(eng):
            plan._run_engine("act")

        @block.vector
        def _(eng):
            plan._run_engine("dve")

        @block.tensor
        def _(eng):
            plan._run_engine("pe")

        @block.gpsimd
        def _(eng):
            plan._run_engine("pool")
            plan.finish_waits("pool")

    outer.close()
    plan._es.close()
    return nc


def kernel(**inputs):
    import ml_dtypes
    from concourse.bass_utils import run_bass_kernel_spmd

    x = np.ascontiguousarray(np.asarray(inputs["L0_attn_a_x"], np.float32))
    B, L, _ = x.shape
    nc = build_nc(L=L)
    iden_bf = np.eye(128, dtype=ml_dtypes.bfloat16)
    iden_f32 = np.eye(128, dtype=np.float32)
    wnames = [k for k in inputs.keys() if k != "L0_attn_a_x"]
    in_maps = []
    for c in range(B):
        m = {nm: np.ascontiguousarray(np.asarray(inputs[nm], np.float32))
             for nm in wnames}
        m["x0"] = np.ascontiguousarray(x[c])
        m["iden_bf"] = iden_bf
        m["iden_f32"] = iden_f32
        in_maps.append(m)
    res = run_bass_kernel_spmd(nc, in_maps, core_ids=list(range(B)))
    return np.stack([np.asarray(r["out"], np.float32) for r in res.results], axis=0)


# version 25 (final)
# speedup vs baseline: 1.1936x; 1.0274x over previous
import numpy as np
from contextlib import ExitStack

import concourse.bass as bass
import concourse.mybir as mybir

F32 = mybir.dt.float32
BF16 = mybir.dt.bfloat16
AF = mybir.ActivationFunctionType
ALU = mybir.AluOpType
AXX = mybir.AxisListType.X

D = 1024
DI = 2816
NDT = D // 128
NIT = DI // 128
C2V = float(2 ** 22 + 1)
CBV = float(1.5 * 2 ** 23)

ENGS = ("sp", "act", "dve", "pe", "pool")
_EMAP = {"sp": "sync", "act": "scalar", "dve": "vector", "pe": "tensor",
         "pool": "gpsimd"}


class Buf:
    __slots__ = ("name", "w", "r")

    def __init__(self, name):
        self.name = name
        self.w = None
        self.r = []


class _MultiBuf:

    def __init__(self, bufs):
        self.bufs = bufs

    @property
    def w(self):
        ws = [b.w for b in self.bufs if b.w is not None]
        return max(ws) if ws else None

    @w.setter
    def w(self, v):
        for b in self.bufs:
            b.w = v

    @property
    def r(self):
        out = []
        for b in self.bufs:
            out.extend(b.r)
        return out

    @r.setter
    def r(self, v):
        for b in self.bufs:
            b.r = list(v)


class Node:
    __slots__ = ("idx", "eng", "fns", "deps", "dur", "is_dma", "sem", "val")

    def __init__(self, idx, eng, fns, deps, dur, is_dma=False):
        self.idx = idx
        self.eng = eng
        self.fns = fns
        self.deps = deps
        self.dur = dur
        self.is_dma = is_dma
        self.sem = None
        self.val = None


class Plan:

    def __init__(self, nc):
        self.nc = nc
        self.nodes = []
        self.sem = {}
        self._es = ExitStack()
        for e in ENGS:
            self.sem[e] = self._es.enter_context(nc.semaphore(f"sem_{e}"))
        self.NDMA = 24
        self.dpool = {"hw": list(range(16)), "sw": list(range(16, 24))}
        for i in range(self.NDMA):
            self.sem[f"dp{i}"] = self._es.enter_context(
                nc.semaphore(f"sem_dp{i}"))
        self._barrier_idx = -1

    def _mknode(self, eng, fns, reads, writes, dur, is_dma=False):
        deps = set()
        for b in reads:
            if b.w is not None:
                deps.add(b.w)
        for b in writes:
            if b.w is not None:
                deps.add(b.w)
            deps.update(b.r)
        if self._barrier_idx >= 0:
            deps.add(self._barrier_idx)
        idx = len(self.nodes)
        n = Node(idx, eng, fns, deps, dur, is_dma)
        self.nodes.append(n)
        for b in writes:
            b.w = idx
            b.r = []
        for b in reads:
            b.r.append(idx)
        return idx

    def op(self, eng, fn, reads=(), writes=(), dur=600):
        return self._mknode(eng, [fn], reads, writes, dur)

    def group(self, eng, fns, reads=(), writes=(), dur=None):
        if dur is None:
            dur = 250 * len(fns)
        return self._mknode(eng, list(fns), reads, writes, dur)

    def dma(self, queue, fn, reads=(), writes=(), dur=4000):
        return self._mknode(queue, [fn], reads, writes, dur, is_dma=True)

    def barrier(self):
        idx = self._mknode("sp", None, [], [], 0)
        n = self.nodes[idx]
        n.deps = set(range(idx))
        self._barrier_idx = idx
        return idx

    def emit(self):
        import heapq
        nodes = self.nodes
        N = len(nodes)
        nsucc = [[] for _ in range(N)]
        ndep = [0] * N
        for n in nodes:
            ndep[n.idx] = len(n.deps)
            for d in n.deps:
                nsucc[d].append(n.idx)
        blevel = [0.0] * N
        for i in range(N - 1, -1, -1):
            n = nodes[i]
            succ_max = 0.0
            for s_ in nsucc[i]:
                if blevel[s_] > succ_max:
                    succ_max = blevel[s_]
            blevel[i] = succ_max + n.dur + (1500 if n.is_dma else 0)
        eng_clock = {e: 0.0 for e in ENGS}
        finish = [0.0] * N
        ready_t = [0.0] * N
        heap = []
        for n in nodes:
            if ndep[n.idx] == 0:
                heapq.heappush(heap, (0.0, n.idx))
        order = {e: [] for e in ENGS}
        done = 0
        WINDOW = 8192
        scheduled = [False] * N
        min_unsched = 0
        poolq = {e: [] for e in ENGS}
        deferred = []
        while done < N:
            gate = min_unsched + WINDOW
            while heap:
                rt, i = heapq.heappop(heap)
                if i > gate:
                    heapq.heappush(deferred, i)
                else:
                    heapq.heappush(poolq[nodes[i].eng], (-blevel[i], i))
            while deferred and deferred[0] <= gate:
                i = heapq.heappop(deferred)
                heapq.heappush(poolq[nodes[i].eng], (-blevel[i], i))
            cands = [(eng_clock[e], e) for e in ENGS if poolq[e]]
            if not cands:
                raise RuntimeError("scheduler stuck")
            _, e = min(cands)
            (_, i) = heapq.heappop(poolq[e])
            scheduled[i] = True
            while min_unsched < N and scheduled[min_unsched]:
                min_unsched += 1
            n = nodes[i]
            start = max(eng_clock[e], ready_t[i])
            dma_lat = 1500 if n.is_dma else 0
            finish[i] = start + n.dur + dma_lat
            eng_clock[e] = start + (150 if n.is_dma else n.dur)
            order[e].append(i)
            done += 1
            for s in nsucc[i]:
                ready_t[s] = max(ready_t[s], finish[i])
                ndep[s] -= 1
                if ndep[s] == 0:
                    heapq.heappush(heap, (ready_t[s], s))
        eng_pos = {e: 0 for e in ENGS}
        dnext = {"hw": 0, "sw": 0}
        dcnt = [0] * self.NDMA
        dma_prev = {}
        for e in ENGS:
            for i in order[e]:
                n = nodes[i]
                if n.is_dma:
                    pool = "sw" if e == "pool" else "hw"
                    lst = self.dpool[pool]
                    di = lst[dnext[pool]]
                    dnext[pool] = (dnext[pool] + 1) % len(lst)
                    dcnt[di] += 16
                    n.sem = f"dp{di}"
                    n.val = dcnt[di]
                elif n.fns is not None:
                    eng_pos[e] += 1
                    n.sem = e
                    n.val = eng_pos[e]
        waited = {e: {} for e in ENGS}

        def emit_wait(e, semkey, val):
            if val <= waited[e].get(semkey, 0):
                return
            waited[e][semkey] = val
            g = getattr(self.nc, _EMAP[e])
            g.wait_ge(self.sem[semkey], val)

        self.final_dma = [(f"dp{i}", dcnt[i]) for i in range(self.NDMA) if dcnt[i]]
        barrier_cache = {}

        def barrier_maxes(bidx):
            if bidx not in barrier_cache:
                mx = {}
                for d2 in nodes[bidx].deps:
                    dn2 = nodes[d2]
                    if dn2.sem is not None:
                        if dn2.val > mx.get(dn2.sem, 0):
                            mx[dn2.sem] = dn2.val
                    elif dn2.fns is None and d2 in barrier_cache:
                        for k, v in barrier_cache[d2].items():
                            if v > mx.get(k, 0):
                                mx[k] = v
                barrier_cache[bidx] = mx
            return barrier_cache[bidx]

        def run_engine(e):
            prev_dma_on_sem = {}
            for i in order[e]:
                n = nodes[i]
                if n.is_dma:
                    prev = prev_dma_on_sem.get(n.sem)
                    if prev is not None:
                        emit_wait(e, n.sem, prev)
                    elif n.val > 16:
                        emit_wait(e, n.sem, n.val - 16)
                    prev_dma_on_sem[n.sem] = n.val
                for d in sorted(n.deps):
                    dn = nodes[d]
                    if dn.sem is None:
                        if dn.fns is None:
                            for k, v in sorted(barrier_maxes(d).items()):
                                emit_wait(e, k, v)
                        continue
                    emit_wait(e, dn.sem, dn.val)
                if not n.fns:
                    continue
                h = self.sem[n.sem]
                inc = 16 if n.is_dma else 1
                for j, fn in enumerate(n.fns):
                    if j == len(n.fns) - 1:
                        fn().then_inc(h, inc)
                    else:
                        fn()
        self._run_engine = run_engine

    def finish_waits(self, e):
        g = getattr(self.nc, _EMAP[e])
        for key, val in self.final_dma:
            g.wait_ge(self.sem[key], val)


def build_nc(L=8192, K=10, CH=512):
    nc = bass.Bass("TRN2", target_bir_lowering=False, debug=False)
    NCH = L // CH
    TPC = CH // 128
    assert TPC == 4

    x0_d = nc.dram_tensor("x0", [L, D], F32, kind="ExternalInput").ap()
    win = {}
    for nm, shp in [
        ("L0_attn_a_phazor", [D, 2]), ("L0_attn_a_phazor_init", [D, 2]),
        ("L0_attn_a_last_conv_re", [D]), ("L0_attn_a_last_conv_im", [D]),
        ("L0_attn_b_decay", [D]), ("L0_attn_b_Wr", [D, D]),
        ("L0_attn_b_br", [D]), ("L0_attn_b_Wg", [D, D]), ("L0_attn_b_bg", [D]),
        ("L0_attn_b_Wo", [D, D]), ("L0_attn_b_bo", [D]),
        ("L0_attn_b_ln_w", [D]), ("L0_attn_b_ln_b", [D]),
        ("L0_ffn_w_gate", [DI, D]), ("L0_ffn_w_up", [DI, D]),
        ("L0_ffn_w_down", [D, DI]),
    ]:
        win[nm] = nc.dram_tensor(nm, shp, F32, kind="ExternalInput").ap()
    iden_bf_d = nc.dram_tensor("iden_bf", [128, 128], BF16, kind="ExternalInput").ap()
    iden_f32_d = nc.dram_tensor("iden_f32", [128, 128], F32, kind="ExternalInput").ap()
    out_d = nc.dram_tensor("out", [L, D], F32, kind="ExternalOutput").ap()

    xs1 = nc.dram_tensor("xs1", [L, D], F32, kind="Internal").ap()
    xs2 = nc.dram_tensor("xs2", [L, D], F32, kind="Internal").ap()
    xs3 = nc.dram_tensor("xs3", [L, D], F32, kind="Internal").ap()
    wgq_d = nc.dram_tensor("wgq_d", [128, NDT * DI], BF16, kind="Internal").ap()
    wuq_d = nc.dram_tensor("wuq_d", [128, NDT * DI], BF16, kind="Internal").ap()
    wdq_d = nc.dram_tensor("wdq_d", [128, NIT * D], BF16, kind="Internal").ap()
    wgT_d = nc.dram_tensor("wgT_d", [128, NDT * D], BF16, kind="Internal").ap()
    wrT_d = nc.dram_tensor("wrT_d", [128, NDT * D], BF16, kind="Internal").ap()
    woT_d = nc.dram_tensor("woT_d", [128, NDT * D], BF16, kind="Internal").ap()

    plan = Plan(nc)
    outer = ExitStack()

    def mksb(esx, name, shape, dt):
        t = esx.enter_context(nc.sbuf_tensor(name, shape, dt))
        return t, Buf(name)

    def mkps(esx, name, shape, dt):
        t = esx.enter_context(nc.psum_tensor(name, shape, dt))
        return t, Buf(name)

    idbf, b_idbf = mksb(outer, "idbf", [128, 128], BF16)
    idf, b_idf = mksb(outer, "idf", [128, 128], F32)
    pv, b_pv = mksb(outer, "pv", [128, 64], F32)
    (COL_A, COL_OM, COL_BR, COL_BO, COL_LNW, COL_LNB, COL_R) = (
        0, 8, 16, 24, 32, 40, 48)
    bgv, b_bgv = mksb(outer, "bgv", [128, NDT], F32)
    carry, b_carry = mksb(outer, "carry", [128, 2 * NDT], F32)
    xh2t, b_xh2t = mksb(outer, "xh", [128, 2 * D], F32)
    b_xh2 = [Buf("xh_s0"), Buf("xh_s1")]
    lnt, b_lnt = mksb(outer, "lnt", [128, 32], F32)
    tUc, b_tUc = mksb(outer, "tUc", [128, 2 * NDT], F32)
    q0t, b_q0t = mksb(outer, "q0t", [128, 2 * NDT], F32)
    tp_psL = [mkps(outer, "tp_ps0", [128, 512], BF16)]
    tpf_psL = [mkps(outer, "tpf_ps0", [128, 512], F32)]
    _tpctr = [0]
    tp_ps, b_tp = tp_psL[0]
    tpf_ps, b_tpf = tpf_psL[0]

    def dma_in(queue, dst_ap, src_ap, wbuf, rbufs=(), dur=4000):
        plan.dma(queue,
                 lambda d=dst_ap, s=src_ap: getattr(nc, _EMAP[queue]).dma_start(out=d, in_=s),
                 reads=list(rbufs), writes=[wbuf], dur=dur)

    dma_in("sp", idbf[:], iden_bf_d[:], b_idbf)
    dma_in("sp", idf[:], iden_f32_d[:], b_idf)

    def tp_group_bf(srcs, b_srcs, dst_ap, b_dst):
        ps, b_ps = tp_psL[_tpctr[0] % len(tp_psL)]
        _tpctr[0] += 1
        n = len(srcs)
        fns = [lambda s=s, i=i, ps=ps: nc.tensor.transpose(
            out=ps[:, i * 128:(i + 1) * 128], in_=s, identity=idbf[:])
            for i, s in enumerate(srcs)]
        plan.group("pe", fns, reads=list(b_srcs) + [b_idbf], writes=[b_ps],
                   dur=300 * n)
        plan.op("act", lambda n=n, ps=ps: nc.scalar.activation(
            out=dst_ap, in_=ps[:, :n * 128], func=AF.Copy),
            reads=[b_ps], writes=[b_dst], dur=600)

    def tp_group_f32(srcs, b_srcs, dst_ap, b_dst):
        ps, b_ps = tpf_psL[_tpctr[0] % len(tpf_psL)]
        _tpctr[0] += 1
        n = len(srcs)
        fns = [lambda s=s, i=i, ps=ps: nc.tensor.transpose(
            out=ps[:, i * 128:(i + 1) * 128], in_=s, identity=idf[:])
            for i, s in enumerate(srcs)]
        plan.group("pe", fns, reads=list(b_srcs) + [b_idf], writes=[b_ps],
                   dur=300 * n)
        plan.op("act", lambda n=n, ps=ps: nc.scalar.activation(
            out=dst_ap, in_=ps[:, :n * 128], func=AF.Copy),
            reads=[b_ps], writes=[b_dst], dur=600)

    def tp_group_f32_add(srcs, b_srcs, dst_ap, b_dst, add_ap, b_add):
        ps, b_ps = tpf_psL[_tpctr[0] % len(tpf_psL)]
        _tpctr[0] += 1
        n = len(srcs)
        fns = [lambda s=s, i=i, ps=ps: nc.tensor.transpose(
            out=ps[:, i * 128:(i + 1) * 128], in_=s, identity=idf[:])
            for i, s in enumerate(srcs)]
        plan.group("pe", fns, reads=list(b_srcs) + [b_idf], writes=[b_ps],
                   dur=300 * n)
        plan.op("dve", lambda n=n, ps=ps: nc.vector.tensor_tensor(
            out=dst_ap, in0=ps[:, :n * 128], in1=add_ap, op=ALU.add),
            reads=[b_ps, b_add], writes=[b_dst], dur=750)

    def load_vec_to_cols(vec1d_view, dst_ap, b_dst, width):
        dma_in("sp", vstage[0:width, 0:128], vec1d_view, b_vstage, rbufs=[b_vstage])
        plan.op("pe", lambda: nc.tensor.transpose(
            out=tpf_ps[:, 0:width], in_=vstage[0:width, 0:128],
            identity=idf[0:width, 0:width]),
            reads=[b_vstage, b_idf], writes=[b_tpf])
        plan.op("act", lambda: nc.scalar.activation(
            out=dst_ap, in_=tpf_ps[:, 0:width], func=AF.Copy),
            reads=[b_tpf], writes=[b_dst])

    _lnslots = [Buf(f"lnt{i}") for i in range(4)]
    _sqdefault = [None]
    _lnctr = [0]

    def emit_ln(src_ap, b_src, dst_ap, b_dst, sq_over=None):
        s = _lnctr[0] % 4
        _lnctr[0] += 1
        b_ln = _lnslots[s]
        if sq_over is None:
            sq, b_sq = _sqdefault[0]
        else:
            sq, b_sq = sq_over
        ln = lnt[:, s * 8:(s + 1) * 8]
        plan.op("dve", lambda: nc.vector.reduce_sum(
            out=ln[:, 0:1], in_=src_ap, axis=AXX), reads=[b_src], writes=[b_ln],
            dur=1100)
        plan.op("act", lambda: nc.scalar.activation(
            out=sq, in_=src_ap, func=AF.Square), reads=[b_src], writes=[b_sq],
            dur=1000)
        plan.op("dve", lambda: nc.vector.reduce_sum(
            out=ln[:, 1:2], in_=sq, axis=AXX), reads=[b_sq], writes=[b_ln],
            dur=1100)
        plan.op("dve", lambda: nc.vector.tensor_scalar_mul(
            out=ln[:, 0:2], in0=ln[:, 0:2], scalar1=1.0 / D),
            reads=[b_ln], writes=[b_ln], dur=80)
        plan.op("dve", lambda: nc.vector.tensor_tensor(
            out=ln[:, 2:3], in0=ln[:, 0:1], in1=ln[:, 0:1], op=ALU.mult),
            reads=[b_ln], writes=[b_ln], dur=80)
        plan.op("dve", lambda: nc.vector.tensor_tensor(
            out=ln[:, 1:2], in0=ln[:, 1:2], in1=ln[:, 2:3], op=ALU.subtract),
            reads=[b_ln], writes=[b_ln], dur=80)
        plan.op("dve", lambda: nc.vector.tensor_single_scalar(
            out=ln[:, 1:2], in_=ln[:, 1:2], scalar=1e-5, op=ALU.add),
            reads=[b_ln], writes=[b_ln], dur=80)
        plan.op("act", lambda: nc.scalar.activation(
            out=ln[:, 3:4], in_=ln[:, 1:2], func=AF.Sqrt),
            reads=[b_ln], writes=[b_ln], dur=250)
        plan.op("dve", lambda: nc.vector.reciprocal(
            out=ln[:, 3:4], in_=ln[:, 3:4]), reads=[b_ln], writes=[b_ln], dur=80)
        plan.op("dve", lambda: nc.vector.tensor_scalar(
            out=dst_ap, in0=src_ap, scalar1=ln[:, 0:1], scalar2=ln[:, 3:4],
            op0=ALU.subtract, op1=ALU.mult), reads=[b_src, b_ln], writes=[b_dst],
            dur=700)

    def emit_ln_noscratch(src_ap, b_src, dst_ap, b_dst):
        emit_ln(src_ap, b_src, dst_ap, b_dst,
                sq_over=(xh2t[:, 0:D], b_xh2[0]))

    def emit_quant(src_ap, b_src, dst_ap, b_dst, W, scr):
        TT = int((58 + W) / 0.96)
        TS = int((58 + W / 2) / 0.96)
        if isinstance(b_src, list):
            b_src = _MultiBuf(b_src)
        """fp4 quantize-dequantize [128, W] f32 -> [128, W] bf16.
        src is DESTROYED (used as Veltkamp scratch).  scr: z/a2 f32 >= W,
        zs bf16 >= W, red/sc/rcp f32 >= W/16."""
        nb = W // 16
        z, b_z = scr["z"]
        a2, b_a2 = scr["a2"]
        red, b_red = scr["red"]
        sc, b_sc = scr["sc"]
        rcp, b_rcp = scr["rcp"]
        src3 = src_ap.rearrange("p (a b) -> p a b", b=16)
        plan.op("dve", lambda: nc.vector.tensor_reduce(
            out=red[:, :nb], in_=src3, axis=AXX, op=ALU.max,
            apply_absolute_value=True), reads=[b_src], writes=[b_red], dur=TT)
        plan.op("dve", lambda: nc.vector.tensor_scalar(
            out=sc[:, :nb], in0=red[:, :nb],
            scalar1=float(np.float32(1.0) / np.float32(12.0)), scalar2=5e-13,
            op0=ALU.mult, op1=ALU.max), reads=[b_red], writes=[b_sc], dur=120)
        plan.op("dve", lambda: nc.vector.reciprocal(
            out=rcp[:, :nb], in_=sc[:, :nb]), reads=[b_sc], writes=[b_rcp], dur=160)
        rcp3 = rcp[:, :nb].unsqueeze(2).broadcast_to([128, nb, 16])
        sc3 = sc[:, :nb].unsqueeze(2).broadcast_to([128, nb, 16])
        z3 = z[:, :W].rearrange("p (a b) -> p a b", b=16)
        plan.op("dve", lambda: nc.vector.tensor_tensor(
            out=z3, in0=src3, in1=rcp3, op=ALU.mult),
            reads=[b_src, b_rcp], writes=[b_z], dur=TT)
        msk, b_msk = scr["msk"]
        zs, b_zs = scr["zs"]
        plan.op("act", lambda: nc.scalar.activation(
            out=zs[:, :W], in_=z[:, :W], func=AF.Square),
            reads=[b_z], writes=[b_zs], dur=int(224 / 1.2 + W / 1.2))
        plan.op("dve", lambda: nc.vector.tensor_single_scalar(
            out=msk[:, :W], in_=zs[:, :W], scalar=4.0, op=ALU.is_lt),
            reads=[b_zs], writes=[b_msk], dur=TS)
        plan.op("dve", lambda: nc.vector.tensor_scalar(
            out=a2[:, :W], in0=z[:, :W], scalar1=CBV, scalar2=CBV,
            op0=ALU.add, op1=ALU.subtract), reads=[b_z, b_msk], writes=[b_a2])
        plan.op("dve", lambda: nc.vector.tensor_scalar_mul(
            out=src_ap, in0=z[:, :W], scalar1=C2V), reads=[b_z], writes=[b_src], dur=TS)
        plan.op("dve", lambda: nc.vector.tensor_tensor(
            out=z[:, :W], in0=src_ap, in1=z[:, :W], op=ALU.subtract),
            reads=[b_src, b_z], writes=[b_z], dur=700)
        plan.op("dve", lambda: nc.vector.tensor_tensor(
            out=src_ap, in0=src_ap, in1=z[:, :W], op=ALU.subtract),
            reads=[b_src, b_z], writes=[b_src], dur=700)
        plan.op("dve", lambda: nc.vector.copy_predicated(
            out=src_ap, mask=msk[:, :W], data=a2[:, :W]),
            reads=[b_msk, b_a2, b_src], writes=[b_src], dur=TT)
        dst3 = dst_ap.rearrange("p (a b) -> p a b", b=16)
        plan.op("dve", lambda: nc.vector.tensor_tensor(
            out=dst3, in0=src3, in1=sc3, op=ALU.mult),
            reads=[b_src, b_sc], writes=[b_dst], dur=TT)

    p1 = ExitStack()
    tWr, b_tWr = mksb(p1, "tWr", [128, NDT * CH], F32)
    tWi, b_tWi = mksb(p1, "tWi", [128, NDT * CH], F32)
    tA, b_tA = mksb(p1, "tA", [128, NDT * CH], F32)
    tB, b_tB = mksb(p1, "tB", [128, NDT * CH], F32)
    p0 = ExitStack()
    wrow, b_wrow = mksb(p0, "wrow", [128, DI], F32)
    vstage, b_vstage = mksb(p0, "vstage", [8, 256], F32)
    wqr, b_wqr = mksb(p0, "wqr", [128, DI], BF16)
    qz0, b_qz0 = mksb(p0, "qz0", [128, DI], F32)
    qa20, b_qa20 = mksb(p0, "qa20", [128, DI], BF16)
    qred0, b_qred0 = mksb(p0, "qred0", [128, DI // 16], F32)
    qsc0, b_qsc0 = mksb(p0, "qsc0", [128, DI // 16], F32)
    qrcp0, b_qrcp0 = mksb(p0, "qrcp0", [128, DI // 16], F32)
    qmsk0, b_qmsk0 = mksb(p0, "qmsk0", [128, DI], mybir.dt.uint8)
    qzs0, b_qzs0 = mksb(p0, "qzs0", [128, DI], BF16)
    scr0 = {"z": (qz0, b_qz0), "a2": (qa20, b_qa20), "zs": (qzs0, b_qzs0),
            "red": (qred0, b_qred0), "sc": (qsc0, b_qsc0), "rcp": (qrcp0, b_qrcp0),
            "msk": (qmsk0, b_qmsk0)}
    wasm, b_wasm = mksb(p0, "wasm", [128, NDT * DI], BF16)

    def prep_gateup(w_src, dst_dram):
        for it in range(NIT):
            dma_in("sp", wrow[:, :D], w_src[it * 128:(it + 1) * 128, :],
                   b_wrow, rbufs=[b_wrow])
            emit_quant(wrow[:, :D], b_wrow, wqr[:, :D], b_wqr, D, scr0)
            for g0 in range(0, NDT, 4):
                srcs = [wqr[:, (g0 + j) * 128:(g0 + j + 1) * 128] for j in range(4)]
                dst = wasm[:, g0 * DI + it * 128:].rearrange(
                    "p (x) -> p x", x=1) if False else None
                dstap = wasm[:].rearrange("p (t q) -> p t q", q=DI)[
                    :, g0:g0 + 4, it * 128:it * 128 + 128]
                n = 4
                tp_ps_, b_tp_ = tp_psL[_tpctr[0] % len(tp_psL)]
                _tpctr[0] += 1
                fns = [lambda s=s, i=i, tp_ps_=tp_ps_: nc.tensor.transpose(
                    out=tp_ps_[:, i * 128:(i + 1) * 128], in_=s, identity=idbf[:])
                    for i, s in enumerate(srcs)]
                plan.group("pe", fns, reads=[b_wqr, b_idbf], writes=[b_tp_])
                src_ps = tp_ps_[:, :512].rearrange("p (t q) -> p t q", q=128)
                plan.op("act", lambda dstap=dstap, src_ps=src_ps:
                        nc.scalar.activation(out=dstap, in_=src_ps, func=AF.Copy),
                        reads=[b_tp_], writes=[b_wasm])
        plan.dma("sp", lambda d=dst_dram: nc.sync.dma_start(out=d[:], in_=wasm[:]),
                 reads=[b_wasm], writes=[Buf("wdram")])

    prep_gateup(win["L0_ffn_w_gate"], wgq_d)
    prep_gateup(win["L0_ffn_w_up"], wuq_d)

    for dt in range(NDT):
        dma_in("sp", wrow[:, :DI], win["L0_ffn_w_down"][dt * 128:(dt + 1) * 128, :],
               b_wrow, rbufs=[b_wrow])
        emit_quant(wrow[:, :DI], b_wrow, wqr[:, :DI], b_wqr, DI, scr0)
        for g0 in range(0, NIT, 4):
            gn = min(4, NIT - g0)
            srcs = [wqr[:, (g0 + j) * 128:(g0 + j + 1) * 128] for j in range(gn)]
            dstap = wasm[:, :NIT * D].rearrange("p (t q) -> p t q", q=D)[
                :, g0:g0 + gn, dt * 128:dt * 128 + 128]
            fns = [lambda s=s, i=i: nc.tensor.transpose(
                out=tp_ps[:, i * 128:(i + 1) * 128], in_=s, identity=idbf[:])
                for i, s in enumerate(srcs)]
            plan.group("pe", fns, reads=[b_wqr, b_idbf], writes=[b_tp])
            src_ps = tp_ps[:, :gn * 128].rearrange("p (t q) -> p t q", q=128)
            plan.op("act", lambda dstap=dstap, src_ps=src_ps:
                    nc.scalar.activation(out=dstap, in_=src_ps, func=AF.Copy),
                    reads=[b_tp], writes=[b_wasm])
    plan.dma("sp", lambda: nc.sync.dma_start(out=wdq_d[:], in_=wasm[:, :NIT * D]),
             reads=[b_wasm], writes=[Buf("wdram")])

    for w_src, dstd in [(win["L0_attn_b_Wg"], wgT_d), (win["L0_attn_b_Wr"], wrT_d),
                        (win["L0_attn_b_Wo"], woT_d)]:
        for et in range(NDT):
            dma_in("sp", wrow[:, :D], w_src[et * 128:(et + 1) * 128, :],
                   b_wrow, rbufs=[b_wrow])
            plan.op("act", lambda: nc.scalar.activation(
                out=wqr[:, :D], in_=wrow[:, :D], func=AF.Copy),
                reads=[b_wrow], writes=[b_wqr])
            for g0 in range(0, NDT, 4):
                srcs = [wqr[:, (g0 + j) * 128:(g0 + j + 1) * 128] for j in range(4)]
                dstap = wasm[:, :NDT * D].rearrange("p (t q) -> p t q", q=D)[
                    :, g0:g0 + 4, et * 128:et * 128 + 128]
                tp_ps_, b_tp_ = tp_psL[_tpctr[0] % len(tp_psL)]
                _tpctr[0] += 1
                fns = [lambda s=s, i=i, tp_ps_=tp_ps_: nc.tensor.transpose(
                    out=tp_ps_[:, i * 128:(i + 1) * 128], in_=s, identity=idbf[:])
                    for i, s in enumerate(srcs)]
                plan.group("pe", fns, reads=[b_wqr, b_idbf], writes=[b_tp_])
                b_tp = b_tp_
                src_ps = tp_ps_[:, :512].rearrange("p (t q) -> p t q", q=128)
                plan.op("act", lambda dstap=dstap, src_ps=src_ps:
                        nc.scalar.activation(out=dstap, in_=src_ps, func=AF.Copy),
                        reads=[b_tp], writes=[b_wasm])
        plan.dma("sp", lambda d=dstd: nc.sync.dma_start(
            out=d[:], in_=wasm[:, :NDT * D]), reads=[b_wasm], writes=[Buf("wdram")])

    load_vec_to_cols(win["L0_attn_b_decay"].rearrange("(t p) -> t p", p=128),
                     pv[:, COL_A:COL_A + NDT], b_pv, NDT)
    plan.op("act", lambda: nc.scalar.activation(
        out=pv[:, COL_A:COL_A + NDT], in_=pv[:, COL_A:COL_A + NDT],
        func=AF.Sigmoid), reads=[b_pv], writes=[b_pv])
    plan.op("dve", lambda: nc.vector.tensor_scalar(
        out=pv[:, COL_OM:COL_OM + NDT], in0=pv[:, COL_A:COL_A + NDT],
        scalar1=-1.0, scalar2=1.0, op0=ALU.mult, op1=ALU.add),
        reads=[b_pv], writes=[b_pv])
    for nm, col in [("L0_attn_b_br", COL_BR), ("L0_attn_b_bo", COL_BO),
                    ("L0_attn_b_ln_w", COL_LNW), ("L0_attn_b_ln_b", COL_LNB)]:
        load_vec_to_cols(win[nm].rearrange("(t p) -> t p", p=128),
                         pv[:, col:col + NDT], b_pv, NDT)
    load_vec_to_cols(win["L0_attn_b_bg"].rearrange("(t p) -> t p", p=128),
                     bgv[:], b_bgv, NDT)

    phz, b_phz = mksb(p0, "phz", [128, 2 * NDT], F32)
    phi_, b_phi = mksb(p0, "phi_", [128, 2 * NDT], F32)
    lcv, b_lcv = mksb(p0, "lcv", [128, 2 * NDT], F32)
    tmpv, b_tmpv = mksb(p0, "tmpv", [128, 4 * NDT], F32)
    den, b_den = mksb(p0, "den", [128, 2 * NDT], F32)
    for src_d, dst_t, b_dst in [(win["L0_attn_a_phazor"], phz, b_phz),
                                (win["L0_attn_a_phazor_init"], phi_, b_phi)]:
        dma_in("sp", vstage[:, :256],
               src_d.rearrange("(t p) c -> t (p c)", p=128), b_vstage,
               rbufs=[b_vstage])
        for c in range(2):
            sv = vstage[:, :256].rearrange("t (p c) -> t p c", c=2)[:, :, c]
            plan.op("pe", lambda sv=sv, c=c: nc.tensor.transpose(
                out=tpf_ps[:, c * 128:c * 128 + NDT], in_=sv,
                identity=idf[0:NDT, 0:NDT]),
                reads=[b_vstage, b_idf], writes=[b_tpf])
        dsts = dst_t[:].rearrange("p (t c) -> p t c", c=2)
        plan.op("act", lambda dsts=dsts: nc.scalar.activation(
            out=dsts[:, :, 0], in_=tpf_ps[:, 0:NDT], func=AF.Copy),
            reads=[b_tpf], writes=[b_dst])
        plan.op("act", lambda dsts=dsts: nc.scalar.activation(
            out=dsts[:, :, 1], in_=tpf_ps[:, 128:128 + NDT], func=AF.Copy),
            reads=[b_tpf], writes=[b_dst])
    load_vec_to_cols(win["L0_attn_a_last_conv_re"].rearrange("(t p) -> t p", p=128),
                     lcv[:, 0:NDT], b_lcv, NDT)
    load_vec_to_cols(win["L0_attn_a_last_conv_im"].rearrange("(t p) -> t p", p=128),
                     lcv[:, NDT:2 * NDT], b_lcv, NDT)

    prs = phz[:].rearrange("p (t c) -> p t c", c=2)[:, :, 0]
    pis = phz[:].rearrange("p (t c) -> p t c", c=2)[:, :, 1]
    phr = phi_[:].rearrange("p (t c) -> p t c", c=2)[:, :, 0]
    phii = phi_[:].rearrange("p (t c) -> p t c", c=2)[:, :, 1]
    lre = lcv[:, 0:NDT]
    lim = lcv[:, NDT:2 * NDT]

    def vop(fn, reads, writes):
        plan.op("dve", fn, reads=reads, writes=writes)

    vop(lambda: nc.vector.tensor_tensor(out=tmpv[:, 0:NDT], in0=prs, in1=prs,
        op=ALU.mult), [b_phz], [b_tmpv])
    vop(lambda: nc.vector.tensor_tensor(out=tmpv[:, NDT:2 * NDT], in0=pis, in1=pis,
        op=ALU.mult), [b_phz], [b_tmpv])
    vop(lambda: nc.vector.tensor_tensor(out=tmpv[:, 0:NDT], in0=tmpv[:, 0:NDT],
        in1=tmpv[:, NDT:2 * NDT], op=ALU.add), [b_tmpv], [b_tmpv])
    plan.op("act", lambda: nc.scalar.activation(
        out=tmpv[:, NDT:2 * NDT], in_=tmpv[:, 0:NDT], func=AF.Sqrt),
        reads=[b_tmpv], writes=[b_tmpv])
    plan.op("act", lambda: nc.scalar.activation(
        out=pv[:, COL_R:COL_R + NDT], in_=tmpv[:, NDT:2 * NDT], func=AF.Exp,
        scale=-1.0), reads=[b_tmpv], writes=[b_pv])
    vop(lambda: nc.vector.reciprocal(out=tmpv[:, 0:NDT], in_=tmpv[:, NDT:2 * NDT]),
        [b_tmpv], [b_tmpv])
    vop(lambda: nc.vector.tensor_tensor(out=tmpv[:, 2 * NDT:3 * NDT], in0=prs,
        in1=tmpv[:, 0:NDT], op=ALU.mult), [b_phz, b_tmpv], [b_tmpv])
    vop(lambda: nc.vector.tensor_tensor(out=tmpv[:, 3 * NDT:4 * NDT], in0=pis,
        in1=tmpv[:, 0:NDT], op=ALU.mult), [b_phz, b_tmpv], [b_tmpv])

    vop(lambda: nc.vector.tensor_tensor(out=den[:, 0:NDT], in0=phr, in1=phr,
        op=ALU.mult), [b_phi], [b_den])
    vop(lambda: nc.vector.tensor_tensor(out=den[:, NDT:], in0=phii, in1=phii,
        op=ALU.mult), [b_phi], [b_den])
    vop(lambda: nc.vector.tensor_tensor(out=den[:, 0:NDT], in0=den[:, 0:NDT],
        in1=den[:, NDT:], op=ALU.add), [b_den], [b_den])
    vop(lambda: nc.vector.tensor_single_scalar(out=den[:, 0:NDT], in_=den[:, 0:NDT],
        scalar=1e-30, op=ALU.max), [b_den], [b_den])
    vop(lambda: nc.vector.reciprocal(out=den[:, 0:NDT], in_=den[:, 0:NDT]),
        [b_den], [b_den])
    vop(lambda: nc.vector.tensor_tensor(out=q0t[:, 0:NDT], in0=lre, in1=phr,
        op=ALU.mult), [b_lcv, b_phi], [b_q0t])
    vop(lambda: nc.vector.tensor_tensor(out=den[:, NDT:], in0=lim, in1=phii,
        op=ALU.mult), [b_lcv, b_phi], [b_den])
    vop(lambda: nc.vector.tensor_tensor(out=q0t[:, 0:NDT], in0=q0t[:, 0:NDT],
        in1=den[:, NDT:], op=ALU.add), [b_q0t, b_den], [b_q0t])
    vop(lambda: nc.vector.tensor_tensor(out=q0t[:, 0:NDT], in0=q0t[:, 0:NDT],
        in1=den[:, 0:NDT], op=ALU.mult), [b_q0t, b_den], [b_q0t])
    vop(lambda: nc.vector.tensor_tensor(out=q0t[:, NDT:], in0=lim, in1=phr,
        op=ALU.mult), [b_lcv, b_phi], [b_q0t])
    vop(lambda: nc.vector.tensor_tensor(out=den[:, NDT:], in0=lre, in1=phii,
        op=ALU.mult), [b_lcv, b_phi], [b_den])
    vop(lambda: nc.vector.tensor_tensor(out=q0t[:, NDT:], in0=q0t[:, NDT:],
        in1=den[:, NDT:], op=ALU.subtract), [b_q0t, b_den], [b_q0t])
    vop(lambda: nc.vector.tensor_tensor(out=q0t[:, NDT:], in0=q0t[:, NDT:],
        in1=den[:, 0:NDT], op=ALU.mult), [b_q0t, b_den], [b_q0t])
    vop(lambda: nc.vector.tensor_copy(out=carry[:], in_=q0t[:]), [b_q0t], [b_carry])

    for dt in range(NDT):
        base = dt * CH
        ur1 = tmpv[:, 2 * NDT + dt:2 * NDT + dt + 1]
        ui1 = tmpv[:, 3 * NDT + dt:3 * NDT + dt + 1]
        vop(lambda ur1=ur1, base=base: nc.vector.tensor_copy(
            out=tWr[:, base:base + 1], in_=ur1), [b_tmpv], [b_tWr])
        vop(lambda ui1=ui1, base=base: nc.vector.tensor_copy(
            out=tWi[:, base:base + 1], in_=ui1), [b_tmpv], [b_tWi])
        n = 1
        while n < CH:
            m = min(n, CH - n)
            anr = tWr[:, base + n - 1:base + n]
            ani = tWi[:, base + n - 1:base + n]
            s_r = tWr[:, base:base + m]
            s_i = tWi[:, base:base + m]
            d_r = tWr[:, base + n:base + n + m]
            d_i = tWi[:, base + n:base + n + m]
            t1 = tA[:, base:base + m]
            vop(lambda d=d_r, s=s_r, a=anr: nc.vector.tensor_scalar(
                out=d, in0=s, scalar1=a, scalar2=None, op0=ALU.mult),
                [b_tWr], [b_tWr])
            vop(lambda t=t1, s=s_i, a=ani: nc.vector.tensor_scalar(
                out=t, in0=s, scalar1=a, scalar2=None, op0=ALU.mult),
                [b_tWi], [b_tA])
            vop(lambda d=d_r, t=t1: nc.vector.tensor_tensor(
                out=d, in0=d, in1=t, op=ALU.subtract), [b_tWr, b_tA], [b_tWr])
            vop(lambda d=d_i, s=s_r, a=ani: nc.vector.tensor_scalar(
                out=d, in0=s, scalar1=a, scalar2=None, op0=ALU.mult),
                [b_tWr, b_tWi], [b_tWi])
            vop(lambda t=t1, s=s_i, a=anr: nc.vector.tensor_scalar(
                out=t, in0=s, scalar1=a, scalar2=None, op0=ALU.mult),
                [b_tWi, b_tWr], [b_tA])
            vop(lambda d=d_i, t=t1: nc.vector.tensor_tensor(
                out=d, in0=d, in1=t, op=ALU.add), [b_tWi, b_tA], [b_tWi])
            n += m
        vop(lambda dt=dt, base=base: nc.vector.tensor_copy(
            out=tUc[:, dt:dt + 1], in_=tWr[:, base + CH - 1:base + CH]),
            [b_tWr], [b_tUc])
        vop(lambda dt=dt, base=base: nc.vector.tensor_copy(
            out=tUc[:, NDT + dt:NDT + dt + 1], in_=tWi[:, base + CH - 1:base + CH]),
            [b_tWi], [b_tUc])
    for dt in range(NDT):
        base = dt * CH
        pr1 = phr[:, dt:dt + 1]
        pi1 = phii[:, dt:dt + 1]
        vop(lambda base=base, pr1=pr1: nc.vector.tensor_scalar(
            out=tA[:, base:base + CH], in0=tWr[:, base:base + CH], scalar1=pr1,
            scalar2=None, op0=ALU.mult), [b_tWr, b_phi], [b_tA])
        vop(lambda base=base, pi1=pi1: nc.vector.tensor_scalar(
            out=tB[:, base:base + CH], in0=tWi[:, base:base + CH], scalar1=pi1,
            scalar2=None, op0=ALU.mult), [b_tWi, b_phi], [b_tB])
        vop(lambda base=base: nc.vector.tensor_tensor(
            out=tA[:, base:base + CH], in0=tA[:, base:base + CH],
            in1=tB[:, base:base + CH], op=ALU.subtract), [b_tA, b_tB], [b_tA])
        vop(lambda base=base, pr1=pr1: nc.vector.tensor_scalar(
            out=tB[:, base:base + CH], in0=tWi[:, base:base + CH], scalar1=pr1,
            scalar2=None, op0=ALU.mult), [b_tWi, b_phi], [b_tB])
        vop(lambda base=base, pi1=pi1: nc.vector.scalar_tensor_tensor(
            out=tB[:, base:base + CH], in0=tWr[:, base:base + CH], scalar=pi1,
            in1=tB[:, base:base + CH], op0=ALU.mult, op1=ALU.add),
            [b_tWr, b_phi, b_tB], [b_tB])
        vop(lambda base=base: nc.vector.tensor_scalar_mul(
            out=tWi[:, base:base + CH], in0=tWi[:, base:base + CH], scalar1=-1.0),
            [b_tWi], [b_tWi])

    plan.barrier()
    p0.close()

    tpf_psL.append(mkps(p1, "tpf_ps1", [128, 512], F32))
    p1lsq, b_p1lsq = mksb(p1, "p1lsq", [128, D], F32)
    _sqdefault[0] = (p1lsq[:], b_p1lsq)
    xin1, b_xin1 = [], []
    for i in range(2):
        t, b = mksb(p1, f"p1xin{i}", [128, TPC * D], F32)
        xin1.append(t)
        b_xin1.append(b)
    xh4, b_xh4 = mksb(p1, "p1xh4", [128, TPC * D], F32)
    XT, b_XT = mksb(p1, "p1XT", [128, NDT * CH], F32)
    wRt, _bwr = mksb(p1, "p1wR", [128, 2 * CH], F32)
    wIt, _bwi = mksb(p1, "p1wI", [128, 2 * CH], F32)
    b_wRs = [Buf("p1wR0"), Buf("p1wR1")]
    b_wIs = [Buf("p1wI0"), Buf("p1wI1")]
    dRb, _bdr = mksb(p1, "p1dR", [128, NDT * CH], F32)
    dIb, _bdi = mksb(p1, "p1dI", [128, NDT * CH], F32)
    b_dRs = [Buf(f"p1dR{i}") for i in range(NDT)]
    b_dIs = [Buf(f"p1dI{i}") for i in range(NDT)]
    xo1, b_xo1 = mksb(p1, "p1xo", [128, TPC * D], F32)
    ncar, b_ncar = mksb(p1, "p1ncar", [128, 2 * NDT + 1], F32)

    for ch in range(NCH):
        xi, b_xi = xin1[ch % 2], b_xin1[ch % 2]
        src = x0_d[ch * CH:(ch + 1) * CH, :].rearrange("(a p) d -> p a d", p=128)
        dma_in("sp", xi[:].rearrange("p (a d) -> p a d", d=D), src, b_xi, rbufs=[b_xi], dur=6500)
        for a in range(TPC):
            emit_ln(xi[:, a * D:(a + 1) * D], b_xi,
                    xh4[:, a * D:(a + 1) * D], b_xh4)
        for dt in range(NDT):
            srcs = [xh4[:, a * D + dt * 128: a * D + dt * 128 + 128]
                    for a in range(TPC)]
            tp_group_f32(srcs, [b_xh4], XT[:, dt * CH:(dt + 1) * CH], b_XT)
        for dt in range(NDT):
            base = dt * CH
            wRb = wRt[:, (dt % 2) * CH:(dt % 2 + 1) * CH]
            wIb = wIt[:, (dt % 2) * CH:(dt % 2 + 1) * CH]
            b_wR = b_wRs[dt % 2]
            b_wI = b_wIs[dt % 2]
            rcol = pv[:, COL_R + dt:COL_R + dt + 1]
            r_rep = rcol.broadcast_to([128, CH])
            vop(lambda base=base, wRb=wRb: nc.vector.tensor_tensor(
                out=wRb, in0=XT[:, base:base + CH], in1=tWr[:, base:base + CH],
                op=ALU.mult), [b_XT, b_tWr], [b_wR])
            vop(lambda base=base, wIb=wIb: nc.vector.tensor_tensor(
                out=wIb, in0=XT[:, base:base + CH], in1=tWi[:, base:base + CH],
                op=ALU.mult), [b_XT, b_tWi], [b_wI])
            plan.op("dve", lambda base=base, r_rep=r_rep, dt=dt, wRb=wRb:
                nc.vector.tensor_tensor_scan(
                out=dRb[:, base:base + CH], data0=r_rep, data1=wRb,
                initial=carry[:, dt:dt + 1], op0=ALU.mult, op1=ALU.add),
                reads=[b_wR, b_carry, b_pv], writes=[b_dRs[dt]], dur=1300)
            plan.op("dve", lambda base=base, r_rep=r_rep, dt=dt, wIb=wIb:
                nc.vector.tensor_tensor_scan(
                out=dIb[:, base:base + CH], data0=r_rep, data1=wIb,
                initial=carry[:, NDT + dt:NDT + dt + 1], op0=ALU.mult, op1=ALU.add),
                reads=[b_wI, b_carry, b_pv], writes=[b_dIs[dt]], dur=1300)
            dl_r = dRb[:, base + CH - 1:base + CH]
            dl_i = dIb[:, base + CH - 1:base + CH]
            ucr = tUc[:, dt:dt + 1]
            uci = tUc[:, NDT + dt:NDT + dt + 1]
            tc = ncar[:, 2 * NDT:2 * NDT + 1]
            vop(lambda dt=dt, dl_r=dl_r, ucr=ucr: nc.vector.tensor_tensor(
                out=ncar[:, dt:dt + 1], in0=dl_r, in1=ucr, op=ALU.mult),
                [b_dRs[dt], b_tUc], [b_ncar])
            vop(lambda tc=tc, dl_i=dl_i, uci=uci: nc.vector.tensor_tensor(
                out=tc, in0=dl_i, in1=uci, op=ALU.mult), [b_dIs[dt], b_tUc], [b_ncar])
            vop(lambda dt=dt, tc=tc: nc.vector.tensor_tensor(
                out=ncar[:, dt:dt + 1], in0=ncar[:, dt:dt + 1], in1=tc,
                op=ALU.subtract), [b_ncar], [b_ncar])
            vop(lambda dt=dt, dl_r=dl_r, uci=uci: nc.vector.tensor_tensor(
                out=ncar[:, NDT + dt:NDT + dt + 1], in0=dl_r, in1=uci,
                op=ALU.mult), [b_dRs[dt], b_tUc], [b_ncar])
            vop(lambda tc=tc, dl_i=dl_i, ucr=ucr: nc.vector.tensor_tensor(
                out=tc, in0=dl_i, in1=ucr, op=ALU.mult), [b_dIs[dt], b_tUc], [b_ncar])
            vop(lambda dt=dt, tc=tc: nc.vector.tensor_tensor(
                out=ncar[:, NDT + dt:NDT + dt + 1],
                in0=ncar[:, NDT + dt:NDT + dt + 1], in1=tc, op=ALU.add),
                [b_ncar], [b_ncar])
            vop(lambda base=base: nc.vector.tensor_tensor(
                out=dRb[:, base:base + CH], in0=dRb[:, base:base + CH],
                in1=tA[:, base:base + CH], op=ALU.mult), [b_dRs[dt], b_tA], [b_dRs[dt]])
            vop(lambda base=base, wIb=wIb: nc.vector.tensor_tensor(
                out=wIb, in0=dIb[:, base:base + CH], in1=tB[:, base:base + CH],
                op=ALU.mult), [b_dIs[dt], b_tB], [b_wI])
            vop(lambda base=base, wIb=wIb: nc.vector.tensor_tensor(
                out=dRb[:, base:base + CH], in0=dRb[:, base:base + CH], in1=wIb,
                op=ALU.subtract), [b_dRs[dt], b_wI], [b_dRs[dt]])
        vop(lambda: nc.vector.tensor_copy(out=carry[:], in_=ncar[:, 0:2 * NDT]),
            [b_ncar], [b_carry])
        for a in range(TPC):
            for g0 in range(0, NDT, 4):
                srcs = [dRb[:, (g0 + j) * CH + a * 128:(g0 + j) * CH + a * 128 + 128]
                        for j in range(4)]
                dsl = xo1[:, a * D + g0 * 128: a * D + (g0 + 4) * 128]
                xsl = xi[:, a * D + g0 * 128: a * D + (g0 + 4) * 128]
                tp_group_f32_add(srcs, [b_dRs[g0 + j] for j in range(4)],
                                 dsl, b_xo1, xsl, b_xi)
        dst = xs1[ch * CH:(ch + 1) * CH, :].rearrange("(a p) d -> p a d", p=128)
        plan.dma("pool", lambda d=dst: nc.gpsimd.dma_start(out=d, in_=xo1[:].rearrange("p (a d) -> p a d", d=D)),
                 reads=[b_xo1], writes=[Buf("xs1w")])

    plan.barrier()
    tpf_psL.pop()
    p1.close()

    def emit_ffn(esx, tag, src_d, dst_d):
        wg_sb, b_wg = mksb(esx, f"{tag}wg", [128, NDT * DI], BF16)
        wu_sb, b_wu = mksb(esx, f"{tag}wu", [128, NDT * DI], BF16)
        wd_sb, b_wd = mksb(esx, f"{tag}wd", [128, NIT * D], BF16)
        dma_in("sp", wg_sb[:], wgq_d[:], b_wg, dur=16000)
        dma_in("sp", wu_sb[:], wuq_d[:], b_wu, dur=16000)
        dma_in("sp", wd_sb[:], wdq_d[:], b_wd, dur=16000)
        fxi, b_fxi = mksb(esx, f"{tag}xin", [128, TPC * D], F32)
        xq4, b_xq4 = mksb(esx, f"{tag}xq4", [128, TPC * D], BF16)
        xqT, b_xqT = mksb(esx, f"{tag}xqT", [128, NDT * 128], BF16)
        hb, _b_hb = mksb(esx, f"{tag}hb", [128, DI], F32)
        flsq, b_flsq = mksb(esx, f"{tag}lsq", [128, D], F32)
        fxiR, b_fxiR = mksb(esx, f"{tag}xiR", [128, D], F32)
        b_hb2 = [Buf(f"{tag}hb0"), Buf(f"{tag}hb1")]
        hq, b_hq = mksb(esx, f"{tag}hq", [128, DI], BF16)
        hqT, b_hqT = mksb(esx, f"{tag}hqT", [128, NIT * 128], BF16)
        WQ = DI // 2
        z_, b_z_ = mksb(esx, f"{tag}qz", [128, 1024], F32)
        a2_, b_a2_ = mksb(esx, f"{tag}qa2", [128, 1024], BF16)
        zs_, b_zs_ = mksb(esx, f"{tag}qzs", [128, 1024], BF16)
        msk_, b_msk_ = mksb(esx, f"{tag}qmsk", [128, 1024], mybir.dt.uint8)
        red_, b_red_ = mksb(esx, f"{tag}qred", [128, 64], F32)
        sc_, b_sc_ = mksb(esx, f"{tag}qsc", [128, 64], F32)
        rcp_, b_rcp_ = mksb(esx, f"{tag}qrcp", [128, 64], F32)
        scr = {"z": (z_, b_z_), "a2": (a2_, b_a2_), "zs": (zs_, b_zs_),
               "red": (red_, b_red_), "sc": (sc_, b_sc_), "rcp": (rcp_, b_rcp_),
               "msk": (msk_, b_msk_)}

        def scr_next():
            return scr
        gpsL = [mkps(esx, f"{tag}gps0", [128, 512], F32),
                mkps(esx, f"{tag}gps1", [128, 512], F32)]
        upsL = [mkps(esx, f"{tag}ups0", [128, 512], F32),
                mkps(esx, f"{tag}ups1", [128, 512], F32)]
        dps = [mkps(esx, f"{tag}dps0", [128, 512], F32),
               mkps(esx, f"{tag}dps1", [128, 512], F32)]
        ISL = []
        s0 = 0
        while s0 < DI:
            ISL.append((s0, min(512, DI - s0)))
            s0 += 512
        _sqdefault[0] = (flsq[:], b_flsq)
        for ch in range(NCH):
            src = src_d[ch * CH:(ch + 1) * CH, :].rearrange("(a p) d -> p a d", p=128)
            dma_in("sp", fxi[:].rearrange("p (a d) -> p a d", d=D), src, b_fxi,
                   rbufs=[b_fxi], dur=6500)
            for a in range(TPC):
                xh_ap = xh2t[:, (a % 2) * D:(a % 2 + 1) * D]
                b_xh_s = b_xh2[a % 2]
                emit_ln(fxi[:, a * D:(a + 1) * D], b_fxi, xh_ap, b_xh_s,
                        sq_over=(hb[:, 0:D], b_hb2[0]))
                emit_quant(xh_ap, b_xh_s, xq4[:, a * D:(a + 1) * D], b_xq4, D, scr_next())
            for a in range(TPC):
                for g0 in range(0, NDT, 4):
                    srcs = [xq4[:, a * D + (g0 + j) * 128: a * D + (g0 + j + 1) * 128]
                            for j in range(4)]
                    tp_group_bf(srcs, [b_xq4],
                                xqT[:, g0 * 128:(g0 + 4) * 128], b_xqT)
                for si, (s0, sw) in enumerate(ISL):
                    gps, b_gps = gpsL[si % 2]
                    ups, b_ups = upsL[si % 2]
                    for (ps, b_ps, wsb, b_wsb) in ((gps, b_gps, wg_sb, b_wg),
                                                   (ups, b_ups, wu_sb, b_wu)):
                        fns = [lambda dt=dt, ps=ps, wsb=wsb, s0=s0, sw=sw:
                               nc.tensor.matmul(
                                   out=ps[:, :sw],
                                   lhsT=xqT[:, dt * 128:(dt + 1) * 128],
                                   rhs=wsb[:, dt * DI + s0: dt * DI + s0 + sw],
                                   start=(dt == 0), stop=(dt == NDT - 1))
                               for dt in range(NDT)]
                        plan.group("pe", fns, reads=[b_xqT, b_wsb], writes=[b_ps],
                                   dur=2000)
                    hbs = hb[:, s0:s0 + sw]
                    halves = sorted({s0 // (DI // 2), (s0 + sw - 1) // (DI // 2)})
                    wr = [b_hb2[h] for h in halves]
                    plan.op("act", lambda sw=sw, gps=gps, hbs=hbs: nc.scalar.activation(
                        out=hbs, in_=gps[:, :sw], func=AF.Silu),
                        reads=[b_gps], writes=wr, dur=600)
                    plan.op("dve", lambda sw=sw, ups=ups, hbs=hbs:
                            nc.vector.tensor_tensor(
                        out=hbs, in0=hbs, in1=ups[:, :sw],
                        op=ALU.mult), reads=[b_ups] + wr, writes=wr, dur=700)
                    if si == 1:
                        emit_quant(hb[:, 0:1024], b_hb2[0],
                                   hq[:, 0:1024], b_hq, 1024, scr)
                    elif si == 3:
                        emit_quant(hb[:, 1024:2048], b_hb2,
                                   hq[:, 1024:2048], b_hq, 1024, scr)
                    elif si == len(ISL) - 1:
                        emit_quant(hb[:, 2048:DI], b_hb2[1],
                                   hq[:, 2048:DI], b_hq, DI - 2048, scr)
                for g0 in range(0, NIT, 4):
                    gn = min(4, NIT - g0)
                    srcs = [hq[:, (g0 + j) * 128:(g0 + j + 1) * 128]
                            for j in range(gn)]
                    tp_group_bf(srcs, [b_hq],
                                hqT[:, g0 * 128:(g0 + gn) * 128], b_hqT)
                rsrc = src_d[ch * CH + a * 128: ch * CH + (a + 1) * 128, :]
                dma_in("sp", fxiR[:], rsrc, b_fxiR, rbufs=[b_fxiR], dur=2500)
                for half in range(2):
                    ps, b_ps = dps[half]
                    fns = [lambda it=it, ps=ps, half=half: nc.tensor.matmul(
                        out=ps[:, :512],
                        lhsT=hqT[:, it * 128:(it + 1) * 128],
                        rhs=wd_sb[:, it * D + half * 512: it * D + half * 512 + 512],
                        start=(it == 0), stop=(it == NIT - 1))
                        for it in range(NIT)]
                    plan.group("pe", fns, reads=[b_hqT, b_wd], writes=[b_ps])
                    plan.op("dve", lambda a=a, half=half, ps=ps:
                            nc.vector.tensor_tensor(
                                out=fxiR[:, half * 512:(half + 1) * 512],
                                in0=ps[:, :512],
                                in1=fxiR[:, half * 512:(half + 1) * 512],
                                op=ALU.add), reads=[b_ps, b_fxiR],
                            writes=[b_fxiR])
                dstt = dst_d[ch * CH + a * 128: ch * CH + (a + 1) * 128, :]
                plan.dma("pool", lambda d=dstt: nc.gpsimd.dma_start(
                    out=d, in_=fxiR[:]),
                    reads=[b_fxiR], writes=[Buf(f"{tag}out")], dur=2500)

    p2 = ExitStack()
    emit_ffn(p2, "p2", xs1, xs2)
    plan.barrier()
    p2.close()

    p3 = ExitStack()
    wgT_sb, b_wgT = mksb(p3, "p3wgT", [128, NDT * D], BF16)
    wrT_sb, b_wrT = mksb(p3, "p3wrT", [128, NDT * D], BF16)
    woT_sb, b_woT = mksb(p3, "p3woT", [128, NDT * D], BF16)
    dma_in("sp", wgT_sb[:], wgT_d[:], b_wgT)
    dma_in("sp", wrT_sb[:], wrT_d[:], b_wrT)
    dma_in("sp", woT_sb[:], woT_d[:], b_woT)
    xi3, b_xi3 = mksb(p3, "p3xin", [128, TPC * D], F32)
    XT3, b_XT3 = mksb(p3, "p3XT", [128, NDT * CH], F32)
    G3, b_G3 = mksb(p3, "p3G", [128, NDT * CH], F32)
    V3t, b_V3t = mksb(p3, "p3V", [128, 2 * CH], F32)
    b_V3s = [Buf("V3s0"), Buf("V3s1")]
    HxA, _bA = mksb(p3, "p3HxA", [128, NDT * (CH + 1)], BF16)
    HxB, _bB = mksb(p3, "p3HxB", [128, NDT * (CH + 1)], BF16)
    Hf, _bH = mksb(p3, "p3Hf", [128, NDT * CH], F32)
    b_HxA = [Buf(f"HxA{i}") for i in range(NDT)]
    b_HxB = [Buf(f"HxB{i}") for i in range(NDT)]
    b_Hfb = [Buf(f"Hf{i}") for i in range(NDT)]
    hcar, b_hcar = mksb(p3, "p3hcar", [128, NDT], F32)
    Nrm4, b_Nrm4 = mksb(p3, "p3Nrm4", [128, TPC * D], BF16)
    YT3, b_YT3 = mksb(p3, "p3YT", [128, NDT * CH], BF16)
    OT3, b_OT3 = mksb(p3, "p3OT", [128, NDT * CH], F32)
    XTb = OT3[:].bitcast(BF16)[:, :NDT * CH]
    b_XTb = b_OT3
    o3, b_o3 = mksb(p3, "p3o", [128, TPC * D], F32)
    pps = [mkps(p3, "p3ps0", [128, CH], F32), mkps(p3, "p3ps1", [128, CH], F32)]
    tp_psL.append(mkps(p3, "tp_ps3b", [128, 512], BF16))
    tpf_psL.append(mkps(p3, "tpf_ps3b", [128, 512], F32))
    lsq, b_lsqep = mksb(p3, "p3lsq", [128, D], F32)
    _sqdefault[0] = (xh2t[:, 0:D], b_xh2[0])

    plan.op("dve", lambda: nc.vector.memset(hcar[:], 0.0), writes=[b_hcar])

    for ch in range(NCH):
        src = xs2[ch * CH:(ch + 1) * CH, :].rearrange("(a p) d -> p a d", p=128)
        dma_in("sp", xi3[:].rearrange("p (a d) -> p a d", d=D), src, b_xi3, rbufs=[b_xi3], dur=6500)
        XT3v = XT3[:].rearrange("p (t q) -> p t q", q=CH)
        XTbv = XTb.rearrange("p (t q) -> p t q", q=CH)
        for a in range(TPC):
            xh_ap = xh2t[:, (a % 2) * D:(a % 2 + 1) * D]
            b_xh_s = b_xh2[a % 2]
            emit_ln(xi3[:, a * D:(a + 1) * D], b_xi3, xh_ap, b_xh_s)
            for g0 in range(0, NDT, 4):
                srcs = [xh_ap[:, (g0 + j) * 128:(g0 + j + 1) * 128]
                        for j in range(4)]
                tpf_, b_tpf_ = tpf_psL[_tpctr[0] % len(tpf_psL)]
                _tpctr[0] += 1
                fns = [lambda s=s, i=i, tpf_=tpf_: nc.tensor.transpose(
                    out=tpf_[:, i * 128:(i + 1) * 128], in_=s, identity=idf[:])
                    for i, s in enumerate(srcs)]
                plan.group("pe", fns, reads=[b_xh_s, b_idf], writes=[b_tpf_])
                b_tpf = b_tpf_
                ps3 = tpf_[:, :512].rearrange("p (t q) -> p t q", q=128)
                plan.op("act", lambda a=a, g0=g0, ps3=ps3: nc.scalar.activation(
                    out=XT3v[:, g0:g0 + 4, a * 128:a * 128 + 128], in_=ps3,
                    func=AF.Copy), reads=[b_tpf], writes=[b_XT3])
                plan.op("act", lambda a=a, g0=g0, ps3=ps3: nc.scalar.activation(
                    out=XTbv[:, g0:g0 + 4, a * 128:a * 128 + 128], in_=ps3,
                    func=AF.Copy), reads=[b_tpf], writes=[b_XTb])
        for et in range(NDT):
            ps, b_ps = pps[et % 2]
            fns = [lambda dt=dt, et=et, ps=ps: nc.tensor.matmul(
                out=ps[:, :CH],
                lhsT=wgT_sb[:, dt * D + et * 128: dt * D + et * 128 + 128],
                rhs=XTb[:, dt * CH:(dt + 1) * CH],
                start=(dt == 0), stop=(dt == NDT - 1)) for dt in range(NDT)]
            plan.group("pe", fns, reads=[b_XTb, b_wgT], writes=[b_ps])
            plan.op("act", lambda et=et, ps=ps: nc.scalar.activation(
                out=G3[:, et * CH:(et + 1) * CH], in_=ps[:, :CH], func=AF.Sigmoid,
                bias=bgv[:, et:et + 1]), reads=[b_ps, b_bgv], writes=[b_G3])
            vop(lambda et=et: nc.vector.tensor_scalar(
                out=G3[:, et * CH:(et + 1) * CH], in0=G3[:, et * CH:(et + 1) * CH],
                scalar1=pv[:, COL_OM + et:COL_OM + et + 1], scalar2=None,
                op0=ALU.mult), [b_G3, b_pv], [b_G3])
        for dt in range(NDT):
            vop(lambda dt=dt: nc.vector.tensor_scalar(
                out=XT3[:, dt * CH:(dt + 1) * CH], in0=XT3[:, dt * CH:(dt + 1) * CH],
                scalar1=pv[:, COL_BR + dt:COL_BR + dt + 1], scalar2=None,
                op0=ALU.subtract), [b_XT3, b_pv], [b_XT3])
        plan.op("dve", lambda: nc.vector.memset(HxA[:], 0.0), reads=[],
                writes=list(b_HxA), dur=900)
        for dt in range(NDT):
            vop(lambda dt=dt: nc.vector.tensor_copy(
                out=HxA[:, dt * (CH + 1):dt * (CH + 1) + 1], in_=hcar[:, dt:dt + 1]),
                [b_hcar], [b_HxA[dt]])
        cur, b_cur, nxt, b_nxt = HxA, b_HxA, HxB, b_HxB
        for k in range(K):
            last = (k == K - 1)
            for n in range(NDT):
                ps, b_ps = pps[n % 2]
                fns = [lambda dt=dt, n=n, ps=ps, cur=cur: nc.tensor.matmul(
                    out=ps[:, :CH],
                    lhsT=wrT_sb[:, dt * D + n * 128: dt * D + n * 128 + 128],
                    rhs=cur[:, dt * (CH + 1): dt * (CH + 1) + CH],
                    start=(dt == 0), stop=(dt == NDT - 1)) for dt in range(NDT)]
                plan.group("pe", fns, reads=list(b_cur) + [b_wrT], writes=[b_ps],
                           dur=1900)
                V3 = V3t[:, (n % 2) * CH:(n % 2 + 1) * CH]
                b_V3 = b_V3s[n % 2]
                plan.op("dve", lambda n=n, ps=ps, V3=V3: nc.vector.scalar_tensor_tensor(
                    out=V3, in0=ps[:, :CH], scalar=-1.0,
                    in1=XT3[:, n * CH:(n + 1) * CH], op0=ALU.mult, op1=ALU.add),
                    reads=[b_ps, b_XT3], writes=[b_V3], dur=700)
                plan.op("dve", lambda n=n, V3=V3: nc.vector.tensor_tensor(
                    out=V3, in0=V3, in1=G3[:, n * CH:(n + 1) * CH],
                    op=ALU.mult), reads=[b_V3, b_G3], writes=[b_V3], dur=700)
                a_rep = pv[:, COL_A + n:COL_A + n + 1].broadcast_to([128, CH])
                if last:
                    plan.op("dve", lambda n=n, a_rep=a_rep, V3=V3: nc.vector.tensor_tensor_scan(
                        out=Hf[:, n * CH:(n + 1) * CH], data0=a_rep, data1=V3,
                        initial=hcar[:, n:n + 1], op0=ALU.mult, op1=ALU.add),
                        reads=[b_V3, b_hcar, b_pv], writes=[b_Hfb[n]], dur=1400)
                else:
                    plan.op("dve", lambda n=n, a_rep=a_rep, nxt=nxt, V3=V3: nc.vector.tensor_tensor_scan(
                        out=nxt[:, n * (CH + 1) + 1: n * (CH + 1) + 1 + CH],
                        data0=a_rep, data1=V3,
                        initial=hcar[:, n:n + 1], op0=ALU.mult, op1=ALU.add),
                        reads=[b_V3, b_hcar, b_pv], writes=[b_nxt[n]], dur=1400)
            if not last:
                if k == 0:
                    for dt in range(NDT):
                        vop(lambda dt=dt, nxt=nxt: nc.vector.tensor_copy(
                            out=nxt[:, dt * (CH + 1):dt * (CH + 1) + 1],
                            in_=hcar[:, dt:dt + 1]), [b_hcar], [b_nxt[dt]])
                cur, b_cur, nxt, b_nxt = nxt, b_nxt, cur, b_cur
        for dt in range(NDT):
            vop(lambda dt=dt: nc.vector.tensor_copy(
                out=hcar[:, dt:dt + 1], in_=Hf[:, dt * CH + CH - 1: dt * CH + CH]),
                [b_Hfb[dt]], [b_hcar])
        for a in range(TPC):
            for g0 in range(0, NDT, 4):
                srcs = [Hf[:, (g0 + j) * CH + a * 128:(g0 + j) * CH + a * 128 + 128]
                        for j in range(4)]
                tp_group_f32(srcs, [b_Hfb[g0 + j] for j in range(4)],
                             lsq[:, g0 * 128:(g0 + 4) * 128], b_lsqep)
            emit_ln_noscratch(lsq[:, 0:D], b_lsqep, Nrm4[:, a * D:(a + 1) * D], b_Nrm4)
        for dt in range(NDT):
            srcs = [Nrm4[:, a * D + dt * 128: a * D + dt * 128 + 128]
                    for a in range(TPC)]
            tp_group_bf(srcs, [b_Nrm4], YT3[:, dt * CH:(dt + 1) * CH], b_YT3)
        for dt in range(NDT):
            vop(lambda dt=dt: nc.vector.tensor_scalar(
                out=YT3[:, dt * CH:(dt + 1) * CH], in0=YT3[:, dt * CH:(dt + 1) * CH],
                scalar1=pv[:, COL_LNW + dt:COL_LNW + dt + 1],
                scalar2=pv[:, COL_LNB + dt:COL_LNB + dt + 1],
                op0=ALU.mult, op1=ALU.add), [b_YT3, b_pv], [b_YT3])
        for et in range(NDT):
            ps, b_ps = pps[et % 2]
            fns = [lambda dt=dt, et=et, ps=ps: nc.tensor.matmul(
                out=ps[:, :CH],
                lhsT=woT_sb[:, dt * D + et * 128: dt * D + et * 128 + 128],
                rhs=YT3[:, dt * CH:(dt + 1) * CH],
                start=(dt == 0), stop=(dt == NDT - 1)) for dt in range(NDT)]
            plan.group("pe", fns, reads=[b_YT3, b_woT], writes=[b_ps])
            vop(lambda et=et, ps=ps: nc.vector.tensor_scalar(
                out=OT3[:, et * CH:(et + 1) * CH], in0=ps[:, :CH],
                scalar1=pv[:, COL_BO + et:COL_BO + et + 1], scalar2=None,
                op0=ALU.add), [b_ps, b_pv], [b_OT3])
        for a in range(TPC):
            for g0 in range(0, NDT, 4):
                srcs = [OT3[:, (g0 + j) * CH + a * 128:(g0 + j) * CH + a * 128 + 128]
                        for j in range(4)]
                dsl = o3[:, a * D + g0 * 128: a * D + (g0 + 4) * 128]
                xsl = xi3[:, a * D + g0 * 128: a * D + (g0 + 4) * 128]
                tp_group_f32_add(srcs, [b_OT3], dsl, b_o3, xsl, b_xi3)
        dst = xs3[ch * CH:(ch + 1) * CH, :].rearrange("(a p) d -> p a d", p=128)
        plan.dma("pool", lambda d=dst: nc.gpsimd.dma_start(out=d, in_=o3[:].rearrange("p (a d) -> p a d", d=D)),
                 reads=[b_o3], writes=[Buf("xs3w")])

    plan.barrier()
    tp_psL.pop()
    tpf_psL.pop()
    p3.close()

    p4 = ExitStack()
    emit_ffn(p4, "p4", xs3, out_d)
    plan.barrier()
    p4.close()

    plan.emit()

    with nc.Block() as block:
        @block.sync
        def _(eng):
            plan._run_engine("sp")

        @block.scalar
        def _(eng):
            plan._run_engine("act")

        @block.vector
        def _(eng):
            plan._run_engine("dve")

        @block.tensor
        def _(eng):
            plan._run_engine("pe")

        @block.gpsimd
        def _(eng):
            plan._run_engine("pool")
            plan.finish_waits("pool")

    outer.close()
    plan._es.close()
    return nc


def kernel(**inputs):
    import ml_dtypes
    from concourse.bass_utils import run_bass_kernel_spmd

    x = np.ascontiguousarray(np.asarray(inputs["L0_attn_a_x"], np.float32))
    B, L, _ = x.shape
    nc = build_nc(L=L)
    iden_bf = np.eye(128, dtype=ml_dtypes.bfloat16)
    iden_f32 = np.eye(128, dtype=np.float32)
    wnames = [k for k in inputs.keys() if k != "L0_attn_a_x"]
    in_maps = []
    for c in range(B):
        m = {nm: np.ascontiguousarray(np.asarray(inputs[nm], np.float32))
             for nm in wnames}
        m["x0"] = np.ascontiguousarray(x[c])
        m["iden_bf"] = iden_bf
        m["iden_f32"] = iden_f32
        in_maps.append(m)
    res = run_bass_kernel_spmd(nc, in_maps, core_ids=list(range(B)))
    return np.stack([np.asarray(r["out"], np.float32) for r in res.results], axis=0)
